# version 25
# speedup vs baseline: 1.1785x; 1.0039x over previous
import math
import numpy as np
import concourse.bass as bass
import concourse.mybir as mybir
from concourse.bass_utils import run_bass_kernel_spmd

F32 = mybir.dt.float32
BF16 = mybir.dt.bfloat16
U32 = mybir.dt.uint32
I32 = mybir.dt.int32
AF = mybir.ActivationFunctionType
ALU = mybir.AluOpType

NCORES = 8
ROWS = 4
T = 2048
D = 1024
NCH = 8
H = 16
DH = 64
FFN = 4096
KSEL = 512
SHIFT_MARGIN = 60.0


STREAMS = ("tensor", "scalar", "vector", "gpsimd", "sync")


class Buf:
    __slots__ = ("name", "writer", "readers")

    def __init__(self, name):
        self.name = name
        self.writer = None
        self.readers = []


class Sched:

    DMA_POOL = 24

    def __init__(self, nc, stack):
        self.nc = nc
        self.engines = {"tensor": nc.tensor, "scalar": nc.scalar, "vector": nc.vector,
                        "gpsimd": nc.gpsimd, "sync": nc.sync}
        self.queues = list(STREAMS)
        for s in ("sync", "gpsimd", "scalar"):
            for i in range(self.DMA_POOL):
                self.queues.append(f"dma_{s}_{i}")
        self.sems = {q: stack.enter_context(nc.semaphore("sem_" + q)) for q in self.queues}
        self.cnt = {q: 0 for q in self.queues}
        self.dma_idx = {s: 0 for s in STREAMS}
        self.waited = {s: {} for s in STREAMS}
        self._bufs = {}
        self.fence_base = {}
        self.debug_serialize = False
        self.serialize_same = True

    def fence(self):
        self.fence_base = {q: c for q, c in self.cnt.items() if c > 0}

    def buf(self, name):
        b = self._bufs.get(name)
        if b is None:
            b = Buf(name)
            self._bufs[name] = b
        return b

    def op(self, stream, fn, reads=(), writes=(), deps=(), dma=False):
        if dma:
            i = self.dma_idx[stream]
            self.dma_idx[stream] = (i + 1) % self.DMA_POOL
            q = f"dma_{stream}_{i}"
            inc = 16
        else:
            q = stream
            inc = 1
        self.cnt[q] += inc
        tok = (q, self.cnt[q])
        depmap = {}

        def add(d):
            if d is None:
                return
            dq, dc = d
            if dq == stream and not self.debug_serialize:
                if stream == "tensor" or not self.serialize_same:
                    return
            if dc > depmap.get(dq, 0):
                depmap[dq] = dc

        if dma and self.cnt[q] > 16:
            depmap[q] = self.cnt[q] - 16
        for r in reads:
            add(r.writer)
            if stream in ("vector", "scalar") and r.name.startswith("ps"):
                other = "scalar" if stream == "vector" else "vector"
                for dq, dc in r.readers:
                    if dq == other:
                        add((dq, dc))
        for w in writes:
            add(w.writer)
            for rd in w.readers:
                add(rd)
        for d in deps:
            add(d)
        for fq, fc in self.fence_base.items():
            add((fq, fc))
        eng = self.engines[stream]
        wt = self.waited[stream]
        for dq, dc in sorted(depmap.items()):
            if wt.get(dq, 0) >= dc:
                continue
            eng.wait_ge(self.sems[dq], dc)
            wt[dq] = dc
        try:
            inst = fn(eng)
        except Exception:
            self.cnt[q] -= inc
            raise
        inst.then_inc(self.sems[q], inc)
        for r in reads:
            r.readers.append(tok)
        for w in writes:
            w.writer = tok
            w.readers = []
        if self.debug_serialize:
            self.fence()
        return tok

    def finish(self):
        wt = self.waited["sync"]
        for q in self.queues:
            c = self.cnt[q]
            if q.startswith("dma_") and c > 0 and wt.get(q, 0) < c:
                self.nc.sync.wait_ge(self.sems[q], c)
                wt[q] = c



def make_constants():
    c = {}
    c["c_ones_col"] = np.ones((128, 1), np.float32)
    c["c_ones_col_f32"] = np.ones((128, 1), np.float32)
    c["c_ones_row_f32"] = np.ones((1, 128), np.float32)
    c["c_ones_row"] = np.ones((1, 128), np.float32)
    c["c_ident"] = np.eye(128, dtype=np.float32)
    k = np.arange(128)[:, None]
    q = np.arange(128)[None, :]
    c["c_mask_add"] = np.where(k <= q, 0.0, -1e30).astype(np.float32)
    c["c_mask_mul"] = (k <= q).astype(np.float32)
    c["c_tril"] = (k <= q).astype(np.float32)
    tri = np.zeros((64, 64), np.float32)
    for b in range(4):
        for i in range(16):
            for j in range(16):
                if i < j:
                    tri[b * 16 + i, b * 16 + j] = 1.0
    c["c_triu16"] = tri
    inv = (1.0 / 10000.0 ** (np.arange(32, dtype=np.float64) / 32.0))
    ang = np.arange(T, dtype=np.float64)[None, :] * inv[:, None]
    cosv = np.cos(ang).astype(np.float32)
    sinv = np.sin(ang).astype(np.float32)
    c["c_rope_cs"] = np.concatenate([cosv, cosv, cosv, cosv], 0)
    c["c_rope_ss"] = np.concatenate([-sinv, sinv, -sinv, sinv], 0)
    c["c_invfreq4"] = np.concatenate([inv, inv, inv, inv]).astype(np.float32)[None, :]
    c["c_invfreq4_f32"] = c["c_invfreq4"].copy()
    c["c_ropesign"] = np.concatenate([-np.ones(32), np.ones(32), -np.ones(32), np.ones(32)]).astype(np.float32)[:, None]
    ind = np.zeros((1024, 16), np.float32)
    for d_ in range(1024):
        ind[d_, d_ // 64] = 1.0
    c["c_headind"] = ind.reshape(8, 128, 16).transpose(1, 0, 2).reshape(128, 128).copy()
    c["c_iota_row128"] = np.broadcast_to(np.arange(128, dtype=np.float32)[None, :],
                                          (128, 128)).copy()
    it = (np.arange(16)[None, :] * 128 + np.arange(128)[:, None]).astype(np.float32)
    c["c_iota_f"] = it
    c["c_iota_u"] = it.astype(np.uint32)
    c["c_m8"] = (np.eye(8) - 1.0 / 8.0).astype(np.float32)
    p8 = np.zeros((8, 64), np.float32)
    q8 = np.zeros((8, 64), np.float32)
    for e_ in range(8):
        for r_ in range(8):
            p8[r_, e_ * 8 + r_] = 2.0
            q8[e_, e_ * 8 + r_] = 1.0
    c["c_p8"] = p8
    c["c_q8"] = q8
    cs = np.zeros((128, 16 * 16), np.float32)
    for h_ in range(16):
        cs[:, h_ * 16 + h_] = 1.0
    c["c_colsel"] = cs
    sel = np.zeros((16, 16 * 64), np.float32)
    for h_ in range(16):
        sel[h_, h_ * 64:(h_ + 1) * 64] = 1.0
    c["c_sel16"] = sel
    return c


CONSTS = make_constants()

CONST_DT = {
    "c_ones_col": BF16, "c_ones_row": BF16, "c_ident": F32,
    "c_mask_add": F32, "c_mask_mul": F32, "c_tril": BF16, "c_triu16": BF16,
    "c_rope_cs": BF16, "c_rope_ss": BF16, "c_invfreq4": BF16, "c_invfreq4_f32": F32, "c_ropesign": F32,
    "c_headind": BF16, "c_iota_f": F32, "c_iota_u": U32, "c_iota_row128": F32, "c_sel16": BF16, "c_colsel": BF16, "c_ones_col_f32": F32, "c_ones_row_f32": F32,
    "c_m8": BF16, "c_p8": BF16, "c_q8": BF16,
}


class G:
    def __init__(self, nc, s):
        self.nc = nc
        self.s = s
        self.t = {}
        self.stack = None
        self._used_names = set()

    def dram_in(self, name, shape, dtype=F32):
        self.t[name] = self.nc.dram_tensor(name, list(shape), dtype, kind="ExternalInput").ap()
        return self.t[name]

    def dram_out(self, name, shape, dtype=F32):
        self.t[name] = self.nc.dram_tensor(name, list(shape), dtype, kind="ExternalOutput").ap()
        return self.t[name]

    def dram_tmp(self, name, shape, dtype=F32):
        self.t[name] = self.nc.dram_tensor(name, list(shape), dtype).ap()
        return self.t[name]

    _uid = [0]

    def sb(self, name, shape, dtype):
        nm = name
        if name in self._used_names:
            G._uid[0] += 1
            nm = f"{name}__{G._uid[0]}"
        self._used_names.add(name)
        ap = self.stack.enter_context(self.nc.sbuf_tensor(nm, list(shape), dtype))
        self.t[name] = ap
        return ap

    def ps_pool(self):
        self.pall = self.stack.enter_context(self.nc.psum_tensor("pall", [128, 4096], F32))
        return self.pall

    def pb(self, k, p0=0, p1=128, n=512):
        return self.pall[p0:p1, k * 512: k * 512 + n]

    def pbB(self, k0, k1=None):
        if k1 is None:
            k1 = k0
        return [self.s.buf(f"ps{k}") for k in range(k0, k1 + 1)]

    def load_consts(self, names):
        self.const_names = list(names)
        for cname in names:
            arr = CONSTS[cname]
            dt = CONST_DT[cname]
            self.dram_in(cname + "_hbm", arr.shape, U32 if arr.dtype == np.uint32 else F32)
            tile = self.sb(cname, arr.shape, dt)
            hbm = self.t[cname + "_hbm"]
            self.s.op("gpsimd",
                      lambda g, tile=tile, hbm=hbm: g.dma_start(out=tile[:, :], in_=hbm[:, :]),
                      writes=[self.s.buf(cname)], dma=True)

    def const_inputs(self):
        return {cname + "_hbm": CONSTS[cname] for cname in self.const_names}

    def fconst(self, val):
        name = "c_f%r" % (val,)
        if name not in self.t:
            ap = self.top_stack.enter_context(
                self.nc.sbuf_tensor(name, [128, 1], F32, side="right"))
            self.t[name] = ap
            self.s.op("gpsimd", lambda e, a=ap, v=float(val): e.memset(a[:, :], v),
                      writes=[self.s.buf("c_fconsts")])
        return self.t[name]



def op_dma(g, stream, out_ap, in_ap, reads=(), writes=(), deps=()):
    return g.s.op(stream, lambda e, o=out_ap, i=in_ap: e.dma_start(out=o, in_=i),
                  reads=reads, writes=writes, deps=deps, dma=True)


def op_mm(g, out_ap, lhsT, rhs, reads=(), writes=(), start=True, stop=True):
    return g.s.op("tensor",
                  lambda e, o=out_ap, l=lhsT, r=rhs, st=start, sp=stop:
                      e.matmul(o, l, r, start=st, stop=sp),
                  reads=reads, writes=writes)


def op_mm_group(g, items, reads=(), writes=()):
    def fn(e, items=tuple(items)):
        inst = None
        for (o, l, r, st, sp) in items:
            inst = e.matmul(o, l, r, start=st, stop=sp)
        return inst
    return g.s.op("tensor", fn, reads=reads, writes=writes)


def op_act(g, out_ap, in_ap, func, reads=(), writes=(), scale=None, bias=None,
           accum=None):
    kw = {}
    reads = list(reads)
    if scale is not None:
        kw["scale"] = scale
    if bias is not None:
        if isinstance(bias, (int, float)):
            bias = g.fconst(bias)[0:in_ap.shape[0], 0:1]
            reads.append(g.s.buf("c_fconsts"))
        kw["bias"] = bias
    if accum is not None:
        kw["accum_out"] = accum
    return g.s.op("scalar",
                  lambda e, o=out_ap, i=in_ap, f=func, kw=kw:
                      e.activation(out=o, in_=i, func=f, **kw),
                  reads=reads, writes=writes)


def op_tt(g, out_ap, a, b, op, reads=(), writes=(), stream="vector"):
    return g.s.op(stream,
                  lambda e, o=out_ap, x=a, y=b, p=op: e.tensor_tensor(out=o, in0=x, in1=y, op=p),
                  reads=reads, writes=writes)


def op_ts(g, out_ap, a, s1, s2, op0, op1=None, reads=(), writes=(), accum=None,
          stream="vector"):
    kw = {"accum_out": accum} if accum is not None else {}
    if op1 is None:
        return g.s.op(stream,
                      lambda e, o=out_ap, x=a, v1=s1, p0=op0, kw=kw:
                          e.tensor_scalar(o, x, v1, None, op0=p0, **kw),
                      reads=reads, writes=writes)
    return g.s.op(stream,
                  lambda e, o=out_ap, x=a, v1=s1, v2=s2, p0=op0, p1=op1, kw=kw:
                      e.tensor_scalar(o, x, v1, v2, op0=p0, op1=p1, **kw),
                  reads=reads, writes=writes)


def op_stt(g, out_ap, in0, scalar, in1, op0, op1, reads=(), writes=()):
    return g.s.op("vector",
                  lambda e, o=out_ap, a=in0, sc=scalar, b=in1, p0=op0, p1=op1:
                      e.scalar_tensor_tensor(out=o, in0=a, scalar=sc, in1=b, op0=p0, op1=p1),
                  reads=reads, writes=writes)


def op_copy(g, out_ap, in_ap, reads=(), writes=(), stream="vector"):
    return g.s.op(stream,
                  lambda e, o=out_ap, i=in_ap: e.tensor_copy(out=o, in_=i) if hasattr(e, "tensor_copy") else e.copy(out=o, in_=i),
                  reads=reads, writes=writes)


def op_memset(g, ap, val, writes=(), stream="vector", reads=()):
    return g.s.op(stream, lambda e, a=ap, v=val: e.memset(a, v),
                  reads=reads, writes=writes)


def op_red(g, out_ap, in_ap, op, reads=(), writes=(), negate=False):
    import concourse.mybir as _mb
    return g.s.op("vector",
                  lambda e, o=out_ap, i=in_ap, p=op, n=negate:
                      e.tensor_reduce(o, i, axis=_mb.AxisListType.X, op=p, negate=n),
                  reads=reads, writes=writes)


def op_recip(g, out_ap, in_ap, reads=(), writes=()):
    def fn(e, o=out_ap, i=in_ap, nc=g.nc):
        with nc.allow_low_precision(reason="recip feeds bf16 matmul operand"):
            return e.reciprocal(o, i)
    return g.s.op("vector", fn, reads=reads, writes=writes)


def op_transpose(g, out_ap, in_ap, ident, reads=(), writes=()):
    return g.s.op("tensor",
                  lambda e, o=out_ap, i=in_ap, d=ident: e.transpose(o, i, d),
                  reads=reads, writes=writes)


def sview(handle, offset, dims):
    return bass.AP(handle, offset, [list(d) for d in dims])


def load_vec_pcol(g, hbm_ap, name, bank=6, nch=NCH):
    B = g.s.buf
    stg = g.sb(name + "_stg", [nch, 128], F32)
    tile = g.sb(name, [128, nch], F32)
    pt = g.pb(bank, 0, 128, nch)
    op_dma(g, "gpsimd", stg[:, :], hbm_ap.rearrange("(a p) -> a p", p=128),
           writes=[B(name + "_stg")])
    op_transpose(g, pt, stg[:, :], g.t["c_ident"][0:nch, 0:nch],
                 reads=[B(name + "_stg"), B("c_ident")], writes=g.pbB(bank))
    op_copy(g, tile[:, :], pt, reads=g.pbB(bank), writes=[B(name)])
    return tile


def load_w_fm(g, wsb, wsb_buf, hbm_ap, col0, col1, stream="gpsimd"):
    n = col1 - col0
    return op_dma(g, stream, wsb[:, 0:NCH * n],
                  hbm_ap[:, col0:col1].rearrange("(a p) n -> p a n", p=128),
                  writes=[wsb_buf])







W_BIG = ["L0_attn_w_qkv", "L0_attn_w_proj",
         "L0_ffn_wq1", "L0_ffn_wk1", "L0_ffn_wv1", "L0_ffn_wo1",
         "L0_ffn_wg1", "L0_ffn_wu1", "L0_ffn_wd1", "L0_ffn_prior_w",
         "L0_ffn_wq2", "L0_ffn_wk2", "L0_ffn_wv2", "L0_ffn_wo2",
         "L0_ffn_wg2", "L0_ffn_wu2", "L0_ffn_wd2",
         "L1_attn_Wq", "L1_attn_Wk", "L1_attn_Wv", "L1_attn_Wo"]


def phase0_precast(g):
    from contextlib import ExitStack
    B = g.s.buf
    with ExitStack() as st:
        sv = g.stack
        g.stack = st
        tiles = [g.sb(f"p0_t{i}", [128, 4096], BF16) for i in range(4)]
        ti = [0]

        def cast_load(src2d):
            i = ti[0] % 4
            ti[0] += 1
            t, b = tiles[i], B(f"p0_t{i}")
            op_dma(g, "gpsimd", t[:, 0:src2d.shape[1]], src2d, writes=[b])
            return t, b

        def pack_full(dst_name, srcs):
            ncols = srcs[0][2]
            A = srcs[0][0].shape[0] // 128
            dst = g.dram_tmp(dst_name, [len(srcs) * 128, A * ncols], BF16)
            for t_i, (w, c0, nc_) in enumerate(srcs):
                for a in range(A):
                    t, b = cast_load(w[a * 128:(a + 1) * 128, c0:c0 + nc_])
                    op_dma(g, "sync", dst[t_i * 128:(t_i + 1) * 128, a * nc_:(a + 1) * nc_],
                           t[:, 0:nc_], reads=[b])
            return dst

        def pack_gu(dst_name, wg, wu):
            dst = g.dram_tmp(dst_name, [8 * 128, 8 * 1024], BF16)
            for a in range(8):
                tg, bg = cast_load(wg[a * 128:(a + 1) * 128, :])
                tu, bu = cast_load(wu[a * 128:(a + 1) * 128, :])
                for blk in range(8):
                    op_dma(g, "sync",
                           dst[blk * 128:(blk + 1) * 128, a * 1024:a * 1024 + 512],
                           tg[:, blk * 512:(blk + 1) * 512], reads=[bg])
                    op_dma(g, "sync",
                           dst[blk * 128:(blk + 1) * 128, a * 1024 + 512:(a + 1) * 1024],
                           tu[:, blk * 512:(blk + 1) * 512], reads=[bu])
            return dst

        def pack_pairs(dst_name, wq, wk, wv):
            dst = g.dram_tmp(dst_name, [8 * 128, 8 * 384], BF16)
            for a in range(8):
                parts = [cast_load(w[a * 128:(a + 1) * 128, :]) for w in (wq, wk, wv)]
                for hc in range(8):
                    for j, (t, b) in enumerate(parts):
                        op_dma(g, "sync",
                               dst[hc * 128:(hc + 1) * 128,
                                   a * 384 + j * 128:a * 384 + (j + 1) * 128],
                               t[:, hc * 128:(hc + 1) * 128], reads=[b])
            return dst

        def copy_rowmajor(name):
            w = g.t[name]
            R, C = w.shape
            bf = g.dram_tmp("bf_" + name, [R, C], BF16)
            for r0 in range(0, R, 128):
                t, b = cast_load(w[r0:r0 + 128, :])
                op_dma(g, "sync", bf[r0:r0 + 128, :], t[:, 0:C], reads=[b])
            g.t[name] = bf

        has = lambda *ns: all(n in g.t for n in ns)
        if has("L0_attn_w_qkv"):
            qkv = g.t["L0_attn_w_qkv"]
            pack_full("pk_qkv", [(qkv, 1024, 1024), (qkv, 0, 1024), (qkv, 2048, 1024)])
            pack_full("pk_proj", [(g.t["L0_attn_w_proj"], 0, 1024)])
        if has("L0_ffn_prior_w"):
            pack_full("pk_prior", [(g.t["L0_ffn_prior_w"], ob * 512, 512) for ob in range(4)])
            pack_full("pk_f1qkv", [(g.t["L0_ffn_wq1"], 0, 1024),
                                   (g.t["L0_ffn_wk1"], 0, 1024),
                                   (g.t["L0_ffn_wv1"], 0, 1024)])
            pack_full("pk_wo1", [(g.t["L0_ffn_wo1"], 0, 1024)])
            pack_gu("pk_f1gu", g.t["L0_ffn_wg1"], g.t["L0_ffn_wu1"])
            copy_rowmajor("L0_ffn_wd1")
        if has("L0_ffn_wq2"):
            pack_pairs("pk_qkv2", g.t["L0_ffn_wq2"], g.t["L0_ffn_wk2"], g.t["L0_ffn_wv2"])
            pack_full("pk_wo2", [(g.t["L0_ffn_wo2"], 0, 1024)])
            pack_gu("pk_f2gu", g.t["L0_ffn_wg2"], g.t["L0_ffn_wu2"])
            copy_rowmajor("L0_ffn_wd2")
        if has("L1_attn_Wq"):
            pack_pairs("pk_qkvL1", g.t["L1_attn_Wq"], g.t["L1_attn_Wk"], g.t["L1_attn_Wv"])
            pack_full("pk_WoL1", [(g.t["L1_attn_Wo"], 0, 1024)])
        g.stack = sv
    g.s.fence()


P1_CONSTS = ["c_ones_col", "c_ones_row", "c_ident", "c_mask_add", "c_headind",
             "c_sel16", "c_colsel"]
TH = 1024


def phase1(g):
    from contextlib import ExitStack
    B = g.s.buf
    x0, x1 = g.t["x0"], g.t["x1"]
    wq_hbm = g.t["L0_attn_w_qkv"]
    wp_hbm = g.t["L0_attn_w_proj"]

    with ExitStack() as st:
        sv = g.stack
        g.stack = st
        x_fm = g.sb("p1_xfm", [128, NCH * TH], BF16)
        xin = g.sb("p1_xin", [128, 2 * 1024], F32)
        wsb = [g.sb("p1_w0", [128, NCH * 1024], BF16)]
        wsb.append(wsb[0])
        qn = g.sb("p1_qn", [128, NCH * TH], BF16)
        kn = g.sb("p1_kn", [128, NCH * TH], BF16)
        vt = g.sb("p1_vt", [128, H * 8 * 65], BF16)
        sq = g.sb("p1_sq", [128, 2 * 512], BF16)
        rrow = g.sb("p1_rrow", [1, 2 * 512], F32)
        rbf = g.sb("p1_rbf", [1, 2 * 512], BF16)
        nrm = g.sb("p1_nrm", [16, 512], F32)
        Ball = g.sb("p1_Ball", [16, TH], BF16)
        mk = g.sb("p1_mk", [16, 4], F32)
        qa = g.sb("p1_qa", [128, TH], BF16)
        ka = g.sb("p1_ka", [128, TH], BF16)
        op_memset(g, qa[64:128, :], 0.0, writes=[B("p1_qa_pad")])
        op_memset(g, ka[64:128, :], 0.0, writes=[B("p1_ka_pad")])
        y_fm = g.sb("p1_yfm", [128, NCH * TH], BF16)
        Pb = [g.sb("p1_P0", [128, 2048], BF16), g.sb("p1_P1", [128, 2048], BF16)]
        den_all = g.sb("p1_den_all", [16, 1024], F32)
        dra = g.sb("p1_dra", [16, 1024], BF16)
        rbs = g.sb("p1_rbs", [128, 1024], BF16)
        xs = g.sb("p1_xs", [128, 1024], F32)
        sc = g.sb("p1_sc", [128, 1024], F32)
        stt_ = g.sb("p1_st", [128, 8], F32)
        x1t = g.sb("p1_x1t", [128, 2 * 1024], F32)
        lnw2 = g.sb("p1_lnw2", [128, NCH], F32)
        ewt = g.sb("p1_ewt", [128, NCH * 16], BF16)
        ones_bf = g.t["c_ones_col"]
        ones_row = g.t["c_ones_row"]
        ident = g.t["c_ident"]
        mask_add = g.t["c_mask_add"]
        hind = g.t["c_headind"]
        sel16 = g.t["c_sel16"]
        colsel = g.t["c_colsel"]

        lnw = load_vec_pcol(g, g.t["L0_attn_ln_w"], "p1_lnw")
        op_act(g, lnw2[:, :], lnw[:, :], AF.Square, reads=[B("p1_lnw")], writes=[B("p1_lnw2")])
        for oc in range(NCH):
            op_ts(g, ewt[:, oc * 16:(oc + 1) * 16], hind[:, oc * 16:(oc + 1) * 16],
                  lnw2[:, oc:oc + 1], None, ALU.mult,
                  reads=[B("c_headind"), B("p1_lnw2")], writes=[B("p1_ewt")])
        xfmB = [B(f"p1_xfm{c}") for c in range(NCH)]

        for r in range(ROWS):
            for hf in range(2):
                t0 = r * T + hf * TH
                for tb in range(8):
                    sl = tb % 2
                    xi = xin[:, sl * 1024:(sl + 1) * 1024]
                    op_dma(g, "sync", xi, x0[t0 + tb * 128: t0 + (tb + 1) * 128, :],
                           writes=[B(f"p1_xin{sl}")])
                    for c in range(NCH):
                        po = g.pall[:, 7 * 512 + (c % 4) * 128: 7 * 512 + (c % 4 + 1) * 128]
                        op_transpose(g, po, xi[:, c * 128:(c + 1) * 128], ident[:, :],
                                     reads=[B(f"p1_xin{sl}"), B("c_ident")], writes=g.pbB(7))
                        op_copy(g, x_fm[:, c * TH + tb * 128: c * TH + (tb + 1) * 128], po,
                                reads=g.pbB(7), writes=[xfmB[c]])
                op_memset(g, vt[:, :], 1.0, writes=[B("p1_vt")])
                op_memset(g, ka[64:65, :], 1.0, writes=[B("p1_ka_hi")])
                op_memset(g, mk[:, 0:1], 0.0, writes=[B("p1_mk")])

                for (tens, tname, wti, is_q) in ((kn, "p1_kn", 0, False),
                                                 (qn, "p1_qn", 1, True)):
                    w = wsb[0]
                    wB = B("p1_w0")
                    op_dma(g, "sync", w[:, :],
                           g.t["pk_qkv"][wti * 128:(wti + 1) * 128, :],
                           writes=[wB])
                    for tt in range(2):
                        for oc in range(NCH):
                            pq = g.pb(oc % 2)
                            pqB = g.pbB(oc % 2)
                            items = [(pq, w[:, kc * 1024 + oc * 128: kc * 1024 + (oc + 1) * 128],
                                      x_fm[:, kc * TH + tt * 512: kc * TH + (tt + 1) * 512],
                                      kc == 0, kc == NCH - 1) for kc in range(NCH)]
                            op_mm_group(g, items, reads=[wB] + xfmB, writes=pqB)
                            op_act(g, tens[:, oc * TH + tt * 512: oc * TH + (tt + 1) * 512], pq,
                                   AF.Identity, reads=pqB, writes=[B(f"{tname}{oc}")])
                            sqs = sq[:, (oc % 2) * 512:((oc % 2) + 1) * 512]
                            sqB = B(f"p1_sq{oc % 2}")
                            op_act(g, sqs, pq, AF.Square, reads=pqB, writes=[sqB])
                            op_mm(g, g.pb(2, 0, 1), ones_bf[:, :], sqs,
                                  reads=[sqB, B("c_ones_col")], writes=g.pbB(2),
                                  start=(oc == 0), stop=(oc == NCH - 1))
                            op_mm(g, g.pb(3, 0, 16), ewt[:, oc * 16:(oc + 1) * 16], sqs,
                                  reads=[sqB, B("p1_ewt")], writes=g.pbB(3),
                                  start=(oc == 0), stop=(oc == NCH - 1))
                        sl = tt
                        rr = rrow[0:1, sl * 512:(sl + 1) * 512]
                        rf = rbf[0:1, sl * 512:(sl + 1) * 512]
                        op_act(g, rr, g.pb(2, 0, 1), AF.Ln, scale=1.0 / 1024.0, bias=1e-6,
                               reads=g.pbB(2), writes=[B(f"p1_rr{sl}")])
                        op_act(g, rr, rr, AF.Exp, scale=-0.5,
                               reads=[B(f"p1_rr{sl}")], writes=[B(f"p1_rr{sl}")])
                        op_copy(g, rf, rr, reads=[B(f"p1_rr{sl}")], writes=[B(f"p1_rbf{sl}")])
                        op_act(g, nrm[:, :], g.pb(3, 0, 16), AF.Sqrt, reads=g.pbB(3),
                               writes=[B("p1_nrm")])
                        op_mm(g, g.pb(6, 0, 16), ones_row[0:1, 0:16], rf,
                              reads=[B(f"p1_rbf{sl}"), B("c_ones_row")], writes=g.pbB(6))
                        if is_q:
                            op_tt(g, nrm[:, :], nrm[:, :], g.pb(6, 0, 16), ALU.mult,
                                  reads=[B("p1_nrm")] + g.pbB(6), writes=[B("p1_nrm")])
                            op_ts(g, Ball[:, tt * 512:(tt + 1) * 512], nrm[:, :],
                                  mk[:, 1:2], 8.0 * SHIFT_MARGIN, ALU.mult, ALU.add,
                                  reads=[B("p1_nrm"), B("p1_mkn")], writes=[B("p1_Ball")])
                        else:
                            op_tt(g, nrm[:, :], nrm[:, :], g.pb(6, 0, 16), ALU.mult,
                                  reads=[B("p1_nrm")] + g.pbB(6), writes=[B("p1_nrm")])
                            op_red(g, mk[:, 2:3], nrm[:, :], ALU.max,
                                   reads=[B("p1_nrm")], writes=[B("p1_mk2")])
                            op_tt(g, mk[:, 0:1], mk[:, 0:1], mk[:, 2:3], ALU.max,
                                  reads=[B("p1_mk2"), B("p1_mk")], writes=[B("p1_mk")])
                        op_mm(g, g.pb(6), ones_row[0:1, :], rf,
                              reads=[B(f"p1_rbf{sl}"), B("c_ones_row")], writes=g.pbB(6))
                        for oc in range(NCH):
                            slc = tens[:, oc * TH + tt * 512: oc * TH + (tt + 1) * 512]
                            op_stt(g, slc, slc, lnw[:, oc:oc + 1], g.pb(6), ALU.mult, ALU.mult,
                                   reads=[B(f"{tname}{oc}"), B("p1_lnw")] + g.pbB(6),
                                   writes=[B(f"{tname}{oc}")])
                    if not is_q:
                        op_ts(g, mk[:, 1:2], mk[:, 0:1], -1.0, None, ALU.mult,
                              reads=[B("p1_mk")], writes=[B("p1_mkn")])
                op_dma(g, "sync", wsb[0][:, :],
                       g.t["pk_qkv"][2 * 128:3 * 128, :],
                       writes=[B("p1_w0")])
                for tb in range(8):
                    for vb in range(2):
                        pq = g.pb((tb * 2 + vb) % 2)
                        pqB = g.pbB((tb * 2 + vb) % 2)
                        items = [(pq, x_fm[:, kc * TH + tb * 128: kc * TH + (tb + 1) * 128],
                                  wsb[0][:, kc * 1024 + vb * 512: kc * 1024 + (vb + 1) * 512],
                                  kc == 0, kc == NCH - 1) for kc in range(NCH)]
                        op_mm_group(g, items, reads=[B("p1_w0")] + xfmB, writes=pqB)
                        for j in range(8):
                            h = vb * 8 + j
                            op_copy(g, vt[:, (h * 8 + tb) * 65: (h * 8 + tb) * 65 + 64],
                                    pq[:, j * 64:(j + 1) * 64], reads=pqB, writes=[B("p1_vt")])

                def attn_part1(h):
                    hc, hr = h // 2, (h % 2) * 64
                    op_copy(g, qa[0:64, :], qn[hr:hr + 64, hc * TH:(hc + 1) * TH],
                            reads=[B(f"p1_qn{hc}")], writes=[B("p1_qa_lo")])
                    op_dma(g, "sync", qa[64:65, :], Ball[h:h + 1, :],
                           reads=[B("p1_Ball")], writes=[B("p1_qa_hi")])
                    op_copy(g, ka[0:64, :], kn[hr:hr + 64, hc * TH:(hc + 1) * TH],
                            reads=[B(f"p1_kn{hc}")], writes=[B("p1_ka_lo")])
                    items = []
                    for wl in range(4):
                        w0 = wl * 256
                        items.append((g.pall[:, wl * 512: wl * 512 + 256],
                                      ka[:, w0: w0 + 128], qa[:, w0: w0 + 256], True, True))
                        items.append((g.pall[:, wl * 512 + 256: wl * 512 + 384],
                                      ka[:, w0 + 128: w0 + 256], qa[:, w0 + 128: w0 + 256],
                                      True, True))
                    op_mm_group(g, items, reads=[B("p1_qa_lo"), B("p1_qa_hi"), B("p1_qa_pad"),
                                                 B("p1_ka_lo"), B("p1_ka_hi"), B("p1_ka_pad")],
                                writes=g.pbB(0, 3))
                    for off in (0, 256):
                        v = sview(g.pall, off, [[4096, 128], [512, 4], [1, 128]])
                        m = sview(g.t["c_mask_add"], 0, [[128, 128], [0, 4], [1, 128]])
                        op_tt(g, v, v, m, ALU.add,
                              reads=[B("c_mask_add")] + g.pbB(0, 3), writes=g.pbB(0, 3))
                    P = Pb[h % 2]
                    op_act(g, P[:, :], g.pall[:, 0:2048], AF.Exp, scale=0.125,
                           reads=g.pbB(0, 3), writes=[B(f"p1_P{h % 2}")])

                def attn_part2(h):
                    hc, hr = h // 2, (h % 2) * 64
                    P = Pb[h % 2]
                    PB = B(f"p1_P{h % 2}")
                    items = []
                    for wl in range(4):
                        base = (h * 8 + wl * 2) * 65
                        items.append((g.pall[0:65, 2048 + wl * 256: 2048 + wl * 256 + 256],
                                      vt[:, base: base + 65],
                                      P[:, wl * 512: wl * 512 + 256], True, False))
                        items.append((g.pall[0:65, 2048 + wl * 256 + 128: 2048 + wl * 256 + 256],
                                      vt[:, base + 65: base + 130],
                                      P[:, wl * 512 + 256: wl * 512 + 384], False, True))
                    op_mm_group(g, items, reads=[PB, B("p1_vt")], writes=g.pbB(4, 5))
                    items = []
                    for wl in range(4):
                        items.append((g.pall[0:16, 3072 + wl * 256: 3072 + wl * 256 + 256],
                                      colsel[:, h * 16:(h + 1) * 16],
                                      P[:, wl * 512: wl * 512 + 256], h == 0 and wl % 2 == 0, False))
                        items.append((g.pall[0:16, 3072 + wl * 256 + 128: 3072 + wl * 256 + 256],
                                      colsel[:, h * 16:(h + 1) * 16],
                                      P[:, wl * 512 + 256: wl * 512 + 384], False, False))
                    op_mm_group(g, items, reads=[PB, B("c_colsel")], writes=g.pbB(6, 7))
                    op_act(g, y_fm[hr:hr + 64, hc * TH: (hc + 1) * TH],
                           g.pall[0:64, 2048:3072], AF.Identity,
                           reads=g.pbB(4, 5), writes=[B(f"p1_yfm{hc}")])

                prev = None
                for h in range(H):
                    attn_part1(h)
                    if prev is not None:
                        attn_part2(prev)
                    prev = h
                attn_part2(prev)
                op_act(g, den_all[:, :], g.pall[0:16, 3072:4096], AF.Identity,
                       reads=g.pbB(6, 7), writes=[B("p1_den")])
                op_recip(g, dra[:, :], den_all[:, :], reads=[B("p1_den")], writes=[B("p1_dra")])
                for hc in range(NCH):
                    he, ho = 2 * hc, 2 * hc + 1
                    for (h_, p0, p1_) in ((he, 0, 64), (ho, 64, 128)):
                        op_mm(g, g.pall[p0:p1_, 3072:3584], sel16[:, h_ * 64:(h_ + 1) * 64],
                              dra[:, 0:512], reads=[B("p1_dra"), B("c_sel16")], writes=g.pbB(6))
                        op_mm(g, g.pall[p0:p1_, 3584:4096], sel16[:, h_ * 64:(h_ + 1) * 64],
                              dra[:, 512:1024], reads=[B("p1_dra"), B("c_sel16")], writes=g.pbB(7))
                    op_act(g, rbs[:, :], g.pall[:, 3072:4096], AF.Identity,
                           reads=g.pbB(6, 7), writes=[B("p1_rbs")])
                    op_tt(g, y_fm[:, hc * TH: (hc + 1) * TH],
                          y_fm[:, hc * TH: (hc + 1) * TH], rbs[:, :], ALU.mult,
                          reads=[B(f"p1_yfm{hc}"), B("p1_rbs")], writes=[B(f"p1_yfm{hc}")])

                op_dma(g, "sync", wsb[1][:, :], g.t["pk_proj"][0:128, :],
                       writes=[B("p1_w0")])
                yfmB = [B(f"p1_yfm{c}") for c in range(NCH)]
                for tb in range(8):
                    sl = tb % 2
                    xi = xin[:, sl * 1024:(sl + 1) * 1024]
                    op_dma(g, "sync", xi, x0[t0 + tb * 128: t0 + (tb + 1) * 128, :],
                           writes=[B(f"p1_xin{sl}")])
                    for ob in range(2):
                        pq = g.pb(ob)
                        pqB = g.pbB(ob)
                        items = [(pq, y_fm[:, kc * TH + tb * 128: kc * TH + (tb + 1) * 128],
                                  wsb[1][:, kc * 1024 + ob * 512: kc * 1024 + (ob + 1) * 512],
                                  kc == 0, kc == NCH - 1) for kc in range(NCH)]
                        op_mm_group(g, items, reads=[B("p1_w0")] + yfmB, writes=pqB)
                        op_tt(g, xs[:, ob * 512:(ob + 1) * 512], pq, xi[:, ob * 512:(ob + 1) * 512],
                              ALU.add, reads=pqB + [B(f"p1_xin{sl}")], writes=[B("p1_xs")])
                    emit_ln_apply(g, xs, sc, stt_, x1t[:, sl * 1024:(sl + 1) * 1024],
                                  "p1_xs", "p1_sc", "p1_st", f"p1_x1t{sl}")
                    op_dma(g, "sync", x1[t0 + tb * 128: t0 + (tb + 1) * 128, :],
                           x1t[:, sl * 1024:(sl + 1) * 1024],
                           reads=[B(f"p1_x1t{sl}")], writes=[B("x1_hbm")])
        g.stack = sv
    g.s.fence()


def emit_ln_apply(g, xs, sc, stt_, out_ap, xsN, scN, stN, outN):
    B = g.s.buf
    op_red(g, stt_[:, 0:1], xs[:, :], ALU.add, reads=[B(xsN)], writes=[B(stN + "0")])
    op_act(g, sc[:, :], xs[:, :], AF.Square, accum=stt_[:, 1:2],
           reads=[B(xsN)], writes=[B(scN), B(stN + "1")])
    op_ts(g, stt_[:, 2:3], stt_[:, 0:1], 1.0 / 1024.0, None, ALU.mult,
          reads=[B(stN + "0")], writes=[B(stN + "2")])
    op_tt(g, stt_[:, 3:4], stt_[:, 2:3], stt_[:, 2:3], ALU.mult,
          reads=[B(stN + "2")], writes=[B(stN + "3")])
    op_stt(g, stt_[:, 4:5], stt_[:, 1:2], 1.0 / 1024.0, stt_[:, 3:4],
           ALU.mult, ALU.subtract,
           reads=[B(stN + "1"), B(stN + "3")], writes=[B(stN + "4")])
    op_act(g, stt_[:, 5:6], stt_[:, 4:5], AF.Ln, bias=1e-5,
           reads=[B(stN + "4")], writes=[B(stN + "5")])
    op_act(g, stt_[:, 6:7], stt_[:, 5:6], AF.Exp, scale=-0.5,
           reads=[B(stN + "5")], writes=[B(stN + "6")])
    op_stt(g, stt_[:, 7:8], stt_[:, 2:3], -1.0, stt_[:, 6:7], ALU.mult, ALU.mult,
           reads=[B(stN + "2"), B(stN + "6")], writes=[B(stN + "7")])
    op_act(g, out_ap, xs[:, :], AF.Identity, scale=stt_[:, 6:7], bias=stt_[:, 7:8],
           reads=[B(xsN), B(stN + "6"), B(stN + "7")], writes=[B(outN)])



def build_program(decls, consts, phases, serialize=False, dma_scratch=16384):
    from contextlib import ExitStack
    nc = bass.Bass("TRN2", target_bir_lowering=False,
                   dynamic_dma_scratch_size=dma_scratch)
    with ExitStack() as top:
        s = Sched(nc, top)
        s.debug_serialize = (serialize is True)
        g = G(nc, s)
        g.stack = top
        g.top_stack = top
        for kind, name, shape, dtype in decls:
            if kind == "in":
                g.dram_in(name, shape, dtype)
            elif kind == "out":
                g.dram_out(name, shape, dtype)
            else:
                g.dram_tmp(name, shape, dtype)
        g.ps_pool()
        g.load_consts(consts)
        for f in phases:
            with nc.named_scope(getattr(f, "__name__", "phase")):
                f(g)
        s.finish()
    return nc, g.const_inputs()



P2_CONSTS = ["c_ones_col", "c_ones_row", "c_ident", "c_mask_add", "c_headind",
             "c_sel16", "c_rope_cs", "c_rope_ss", "c_ones_col_f32", "c_ones_row_f32"]


def swap_ap(wtile, col0):
    return bass.AP(wtile, col0 + 32,
                   [[wtile.shape[1], 128], [64, 2], [-32, 2], [1, 32]])


def rms_rows(g, pfx, src_fm, srcB, t0, t1, sq, rrow, rbf, bank=2):
    B = g.s.buf
    Tl = src_fm.shape[1] // NCH
    for tt in range(t0 // 512, t1 // 512):
        for oc in range(NCH):
            sqs = sq[:, (oc % 2) * 512:(oc % 2 + 1) * 512]
            sqB = B(f"{pfx}_sq{oc % 2}")
            op_act(g, sqs, src_fm[:, oc * Tl + tt * 512: oc * Tl + (tt + 1) * 512],
                   AF.Square, reads=[srcB[oc]], writes=[sqB])
            op_mm(g, g.pb(bank, 0, 1), g.t["c_ones_col"][:, :], sqs,
                  reads=[sqB, B("c_ones_col")], writes=g.pbB(bank),
                  start=(oc == 0), stop=(oc == NCH - 1))
        sl = tt % 2
        rr = rrow[0:1, sl * 512:(sl + 1) * 512]
        rf = rbf[0:1, sl * 512:(sl + 1) * 512]
        op_act(g, rr, g.pb(bank, 0, 1), AF.Ln, scale=1.0 / 1024.0, bias=1e-6,
               reads=g.pbB(bank), writes=[B(f"{pfx}_rr{sl}")])
        op_act(g, rr, rr, AF.Exp, scale=-0.5,
               reads=[B(f"{pfx}_rr{sl}")], writes=[B(f"{pfx}_rr{sl}")])
        op_copy(g, rf, rr, reads=[B(f"{pfx}_rr{sl}")], writes=[B(f"{pfx}_rbf{sl}")])
        yield tt, rf, B(f"{pfx}_rbf{sl}")




def load_vec_bcast(g, hbm_ap, name, bank=6):
    from contextlib import ExitStack
    B = g.s.buf
    tile = g.sb(name, [128, 1024], F32)
    with ExitStack() as tmps:
        sv = g.stack
        g.stack = tmps
        row = g.sb(name + "_row", [1, 1024], F32)
        op_dma(g, "gpsimd", row[0:1, :], hbm_ap.rearrange("(a d) -> a d", a=1),
               writes=[B(name + "_row")])
        onesf = g.t["c_ones_row_f32"]
        for half in range(2):
            op_mm(g, g.pb(bank), onesf[0:1, :],
                  row[0:1, half * 512:(half + 1) * 512],
                  reads=[B(name + "_row"), B("c_ones_row_f32")], writes=g.pbB(bank))
            op_copy(g, tile[:, half * 512:(half + 1) * 512], g.pb(bank),
                    reads=g.pbB(bank), writes=[B(name)])
        g.stack = sv
    g.s.fence()
    return tile


P2_STOP = [None]
P2_ROWS = [ROWS]


def phase2(g):
    from contextlib import ExitStack
    B = g.s.buf
    x1, phbm = g.t["x1"], g.t["processed"]
    ml_hbm = g.t["p2_ml"]
    xa_hbm = g.t["p2_xa"]
    dst_hbm, dch_hbm = g.t["d_st"], g.t["d_ch"]
    part_hbm = g.t["partials"]
    ones_row = g.t["c_ones_row"]
    ident = g.t["c_ident"]
    CS, SS = g.t["c_rope_cs"], g.t["c_rope_ss"]
    hind = g.t["c_headind"]
    sel16 = g.t["c_sel16"]
    onesf = g.t["c_ones_col_f32"]

    with ExitStack() as st:
        sv = g.stack
        g.stack = st
        h1 = g.sb("p2_h1", [128, NCH * T], BF16)
        y_fm = g.sb("p2_yfm", [128, NCH * T], BF16)
        dtmp = g.sb("p2_dtmp", [1, 2 * 512], BF16)
        mk = g.sb("p2_mk", [16, 4], F32)
        ds_all = g.sb("p2_dsall", [128, 16], F32)
        dc_all = g.sb("p2_dcall", [128, 16], F32)
        ln1b = load_vec_bcast(g, g.t["L0_ffn_ln1_1"], "p2_ln1")
        ln2b = load_vec_bcast(g, g.t["L0_ffn_ln2_1"], "p2_ln2")
        pnwb = load_vec_bcast(g, g.t["L0_ffn_prior_norm_w"], "p2_pnw")
        stt_ = g.sb("p2_stt", [128, 8], F32)

        for r in range(P2_ROWS[0]):
            rb = r * T
            with ExitStack() as stA:
                g.stack = stA
                xin = g.sb("p2_xin", [128, 2 * 1024], F32)
                hs = g.sb("p2_hs", [128, 2 * 1024], F32)
                ph = g.sb("p2_ph", [128, NCH * T], BF16)
                wml = g.sb("p2_wml0", [128, NCH * 512], BF16)
                mlt = g.sb("p2_mlt", [128, 2 * 512], F32)
                phB = [B(f"p2_ph{c}") for c in range(NCH)]
                h1B = [B(f"p2_h1{c}") for c in range(NCH)]
                for tb in range(16):
                    sl = tb % 2
                    xi = xin[:, sl * 1024:(sl + 1) * 1024]
                    op_dma(g, "sync", xi, x1[rb + tb * 128: rb + (tb + 1) * 128, :],
                           writes=[B(f"p2_xin{sl}")])
                    op_act(g, hs[:, 0:1024], xi, AF.Square, accum=stt_[:, 0:1],
                           reads=[B(f"p2_xin{sl}")], writes=[B("p2_hs0"), B("p2_st0")])
                    op_act(g, stt_[:, 1:2], stt_[:, 0:1], AF.Ln, scale=1.0 / 1024.0, bias=1e-6,
                           reads=[B("p2_st0")], writes=[B("p2_st1")])
                    op_act(g, stt_[:, 2:3], stt_[:, 1:2], AF.Exp, scale=-0.5,
                           reads=[B("p2_st1")], writes=[B("p2_st2")])
                    for (dstB, wb, wbn, dst) in ((h1B, ln1b, "p2_ln1", h1), (phB, pnwb, "p2_pnw", ph)):
                        op_stt(g, hs[:, 0:1024], xi, stt_[:, 2:3], wb[:, :],
                               ALU.mult, ALU.mult,
                               reads=[B(f"p2_xin{sl}"), B("p2_st2"), B(wbn)],
                               writes=[B("p2_hs0")])
                        for c in range(NCH):
                            po = g.pall[:, 7 * 512 + (c % 4) * 128: 7 * 512 + (c % 4 + 1) * 128]
                            op_transpose(g, po, hs[:, c * 128:(c + 1) * 128], ident[:, :],
                                         reads=[B("p2_hs0"), B("c_ident")], writes=g.pbB(7))
                            op_copy(g, dst[:, c * T + tb * 128: c * T + (tb + 1) * 128], po,
                                    reads=g.pbB(7), writes=[dstB[c]])
                pw = g.t["L0_ffn_prior_w"]
                for ob in range(4):
                    wbB = B("p2_wml0")
                    op_dma(g, "sync", wml[:, :],
                           g.t["pk_prior"][ob * 128:(ob + 1) * 128, :],
                           writes=[wbB])
                    for tb in range(16):
                        pq = g.pb(tb % 2)
                        pqB = g.pbB(tb % 2)
                        items = [(pq, ph[:, kc * T + tb * 128: kc * T + (tb + 1) * 128],
                                  wml[:, kc * 512:(kc + 1) * 512],
                                  kc == 0, kc == NCH - 1) for kc in range(NCH)]
                        op_mm_group(g, items, reads=[wbB] + phB, writes=pqB)
                        sl = tb % 2
                        op_act(g, mlt[:, sl * 512:(sl + 1) * 512], pq, AF.Identity,
                               reads=pqB, writes=[B(f"p2_mlt{sl}")])
                        op_dma(g, "sync",
                               ml_hbm[rb + tb * 128: rb + (tb + 1) * 128, ob * 512:(ob + 1) * 512],
                               mlt[:, sl * 512:(sl + 1) * 512],
                               reads=[B(f"p2_mlt{sl}")], writes=[B("p2_mlhbm")])
                g.stack = st
            g.s.fence()

            if P2_STOP[0] == "A":
                continue
            stB = ExitStack()
            g.stack = stB
            wq_sb = g.sb("p2_wqsb", [128, NCH * 1024], BF16)
            wk_sb = g.sb("p2_wksb", [128, NCH * 1024], BF16)
            wv_sb = g.sb("p2_wvsb", [128, NCH * 1024], BF16)
            for (wt_, wn_, ti_) in ((wq_sb, "p2_wqsb", 0),
                                    (wk_sb, "p2_wksb", 1),
                                    (wv_sb, "p2_wvsb", 2)):
                op_dma(g, "sync", wt_[:, :],
                       g.t["pk_f1qkv"][ti_ * 128:(ti_ + 1) * 128, :], writes=[B(wn_)])
            qa = [g.sb("p2_qa0", [128, T], BF16), g.sb("p2_qa1", [128, T], BF16)]
            ka = [g.sb("p2_ka0", [128, T], BF16), g.sb("p2_ka1", [128, T], BF16)]
            for _t, _n in ((qa[0], "p2_qa0"), (qa[1], "p2_qa1"), (ka[0], "p2_ka0"), (ka[1], "p2_ka1")):
                op_memset(g, _t[64:128, :], 0.0, writes=[B(_n + "_pad")])
            vt = [g.sb("p2_vt0", [128, 16 * 65], BF16), g.sb("p2_vt1", [128, 16 * 65], BF16)]
            Pb = [g.sb("p2_P0", [128, 2048], BF16), g.sb("p2_P1", [128, 2048], BF16)]
            den_all = g.sb("p2_den", [16, T], BF16)
            den_f32 = g.sb("p2_denf", [16, 512], F32)
            dra = g.sb("p2_dra", [16, T], BF16)
            rbs = g.sb("p2_rbs", [128, 1024], BF16)
            Ball = g.sb("p2_Ball", [16, T], BF16)
            nrm = g.sb("p2_nrm", [16, 512], F32)
            sq = g.sb("p2_sq", [128, 2 * 512], BF16)
            wsw2 = [g.sb("p2_wsw0", [128, NCH * 256], BF16),
                    g.sb("p2_wsw1", [128, NCH * 256], BF16)]
            u1 = g.sb("p2_u1", [128, 512], F32)
            u2 = g.sb("p2_u2", [128, 512], F32)
            wq_h, wk_h, wv_h = g.t["L0_ffn_wq1"], g.t["L0_ffn_wk1"], g.t["L0_ffn_wv1"]
            h1B = [B(f"p2_h1{c}") for c in range(NCH)]
            def emit_wswap(hc_):
                wsw_ = wsw2[hc_ % 2]
                for (wsrc, wsrcB, soff) in ((wq_sb, B("p2_wqsb"), 0), (wk_sb, B("p2_wksb"), 128)):
                    op_copy(g, bass.AP(wsw_, soff, [[NCH * 256, 128], [256, NCH], [64, 2], [1, 32]]),
                            bass.AP(wsrc, hc_ * 128 + 32, [[NCH * 1024, 128], [1024, NCH], [64, 2], [1, 32]]),
                            reads=[wsrcB], writes=[B(f"p2_wsw{hc_ % 2}")], stream="gpsimd")
                    op_copy(g, bass.AP(wsw_, soff + 32, [[NCH * 256, 128], [256, NCH], [64, 2], [1, 32]]),
                            bass.AP(wsrc, hc_ * 128, [[NCH * 1024, 128], [1024, NCH], [64, 2], [1, 32]]),
                            reads=[wsrcB], writes=[B(f"p2_wsw{hc_ % 2}")], stream="gpsimd")
                op_memset(g, mk[:, 0:1], 0.0, writes=[B("p2_mk")])

            emit_wswap(0)
            for hc in range(NCH):
                wswap = wsw2[hc % 2]
                wswB = B(f"p2_wsw{hc % 2}")
                he, ho = 2 * hc, 2 * hc + 1
                qa_e, qa_o = qa[0], qa[1]
                ka_e, ka_o = ka[0], ka[1]
                if hc == 0:
                    op_memset(g, ka_e[64:65, :], 1.0, writes=[B("p2_ka0_hi")])
                    op_memset(g, ka_o[64:65, :], 1.0, writes=[B("p2_ka1_hi")])
                if P2_STOP[0] == "B1":
                    continue
                for (isq, dst_e, dst_o, deB, doB, wres, wresB, wcol2) in (
                        (False, ka_e, ka_o, B("p2_ka0_lo"), B("p2_ka1_lo"), wk_sb, B("p2_wksb"), 128),
                        (True, qa_e, qa_o, B("p2_qa0_lo"), B("p2_qa1_lo"), wq_sb, B("p2_wqsb"), 0)):
                    for tt in range(4):
                        pA, pAB = g.pb(6), g.pbB(6)
                        pB_, pBB = g.pb(7), g.pbB(7)
                        items = [(pA, bass.AP(wres, kc * 1024 + hc * 128, [[NCH * 1024, 128], [1, 128]]),
                                  h1[:, kc * T + tt * 512: kc * T + (tt + 1) * 512],
                                  kc == 0, kc == NCH - 1) for kc in range(NCH)]
                        op_mm_group(g, items, reads=[wresB] + h1B, writes=pAB)
                        if P2_STOP[0] == "B1a":
                            continue
                        items = [(pB_, bass.AP(wswap, kc * 256 + wcol2, [[NCH * 256, 128], [1, 128]]),
                                  h1[:, kc * T + tt * 512: kc * T + (tt + 1) * 512],
                                  kc == 0, kc == NCH - 1) for kc in range(NCH)]
                        op_mm_group(g, items, reads=[wswB] + h1B, writes=pBB)
                        if P2_STOP[0] == "B1b":
                            continue
                        tsl = tt * 512
                        op_tt(g, u1[:, :], pA, CS[:, tsl: tsl + 512], ALU.mult,
                              reads=pAB + [B("c_rope_cs")], writes=[B("p2_u1")])
                        op_tt(g, u2[:, :], pB_, SS[:, tsl: tsl + 512], ALU.mult,
                              reads=pBB + [B("c_rope_ss")], writes=[B("p2_u2")])
                        if P2_STOP[0] == "B1c":
                            continue
                        op_tt(g, dst_e[0:64, tsl: tsl + 512], u1[0:64, :], u2[0:64, :],
                              ALU.add, reads=[B("p2_u1"), B("p2_u2")], writes=[deB])
                        op_tt(g, dst_o[0:64, tsl: tsl + 512], u1[64:128, :], u2[64:128, :],
                              ALU.add, reads=[B("p2_u1"), B("p2_u2")], writes=[doB])
                        if P2_STOP[0] == "B1d":
                            continue
                        sqs = sq[:, (tt % 2) * 512:(tt % 2 + 1) * 512]
                        sqB = B(f"p2_sq{tt % 2}")
                        op_act(g, sqs, pA, AF.Square, reads=pAB, writes=[sqB])
                        if P2_STOP[0] == "B1e2":
                            continue
                        op_mm(g, g.pb(5, 0, 16), hind[:, hc * 16:(hc + 1) * 16], sqs,
                              reads=[sqB, B("c_headind")], writes=g.pbB(5))
                        if P2_STOP[0] == "B1e":
                            continue
                        op_act(g, nrm[:, :], g.pb(5, 0, 16), AF.Sqrt,
                               reads=g.pbB(5), writes=[B("p2_nrm")])
                        if isq:
                            op_ts(g, Ball[:, tsl: tsl + 512], nrm[:, :],
                                  mk[:, 1:2], 8.0 * SHIFT_MARGIN, ALU.mult, ALU.add,
                                  reads=[B("p2_nrm"), B("p2_mkn")], writes=[B("p2_Ball")])
                        else:
                            op_red(g, mk[:, 2:3], nrm[:, :], ALU.max,
                                   reads=[B("p2_nrm")], writes=[B("p2_mk2")])
                            op_tt(g, mk[:, 0:1], mk[:, 0:1], mk[:, 2:3], ALU.max,
                                  reads=[B("p2_mk2"), B("p2_mk")], writes=[B("p2_mk")])
                    if not isq:
                        op_ts(g, mk[:, 1:2], mk[:, 0:1], -1.0, None, ALU.mult,
                              reads=[B("p2_mk")], writes=[B("p2_mkn")])
                if P2_STOP[0] == "B2":
                    continue
                op_dma(g, "sync", qa_e[64:65, :], Ball[he:he + 1, :],
                       reads=[B("p2_Ball")], writes=[B("p2_qa0_hi")])
                op_dma(g, "sync", qa_o[64:65, :], Ball[ho:ho + 1, :],
                       reads=[B("p2_Ball")], writes=[B("p2_qa1_hi")])
                if hc == 0:
                    op_memset(g, vt[0][:, :], 1.0, writes=[B("p2_vt0")])
                    op_memset(g, vt[1][:, :], 1.0, writes=[B("p2_vt1")])
                for tb in range(16):
                    pA, pAB = g.pb(6 + tb % 2), g.pbB(6 + tb % 2)
                    items = [(pA[:, 0:128],
                              h1[:, kc * T + tb * 128: kc * T + (tb + 1) * 128],
                              bass.AP(wv_sb, kc * 1024 + hc * 128, [[NCH * 1024, 128], [1, 128]]),
                              kc == 0, kc == NCH - 1) for kc in range(NCH)]
                    op_mm_group(g, items, reads=[B("p2_wvsb")] + h1B, writes=pAB)
                    op_copy(g, vt[0][:, tb * 65: tb * 65 + 64], pA[:, 0:64],
                            reads=pAB, writes=[B("p2_vt0")])
                    op_copy(g, vt[1][:, tb * 65: tb * 65 + 64], pA[:, 64:128],
                            reads=pAB, writes=[B("p2_vt1")])
                if P2_STOP[0] == "B3":
                    continue
                if hc + 1 < NCH:
                    emit_wswap(hc + 1)
                for (h, qa_h, ka_h, vt_h, qaB, kaB, vtB) in (
                        (he, qa_e, ka_e, vt[0], ("p2_qa0_lo", "p2_qa0_hi"),
                         ("p2_ka0_lo", "p2_ka0_hi"), "p2_vt0"),
                        (ho, qa_o, ka_o, vt[1], ("p2_qa1_lo", "p2_qa1_hi"),
                         ("p2_ka1_lo", "p2_ka1_hi"), "p2_vt1")):
                    hr = (h % 2) * 64
                    for qt in range(4):
                        yb = 4 + (qt % 2)
                        pY = g.pall[0:65, yb * 512:(yb + 1) * 512]
                        pYB = g.pbB(yb)
                        for gg in range(qt + 1):
                            last = (gg == qt)
                            Pt = Pb[gg % 2]
                            PtB = B(f"p2_P{gg % 2}")
                            items = []
                            for j in range(4):
                                kt = 4 * gg + j
                                if last:
                                    items.append((g.pall[:, j * 512 + 128 * j: j * 512 + 512],
                                                  ka_h[:, kt * 128: (kt + 1) * 128],
                                                  qa_h[:, qt * 512 + 128 * j: (qt + 1) * 512],
                                                  True, True))
                                else:
                                    items.append((g.pall[:, j * 512: (j + 1) * 512],
                                                  ka_h[:, kt * 128: (kt + 1) * 128],
                                                  qa_h[:, qt * 512: (qt + 1) * 512],
                                                  True, True))
                            op_mm_group(g, items,
                                        reads=[B(qaB[0]), B(qaB[1]), B(qaB[0].replace("_lo", "_pad")),
                                               B(kaB[0]), B(kaB[1]), B(kaB[0].replace("_lo", "_pad"))],
                                        writes=g.pbB(0, 3))
                            if last:
                                v = sview(g.pall, 0, [[4096, 128], [640, 4], [1, 128]])
                                m = sview(g.t["c_mask_add"], 0, [[128, 128], [0, 4], [1, 128]])
                                op_tt(g, v, v, m, ALU.add,
                                      reads=[B("c_mask_add")] + g.pbB(0, 3), writes=g.pbB(0, 3))
                            for j in range(4):
                                kt = 4 * gg + j
                                PsB = B(f"{PtB.name}_s{j}")
                                op_act(g, Pt[:, j * 512:(j + 1) * 512],
                                       g.pall[:, j * 512:(j + 1) * 512], AF.Exp,
                                       scale=0.125, reads=g.pbB(j, j), writes=[PsB])
                                if last and j > 0:
                                    op_memset(g, Pt[:, j * 512: j * 512 + 128 * j], 0.0,
                                              writes=[PsB], reads=[PsB])
                                op_mm_group(g, [(pY, vt_h[:, kt * 65: kt * 65 + 65],
                                                 Pt[:, j * 512: (j + 1) * 512],
                                                 gg == 0 and j == 0, last and j == 3)],
                                            reads=[PsB, B(vtB)], writes=pYB)
                        sl = qt % 2
                        op_copy(g, dtmp[0:1, sl * 512:(sl + 1) * 512], g.pall[64:65, yb * 512:(yb + 1) * 512],
                                reads=pYB, writes=[B(f"p2_dtmp{sl}")])
                        op_dma(g, "sync", den_all[h:h + 1, qt * 512:(qt + 1) * 512],
                               dtmp[0:1, sl * 512:(sl + 1) * 512],
                               reads=[B(f"p2_dtmp{sl}")], writes=[B("p2_denall")])
                        op_act(g, y_fm[hr:hr + 64, hc * T + qt * 512: hc * T + (qt + 1) * 512],
                               g.pall[0:64, yb * 512:(yb + 1) * 512], AF.Identity,
                               reads=pYB, writes=[B(f"p2_yfm{hc}")])
            if P2_STOP[0] == "Bnt":
                stB.close(); g.stack = st; g.s.fence(); continue
            op_recip(g, dra[:, :], den_all[:, :], reads=[B("p2_denall")], writes=[B("p2_dra")])
            for hc in range(NCH):
                he, ho = 2 * hc, 2 * hc + 1
                for seg in range(2):
                    for (h_, p0, p1_) in ((he, 0, 64), (ho, 64, 128)):
                        op_mm(g, g.pall[p0:p1_, 3072:3584], sel16[:, h_ * 64:(h_ + 1) * 64],
                              dra[:, seg * 1024: seg * 1024 + 512],
                              reads=[B("p2_dra"), B("c_sel16")], writes=g.pbB(6))
                        op_mm(g, g.pall[p0:p1_, 3584:4096], sel16[:, h_ * 64:(h_ + 1) * 64],
                              dra[:, seg * 1024 + 512: seg * 1024 + 1024],
                              reads=[B("p2_dra"), B("c_sel16")], writes=g.pbB(7))
                    op_act(g, rbs[:, :], g.pall[:, 3072:4096], AF.Identity,
                           reads=g.pbB(6, 7), writes=[B("p2_rbs")])
                    op_tt(g, y_fm[:, hc * T + seg * 1024: hc * T + (seg + 1) * 1024],
                          y_fm[:, hc * T + seg * 1024: hc * T + (seg + 1) * 1024],
                          rbs[:, :], ALU.mult,
                          reads=[B(f"p2_yfm{hc}"), B("p2_rbs")], writes=[B(f"p2_yfm{hc}")])
            stB.close()
            g.stack = st
            g.s.fence()

            if P2_STOP[0] == "B":
                continue
            with ExitStack() as stC:
                g.stack = stC
                wsb = g.sb("p2_wo", [128, NCH * 1024], BF16)
                xin = g.sb("p2_xinC", [128, 2 * 1024], F32)
                xat = g.sb("p2_xat", [128, 2 * 1024], F32)
                hs = g.sb("p2_hsC", [128, 1024], F32)
                op_dma(g, "sync", wsb[:, :], g.t["pk_wo1"][0:128, :],
                       writes=[B("p2_wo")])
                yfmB = [B(f"p2_yfm{c}") for c in range(NCH)]
                h1B = [B(f"p2_h1{c}") for c in range(NCH)]
                for tb in range(16):
                    sl = tb % 2
                    xi = xin[:, sl * 1024:(sl + 1) * 1024]
                    xa_t = xat[:, sl * 1024:(sl + 1) * 1024]
                    op_dma(g, "sync", xi, x1[rb + tb * 128: rb + (tb + 1) * 128, :],
                           writes=[B(f"p2_xinC{sl}")])
                    for ob in range(2):
                        pq = g.pb(ob)
                        pqB = g.pbB(ob)
                        items = [(pq, y_fm[:, kc * T + tb * 128: kc * T + (tb + 1) * 128],
                                  wsb[:, kc * 1024 + ob * 512: kc * 1024 + (ob + 1) * 512],
                                  kc == 0, kc == NCH - 1) for kc in range(NCH)]
                        op_mm_group(g, items, reads=[B("p2_wo")] + yfmB, writes=pqB)
                        op_tt(g, xa_t[:, ob * 512:(ob + 1) * 512], pq, xi[:, ob * 512:(ob + 1) * 512],
                              ALU.add, reads=pqB + [B(f"p2_xinC{sl}")], writes=[B(f"p2_xat{sl}")])
                    op_dma(g, "sync", xa_hbm[rb + tb * 128: rb + (tb + 1) * 128, :], xa_t,
                           reads=[B(f"p2_xat{sl}")], writes=[B("p2_xahbm")])
                    op_act(g, hs[:, :], xa_t, AF.Square, accum=stt_[:, 0:1],
                           reads=[B(f"p2_xat{sl}")], writes=[B("p2_hsC"), B("p2_st0")])
                    op_act(g, stt_[:, 1:2], stt_[:, 0:1], AF.Ln, scale=1.0 / 1024.0, bias=1e-6,
                           reads=[B("p2_st0")], writes=[B("p2_st1")])
                    op_act(g, stt_[:, 2:3], stt_[:, 1:2], AF.Exp, scale=-0.5,
                           reads=[B("p2_st1")], writes=[B("p2_st2")])
                    op_stt(g, hs[:, :], xa_t, stt_[:, 2:3], ln2b[:, :], ALU.mult, ALU.mult,
                           reads=[B(f"p2_xat{sl}"), B("p2_st2"), B("p2_ln2")],
                           writes=[B("p2_hsC")])
                    for c in range(NCH):
                        po = g.pall[:, 7 * 512 + (c % 4) * 128: 7 * 512 + (c % 4 + 1) * 128]
                        op_transpose(g, po, hs[:, c * 128:(c + 1) * 128], ident[:, :],
                                     reads=[B("p2_hsC"), B("c_ident")], writes=g.pbB(7))
                        op_copy(g, h1[:, c * T + tb * 128: c * T + (tb + 1) * 128], po,
                                reads=g.pbB(7), writes=[h1B[c]])
                g.stack = st
            g.s.fence()

            if P2_STOP[0] == "C":
                continue
            with ExitStack() as stE:
                g.stack = stE
                gated = g.sb("p2_gated", [128, 32 * 512], BF16)
                wgu = [g.sb("p2_wgu0", [128, NCH * 1024], BF16),
                       g.sb("p2_wgu1", [128, NCH * 1024], BF16)]
                wdb = [g.sb("p2_wd0", [128, 1024], BF16), g.sb("p2_wd1", [128, 1024], BF16)]
                sg_ = g.sb("p2_sg", [128, 512], BF16)
                xin = g.sb("p2_xinE", [128, 1024], F32)
                ptok = g.sb("p2_ptok", [128, 1024], F32)
                res = g.sb("p2_res", [128, 1024], F32)
                mlin = g.sb("p2_mlin", [128, 1024], F32)
                ee = g.sb("p2_ee", [128, 1024], F32)
                uu = g.sb("p2_uu", [128, 1024], F32)
                wg_h, wu_h, wd_h = g.t["L0_ffn_wg1"], g.t["L0_ffn_wu1"], g.t["L0_ffn_wd1"]
                h1B = [B(f"p2_h1{c}") for c in range(NCH)]
                gatB = [B(f"p2_gated{gc}") for gc in range(32)]
                def emit_stats_tb(tt, tb):
                    gt = tt * 4 + tb
                    pt_ = ptok[:, 0:1024]
                    op_dma(g, "gpsimd", pt_, phbm[rb + gt * 128: rb + (gt + 1) * 128, :],
                           writes=[B("p2_ptok0")], reads=[B("p2_phbm")])
                    op_dma(g, "gpsimd", xin[:, :], x1[rb + gt * 128: rb + (gt + 1) * 128, :],
                           writes=[B("p2_xinE")])
                    op_dma(g, "gpsimd", mlin[:, :],
                           ml_hbm[rb + gt * 128: rb + (gt + 1) * 128, 0:1024],
                           writes=[B("p2_mlin")], reads=[B("p2_mlhbm")])
                    op_tt(g, res[:, :], pt_, xin[:, :], ALU.subtract,
                          reads=[B("p2_ptok0"), B("p2_xinE")], writes=[B("p2_res")])
                    op_act(g, uu[:, :], res[:, :], AF.Square, accum=ds_all[:, gt:gt + 1],
                           reads=[B("p2_res")], writes=[B("p2_uu"), B("p2_dsall")])
                    op_tt(g, uu[:, :], res[:, :], mlin[:, :], ALU.subtract,
                          reads=[B("p2_res"), B("p2_mlin")], writes=[B("p2_uu")])
                    op_dma(g, "gpsimd", mlin[:, :],
                           ml_hbm[rb + gt * 128: rb + (gt + 1) * 128, 1024:2048],
                           writes=[B("p2_mlin")], reads=[B("p2_mlhbm")])
                    op_act(g, ee[:, :], mlin[:, :], AF.Exp, scale=-1.0,
                           reads=[B("p2_mlin")], writes=[B("p2_ee")])
                    op_act(g, uu[:, :], uu[:, :], AF.Square,
                           reads=[B("p2_uu")], writes=[B("p2_uu")])
                    op_stt(g, uu[:, :], uu[:, :], 1.0, ee[:, :], ALU.add, ALU.mult,
                           reads=[B("p2_uu"), B("p2_ee")], writes=[B("p2_uu")])
                    op_tt(g, uu[:, :], uu[:, :], mlin[:, :], ALU.add,
                          reads=[B("p2_uu"), B("p2_mlin")], writes=[B("p2_uu")])
                    op_act(g, ee[:, :], uu[:, :], AF.Identity, accum=dc_all[:, gt:gt + 1],
                           reads=[B("p2_uu")], writes=[B("p2_ee"), B("p2_dcall")])

                pending_stats = []
                for tt in range(4):
                    for blk in range(8):
                        wb = wgu[blk % 2]
                        wbB = B(f"p2_wgu{blk % 2}")
                        op_dma(g, "sync", wb[:, :],
                               g.t["pk_f1gu"][blk * 128:(blk + 1) * 128, :],
                               writes=[wbB])
                        for sub in range(4):
                            gc = blk * 4 + sub
                            pG, pGB = g.pb(0 + sub % 2), g.pbB(0 + sub % 2)
                            pU, pUB = g.pb(2 + sub % 2), g.pbB(2 + sub % 2)
                            items = [(pG[:, 0:512],
                                      bass.AP(wb, kc * 1024 + sub * 128, [[NCH * 1024, 128], [1, 128]]),
                                      h1[:, kc * T + tt * 512: kc * T + (tt + 1) * 512],
                                      kc == 0, kc == NCH - 1) for kc in range(NCH)]
                            op_mm_group(g, items, reads=[wbB] + h1B, writes=pGB)
                            items = [(pU[:, 0:512],
                                      bass.AP(wb, kc * 1024 + 512 + sub * 128, [[NCH * 1024, 128], [1, 128]]),
                                      h1[:, kc * T + tt * 512: kc * T + (tt + 1) * 512],
                                      kc == 0, kc == NCH - 1) for kc in range(NCH)]
                            op_mm_group(g, items, reads=[wbB] + h1B, writes=pUB)
                            op_act(g, sg_[:, :], pG[:, 0:512], AF.Silu,
                                   reads=pGB, writes=[B("p2_sg")])
                            op_tt(g, gated[:, gc * 512:(gc + 1) * 512], sg_[:, :], pU[:, 0:512],
                                  ALU.mult, reads=[B("p2_sg")] + pUB, writes=[gatB[gc]])
                        if blk < len(pending_stats):
                            pending_stats[blk]()
                    for gc in range(32):
                        wd_t = wdb[gc % 2]
                        wdB = B(f"p2_wd{gc % 2}")
                        op_dma(g, "sync", wd_t[:, :], wd_h[gc * 128:(gc + 1) * 128, :],
                               writes=[wdB])
                        items = []
                        for tb in range(4):
                            for ob in range(2):
                                items.append((g.pb(tb * 2 + ob),
                                              gated[:, gc * 512 + tb * 128: gc * 512 + (tb + 1) * 128],
                                              wd_t[:, ob * 512:(ob + 1) * 512],
                                              gc == 0, gc == 31))
                        op_mm_group(g, items, reads=[wdB, gatB[gc]], writes=g.pbB(0, 7))
                    for tb in range(4):
                        gt = tt * 4 + tb
                        op_dma(g, "gpsimd", xin[:, :], xa_hbm[rb + gt * 128: rb + (gt + 1) * 128, :],
                               writes=[B("p2_xinE")])
                        pt_ = ptok[:, 0:1024]
                        for ob in range(2):
                            op_tt(g, pt_[:, ob * 512:(ob + 1) * 512], g.pb(tb * 2 + ob),
                                  xin[:, ob * 512:(ob + 1) * 512], ALU.add,
                                  reads=g.pbB(tb * 2 + ob) + [B("p2_xinE")],
                                  writes=[B("p2_ptok0")])
                        op_dma(g, "gpsimd", phbm[rb + gt * 128: rb + (gt + 1) * 128, :], pt_,
                               reads=[B("p2_ptok0")], writes=[B("p2_phbm")])
                    pending_stats = [
                        (lambda tt_=tt, tb_=tb: emit_stats_tb(tt_, tb_)) for tb in range(4)]
                for fn in pending_stats:
                    fn()
                g.stack = st
            g.s.fence()
            if P2_STOP[0] == "E":
                continue
            op_dma(g, "sync", dst_hbm[r * 128:(r + 1) * 128, :], ds_all[:, :],
                   reads=[B("p2_dsall")], writes=[B("p2_dsthbm")])
            op_dma(g, "sync", dch_hbm[r * 128:(r + 1) * 128, :], dc_all[:, :],
                   reads=[B("p2_dcall")], writes=[B("p2_dchhbm")])
            op_red(g, stt_[:, 3:4], ds_all[:, :], ALU.add, reads=[B("p2_dsall")], writes=[B("p2_mk3")])
            op_mm(g, g.pb(6, 0, 1, 1), onesf[:, :], stt_[:, 3:4],
                  reads=[B("p2_mk3"), B("c_ones_col_f32")], writes=g.pbB(6))
            op_red(g, stt_[:, 3:4], dc_all[:, :], ALU.add, reads=[B("p2_dcall")], writes=[B("p2_mk3")])
            op_mm(g, bass.AP(g.pall, 6 * 512 + 1, [[4096, 1], [1, 1]]), onesf[:, :], stt_[:, 3:4],
                  reads=[B("p2_mk3"), B("c_ones_col_f32")], writes=g.pbB(6))
            op_copy(g, stt_[0:1, 4:6], g.pall[0:1, 3072:3074],
                    reads=g.pbB(6), writes=[B("p2_pttmp")])
            op_dma(g, "sync", part_hbm[r:r + 1, :], stt_[0:1, 4:6],
                  reads=[B("p2_pttmp")], writes=[B("p2_parthbm")])
        g.stack = sv
    g.s.fence()



P4_CONSTS = ["c_ones_col", "c_ones_col_f32", "c_ones_row_f32", "c_ident",
             "c_tril", "c_triu16", "c_iota_f", "c_iota_row128"]
NTOK = NCORES * ROWS * T


def cumsum_tokens(g, src_bf, srcB, out_f32, outB, csr, cst, cstb, offr, pfx):
    B = g.s.buf
    tril = g.t["c_tril"]
    ones = g.t["c_ones_col"]
    onesr = g.t["c_ones_row_f32"]
    tri16 = g.t["c_triu16"]
    ident = g.t["c_ident"]
    op_mm(g, g.pb(4, 0, 128, 64), tril[:, :], src_bf[:, :],
          reads=[srcB, B("c_tril")], writes=g.pbB(4))
    op_mm(g, g.pb(5, 0, 1, 64), ones[:, :], src_bf[:, :],
          reads=[srcB, B("c_ones_col")], writes=g.pbB(5))
    op_copy(g, csr[0:1, :], g.pb(5, 0, 1, 64), reads=g.pbB(5), writes=[B(pfx + "csr")])
    op_transpose(g, g.pb(6, 0, 64, 1), csr[0:1, :], ident[0:1, 0:1],
                 reads=[B(pfx + "csr"), B("c_ident")], writes=g.pbB(6))
    op_copy(g, cstb[:, :], g.pb(6, 0, 64, 1), reads=g.pbB(6), writes=[B(pfx + "cstb")])
    op_mm(g, g.pb(5, 0, 1, 64), cstb[:, :], tri16[:, :],
          reads=[B(pfx + "cstb"), B("c_triu16")], writes=g.pbB(5))
    op_copy(g, offr[0:1, :], g.pb(5, 0, 1, 64), reads=g.pbB(5), writes=[B(pfx + "offr")])
    op_mm(g, g.pb(7, 0, 128, 64), onesr[0:1, :], offr[0:1, :],
          reads=[B(pfx + "offr"), B("c_ones_row_f32")], writes=g.pbB(7))
    op_copy(g, out_f32[:, :], g.pb(4, 0, 128, 64), reads=g.pbB(4), writes=[outB])
    op_tt(g, out_f32[:, :], out_f32[:, :], g.pb(7, 0, 128, 64), ALU.add,
          reads=[outB] + g.pbB(7), writes=[outB])


def phase4(g):
    from contextlib import ExitStack
    import bass_rust
    B = g.s.buf
    dst_hbm, dch_hbm = g.t["d_st"], g.t["d_ch"]
    pall_hbm = g.t["partials_all"]
    idsco_hbm = g.t["idsco"]
    onesf = g.t["c_ones_col_f32"]
    onesr = g.t["c_ones_row_f32"]
    with ExitStack() as st:
        sv = g.stack
        g.stack = st
        gds = g.sb("p4_gds", [128, 64], F32)
        gdc = g.sb("p4_gdc", [128, 64], F32)
        gt = g.sb("p4_g", [128, 64], F32)
        pt = g.sb("p4_pt", [32, 2], F32)
        pay = g.sb("p4_pay", [128, 128], F32)
        cvec = g.sb("p4_cvec", [1, 2], F32)
        srow = g.sb("p4_srow", [1, 2], F32)
        s0b = g.sb("p4_s0b", [128, 1], F32)
        lo = g.sb("p4_lo", [1, 4], F32)
        hi = g.sb("p4_hi", [1, 4], F32)
        mid = g.sb("p4_mid", [1, 4], F32)
        trow = g.sb("p4_trow", [1, 4], F32)
        crow = g.sb("p4_crow", [1, 4], F32)
        c2row = g.sb("p4_c2row", [1, 4], F32)
        needr = g.sb("p4_needr", [1, 4], F32)
        cscr = g.sb("p4_cscr", [128, 64], F32)
        cntc = g.sb("p4_cntc", [128, 4], F32)
        m_gt = g.sb("p4_mgt", [128, 64], F32)
        m_eq = g.sb("p4_meq", [128, 64], F32)
        mbf = g.sb("p4_mbf", [128, 64], BF16)
        cum = g.sb("p4_cum", [128, 64], F32)
        csr = g.sb("p4_csr", [1, 64], F32)
        cstb = g.sb("p4_cstb", [64, 1], BF16)
        offr = g.sb("p4_offr", [1, 64], F32)
        dest = g.sb("p4_dest", [128, 64], F32)
        destu = g.sb("p4_destu", [128, 64], U32)
        cst = None
        for r in range(ROWS):
            op_dma(g, "sync", gds[:, r * 16:(r + 1) * 16],
                   dst_hbm[r * 128:(r + 1) * 128, :], writes=[B("p4_gds")])
            op_dma(g, "sync", gdc[:, r * 16:(r + 1) * 16],
                   dch_hbm[r * 128:(r + 1) * 128, :], writes=[B("p4_gdc")])
        op_dma(g, "sync", pt[0:pall_hbm.shape[0], :], pall_hbm[:, :], writes=[B("p4_pt")])
        op_memset(g, cvec[0:1, 0:1], -1.0 / (1024.0 * NTOK), writes=[B("p4_cvec")])
        op_memset(g, cvec[0:1, 1:2], -0.5 / (1024.0 * NTOK), writes=[B("p4_cvec")])
        nprow = g.t["partials_all"].shape[0]
        op_mm(g, g.pb(0, 0, 1, 2), onesf[0:nprow, 0:1], pt[0:nprow, :],
              reads=[B("p4_pt"), B("c_ones_col_f32")], writes=g.pbB(0))
        op_copy(g, srow[0:1, :], g.pb(0, 0, 1, 2), reads=g.pbB(0), writes=[B("p4_srow")])
        op_tt(g, srow[0:1, :], srow[0:1, :], cvec[0:1, :], ALU.mult,
              reads=[B("p4_srow"), B("p4_cvec")], writes=[B("p4_srow")])
        op_red(g, trow[0:1, 0:1], srow[0:1, :], ALU.add,
               reads=[B("p4_srow")], writes=[B("p4_s0")])
        op_mm(g, g.pb(0, 0, 128, 1), onesr[0:1, :], trow[0:1, 0:1],
              reads=[B("p4_s0"), B("c_ones_row_f32")], writes=g.pbB(0))
        op_copy(g, s0b[:, :], g.pb(0, 0, 128, 1), reads=g.pbB(0), writes=[B("p4_s0b")])
        op_ts(g, cscr[:, :], gdc[:, :], 0.5, None, ALU.mult,
              reads=[B("p4_gdc")], writes=[B("p4_cscr")])
        op_tt(g, cscr[:, :], cscr[:, :], gds[:, :], ALU.add,
              reads=[B("p4_cscr"), B("p4_gds")], writes=[B("p4_cscr")])
        op_act(g, gt[:, :], cscr[:, :], AF.Sigmoid, scale=1.0 / 1024.0, bias=s0b[:, 0:1],
               reads=[B("p4_cscr"), B("p4_s0b")], writes=[B("p4_g")])
        op_memset(g, lo[0:1, :], 0.0, writes=[B("p4_lo")])
        op_memset(g, hi[0:1, :], 1.0, writes=[B("p4_hi")])
        for it in range(45):
            op_tt(g, mid[0:1, :], lo[0:1, :], hi[0:1, :], ALU.add,
                  reads=[B("p4_lo"), B("p4_hi")], writes=[B("p4_mid")])
            op_ts(g, mid[0:1, :], mid[0:1, :], 0.5, None, ALU.mult,
                  reads=[B("p4_mid")], writes=[B("p4_mid")])
            op_mm(g, g.pb(1, 0, 128, 4), onesr[0:1, :], mid[0:1, :],
                  reads=[B("p4_mid"), B("c_ones_row_f32")], writes=g.pbB(1))
            for r in range(ROWS):
                op_ts(g, cscr[:, r * 16:(r + 1) * 16], gt[:, r * 16:(r + 1) * 16],
                      g.pb(1, 0, 128, 4)[:, r:r + 1], 0.0, ALU.is_gt, ALU.add,
                      reads=[B("p4_g")] + g.pbB(1), writes=[B("p4_cscr"), B("p4_cntc")],
                      accum=cntc[:, r:r + 1])
            op_mm(g, g.pb(2, 0, 1, 4), onesf[:, :], cntc[:, :],
                  reads=[B("p4_cntc"), B("c_ones_col_f32")], writes=g.pbB(2))
            op_ts(g, crow[0:1, :], g.pb(2, 0, 1, 4), 512.0, None, ALU.is_ge,
                  reads=g.pbB(2), writes=[B("p4_crow")])
            op_tt(g, trow[0:1, :], mid[0:1, :], lo[0:1, :], ALU.subtract,
                  reads=[B("p4_mid"), B("p4_lo")], writes=[B("p4_trow")])
            op_tt(g, trow[0:1, :], trow[0:1, :], crow[0:1, :], ALU.mult,
                  reads=[B("p4_trow"), B("p4_crow")], writes=[B("p4_trow")])
            op_tt(g, lo[0:1, :], lo[0:1, :], trow[0:1, :], ALU.add,
                  reads=[B("p4_lo"), B("p4_trow")], writes=[B("p4_lo")])
            op_ts(g, c2row[0:1, :], crow[0:1, :], -1.0, 1.0, ALU.mult, ALU.add,
                  reads=[B("p4_crow")], writes=[B("p4_c2row")])
            op_tt(g, trow[0:1, :], mid[0:1, :], hi[0:1, :], ALU.subtract,
                  reads=[B("p4_mid"), B("p4_hi")], writes=[B("p4_trow")])
            op_tt(g, trow[0:1, :], trow[0:1, :], c2row[0:1, :], ALU.mult,
                  reads=[B("p4_trow"), B("p4_c2row")], writes=[B("p4_trow")])
            op_tt(g, hi[0:1, :], hi[0:1, :], trow[0:1, :], ALU.add,
                  reads=[B("p4_hi"), B("p4_trow")], writes=[B("p4_hi")])
        op_mm(g, g.pb(1, 0, 128, 4), onesr[0:1, :], hi[0:1, :],
              reads=[B("p4_hi"), B("c_ones_row_f32")], writes=g.pbB(1))
        for r in range(ROWS):
            op_ts(g, m_gt[:, r * 16:(r + 1) * 16], gt[:, r * 16:(r + 1) * 16],
                  g.pb(1, 0, 128, 4)[:, r:r + 1], 0.0, ALU.is_gt, ALU.add,
                  reads=[B("p4_g")] + g.pbB(1), writes=[B("p4_mgt"), B("p4_cntc")],
                  accum=cntc[:, r:r + 1])
            op_ts(g, m_eq[:, r * 16:(r + 1) * 16], gt[:, r * 16:(r + 1) * 16],
                  g.pb(1, 0, 128, 4)[:, r:r + 1], None, ALU.is_equal,
                  reads=[B("p4_g")] + g.pbB(1), writes=[B("p4_meq")])
        op_mm(g, g.pb(2, 0, 1, 4), onesf[:, :], cntc[:, :],
              reads=[B("p4_cntc"), B("c_ones_col_f32")], writes=g.pbB(2))
        op_ts(g, needr[0:1, :], g.pb(2, 0, 1, 4), -1.0, 512.0, ALU.mult, ALU.add,
              reads=g.pbB(2), writes=[B("p4_needr")])
        op_mm(g, g.pb(3, 0, 128, 4), onesr[0:1, :], needr[0:1, :],
              reads=[B("p4_needr"), B("c_ones_row_f32")], writes=g.pbB(3))
        op_copy(g, mbf[:, :], m_eq[:, :], reads=[B("p4_meq")], writes=[B("p4_mbf")])
        cumsum_tokens(g, mbf, B("p4_mbf"), cum, B("p4_cum"), csr, cst, cstb, offr, "p4a_")
        for r in range(ROWS):
            sl = slice(r * 16, (r + 1) * 16)
            op_ts(g, cscr[:, sl], cum[:, sl], g.pb(3, 0, 128, 4)[:, r:r + 1], None,
                  ALU.is_le, reads=[B("p4_cum")] + g.pbB(3), writes=[B("p4_cscr")])
            op_tt(g, cscr[:, sl], cscr[:, sl], m_eq[:, sl], ALU.mult,
                  reads=[B("p4_cscr"), B("p4_meq")], writes=[B("p4_cscr")])
            op_tt(g, m_gt[:, sl], m_gt[:, sl], cscr[:, sl], ALU.add,
                  reads=[B("p4_mgt"), B("p4_cscr")], writes=[B("p4_mgt")])
        op_copy(g, mbf[:, :], m_gt[:, :], reads=[B("p4_mgt")], writes=[B("p4_mbf")])
        cumsum_tokens(g, mbf, B("p4_mbf"), cum, B("p4_cum"), csr, cst, cstb, offr, "p4b_")
        op_ts(g, cum[:, :], cum[:, :], -1.0, None, ALU.add,
              reads=[B("p4_cum")], writes=[B("p4_cum")])
        op_tt(g, cum[:, :], cum[:, :], m_gt[:, :], ALU.mult,
              reads=[B("p4_cum"), B("p4_mgt")], writes=[B("p4_cum")])
        op_ts(g, cscr[:, :], m_gt[:, :], -float(1 << 20), float(1 << 20), ALU.mult, ALU.add,
              reads=[B("p4_mgt")], writes=[B("p4_cscr")])
        op_tt(g, dest[:, :], cum[:, :], cscr[:, :], ALU.add,
              reads=[B("p4_cum"), B("p4_cscr")], writes=[B("p4_dest")])
        op_copy(g, destu[:, :], dest[:, :], reads=[B("p4_dest")], writes=[B("p4_destu")])
        iota_f = g.t["c_iota_f"]
        iorow = g.t["c_iota_row128"]
        msel = g.sb("p4_msel", [128, 128], F32)
        dsb = g.sb("p4_dsb", [128, 1], F32)
        outt = g.sb("p4_outt", [128, 2], F32)
        op_copy(g, bass.AP(pay, 0, [[128, 128], [32, 4], [2, 16]]),
                bass.AP(iota_f.tensor if hasattr(iota_f, "tensor") else iota_f, 0,
                        [[16, 128], [0, 4], [1, 16]]),
                reads=[B("c_iota_f")], writes=[B("p4_pay")])
        op_copy(g, bass.AP(pay, 1, [[128, 128], [32, 4], [2, 16]]), gt[:, :],
                reads=[B("p4_g")], writes=[B("p4_pay")])
        for r in range(ROWS):
            for sb_ in range(4):
                for c in range(16):
                    col = r * 16 + c
                    op_ts(g, dsb[:, 0:1], dest[:, col:col + 1], -128.0 * sb_, None,
                          ALU.add, reads=[B("p4_dest")], writes=[B("p4_dsb")])
                    op_ts(g, msel[:, :], iorow[:, :], dsb[:, 0:1], None, ALU.is_equal,
                          reads=[B("c_iota_row128"), B("p4_dsb")], writes=[B("p4_msel")])
                    op_mm(g, g.pb(0, 0, 128, 2), msel[:, :], pay[:, col * 2: col * 2 + 2],
                          reads=[B("p4_msel"), B("p4_pay")], writes=g.pbB(0),
                          start=(c == 0), stop=(c == 15))
                op_copy(g, outt[:, 0:2], g.pb(0, 0, 128, 2), reads=g.pbB(0),
                        writes=[B("p4_outt")])
                op_dma(g, "sync",
                       idsco_hbm[r * KSEL + sb_ * 128: r * KSEL + (sb_ + 1) * 128, :],
                       outt[:, 0:2], reads=[B("p4_outt")], writes=[B("idsco_hbm")])
        g.stack = sv
    g.s.fence()



P5_CONSTS = ["c_ones_col", "c_ones_row", "c_ident", "c_mask_add", "c_headind",
             "c_sel16", "c_invfreq4_f32", "c_ropesign", "c_ones_col_f32",
             "c_ones_row_f32"]
T2 = 512


def phase5(g):
    from contextlib import ExitStack
    import bass_rust
    B = g.s.buf
    phbm = g.t["processed"]
    p2hbm = g.t["processed2"]
    idsco_hbm = g.t["idsco"]
    ident = g.t["c_ident"]
    hind = g.t["c_headind"]
    sel16 = g.t["c_sel16"]
    onesr_f = g.t["c_ones_row_f32"]
    invf = g.t["c_invfreq4_f32"]
    rsign = g.t["c_ropesign"]
    TWO_PI = 2.0 * math.pi

    with ExitStack() as st:
        sv = g.stack
        g.stack = st
        sel = g.sb("p5_sel", [128, 4 * 1024], F32)
        xa2 = g.sb("p5_xa2", [128, 4 * 1024], F32)
        h2 = g.sb("p5_h2", [128, NCH * T2], BF16)
        y_fm = g.sb("p5_yfm", [128, NCH * T2], BF16)
        CS2 = g.sb("p5_cs", [128, T2], F32)
        SS2 = g.sb("p5_ss", [128, T2], F32)
        angs = g.sb("p5_angs", [128, T2], F32)

        ri = g.sb("p5_ri", [128, T2], I32)
        idxt = g.sb("p5_idxt", [128, 4], U32)
        sct = g.sb("p5_sct", [128, 4], F32)
        ds2 = g.sb("p5_ds2", [128, 2], F32)
        posf = g.sb("p5_posf", [1, T2], F32)
        qa = [g.sb("p5_qa0", [128, T2], BF16), g.sb("p5_qa1", [128, T2], BF16)]
        ka = [g.sb("p5_ka0", [128, T2], BF16), g.sb("p5_ka1", [128, T2], BF16)]
        for _t, _n in ((qa[0], "p5_qa0"), (qa[1], "p5_qa1"), (ka[0], "p5_ka0"), (ka[1], "p5_ka1")):
            op_memset(g, _t[64:128, :], 0.0, writes=[B(_n + "_pad")])
        vt = [g.sb("p5_vt0", [128, 4 * 65], BF16), g.sb("p5_vt1", [128, 4 * 65], BF16)]
        Pb = [g.sb("p5_P0", [128, 2048], BF16)]
        Pb.append(Pb[0])
        den_all = g.sb("p5_den", [16, T2], F32)
        dra = g.sb("p5_dra", [16, T2], BF16)
        rbs = g.sb("p5_rbs", [128, T2], BF16)
        dtmp = g.sb("p5_dtmp", [1, T2], F32)
        Ball = g.sb("p5_Ball", [16, T2], BF16)
        nrm = g.sb("p5_nrm", [16, T2], F32)
        mk = g.sb("p5_mk", [16, 4], F32)
        sq = g.sb("p5_sq", [128, T2], BF16)
        wpair = g.sb("p5_wp0", [128, NCH * 384], BF16)
        wswap = g.sb("p5_wsw", [128, NCH * 256], BF16)
        u1 = g.sb("p5_u1", [128, T2], F32)
        u2 = g.sb("p5_u2", [128, T2], F32)
        rr1 = u2
        stt_ = g.sb("p5_stt", [128, 8], F32)
        hs = g.sb("p5_hs", [128, 1024], F32)
        xin = g.sb("p5_xin", [128, 2 * 1024], F32)
        gated = g.sb("p5_gated", [128, 32 * T2], BF16)
        wgu = g.sb("p5_wgu", [128, NCH * 1024], BF16)
        wdb = [g.sb("p5_wd0", [128, 1024], BF16), g.sb("p5_wd1", [128, 1024], BF16)]
        sg_ = g.sb("p5_sg", [128, T2], BF16)
        ln12b = load_vec_bcast(g, g.t["L0_ffn_ln1_2"], "p5_ln1")
        ln22b = load_vec_bcast(g, g.t["L0_ffn_ln2_2"], "p5_ln2")
        breg = g.top_stack.enter_context(g.nc.gpsimd.register("p5bnd"))
        g.nc.gpsimd.reg_mov(breg, T - 1)

        wq_h, wk_h, wv_h = g.t["L0_ffn_wq2"], g.t["L0_ffn_wk2"], g.t["L0_ffn_wv2"]

        for r in range(ROWS):
            rb = r * T
            for tb in range(16):
                sl = tb % 2
                xi = xin[:, sl * 1024:(sl + 1) * 1024]
                op_dma(g, "sync", xi, phbm[rb + tb * 128: rb + (tb + 1) * 128, :],
                       writes=[B(f"p5_xin{sl}")])
                op_dma(g, "sync", p2hbm[rb + tb * 128: rb + (tb + 1) * 128, :], xi,
                       reads=[B(f"p5_xin{sl}")], writes=[B("p5_p2hbm")])
            for gg in range(4):
                op_dma(g, "sync", ds2[:, :],
                       idsco_hbm[r * KSEL + gg * 128: r * KSEL + (gg + 1) * 128, :],
                       writes=[B("p5_ds2")])
                op_copy(g, idxt[:, gg:gg + 1], ds2[:, 0:1],
                        reads=[B("p5_ds2")], writes=[B("p5_idxt")])
                op_copy(g, sct[:, gg:gg + 1], ds2[:, 1:2],
                        reads=[B("p5_ds2")], writes=[B("p5_sct")])
                op_transpose(g, g.pb(1, 0, 1, 128), ds2[:, 0:1], ident[:, :],
                             reads=[B("p5_ds2"), B("c_ident")], writes=g.pbB(1))
                op_copy(g, posf[0:1, gg * 128:(gg + 1) * 128], g.pb(1, 0, 1, 128),
                        reads=g.pbB(1), writes=[B("p5_posf")])
            op_mm(g, g.pb(0), invf[0:1, :], posf[0:1, :],
                  reads=[B("p5_posf"), B("c_invfreq4_f32")], writes=g.pbB(0))
            op_copy(g, angs[:, :], g.pb(0), reads=g.pbB(0), writes=[B("p5_angs")])
            for (tbl, tblB, shift) in ((SS2, "p5_ss", 0.0), (CS2, "p5_cs", math.pi / 2)):
                src = angs
                if shift:
                    op_ts(g, rr1[:, :], angs[:, :], shift, None, ALU.add,
                          reads=[B("p5_angs")], writes=[B("p5_rr1")])
                    src, srcB = rr1, "p5_rr1"
                else:
                    srcB = "p5_angs"
                op_ts(g, tbl[:, :], src[:, :], 1.0 / TWO_PI, None, ALU.mult,
                      reads=[B(srcB)], writes=[B(tblB)])
                op_copy(g, ri[:, :], tbl[:, :], reads=[B(tblB)], writes=[B("p5_ri")])
                op_copy(g, tbl[:, :], ri[:, :], reads=[B("p5_ri")], writes=[B(tblB)])
                op_stt(g, tbl[:, :], tbl[:, :], -TWO_PI, src[:, :], ALU.mult, ALU.add,
                       reads=[B(tblB), B(srcB)], writes=[B(tblB)])
                op_act(g, tbl[:, :], tbl[:, :], AF.Sin, reads=[B(tblB)], writes=[B(tblB)])
            op_ts(g, SS2[:, :], SS2[:, :], rsign[:, 0:1], None, ALU.mult,
                  reads=[B("p5_ss"), B("c_ropesign")], writes=[B("p5_ss")])
            for gg in range(4):
                def f_g(e, o=sel[:, gg * 1024:(gg + 1) * 1024],
                        off=bass_rust.IndirectOffsetOnAxis(axis=0, ap=idxt[:, gg:gg + 1]),
                        eo=rb * 1024):
                    return e.indirect_dma_start(out=o, out_offset=None, in_=phbm[:, :],
                                                in_offset=off, element_offset=eo)
                g.s.op("gpsimd", f_g, reads=[B("p5_idxt"), B("p2_phbm")],
                       writes=[B("p5_sel")], dma=True)
            for gg in range(4):
                s_t = sel[:, gg * 1024:(gg + 1) * 1024]
                op_act(g, hs[:, :], s_t, AF.Square, accum=stt_[:, 0:1],
                       reads=[B("p5_sel")], writes=[B("p5_hs"), B("p5_st0")])
                op_act(g, stt_[:, 1:2], stt_[:, 0:1], AF.Ln, scale=1.0 / 1024.0, bias=1e-6,
                       reads=[B("p5_st0")], writes=[B("p5_st1")])
                op_act(g, stt_[:, 2:3], stt_[:, 1:2], AF.Exp, scale=-0.5,
                       reads=[B("p5_st1")], writes=[B("p5_st2")])
                op_stt(g, hs[:, :], s_t, stt_[:, 2:3], ln12b[:, :], ALU.mult, ALU.mult,
                       reads=[B("p5_sel"), B("p5_st2"), B("p5_ln1")], writes=[B("p5_hs")])
                for c in range(NCH):
                    po = g.pall[:, 7 * 512 + (c % 4) * 128: 7 * 512 + (c % 4 + 1) * 128]
                    op_transpose(g, po, hs[:, c * 128:(c + 1) * 128], ident[:, :],
                                 reads=[B("p5_hs"), B("c_ident")], writes=g.pbB(7))
                    op_copy(g, h2[:, c * T2 + gg * 128: c * T2 + (gg + 1) * 128], po,
                            reads=g.pbB(7), writes=[B(f"p5_h2{c}")])
            h2B = [B(f"p5_h2{c}") for c in range(NCH)]
            for hc in range(NCH):
                wp, wpB = wpair, B("p5_wp0")
                op_dma(g, "sync", wp[:, :],
                       g.t["pk_qkv2"][hc * 128:(hc + 1) * 128, :], writes=[wpB])
                for (woff, soff) in ((0, 0), (128, 128)):
                    op_copy(g, bass.AP(wswap, soff, [[NCH * 256, 128], [256, NCH], [64, 2], [1, 32]]),
                            bass.AP(wp, woff + 32, [[NCH * 384, 128], [384, NCH], [64, 2], [1, 32]]),
                            reads=[wpB], writes=[B("p5_wsw")])
                    op_copy(g, bass.AP(wswap, soff + 32, [[NCH * 256, 128], [256, NCH], [64, 2], [1, 32]]),
                            bass.AP(wp, woff, [[NCH * 384, 128], [384, NCH], [64, 2], [1, 32]]),
                            reads=[wpB], writes=[B("p5_wsw")])
                he, ho = 2 * hc, 2 * hc + 1
                qa_e, qa_o = qa
                ka_e, ka_o = ka
                op_memset(g, ka_e[64:65, :], 1.0, writes=[B("p5_ka0_hi")])
                op_memset(g, ka_o[64:65, :], 1.0, writes=[B("p5_ka1_hi")])
                op_memset(g, mk[:, 0:1], 0.0, writes=[B("p5_mk")])
                for (isq, dst_e, dst_o, deB, doB, wcol) in (
                        (False, ka_e, ka_o, B("p5_ka0_lo"), B("p5_ka1_lo"), 128),
                        (True, qa_e, qa_o, B("p5_qa0_lo"), B("p5_qa1_lo"), 0)):
                    pA, pAB = g.pb(6), g.pbB(6)
                    pB_, pBB = g.pb(7), g.pbB(7)
                    items = [(pA, bass.AP(wp, kc * 384 + wcol, [[NCH * 384, 128], [1, 128]]),
                              h2[:, kc * T2: (kc + 1) * T2],
                              kc == 0, kc == NCH - 1) for kc in range(NCH)]
                    op_mm_group(g, items, reads=[wpB] + h2B, writes=pAB)
                    items = [(pB_, bass.AP(wswap, kc * 256 + wcol, [[NCH * 256, 128], [1, 128]]),
                              h2[:, kc * T2: (kc + 1) * T2],
                              kc == 0, kc == NCH - 1) for kc in range(NCH)]
                    op_mm_group(g, items, reads=[B("p5_wsw")] + h2B, writes=pBB)
                    op_tt(g, u1[:, :], pA, CS2[:, :], ALU.mult,
                          reads=pAB + [B("p5_cs")], writes=[B("p5_u1")])
                    op_tt(g, u2[:, :], pB_, SS2[:, :], ALU.mult,
                          reads=pBB + [B("p5_ss")], writes=[B("p5_u2")])
                    op_tt(g, dst_e[0:64, :], u1[0:64, :], u2[0:64, :], ALU.add,
                          reads=[B("p5_u1"), B("p5_u2")], writes=[deB])
                    op_tt(g, dst_o[0:64, :], u1[64:128, :], u2[64:128, :], ALU.add,
                          reads=[B("p5_u1"), B("p5_u2")], writes=[doB])
                    sqs = sq[:, 0:T2]
                    sqB = B("p5_sq0")
                    op_act(g, sqs, pA, AF.Square, reads=pAB, writes=[sqB])
                    op_mm(g, g.pb(5, 0, 16, T2), hind[:, hc * 16:(hc + 1) * 16], sqs,
                          reads=[sqB, B("c_headind")], writes=g.pbB(5))
                    op_act(g, nrm[:, :], g.pb(5, 0, 16, T2), AF.Sqrt,
                           reads=g.pbB(5), writes=[B("p5_nrm")])
                    if isq:
                        op_ts(g, Ball[:, :], nrm[:, :], mk[:, 1:2], 8.0 * SHIFT_MARGIN,
                              ALU.mult, ALU.add,
                              reads=[B("p5_nrm"), B("p5_mkn")], writes=[B("p5_Ball")])
                    else:
                        op_red(g, mk[:, 2:3], nrm[:, :], ALU.max,
                               reads=[B("p5_nrm")], writes=[B("p5_mk2")])
                        op_tt(g, mk[:, 0:1], mk[:, 0:1], mk[:, 2:3], ALU.max,
                              reads=[B("p5_mk2"), B("p5_mk")], writes=[B("p5_mk")])
                        op_ts(g, mk[:, 1:2], mk[:, 0:1], -1.0, None, ALU.mult,
                              reads=[B("p5_mk")], writes=[B("p5_mkn")])
                op_dma(g, "sync", qa_e[64:65, :], Ball[he:he + 1, :],
                       reads=[B("p5_Ball")], writes=[B("p5_qa0_hi")])
                op_dma(g, "sync", qa_o[64:65, :], Ball[ho:ho + 1, :],
                       reads=[B("p5_Ball")], writes=[B("p5_qa1_hi")])
                if hc == 0:
                    op_memset(g, vt[0][:, :], 1.0, writes=[B("p5_vt0")])
                    op_memset(g, vt[1][:, :], 1.0, writes=[B("p5_vt1")])
                for tb in range(4):
                    pA, pAB = g.pb(6 + tb % 2), g.pbB(6 + tb % 2)
                    items = [(pA[:, 0:128],
                              h2[:, kc * T2 + tb * 128: kc * T2 + (tb + 1) * 128],
                              bass.AP(wp, kc * 384 + 256, [[NCH * 384, 128], [1, 128]]),
                              kc == 0, kc == NCH - 1) for kc in range(NCH)]
                    op_mm_group(g, items, reads=[wpB] + h2B, writes=pAB)
                    op_copy(g, vt[0][:, tb * 65: tb * 65 + 64], pA[:, 0:64],
                            reads=pAB, writes=[B("p5_vt0")])
                    op_copy(g, vt[1][:, tb * 65: tb * 65 + 64], pA[:, 64:128],
                            reads=pAB, writes=[B("p5_vt1")])
                for (h, qa_h, ka_h, vt_h, qaB, kaB, vtB) in (
                        (he, qa_e, ka_e, vt[0], ("p5_qa0_lo", "p5_qa0_hi"),
                         ("p5_ka0_lo", "p5_ka0_hi"), "p5_vt0"),
                        (ho, qa_o, ka_o, vt[1], ("p5_qa1_lo", "p5_qa1_hi"),
                         ("p5_ka1_lo", "p5_ka1_hi"), "p5_vt1")):
                    hr = (h % 2) * 64
                    yb = 4
                    pY = g.pall[0:65, yb * 512:(yb + 1) * 512]
                    pYB = g.pbB(yb)
                    Pt = Pb[0]
                    PtB = B(f"p5_P0")
                    items = []
                    for j in range(4):
                        items.append((g.pall[:, j * 512 + 128 * j: j * 512 + 512],
                                      ka_h[:, j * 128: (j + 1) * 128],
                                      qa_h[:, 128 * j: T2], True, True))
                    op_mm_group(g, items, reads=[B(qaB[0]), B(qaB[1]), B(qaB[0].replace("_lo", "_pad")),
                                                 B(kaB[0]), B(kaB[1]), B(kaB[0].replace("_lo", "_pad"))],
                                writes=g.pbB(0, 3))
                    v = sview(g.pall, 0, [[4096, 128], [640, 4], [1, 128]])
                    m = sview(g.t["c_mask_add"], 0, [[128, 128], [0, 4], [1, 128]])
                    op_tt(g, v, v, m, ALU.add,
                          reads=[B("c_mask_add")] + g.pbB(0, 3), writes=g.pbB(0, 3))
                    for j in range(4):
                        PsB = B(f"{PtB.name}_s{j}")
                        op_act(g, Pt[:, j * 512:(j + 1) * 512],
                               g.pall[:, j * 512:(j + 1) * 512], AF.Exp, scale=0.125,
                               reads=g.pbB(j, j), writes=[PsB])
                        if j > 0:
                            op_memset(g, Pt[:, j * 512: j * 512 + 128 * j], 0.0,
                                      writes=[PsB], reads=[PsB])
                        op_mm_group(g, [(pY, vt_h[:, j * 65: j * 65 + 65],
                                         Pt[:, j * 512: (j + 1) * 512], j == 0, j == 3)],
                                    reads=[PsB, B(vtB)], writes=pYB)
                    op_copy(g, dtmp[0:1, :], g.pall[64:65, yb * 512:(yb + 1) * 512],
                            reads=pYB, writes=[B("p5_dtmp")])
                    op_dma(g, "sync", den_all[h:h + 1, :], dtmp[0:1, :],
                           reads=[B("p5_dtmp")], writes=[B("p5_denall")])
                    op_act(g, y_fm[hr:hr + 64, hc * T2: (hc + 1) * T2],
                           g.pall[0:64, yb * 512:(yb + 1) * 512], AF.Identity,
                           reads=pYB, writes=[B(f"p5_yfm{hc}")])
            op_recip(g, dra[:, :], den_all[:, :], reads=[B("p5_denall")], writes=[B("p5_dra")])
            for hc in range(NCH):
                he, ho = 2 * hc, 2 * hc + 1
                for (h_, p0, p1_) in ((he, 0, 64), (ho, 64, 128)):
                    op_mm(g, g.pall[p0:p1_, 3072:3584], sel16[:, h_ * 64:(h_ + 1) * 64],
                          dra[:, :], reads=[B("p5_dra"), B("c_sel16")], writes=g.pbB(6))
                op_act(g, rbs[:, :], g.pall[:, 3072:3584], AF.Identity,
                       reads=g.pbB(6), writes=[B("p5_rbs")])
                op_tt(g, y_fm[:, hc * T2: (hc + 1) * T2],
                      y_fm[:, hc * T2: (hc + 1) * T2], rbs[:, :], ALU.mult,
                      reads=[B(f"p5_yfm{hc}"), B("p5_rbs")], writes=[B(f"p5_yfm{hc}")])
            op_dma(g, "sync", wgu[:, :], g.t["pk_wo2"][0:128, :],
                   writes=[B("p5_wgu")])
            yfmB = [B(f"p5_yfm{c}") for c in range(NCH)]
            for tb in range(4):
                for ob in range(2):
                    pq = g.pb(ob)
                    pqB = g.pbB(ob)
                    items = [(pq, y_fm[:, kc * T2 + tb * 128: kc * T2 + (tb + 1) * 128],
                              wgu[:, kc * 1024 + ob * 512: kc * 1024 + (ob + 1) * 512],
                              kc == 0, kc == NCH - 1) for kc in range(NCH)]
                    op_mm_group(g, items, reads=[B("p5_wgu")] + yfmB, writes=pqB)
                    op_tt(g, xa2[:, tb * 1024 + ob * 512: tb * 1024 + (ob + 1) * 512],
                          pq, sel[:, tb * 1024 + ob * 512: tb * 1024 + (ob + 1) * 512],
                          ALU.add, reads=pqB + [B("p5_sel")], writes=[B("p5_xa2")])
                xat = xa2[:, tb * 1024:(tb + 1) * 1024]
                op_act(g, hs[:, :], xat, AF.Square, accum=stt_[:, 0:1],
                       reads=[B("p5_xa2")], writes=[B("p5_hs"), B("p5_st0")])
                op_act(g, stt_[:, 1:2], stt_[:, 0:1], AF.Ln, scale=1.0 / 1024.0, bias=1e-6,
                       reads=[B("p5_st0")], writes=[B("p5_st1")])
                op_act(g, stt_[:, 2:3], stt_[:, 1:2], AF.Exp, scale=-0.5,
                       reads=[B("p5_st1")], writes=[B("p5_st2")])
                op_stt(g, hs[:, :], xat, stt_[:, 2:3], ln22b[:, :], ALU.mult, ALU.mult,
                       reads=[B("p5_xa2"), B("p5_st2"), B("p5_ln2")], writes=[B("p5_hs")])
                for c in range(NCH):
                    po = g.pall[:, 7 * 512 + (c % 4) * 128: 7 * 512 + (c % 4 + 1) * 128]
                    op_transpose(g, po, hs[:, c * 128:(c + 1) * 128], ident[:, :],
                                 reads=[B("p5_hs"), B("c_ident")], writes=g.pbB(7))
                    op_copy(g, h2[:, c * T2 + tb * 128: c * T2 + (tb + 1) * 128], po,
                            reads=g.pbB(7), writes=[B(f"p5_h2{c}")])
            wg_h, wu_h, wd_h = g.t["L0_ffn_wg2"], g.t["L0_ffn_wu2"], g.t["L0_ffn_wd2"]
            gatB = [B(f"p5_gated{gc}") for gc in range(32)]
            for blk in range(8):
                wbB = B("p5_wgu")
                op_dma(g, "sync", wgu[:, :],
                       g.t["pk_f2gu"][blk * 128:(blk + 1) * 128, :],
                       writes=[wbB])
                for sub in range(4):
                    gc = blk * 4 + sub
                    pG, pGB = g.pb(0 + sub % 2), g.pbB(0 + sub % 2)
                    pU, pUB = g.pb(2 + sub % 2), g.pbB(2 + sub % 2)
                    items = [(pG[:, 0:T2],
                              bass.AP(wgu, kc * 1024 + sub * 128, [[NCH * 1024, 128], [1, 128]]),
                              h2[:, kc * T2: (kc + 1) * T2],
                              kc == 0, kc == NCH - 1) for kc in range(NCH)]
                    op_mm_group(g, items, reads=[wbB] + h2B, writes=pGB)
                    items = [(pU[:, 0:T2],
                              bass.AP(wgu, kc * 1024 + 512 + sub * 128, [[NCH * 1024, 128], [1, 128]]),
                              h2[:, kc * T2: (kc + 1) * T2],
                              kc == 0, kc == NCH - 1) for kc in range(NCH)]
                    op_mm_group(g, items, reads=[wbB] + h2B, writes=pUB)
                    op_act(g, sg_[:, :], pG[:, 0:T2], AF.Silu, reads=pGB, writes=[B("p5_sg")])
                    op_tt(g, gated[:, gc * T2:(gc + 1) * T2], sg_[:, :], pU[:, 0:T2],
                          ALU.mult, reads=[B("p5_sg")] + pUB, writes=[gatB[gc]])
            for gc in range(32):
                wd_t = wdb[gc % 2]
                wdB = B(f"p5_wd{gc % 2}")
                op_dma(g, "sync", wd_t[:, :], wd_h[gc * 128:(gc + 1) * 128, :], writes=[wdB])
                items = []
                for tb in range(4):
                    for ob in range(2):
                        items.append((g.pb(tb * 2 + ob),
                                      gated[:, gc * T2 + tb * 128: gc * T2 + (tb + 1) * 128],
                                      wd_t[:, ob * 512:(ob + 1) * 512],
                                      gc == 0, gc == 31))
                op_mm_group(g, items, reads=[wdB, gatB[gc]], writes=g.pbB(0, 7))
            for tb in range(4):
                sl = tb % 2
                yt = xin[:, sl * 1024:(sl + 1) * 1024]
                for ob in range(2):
                    op_tt(g, yt[:, ob * 512:(ob + 1) * 512], g.pb(tb * 2 + ob),
                          xa2[:, tb * 1024 + ob * 512: tb * 1024 + (ob + 1) * 512], ALU.add,
                          reads=g.pbB(tb * 2 + ob) + [B("p5_xa2")], writes=[B(f"p5_xin{sl}")])
                op_tt(g, yt[:, :], yt[:, :], sel[:, tb * 1024:(tb + 1) * 1024], ALU.subtract,
                      reads=[B(f"p5_xin{sl}"), B("p5_sel")], writes=[B(f"p5_xin{sl}")])
                op_stt(g, yt[:, :], yt[:, :], sct[:, tb:tb + 1],
                       sel[:, tb * 1024:(tb + 1) * 1024], ALU.mult, ALU.add,
                       reads=[B(f"p5_xin{sl}"), B("p5_sct"), B("p5_sel")],
                       writes=[B(f"p5_xin{sl}")])
                def f_s(e, i=yt,
                        off=bass_rust.IndirectOffsetOnAxis(axis=0, ap=idxt[:, tb:tb + 1]),
                        eo=rb * 1024, breg=breg):
                    return e.indirect_dma_start(out=p2hbm[:, :], out_offset=off, in_=i,
                                                in_offset=None, element_offset=eo,
                                                bounds_check=breg, oob_is_err=False)
                g.s.op("gpsimd", f_s, reads=[B(f"p5_xin{sl}"), B("p5_idxt"), B("p5_p2hbm")],
                       writes=[B("p5_p2hbm")], dma=True)
        g.stack = sv
    g.s.fence()



P678_CONSTS = ["c_ones_col", "c_ones_row", "c_ident", "c_mask_mul", "c_sel16",
               "c_ones_col_f32", "c_ones_row_f32", "c_m8", "c_p8", "c_q8"]


def phase678(g):
    from contextlib import ExitStack
    B = g.s.buf
    x1, p2h = g.t["x1"], g.t["processed2"]
    x2h = g.t["p67_x2"]
    x3h = g.t["p67_x3"]
    outh = g.t["out"]
    ident = g.t["c_ident"]
    mmul = g.t["c_mask_mul"]
    sel16 = g.t["c_sel16"]
    ones = g.t["c_ones_col"]
    onesr = g.t["c_ones_row"]

    with ExitStack() as st:
        sv = g.stack
        g.stack = st
        wti = g.sb("p8_wti", [128, NCH * 1024], BF16)
        rprime = g.sb("p8_rp", [128, NCH * 8], BF16)
        laA = g.sb("p8_laA", [128, NCH * 8], BF16)
        ballB = g.sb("p8_ballB", [64, 1024], BF16)
        m8 = g.t["c_m8"]
        bpcol = g.sb("p8_bpcol", [8, 1], F32)
        brow = g.sb("p8_brow", [8, 2], BF16)
        bbb = load_vec_bcast(g, g.t["L1_ffn_base_b"], "p8_bb")
        _prep = ExitStack()
        _sv0 = g.stack
        g.stack = _prep
        wld = g.sb("p8_wld", [128, 1024], F32)
        for oc in range(NCH):
            op_dma(g, "sync", wld[:, :],
                   g.t["L1_ffn_base_w"][oc * 128:(oc + 1) * 128, :], writes=[B("p8_wld")])
            for kc in range(NCH):
                po = g.pall[:, 7 * 512 + (kc % 4) * 128: 7 * 512 + (kc % 4 + 1) * 128]
                op_transpose(g, po, wld[:, kc * 128:(kc + 1) * 128], ident[:, :],
                             reads=[B("p8_wld"), B("c_ident")], writes=g.pbB(7))
                dst = wti[:, kc * 1024 + oc * 128: kc * 1024 + (oc + 1) * 128]
                if kc == oc:
                    op_tt(g, dst, po, ident[:, :], ALU.add,
                          reads=g.pbB(7) + [B("c_ident")], writes=[B("p8_wti")])
                else:
                    op_copy(g, dst, po, reads=g.pbB(7), writes=[B("p8_wti")])
        rsb = g.sb("p8_rsb", [8, 1024], BF16)
        op_dma(g, "gpsimd", rsb[:, :], g.t["L1_ffn_router_w"][:, :], writes=[B("p8_rsb")])
        op_dma(g, "gpsimd", bpcol[:, 0:1],
               g.t["L1_ffn_router_b"].rearrange("(a b) -> a b", b=1), writes=[B("p8_bp")])
        for kc in range(NCH):
            op_mm(g, g.pb(6, 0, 128, 8), rsb[:, kc * 128:(kc + 1) * 128], m8[:, :],
                  reads=[B("p8_rsb"), B("c_m8")], writes=g.pbB(6))
            op_copy(g, rprime[:, kc * 8:(kc + 1) * 8], g.pb(6, 0, 128, 8),
                    reads=g.pbB(6), writes=[B("p8_rp")])
        op_copy(g, brow[:, 0:1], bpcol[:, 0:1], reads=[B("p8_bp")], writes=[B("p8_brow")])
        op_mm(g, g.pb(6, 0, 1, 8), brow[:, 0:1], m8[:, :],
              reads=[B("p8_brow"), B("c_m8")], writes=g.pbB(6))
        bprow = g.sb("p8_bprow", [1, 8], F32)
        op_copy(g, bprow[0:1, :], g.pb(6, 0, 1, 8), reads=g.pbB(6), writes=[B("p8_bprow")])
        op_transpose(g, g.pb(6, 0, 8, 1), bprow[0:1, :], ident[0:1, 0:1],
                     reads=[B("p8_bprow"), B("c_ident")], writes=g.pbB(6))
        op_copy(g, bpcol[:, 0:1], g.pb(6, 0, 8, 1), reads=g.pbB(6), writes=[B("p8_bp2")])
        op_dma(g, "gpsimd", laA[:, :],
               g.t["L1_ffn_lora_A"].rearrange("(a p) r -> p a r", p=128), writes=[B("p8_laA")])
        op_dma(g, "gpsimd", ballB[:, :],
               g.t["L1_ffn_lora_B"].rearrange("e r o -> (e r) o"), writes=[B("p8_ballB")])
        _prep.close()
        g.stack = _sv0
        g.s.fence()

        sfm = g.sb("p67_sfm", [128, NCH * T], BF16)
        y_fm = g.sb("p67_yfm", [128, NCH * T], BF16)
        xin = g.sb("p67_xin", [128, 2 * 1024], F32)
        xin2 = g.sb("p67_xin2", [128, 1024], F32)
        hs = g.sb("p67_hs", [128, 2 * 1024], F32)
        stt_ = g.sb("p67_stt", [128, 8], F32)
        sc = g.sb("p67_sc", [128, 1024], F32)
        qp = g.sb("p67_qp", [128, T], BF16)
        kp = g.sb("p67_kp", [128, T], BF16)
        ktok = [g.sb("p67_kt0", [128, 16 * 64], BF16), g.sb("p67_kt1", [128, 16 * 64], BF16)]
        vtok = [g.sb("p67_vt0", [128, 16 * 65], BF16), g.sb("p67_vt1", [128, 16 * 65], BF16)]
        Ps = [g.sb("p67_P0", [128, 2048], BF16), g.sb("p67_P1", [128, 2048], BF16)]
        ssb = g.sb("p67_Ssb", [128, 65], BF16)
        Ssb = [ssb[0:64, :], ssb[64:128, :]]
        den_all = g.sb("p67_den", [16, T], BF16)
        den_f32 = g.sb("p67_denf", [16, 512], F32)
        dra = g.sb("p67_dra", [16, T], BF16)
        rbs = g.sb("p67_rbs", [128, 1024], BF16)
        dtmp = g.sb("p67_dtmp", [1, 2 * 512], BF16)
        wpair = g.sb("p67_wp", [128, NCH * 384], BF16)
        u1 = g.sb("p67_u1", [128, 512], F32)
        wq_h, wk_h, wv_h = g.t["L1_attn_Wq"], g.t["L1_attn_Wk"], g.t["L1_attn_Wv"]
        sfmB = [B(f"p67_sfm{c}") for c in range(NCH)]

        for r in range(ROWS):
            rb = r * T
            for tb in range(16):
                sl = tb % 2
                xi = xin[:, sl * 1024:(sl + 1) * 1024]
                x2_ = xin2[:, 0:1024]
                op_dma(g, "sync", xi, x1[rb + tb * 128: rb + (tb + 1) * 128, :],
                       writes=[B(f"p67_xin{sl}")])
                op_dma(g, "sync", x2_, p2h[rb + tb * 128: rb + (tb + 1) * 128, :],
                       writes=[B("p67_xin2")], reads=[B("p5_p2hbm")])
                hst = hs[:, sl * 1024:(sl + 1) * 1024]
                op_tt(g, hst, xi, x2_, ALU.add,
                      reads=[B(f"p67_xin{sl}"), B("p67_xin2")], writes=[B(f"p67_hs{sl}")])
                emit_ln_apply(g, hs[:, sl * 1024:(sl + 1) * 1024], sc, stt_,
                              hs[:, sl * 1024:(sl + 1) * 1024],
                              f"p67_hs{sl}", "p67_sc", "p67_st", f"p67_hs{sl}")
                op_dma(g, "sync", x2h[rb + tb * 128: rb + (tb + 1) * 128, :], hst,
                       reads=[B(f"p67_hs{sl}")], writes=[B("p67_x2hbm")])
                for c in range(NCH):
                    po = g.pall[:, 7 * 512 + (c % 4) * 128: 7 * 512 + (c % 4 + 1) * 128]
                    op_transpose(g, po, hst[:, c * 128:(c + 1) * 128], ident[:, :],
                                 reads=[B(f"p67_hs{sl}"), B("c_ident")], writes=g.pbB(7))
                    op_copy(g, sfm[:, c * T + tb * 128: c * T + (tb + 1) * 128], po,
                            reads=g.pbB(7), writes=[sfmB[c]])
            for hc in range(NCH):
                wpB = B("p67_wp")
                op_dma(g, "sync", wpair[:, :],
                       g.t["pk_qkvL1"][hc * 128:(hc + 1) * 128, :], writes=[wpB])
                he, ho = 2 * hc, 2 * hc + 1
                for (dst, dstB, wcol) in ((qp, B("p67_qp"), 0), (kp, B("p67_kp"), 128)):
                    for tt in range(4):
                        pA, pAB = g.pb(4), g.pbB(4)
                        items = [(pA, bass.AP(wpair, kc * 384 + wcol, [[NCH * 384, 128], [1, 128]]),
                                  sfm[:, kc * T + tt * 512: kc * T + (tt + 1) * 512],
                                  kc == 0, kc == NCH - 1) for kc in range(NCH)]
                        op_mm_group(g, items, reads=[wpB] + sfmB, writes=pAB)
                        op_ts(g, u1[:, :], pA, 0.0, None, ALU.min,
                              reads=pAB, writes=[B("p67_u1")])
                        op_act(g, u1[:, :], u1[:, :], AF.Exp,
                               reads=[B("p67_u1")], writes=[B("p67_u1")])
                        op_stt(g, dst[:, tt * 512:(tt + 1) * 512], pA, 0.0, u1[:, :],
                               ALU.max, ALU.add, reads=pAB + [B("p67_u1")], writes=[dstB])
                if hc == 0:
                    op_memset(g, vtok[0][:, :], 1.0, writes=[B("p67_vt0")])
                    op_memset(g, vtok[1][:, :], 1.0, writes=[B("p67_vt1")])
                for tb in range(16):
                    pA, pAB = g.pb(4), g.pbB(4)
                    items = [(pA[:, 0:128],
                              sfm[:, kc * T + tb * 128: kc * T + (tb + 1) * 128],
                              bass.AP(wpair, kc * 384 + 128, [[NCH * 384, 128], [1, 128]]),
                              kc == 0, kc == NCH - 1) for kc in range(NCH)]
                    op_mm_group(g, items, reads=[wpB] + sfmB, writes=pAB)
                    op_ts(g, u1[:, 0:128], pA[:, 0:128], 0.0, None, ALU.min,
                          reads=pAB, writes=[B("p67_u1")])
                    op_act(g, u1[:, 0:128], u1[:, 0:128], AF.Exp,
                           reads=[B("p67_u1")], writes=[B("p67_u1")])
                    op_stt(g, u1[:, 128:256], pA[:, 0:128], 0.0, u1[:, 0:128],
                           ALU.max, ALU.add, reads=pAB + [B("p67_u1")], writes=[B("p67_u1b")])
                    op_copy(g, ktok[0][:, tb * 64: tb * 64 + 64], u1[:, 128:192],
                            reads=[B("p67_u1b")], writes=[B("p67_kt0")])
                    op_copy(g, ktok[1][:, tb * 64: tb * 64 + 64], u1[:, 192:256],
                            reads=[B("p67_u1b")], writes=[B("p67_kt1")])
                    pV, pVB = g.pb(5), g.pbB(5)
                    items = [(pV[:, 0:128],
                              sfm[:, kc * T + tb * 128: kc * T + (tb + 1) * 128],
                              bass.AP(wpair, kc * 384 + 256, [[NCH * 384, 128], [1, 128]]),
                              kc == 0, kc == NCH - 1) for kc in range(NCH)]
                    op_mm_group(g, items, reads=[wpB] + sfmB, writes=pVB)
                    op_copy(g, vtok[0][:, tb * 65: tb * 65 + 64], pV[:, 0:64],
                            reads=pVB, writes=[B("p67_vt0")])
                    op_copy(g, vtok[1][:, tb * 65: tb * 65 + 64], pV[:, 64:128],
                            reads=pVB, writes=[B("p67_vt1")])
                heads = ((0, he, 0, ktok[0], vtok[0], Ssb[0], B("p67_S0"), 6),
                         (1, ho, 64, ktok[1], vtok[1], Ssb[1], B("p67_S1"), 7))
                seq = [(C, hd) for C in range(4) for hd in heads]

                def emit_A(i):
                    C, (idx_h, h_, hr, kt, vt_, SsbT, SsbB, sbank) = seq[i]
                    qpB, kpB = B("p67_qp"), B("p67_kp")
                    items = []
                    for j in range(4):
                        kti = C * 4 + j
                        items.append((g.pall[:, j * 512 + 128 * j: j * 512 + 512],
                                      kp[hr:hr + 64, kti * 128: (kti + 1) * 128],
                                      qp[hr:hr + 64, C * 512 + 128 * j: (C + 1) * 512],
                                      True, True))
                    op_mm_group(g, items, reads=[kpB, qpB], writes=g.pbB(0, 3))

                def emit_evac(i):
                    Pt = Ps[i % 2]
                    PtB = B(f"p67_P{i % 2}")
                    for j in range(4):
                        op_tt(g, Pt[:, j * 512 + 128 * j: j * 512 + 128 * j + 128],
                              g.pall[:, j * 512 + 128 * j: j * 512 + 128 * j + 128],
                              g.t["c_mask_mul"][:, :], ALU.mult,
                              reads=g.pbB(j, j) + [B("c_mask_mul")], writes=[PtB])
                        if j < 3:
                            op_act(g, Pt[:, j * 512 + 128 * j + 128: (j + 1) * 512],
                                   g.pall[:, j * 512 + 128 * j + 128: (j + 1) * 512],
                                   AF.Identity, reads=g.pbB(j, j), writes=[PtB])
                        if j > 0:
                            op_memset(g, Pt[:, j * 512: j * 512 + 128 * j], 0.0,
                                      writes=[PtB], reads=[PtB])

                def emit_tail(i):
                    C, (idx_h, h_, hr, kt, vt_, SsbT, SsbB, sbank) = seq[i]
                    ktB, vtB = B(f"p67_kt{idx_h}"), B(f"p67_vt{idx_h}")
                    qpB = B("p67_qp")
                    Pt = Ps[i % 2]
                    PtB = B(f"p67_P{i % 2}")
                    yb = 4 + idx_h
                    pY = g.pall[0:65, yb * 512:(yb + 1) * 512]
                    pYB = g.pbB(yb)
                    items = []
                    for j in range(4):
                        kti = C * 4 + j
                        items.append((pY, vt_[:, kti * 65: kti * 65 + 65],
                                      Pt[:, j * 512: (j + 1) * 512], j == 0,
                                      C == 0 and j == 3))
                    if C > 0:
                        items.append((pY, SsbT[:, :],
                                      qp[hr:hr + 64, C * 512: (C + 1) * 512], False, True))
                    op_mm_group(g, items, reads=[PtB, vtB, SsbB, qpB], writes=pYB)
                    items = []
                    for j in range(4):
                        kti = C * 4 + j
                        items.append((g.pall[hr:hr + 64, sbank * 512: sbank * 512 + 65],
                                      kt[:, kti * 64:(kti + 1) * 64],
                                      vt_[:, kti * 65:(kti + 1) * 65],
                                      C == 0 and j == 0, C == 3 and j == 3))
                    op_mm_group(g, items, reads=[ktB, vtB], writes=g.pbB(sbank))
                    op_copy(g, SsbT[:, :],
                            g.pall[hr:hr + 64, sbank * 512: sbank * 512 + 65],
                            reads=g.pbB(sbank), writes=[SsbB])
                    sl = idx_h
                    op_copy(g, dtmp[0:1, sl * 512:(sl + 1) * 512],
                            g.pall[64:65, yb * 512:(yb + 1) * 512],
                            reads=pYB, writes=[B(f"p67_dt{sl}")])
                    op_dma(g, "sync", den_all[h_:h_ + 1, C * 512:(C + 1) * 512],
                           dtmp[0:1, sl * 512:(sl + 1) * 512],
                           reads=[B(f"p67_dt{sl}")], writes=[B("p67_denall")])
                    op_act(g, y_fm[hr:hr + 64, hc * T + C * 512: hc * T + (C + 1) * 512],
                           g.pall[0:64, yb * 512:(yb + 1) * 512], AF.Identity,
                           reads=pYB, writes=[B(f"p67_yfm{hc}")])

                for i in range(len(seq)):
                    emit_A(i)
                    emit_evac(i)
                    if i > 0:
                        emit_tail(i - 1)
                emit_tail(len(seq) - 1)
            for seg4 in range(4):
                op_ts(g, den_f32[:, :], den_all[:, seg4 * 512:(seg4 + 1) * 512], 1e-6, None,
                      ALU.add, reads=[B("p67_denall")], writes=[B("p67_denf")])
                op_recip(g, dra[:, seg4 * 512:(seg4 + 1) * 512], den_f32[:, :],
                         reads=[B("p67_denf")], writes=[B("p67_dra")])
            for hc in range(NCH):
                he, ho = 2 * hc, 2 * hc + 1
                for seg in range(2):
                    for (h_, p0, p1_) in ((he, 0, 64), (ho, 64, 128)):
                        op_mm(g, g.pall[p0:p1_, 3072:3584], sel16[:, h_ * 64:(h_ + 1) * 64],
                              dra[:, seg * 1024: seg * 1024 + 512],
                              reads=[B("p67_dra"), B("c_sel16")], writes=g.pbB(6))
                        op_mm(g, g.pall[p0:p1_, 3584:4096], sel16[:, h_ * 64:(h_ + 1) * 64],
                              dra[:, seg * 1024 + 512: seg * 1024 + 1024],
                              reads=[B("p67_dra"), B("c_sel16")], writes=g.pbB(7))
                    op_act(g, rbs[:, :], g.pall[:, 3072:4096], AF.Identity,
                           reads=g.pbB(6, 7), writes=[B("p67_rbs")])
                    op_tt(g, y_fm[:, hc * T + seg * 1024: hc * T + (seg + 1) * 1024],
                          y_fm[:, hc * T + seg * 1024: hc * T + (seg + 1) * 1024],
                          rbs[:, :], ALU.mult,
                          reads=[B(f"p67_yfm{hc}"), B("p67_rbs")], writes=[B(f"p67_yfm{hc}")])
            if r == 0 and "dbg_yfm" in g.t:
                op_dma(g, "sync", g.t["dbg_yfm"][:, :], y_fm[:, :],
                       reads=[B("p67_yfm0")] + [B(f"p67_yfm{c}") for c in range(NCH)],
                       writes=[B("dbg1")])
                op_dma(g, "sync", g.t["dbg_den"][:, :], den_all[:, :],
                       reads=[B("p67_denall")], writes=[B("dbg2")])
            wo_sb = g.sb("p67_wo", [128, NCH * 1024], BF16) if r == 0 else g.t["p67_wo"]
            if r == 0:
                op_dma(g, "sync", wo_sb[:, :], g.t["pk_WoL1"][0:128, :],
                       writes=[B("p67_wo")])
            yfmB = [B(f"p67_yfm{c}") for c in range(NCH)]
            for tb in range(16):
                sl = tb % 2
                xi = xin[:, sl * 1024:(sl + 1) * 1024]
                op_dma(g, "sync", xi, x2h[rb + tb * 128: rb + (tb + 1) * 128, :],
                       writes=[B(f"p67_xin{sl}")], reads=[B("p67_x2hbm")])
                hst = hs[:, sl * 1024:(sl + 1) * 1024]
                for ob in range(2):
                    pq = g.pb(ob)
                    pqB = g.pbB(ob)
                    items = [(pq, y_fm[:, kc * T + tb * 128: kc * T + (tb + 1) * 128],
                              wo_sb[:, kc * 1024 + ob * 512: kc * 1024 + (ob + 1) * 512],
                              kc == 0, kc == NCH - 1) for kc in range(NCH)]
                    op_mm_group(g, items, reads=[B("p67_wo")] + yfmB, writes=pqB)
                    op_tt(g, hst[:, ob * 512:(ob + 1) * 512], pq, xi[:, ob * 512:(ob + 1) * 512],
                          ALU.add, reads=pqB + [B(f"p67_xin{sl}")], writes=[B(f"p67_hs{sl}")])
                emit_ln_apply(g, hst, sc, stt_, hst,
                              f"p67_hs{sl}", "p67_sc", "p67_st", f"p67_hs{sl}")
                op_dma(g, "sync", x3h[rb + tb * 128: rb + (tb + 1) * 128, :], hst,
                       reads=[B(f"p67_hs{sl}")], writes=[B("p67_x3hbm")])
                for c in range(NCH):
                    po = g.pall[:, 7 * 512 + (c % 4) * 128: 7 * 512 + (c % 4 + 1) * 128]
                    op_transpose(g, po, hst[:, c * 128:(c + 1) * 128], ident[:, :],
                                 reads=[B(f"p67_hs{sl}"), B("c_ident")], writes=g.pbB(7))
                    op_copy(g, sfm[:, c * T + tb * 128: c * T + (tb + 1) * 128], po,
                            reads=g.pbB(7), writes=[sfmB[c]])
            for tt in range(4):
                ts0 = tt * 512
                items = [(g.pb(5, 0, 8), rprime[:, kc * 8:(kc + 1) * 8],
                          sfm[:, kc * T + ts0: kc * T + ts0 + 512],
                          kc == 0, kc == NCH - 1) for kc in range(NCH)]
                op_mm_group(g, items, reads=[B("p8_rp")] + sfmB, writes=g.pbB(5))
                gun = hs[:, 0:512]
                op_ts(g, gun[0:8, :], g.pb(5, 0, 8), bpcol[:, 0:1], None, ALU.add,
                      reads=g.pbB(5) + [B("p8_bp2")], writes=[B("p67_hs0")])
                op_act(g, gun[0:8, :], gun[0:8, :], AF.Exp,
                       reads=[B("p67_hs0")], writes=[B("p67_hs0")])
                op_copy(g, rbs[0:8, 0:512], gun[0:8, :], reads=[B("p67_hs0")], writes=[B("p67_rbs")])
                op_mm(g, g.pb(5, 0, 1), ones[0:8, 0:1], rbs[0:8, 0:512],
                      reads=[B("p67_rbs"), B("c_ones_col")], writes=g.pbB(5))
                op_recip(g, dtmp[0:1, 0:128].bitcast(F32) if False else u1[0:1, 0:512],
                         g.pb(5, 0, 1), reads=g.pbB(5), writes=[B("p67_u1")])
                op_copy(g, rbs[0:1, 512:1024], u1[0:1, 0:512],
                        reads=[B("p67_u1")], writes=[B("p67_rbs")])
                op_mm(g, g.pb(5, 0, 8), onesr[0:1, 0:8], rbs[0:1, 512:1024],
                      reads=[B("p67_rbs"), B("c_ones_row")], writes=g.pbB(5))
                gat = sc[0:8, 0:512]
                op_tt(g, gat, gun[0:8, :], g.pb(5, 0, 8), ALU.mult,
                      reads=[B("p67_hs0")] + g.pbB(5), writes=[B("p67_sc")])
                gatbf = rbs[0:8, 0:512]
                op_copy(g, gatbf, gat, reads=[B("p67_sc")], writes=[B("p67_rbs")])
                items = [(g.pb(5, 0, 8), laA[:, kc * 8:(kc + 1) * 8],
                          sfm[:, kc * T + ts0: kc * T + ts0 + 512],
                          kc == 0, kc == NCH - 1) for kc in range(NCH)]
                op_mm_group(g, items, reads=[B("p8_laA")] + sfmB, writes=g.pbB(5))
                op_copy(g, rbs[0:8, 512:1024], g.pb(5, 0, 8), reads=g.pbB(5), writes=[B("p67_rbs")])
                op_mm(g, g.pb(5, 0, 64), g.t["c_p8"][:, :], rbs[0:8, 512:1024],
                      reads=[B("p67_rbs"), B("c_p8")], writes=g.pbB(5))
                op_mm(g, g.pb(6, 0, 64), g.t["c_q8"][:, :], gatbf,
                      reads=[B("p67_rbs"), B("c_q8")], writes=g.pbB(6))
                op_copy(g, hs[0:64, 0:512], g.pb(5, 0, 64), reads=g.pbB(5), writes=[B("p67_hs0")])
                mall = rbs[0:64, 0:512]
                op_tt(g, mall, hs[0:64, 0:512], g.pb(6, 0, 64), ALU.mult,
                      reads=[B("p67_hs0")] + g.pbB(6), writes=[B("p67_rbs")])
                for tb in range(4):
                    gt_ = tt * 4 + tb
                    sl = tb % 2
                    for ob in range(2):
                        pq = g.pb(sl * 2 + ob)
                        pqB = g.pbB(sl * 2 + ob)
                        items = [(pq, sfm[:, kc * T + gt_ * 128: kc * T + (gt_ + 1) * 128],
                                  wti[:, kc * 1024 + ob * 512: kc * 1024 + (ob + 1) * 512],
                                  kc == 0, False) for kc in range(NCH)]
                        items.append((pq, mall[:, tb * 128:(tb + 1) * 128],
                                      ballB[:, ob * 512:(ob + 1) * 512], False, True))
                        op_mm_group(g, items,
                                    reads=[B("p8_wti"), B("p67_rbs"), B("p8_ballB")] + sfmB,
                                    writes=pqB)
                        op_tt(g, xin[:, sl * 1024 + ob * 512: sl * 1024 + (ob + 1) * 512],
                              pq, bbb[:, ob * 512:(ob + 1) * 512], ALU.add,
                              reads=pqB + [B("p8_bb")], writes=[B(f"p67_xin{sl}")])
                    xt_ = xin[:, sl * 1024:(sl + 1) * 1024]
                    emit_ln_apply(g, xt_, sc, stt_, xt_,
                                  f"p67_xin{sl}", "p67_sc", "p67_st", f"p67_xin{sl}")
                    op_dma(g, "sync", outh[rb + gt_ * 128: rb + (gt_ + 1) * 128, :], xt_,
                           reads=[B(f"p67_xin{sl}")], writes=[B("out_hbm")])
        g.stack = sv
    g.s.fence()


def phase3_allreduce(g):
    def fn(e):
        return e.collective_compute(
            "AllReduce", ALU.add,
            replica_groups=[list(range(NCORES))],
            ins=[g.t["partials"].opt()],
            outs=[g.t["partials_all"].opt()])
    g.s.op("gpsimd", fn, reads=[g.s.buf("p2_parthbm")],
           writes=[g.s.buf("partials_all_hbm")])
    g.s.fence()



W_P1 = ["L0_attn_w_qkv", "L0_attn_ln_w", "L0_attn_w_proj"]
W_P2 = ["L0_ffn_ln1_1", "L0_ffn_wq1", "L0_ffn_wk1", "L0_ffn_wv1", "L0_ffn_wo1",
        "L0_ffn_ln2_1", "L0_ffn_wg1", "L0_ffn_wu1", "L0_ffn_wd1",
        "L0_ffn_prior_norm_w", "L0_ffn_prior_w"]
W_P5 = ["L0_ffn_ln1_2", "L0_ffn_wq2", "L0_ffn_wk2", "L0_ffn_wv2", "L0_ffn_wo2",
        "L0_ffn_ln2_2", "L0_ffn_wg2", "L0_ffn_wu2", "L0_ffn_wd2"]
W_P678 = ["L1_attn_Wq", "L1_attn_Wk", "L1_attn_Wv", "L1_attn_Wo",
          "L1_ffn_base_w", "L1_ffn_base_b", "L1_ffn_lora_A", "L1_ffn_lora_B",
          "L1_ffn_router_w", "L1_ffn_router_b"]


def _run(nc, cin, per_core, shared, outs):
    in_maps = []
    for c in range(NCORES):
        m = dict(shared)
        m.update(cin)
        for k, v in per_core.items():
            m[k] = v[c]
        in_maps.append(m)
    res = run_bass_kernel_spmd(nc, in_maps, core_ids=list(range(NCORES)))
    return {k: [res.results[c][k] for c in range(NCORES)] for k in outs}


def kernel(**inputs):
    x = np.ascontiguousarray(inputs["L0_attn_x"].reshape(NCORES, ROWS * T, D))

    decls = [("in", "x0", [ROWS * T, D], F32), ("out", "x1", [ROWS * T, D], F32)]
    decls += [("in", n, list(inputs[n].shape), F32) for n in W_P1]
    nc, cin = build_program(decls, P1_CONSTS, [phase1])
    o1 = _run(nc, cin, {"x0": x}, {n: inputs[n] for n in W_P1}, ["x1"])

    decls = [("in", "x1", [ROWS * T, D], F32),
             ("out", "processed", [ROWS * T, D], F32),
             ("out", "d_st", [ROWS * 128, 16], F32), ("out", "d_ch", [ROWS * 128, 16], F32),
             ("out", "partials", [ROWS, 2], F32),
             ("tmp", "p2_ml", [ROWS * T, 2 * D], F32), ("tmp", "p2_xa", [ROWS * T, D], F32)]
    decls += [("in", n, list(inputs[n].shape), F32) for n in W_P2]
    nc, cin = build_program(decls, P2_CONSTS, [phase2])
    o2 = _run(nc, cin, {"x1": o1["x1"]}, {n: inputs[n] for n in W_P2},
              ["processed", "d_st", "d_ch", "partials"])

    partials_all = np.ascontiguousarray(np.stack(o2["partials"]).reshape(NCORES * ROWS, 2))

    decls = [("in", "d_st", [ROWS * 128, 16], F32), ("in", "d_ch", [ROWS * 128, 16], F32),
             ("in", "partials_all", [NCORES * ROWS, 2], F32),
             ("out", "idsco", [ROWS * KSEL, 2], F32)]
    nc, cin = build_program(decls, P4_CONSTS, [phase4], dma_scratch=48 * 1024)
    o4 = _run(nc, cin, {"d_st": o2["d_st"], "d_ch": o2["d_ch"]},
              {"partials_all": partials_all}, ["idsco"])

    decls = [("in", "processed", [ROWS * T, D], F32),
             ("in", "idsco", [ROWS * KSEL, 2], F32),
             ("out", "processed2", [ROWS * T, D], F32)]
    decls += [("in", n, list(inputs[n].shape), F32) for n in W_P5]
    nc, cin = build_program(decls, P5_CONSTS, [phase5], dma_scratch=28 * 1024)
    o5 = _run(nc, cin, {"processed": o2["processed"], "idsco": o4["idsco"]},
              {n: inputs[n] for n in W_P5}, ["processed2"])

    decls = [("in", "x1", [ROWS * T, D], F32), ("in", "processed2", [ROWS * T, D], F32),
             ("out", "out", [ROWS * T, D], F32),
             ("tmp", "p67_x2", [ROWS * T, D], F32), ("tmp", "p67_x3", [ROWS * T, D], F32)]
    decls += [("in", n, list(inputs[n].shape), F32) for n in W_P678]
    nc, cin = build_program(decls, P678_CONSTS, [phase678])
    o6 = _run(nc, cin, {"x1": o1["x1"], "processed2": o5["processed2"]},
              {n: inputs[n] for n in W_P678}, ["out"])

    return np.concatenate([o6["out"][c].reshape(ROWS, T, D) for c in range(NCORES)], 0)


ALL_CONSTS = sorted(set(P1_CONSTS + P2_CONSTS + P4_CONSTS + P5_CONSTS + P678_CONSTS))
ALL_W = W_P1 + W_P2 + W_P5 + W_P678
W_SHAPES = {
    "L0_attn_w_qkv": [1024, 3072], "L0_attn_ln_w": [1024], "L0_attn_w_proj": [1024, 1024],
    "L0_ffn_ln1_1": [1024], "L0_ffn_wq1": [1024, 1024], "L0_ffn_wk1": [1024, 1024],
    "L0_ffn_wv1": [1024, 1024], "L0_ffn_wo1": [1024, 1024], "L0_ffn_ln2_1": [1024],
    "L0_ffn_wg1": [1024, 4096], "L0_ffn_wu1": [1024, 4096], "L0_ffn_wd1": [4096, 1024],
    "L0_ffn_prior_norm_w": [1024], "L0_ffn_prior_w": [1024, 2048],
    "L0_ffn_ln1_2": [1024], "L0_ffn_wq2": [1024, 1024], "L0_ffn_wk2": [1024, 1024],
    "L0_ffn_wv2": [1024, 1024], "L0_ffn_wo2": [1024, 1024], "L0_ffn_ln2_2": [1024],
    "L0_ffn_wg2": [1024, 4096], "L0_ffn_wu2": [1024, 4096], "L0_ffn_wd2": [4096, 1024],
    "L1_attn_Wq": [1024, 1024], "L1_attn_Wk": [1024, 1024], "L1_attn_Wv": [1024, 1024],
    "L1_attn_Wo": [1024, 1024],
    "L1_ffn_base_w": [1024, 1024], "L1_ffn_base_b": [1024], "L1_ffn_lora_A": [1024, 8],
    "L1_ffn_lora_B": [8, 8, 1024], "L1_ffn_router_w": [8, 1024], "L1_ffn_router_b": [8],
}


def kernel(**inputs):
    x = np.ascontiguousarray(inputs["L0_attn_x"].reshape(NCORES, ROWS * T, D))
    decls = [("in", "x0", [ROWS * T, D], F32), ("out", "out", [ROWS * T, D], F32),
             ("tmp", "x1", [ROWS * T, D], F32),
             ("tmp", "processed", [ROWS * T, D], F32),
             ("tmp", "processed2", [ROWS * T, D], F32),
             ("tmp", "d_st", [ROWS * 128, 16], F32), ("tmp", "d_ch", [ROWS * 128, 16], F32),
             ("tmp", "partials", [ROWS, 2], F32), ("tmp", "partials_all", [ROWS, 2], F32),
             ("tmp", "idsco", [ROWS * KSEL, 2], F32),
             ("tmp", "p2_ml", [ROWS * T, 2 * D], F32), ("tmp", "p2_xa", [ROWS * T, D], F32),
             ("tmp", "p67_x2", [ROWS * T, D], F32), ("tmp", "p67_x3", [ROWS * T, D], F32)]
    decls += [("in", n, W_SHAPES[n], F32) for n in ALL_W]
    nc, cin = build_program(decls, ALL_CONSTS,
                            [phase0_precast, phase1, phase2, phase3_allreduce,
                             phase4, phase5, phase678],
                            dma_scratch=20 * 1024)
    shared = {n: inputs[n] for n in ALL_W}
    o = _run(nc, cin, {"x0": x}, shared, ["out"])
    return np.concatenate([o["out"][c].reshape(ROWS, T, D) for c in range(NCORES)], 0)


# version 28
# speedup vs baseline: 1.1870x; 1.0072x over previous
import math
import numpy as np
import concourse.bass as bass
import concourse.mybir as mybir
from concourse.bass_utils import run_bass_kernel_spmd

F32 = mybir.dt.float32
BF16 = mybir.dt.bfloat16
U32 = mybir.dt.uint32
I32 = mybir.dt.int32
AF = mybir.ActivationFunctionType
ALU = mybir.AluOpType

NCORES = 8
ROWS = 4
T = 2048
D = 1024
NCH = 8
H = 16
DH = 64
FFN = 4096
KSEL = 512
SHIFT_MARGIN = 60.0


STREAMS = ("tensor", "scalar", "vector", "gpsimd", "sync")


class Buf:
    __slots__ = ("name", "writer", "readers")

    def __init__(self, name):
        self.name = name
        self.writer = None
        self.readers = []


class Sched:

    DMA_POOL = 24

    def __init__(self, nc, stack):
        self.nc = nc
        self.engines = {"tensor": nc.tensor, "scalar": nc.scalar, "vector": nc.vector,
                        "gpsimd": nc.gpsimd, "sync": nc.sync}
        self.queues = list(STREAMS)
        for s in ("sync", "gpsimd", "scalar"):
            for i in range(self.DMA_POOL):
                self.queues.append(f"dma_{s}_{i}")
        self.sems = {q: stack.enter_context(nc.semaphore("sem_" + q)) for q in self.queues}
        self.cnt = {q: 0 for q in self.queues}
        self.dma_idx = {s: 0 for s in STREAMS}
        self.waited = {s: {} for s in STREAMS}
        self._bufs = {}
        self.fence_base = {}
        self.debug_serialize = False
        self.serialize_same = True

    def fence(self):
        self.fence_base = {q: c for q, c in self.cnt.items() if c > 0}

    def buf(self, name):
        b = self._bufs.get(name)
        if b is None:
            b = Buf(name)
            self._bufs[name] = b
        return b

    def op(self, stream, fn, reads=(), writes=(), deps=(), dma=False):
        if dma:
            i = self.dma_idx[stream]
            self.dma_idx[stream] = (i + 1) % self.DMA_POOL
            q = f"dma_{stream}_{i}"
            inc = 16
        else:
            q = stream
            inc = 1
        self.cnt[q] += inc
        tok = (q, self.cnt[q])
        depmap = {}

        def add(d):
            if d is None:
                return
            dq, dc = d
            if dq == stream and not self.debug_serialize:
                if stream == "tensor" or not self.serialize_same:
                    return
            if dc > depmap.get(dq, 0):
                depmap[dq] = dc

        if dma and self.cnt[q] > 16:
            depmap[q] = self.cnt[q] - 16
        for r in reads:
            add(r.writer)
            if stream in ("vector", "scalar") and r.name.startswith("ps"):
                other = "scalar" if stream == "vector" else "vector"
                for dq, dc in r.readers:
                    if dq == other:
                        add((dq, dc))
        for w in writes:
            add(w.writer)
            for rd in w.readers:
                add(rd)
        for d in deps:
            add(d)
        for fq, fc in self.fence_base.items():
            add((fq, fc))
        eng = self.engines[stream]
        wt = self.waited[stream]
        for dq, dc in sorted(depmap.items()):
            if wt.get(dq, 0) >= dc:
                continue
            eng.wait_ge(self.sems[dq], dc)
            wt[dq] = dc
        try:
            inst = fn(eng)
        except Exception:
            self.cnt[q] -= inc
            raise
        inst.then_inc(self.sems[q], inc)
        for r in reads:
            r.readers.append(tok)
        for w in writes:
            w.writer = tok
            w.readers = []
        if self.debug_serialize:
            self.fence()
        return tok

    def finish(self):
        wt = self.waited["sync"]
        for q in self.queues:
            c = self.cnt[q]
            if q.startswith("dma_") and c > 0 and wt.get(q, 0) < c:
                self.nc.sync.wait_ge(self.sems[q], c)
                wt[q] = c



def make_constants():
    c = {}
    c["c_ones_col"] = np.ones((128, 1), np.float32)
    c["c_ones_col_f32"] = np.ones((128, 1), np.float32)
    c["c_ones_row_f32"] = np.ones((1, 128), np.float32)
    c["c_ones_row"] = np.ones((1, 128), np.float32)
    c["c_ident"] = np.eye(128, dtype=np.float32)
    k = np.arange(128)[:, None]
    q = np.arange(128)[None, :]
    c["c_mask_add"] = np.where(k <= q, 0.0, -1e30).astype(np.float32)
    c["c_mask_mul"] = (k <= q).astype(np.float32)
    c["c_tril"] = (k <= q).astype(np.float32)
    tri = np.zeros((64, 64), np.float32)
    for b in range(4):
        for i in range(16):
            for j in range(16):
                if i < j:
                    tri[b * 16 + i, b * 16 + j] = 1.0
    c["c_triu16"] = tri
    inv = (1.0 / 10000.0 ** (np.arange(32, dtype=np.float64) / 32.0))
    ang = np.arange(T, dtype=np.float64)[None, :] * inv[:, None]
    cosv = np.cos(ang).astype(np.float32)
    sinv = np.sin(ang).astype(np.float32)
    c["c_rope_cs"] = np.concatenate([cosv, cosv, cosv, cosv], 0)
    c["c_rope_ss"] = np.concatenate([-sinv, sinv, -sinv, sinv], 0)
    c["c_invfreq4"] = np.concatenate([inv, inv, inv, inv]).astype(np.float32)[None, :]
    c["c_invfreq4_f32"] = c["c_invfreq4"].copy()
    c["c_ropesign"] = np.concatenate([-np.ones(32), np.ones(32), -np.ones(32), np.ones(32)]).astype(np.float32)[:, None]
    ind = np.zeros((1024, 16), np.float32)
    for d_ in range(1024):
        ind[d_, d_ // 64] = 1.0
    c["c_headind"] = ind.reshape(8, 128, 16).transpose(1, 0, 2).reshape(128, 128).copy()
    c["c_iota_row128"] = np.broadcast_to(np.arange(128, dtype=np.float32)[None, :],
                                          (128, 128)).copy()
    it = (np.arange(16)[None, :] * 128 + np.arange(128)[:, None]).astype(np.float32)
    c["c_iota_f"] = it
    c["c_iota_u"] = it.astype(np.uint32)
    c["c_m8"] = (np.eye(8) - 1.0 / 8.0).astype(np.float32)
    p8 = np.zeros((8, 64), np.float32)
    q8 = np.zeros((8, 64), np.float32)
    for e_ in range(8):
        for r_ in range(8):
            p8[r_, e_ * 8 + r_] = 2.0
            q8[e_, e_ * 8 + r_] = 1.0
    c["c_p8"] = p8
    c["c_q8"] = q8
    cs = np.zeros((128, 16 * 16), np.float32)
    for h_ in range(16):
        cs[:, h_ * 16 + h_] = 1.0
    c["c_colsel"] = cs
    sel = np.zeros((16, 16 * 64), np.float32)
    for h_ in range(16):
        sel[h_, h_ * 64:(h_ + 1) * 64] = 1.0
    c["c_sel16"] = sel
    return c


CONSTS = make_constants()

CONST_DT = {
    "c_ones_col": BF16, "c_ones_row": BF16, "c_ident": F32,
    "c_mask_add": F32, "c_mask_mul": F32, "c_tril": BF16, "c_triu16": BF16,
    "c_rope_cs": BF16, "c_rope_ss": BF16, "c_invfreq4": BF16, "c_invfreq4_f32": F32, "c_ropesign": F32,
    "c_headind": BF16, "c_iota_f": F32, "c_iota_u": U32, "c_iota_row128": F32, "c_sel16": BF16, "c_colsel": BF16, "c_ones_col_f32": F32, "c_ones_row_f32": F32,
    "c_m8": BF16, "c_p8": BF16, "c_q8": BF16,
}


class G:
    def __init__(self, nc, s):
        self.nc = nc
        self.s = s
        self.t = {}
        self.stack = None
        self._used_names = set()

    def dram_in(self, name, shape, dtype=F32):
        self.t[name] = self.nc.dram_tensor(name, list(shape), dtype, kind="ExternalInput").ap()
        return self.t[name]

    def dram_out(self, name, shape, dtype=F32):
        self.t[name] = self.nc.dram_tensor(name, list(shape), dtype, kind="ExternalOutput").ap()
        return self.t[name]

    def dram_tmp(self, name, shape, dtype=F32):
        self.t[name] = self.nc.dram_tensor(name, list(shape), dtype).ap()
        return self.t[name]

    _uid = [0]

    def sb(self, name, shape, dtype):
        nm = name
        if name in self._used_names:
            G._uid[0] += 1
            nm = f"{name}__{G._uid[0]}"
        self._used_names.add(name)
        ap = self.stack.enter_context(self.nc.sbuf_tensor(nm, list(shape), dtype))
        self.t[name] = ap
        return ap

    def ps_pool(self):
        self.pall = self.stack.enter_context(self.nc.psum_tensor("pall", [128, 4096], F32))
        return self.pall

    def pb(self, k, p0=0, p1=128, n=512):
        return self.pall[p0:p1, k * 512: k * 512 + n]

    def pbB(self, k0, k1=None):
        if k1 is None:
            k1 = k0
        return [self.s.buf(f"ps{k}") for k in range(k0, k1 + 1)]

    def load_consts(self, names):
        self.const_names = list(names)
        for cname in names:
            arr = CONSTS[cname]
            dt = CONST_DT[cname]
            self.dram_in(cname + "_hbm", arr.shape, U32 if arr.dtype == np.uint32 else F32)
            tile = self.sb(cname, arr.shape, dt)
            hbm = self.t[cname + "_hbm"]
            self.s.op("gpsimd",
                      lambda g, tile=tile, hbm=hbm: g.dma_start(out=tile[:, :], in_=hbm[:, :]),
                      writes=[self.s.buf(cname)], dma=True)

    def const_inputs(self):
        return {cname + "_hbm": CONSTS[cname] for cname in self.const_names}

    def fconst(self, val):
        name = "c_f%r" % (val,)
        if name not in self.t:
            ap = self.top_stack.enter_context(
                self.nc.sbuf_tensor(name, [128, 1], F32, side="right"))
            self.t[name] = ap
            self.s.op("gpsimd", lambda e, a=ap, v=float(val): e.memset(a[:, :], v),
                      writes=[self.s.buf("c_fconsts")])
        return self.t[name]



def op_dma(g, stream, out_ap, in_ap, reads=(), writes=(), deps=()):
    return g.s.op(stream, lambda e, o=out_ap, i=in_ap: e.dma_start(out=o, in_=i),
                  reads=reads, writes=writes, deps=deps, dma=True)


def op_mm(g, out_ap, lhsT, rhs, reads=(), writes=(), start=True, stop=True):
    return g.s.op("tensor",
                  lambda e, o=out_ap, l=lhsT, r=rhs, st=start, sp=stop:
                      e.matmul(o, l, r, start=st, stop=sp),
                  reads=reads, writes=writes)


def op_mm_group(g, items, reads=(), writes=()):
    def fn(e, items=tuple(items)):
        inst = None
        for (o, l, r, st, sp) in items:
            inst = e.matmul(o, l, r, start=st, stop=sp)
        return inst
    return g.s.op("tensor", fn, reads=reads, writes=writes)


def op_act(g, out_ap, in_ap, func, reads=(), writes=(), scale=None, bias=None,
           accum=None):
    kw = {}
    reads = list(reads)
    if scale is not None:
        kw["scale"] = scale
    if bias is not None:
        if isinstance(bias, (int, float)):
            bias = g.fconst(bias)[0:in_ap.shape[0], 0:1]
            reads.append(g.s.buf("c_fconsts"))
        kw["bias"] = bias
    if accum is not None:
        kw["accum_out"] = accum
    return g.s.op("scalar",
                  lambda e, o=out_ap, i=in_ap, f=func, kw=kw:
                      e.activation(out=o, in_=i, func=f, **kw),
                  reads=reads, writes=writes)


def op_tt(g, out_ap, a, b, op, reads=(), writes=(), stream="vector"):
    return g.s.op(stream,
                  lambda e, o=out_ap, x=a, y=b, p=op: e.tensor_tensor(out=o, in0=x, in1=y, op=p),
                  reads=reads, writes=writes)


def op_ts(g, out_ap, a, s1, s2, op0, op1=None, reads=(), writes=(), accum=None,
          stream="vector"):
    kw = {"accum_out": accum} if accum is not None else {}
    if op1 is None:
        return g.s.op(stream,
                      lambda e, o=out_ap, x=a, v1=s1, p0=op0, kw=kw:
                          e.tensor_scalar(o, x, v1, None, op0=p0, **kw),
                      reads=reads, writes=writes)
    return g.s.op(stream,
                  lambda e, o=out_ap, x=a, v1=s1, v2=s2, p0=op0, p1=op1, kw=kw:
                      e.tensor_scalar(o, x, v1, v2, op0=p0, op1=p1, **kw),
                  reads=reads, writes=writes)


def op_stt(g, out_ap, in0, scalar, in1, op0, op1, reads=(), writes=()):
    return g.s.op("vector",
                  lambda e, o=out_ap, a=in0, sc=scalar, b=in1, p0=op0, p1=op1:
                      e.scalar_tensor_tensor(out=o, in0=a, scalar=sc, in1=b, op0=p0, op1=p1),
                  reads=reads, writes=writes)


def op_copy(g, out_ap, in_ap, reads=(), writes=(), stream="vector"):
    return g.s.op(stream,
                  lambda e, o=out_ap, i=in_ap: e.tensor_copy(out=o, in_=i) if hasattr(e, "tensor_copy") else e.copy(out=o, in_=i),
                  reads=reads, writes=writes)


def op_memset(g, ap, val, writes=(), stream="vector", reads=()):
    return g.s.op(stream, lambda e, a=ap, v=val: e.memset(a, v),
                  reads=reads, writes=writes)


def op_red(g, out_ap, in_ap, op, reads=(), writes=(), negate=False):
    import concourse.mybir as _mb
    return g.s.op("vector",
                  lambda e, o=out_ap, i=in_ap, p=op, n=negate:
                      e.tensor_reduce(o, i, axis=_mb.AxisListType.X, op=p, negate=n),
                  reads=reads, writes=writes)


def op_recip(g, out_ap, in_ap, reads=(), writes=()):
    def fn(e, o=out_ap, i=in_ap, nc=g.nc):
        with nc.allow_low_precision(reason="recip feeds bf16 matmul operand"):
            return e.reciprocal(o, i)
    return g.s.op("vector", fn, reads=reads, writes=writes)


def op_transpose(g, out_ap, in_ap, ident, reads=(), writes=()):
    return g.s.op("tensor",
                  lambda e, o=out_ap, i=in_ap, d=ident: e.transpose(o, i, d),
                  reads=reads, writes=writes)


def sview(handle, offset, dims):
    return bass.AP(handle, offset, [list(d) for d in dims])


def load_vec_pcol(g, hbm_ap, name, bank=6, nch=NCH):
    B = g.s.buf
    stg = g.sb(name + "_stg", [nch, 128], F32)
    tile = g.sb(name, [128, nch], F32)
    pt = g.pb(bank, 0, 128, nch)
    op_dma(g, "gpsimd", stg[:, :], hbm_ap.rearrange("(a p) -> a p", p=128),
           writes=[B(name + "_stg")])
    op_transpose(g, pt, stg[:, :], g.t["c_ident"][0:nch, 0:nch],
                 reads=[B(name + "_stg"), B("c_ident")], writes=g.pbB(bank))
    op_copy(g, tile[:, :], pt, reads=g.pbB(bank), writes=[B(name)])
    return tile


def load_w_fm(g, wsb, wsb_buf, hbm_ap, col0, col1, stream="gpsimd"):
    n = col1 - col0
    return op_dma(g, stream, wsb[:, 0:NCH * n],
                  hbm_ap[:, col0:col1].rearrange("(a p) n -> p a n", p=128),
                  writes=[wsb_buf])







W_BIG = ["L0_attn_w_qkv", "L0_attn_w_proj",
         "L0_ffn_wq1", "L0_ffn_wk1", "L0_ffn_wv1", "L0_ffn_wo1",
         "L0_ffn_wg1", "L0_ffn_wu1", "L0_ffn_wd1", "L0_ffn_prior_w",
         "L0_ffn_wq2", "L0_ffn_wk2", "L0_ffn_wv2", "L0_ffn_wo2",
         "L0_ffn_wg2", "L0_ffn_wu2", "L0_ffn_wd2",
         "L1_attn_Wq", "L1_attn_Wk", "L1_attn_Wv", "L1_attn_Wo"]


def phase0_precast(g):
    from contextlib import ExitStack
    B = g.s.buf
    with ExitStack() as st:
        sv = g.stack
        g.stack = st
        tiles = [g.sb(f"p0_t{i}", [128, 4096], BF16) for i in range(4)]
        ti = [0]

        def cast_load(src2d):
            i = ti[0] % 4
            ti[0] += 1
            t, b = tiles[i], B(f"p0_t{i}")
            op_dma(g, "gpsimd", t[:, 0:src2d.shape[1]], src2d, writes=[b])
            return t, b

        def pack_full(dst_name, srcs):
            ncols = srcs[0][2]
            A = srcs[0][0].shape[0] // 128
            dst = g.dram_tmp(dst_name, [len(srcs) * 128, A * ncols], BF16)
            for t_i, (w, c0, nc_) in enumerate(srcs):
                for a in range(A):
                    t, b = cast_load(w[a * 128:(a + 1) * 128, c0:c0 + nc_])
                    op_dma(g, "sync", dst[t_i * 128:(t_i + 1) * 128, a * nc_:(a + 1) * nc_],
                           t[:, 0:nc_], reads=[b])
            return dst

        def pack_gu(dst_name, wg, wu):
            dst = g.dram_tmp(dst_name, [8 * 128, 8 * 1024], BF16)
            for a in range(8):
                tg, bg = cast_load(wg[a * 128:(a + 1) * 128, :])
                tu, bu = cast_load(wu[a * 128:(a + 1) * 128, :])
                for blk in range(8):
                    op_dma(g, "sync",
                           dst[blk * 128:(blk + 1) * 128, a * 1024:a * 1024 + 512],
                           tg[:, blk * 512:(blk + 1) * 512], reads=[bg])
                    op_dma(g, "sync",
                           dst[blk * 128:(blk + 1) * 128, a * 1024 + 512:(a + 1) * 1024],
                           tu[:, blk * 512:(blk + 1) * 512], reads=[bu])
            return dst

        def pack_pairs(dst_name, wq, wk, wv):
            dst = g.dram_tmp(dst_name, [8 * 128, 8 * 384], BF16)
            for a in range(8):
                parts = [cast_load(w[a * 128:(a + 1) * 128, :]) for w in (wq, wk, wv)]
                for hc in range(8):
                    for j, (t, b) in enumerate(parts):
                        op_dma(g, "sync",
                               dst[hc * 128:(hc + 1) * 128,
                                   a * 384 + j * 128:a * 384 + (j + 1) * 128],
                               t[:, hc * 128:(hc + 1) * 128], reads=[b])
            return dst

        def copy_rowmajor(name):
            w = g.t[name]
            R, C = w.shape
            bf = g.dram_tmp("bf_" + name, [R, C], BF16)
            for r0 in range(0, R, 128):
                t, b = cast_load(w[r0:r0 + 128, :])
                op_dma(g, "sync", bf[r0:r0 + 128, :], t[:, 0:C], reads=[b])
            g.t[name] = bf

        has = lambda *ns: all(n in g.t for n in ns)
        if has("L0_attn_w_qkv"):
            qkv = g.t["L0_attn_w_qkv"]
            pack_full("pk_qkv", [(qkv, 1024, 1024), (qkv, 0, 1024), (qkv, 2048, 1024)])
            pack_full("pk_proj", [(g.t["L0_attn_w_proj"], 0, 1024)])
        if has("L0_ffn_prior_w"):
            pack_full("pk_prior", [(g.t["L0_ffn_prior_w"], ob * 512, 512) for ob in range(4)])
            pack_full("pk_f1qkv", [(g.t["L0_ffn_wq1"], 0, 1024),
                                   (g.t["L0_ffn_wk1"], 0, 1024),
                                   (g.t["L0_ffn_wv1"], 0, 1024)])
            pack_full("pk_wo1", [(g.t["L0_ffn_wo1"], 0, 1024)])
            pack_gu("pk_f1gu", g.t["L0_ffn_wg1"], g.t["L0_ffn_wu1"])
            copy_rowmajor("L0_ffn_wd1")
        if has("L0_ffn_wq2"):
            pack_pairs("pk_qkv2", g.t["L0_ffn_wq2"], g.t["L0_ffn_wk2"], g.t["L0_ffn_wv2"])
            pack_full("pk_wo2", [(g.t["L0_ffn_wo2"], 0, 1024)])
            pack_gu("pk_f2gu", g.t["L0_ffn_wg2"], g.t["L0_ffn_wu2"])
            copy_rowmajor("L0_ffn_wd2")
        if has("L1_attn_Wq"):
            pack_pairs("pk_qkvL1", g.t["L1_attn_Wq"], g.t["L1_attn_Wk"], g.t["L1_attn_Wv"])
            pack_full("pk_WoL1", [(g.t["L1_attn_Wo"], 0, 1024)])
        g.stack = sv
    g.s.fence()


P1_CONSTS = ["c_ones_col", "c_ones_row", "c_ident", "c_mask_add", "c_headind",
             "c_sel16", "c_colsel"]
TH = 1024


def phase1(g):
    from contextlib import ExitStack
    B = g.s.buf
    x0, x1 = g.t["x0"], g.t["x1"]
    wq_hbm = g.t["L0_attn_w_qkv"]
    wp_hbm = g.t["L0_attn_w_proj"]

    with ExitStack() as st:
        sv = g.stack
        g.stack = st
        x_fm = g.sb("p1_xfm", [128, NCH * TH], BF16)
        xin = g.sb("p1_xin", [128, 2 * 1024], F32)
        wsb = [g.sb("p1_w0", [128, NCH * 1024], BF16)]
        wsb.append(wsb[0])
        qn = g.sb("p1_qn", [128, NCH * TH], BF16)
        kn = g.sb("p1_kn", [128, NCH * TH], BF16)
        vt = g.sb("p1_vt", [128, H * 8 * 65], BF16)
        sq = g.sb("p1_sq", [128, 2 * 512], BF16)
        rrow = g.sb("p1_rrow", [1, 2 * 512], F32)
        rbf = g.sb("p1_rbf", [1, 2 * 512], BF16)
        nrm = g.sb("p1_nrm", [16, 512], F32)
        Ball = g.sb("p1_Ball", [16, TH], BF16)
        mk = g.sb("p1_mk", [16, 4], F32)
        qa = g.sb("p1_qa", [128, TH], BF16)
        ka = g.sb("p1_ka", [128, TH], BF16)
        op_memset(g, qa[64:128, :], 0.0, writes=[B("p1_qa_pad")])
        op_memset(g, ka[64:128, :], 0.0, writes=[B("p1_ka_pad")])
        y_fm = g.sb("p1_yfm", [128, NCH * TH], BF16)
        Pb = [g.sb("p1_P0", [128, 2048], BF16), g.sb("p1_P1", [128, 2048], BF16)]
        den_all = g.sb("p1_den_all", [16, 1024], F32)
        dra = g.sb("p1_dra", [16, 1024], BF16)
        rbs = g.sb("p1_rbs", [128, 1024], BF16)
        xs = g.sb("p1_xs", [128, 1024], F32)
        sc = g.sb("p1_sc", [128, 1024], F32)
        stt_ = g.sb("p1_st", [128, 8], F32)
        x1t = g.sb("p1_x1t", [128, 2 * 1024], F32)
        lnw2 = g.sb("p1_lnw2", [128, NCH], F32)
        ewt = g.sb("p1_ewt", [128, NCH * 16], BF16)
        ones_bf = g.t["c_ones_col"]
        ones_row = g.t["c_ones_row"]
        ident = g.t["c_ident"]
        mask_add = g.t["c_mask_add"]
        hind = g.t["c_headind"]
        sel16 = g.t["c_sel16"]
        colsel = g.t["c_colsel"]

        lnw = load_vec_pcol(g, g.t["L0_attn_ln_w"], "p1_lnw")
        op_act(g, lnw2[:, :], lnw[:, :], AF.Square, reads=[B("p1_lnw")], writes=[B("p1_lnw2")])
        for oc in range(NCH):
            op_ts(g, ewt[:, oc * 16:(oc + 1) * 16], hind[:, oc * 16:(oc + 1) * 16],
                  lnw2[:, oc:oc + 1], None, ALU.mult,
                  reads=[B("c_headind"), B("p1_lnw2")], writes=[B("p1_ewt")])
        xfmB = [B(f"p1_xfm{c}") for c in range(NCH)]

        for r in range(ROWS):
            for hf in range(2):
                t0 = r * T + hf * TH
                for tb in range(8):
                    sl = tb % 2
                    xi = xin[:, sl * 1024:(sl + 1) * 1024]
                    op_dma(g, "sync", xi, x0[t0 + tb * 128: t0 + (tb + 1) * 128, :],
                           writes=[B(f"p1_xin{sl}")])
                    for c in range(NCH):
                        po = g.pall[:, 7 * 512 + (c % 4) * 128: 7 * 512 + (c % 4 + 1) * 128]
                        op_transpose(g, po, xi[:, c * 128:(c + 1) * 128], ident[:, :],
                                     reads=[B(f"p1_xin{sl}"), B("c_ident")], writes=g.pbB(7))
                        op_copy(g, x_fm[:, c * TH + tb * 128: c * TH + (tb + 1) * 128], po,
                                reads=g.pbB(7), writes=[xfmB[c]])
                op_memset(g, vt[:, :], 1.0, writes=[B("p1_vt")])
                op_memset(g, ka[64:65, :], 1.0, writes=[B("p1_ka_hi")])
                op_memset(g, mk[:, 0:1], 0.0, writes=[B("p1_mk")])

                for (tens, tname, wti, is_q) in ((kn, "p1_kn", 0, False),
                                                 (qn, "p1_qn", 1, True)):
                    w = wsb[0]
                    wB = B("p1_w0")
                    op_dma(g, "sync", w[:, :],
                           g.t["pk_qkv"][wti * 128:(wti + 1) * 128, :],
                           writes=[wB])
                    for tt in range(2):
                        for oc in range(NCH):
                            pq = g.pb(oc % 2)
                            pqB = g.pbB(oc % 2)
                            items = [(pq, w[:, kc * 1024 + oc * 128: kc * 1024 + (oc + 1) * 128],
                                      x_fm[:, kc * TH + tt * 512: kc * TH + (tt + 1) * 512],
                                      kc == 0, kc == NCH - 1) for kc in range(NCH)]
                            op_mm_group(g, items, reads=[wB] + xfmB, writes=pqB)
                            op_act(g, tens[:, oc * TH + tt * 512: oc * TH + (tt + 1) * 512], pq,
                                   AF.Identity, reads=pqB, writes=[B(f"{tname}{oc}")])
                            sqs = sq[:, (oc % 2) * 512:((oc % 2) + 1) * 512]
                            sqB = B(f"p1_sq{oc % 2}")
                            op_act(g, sqs, pq, AF.Square, reads=pqB, writes=[sqB])
                            op_mm(g, g.pb(2, 0, 1), ones_bf[:, :], sqs,
                                  reads=[sqB, B("c_ones_col")], writes=g.pbB(2),
                                  start=(oc == 0), stop=(oc == NCH - 1))
                            op_mm(g, g.pb(3, 0, 16), ewt[:, oc * 16:(oc + 1) * 16], sqs,
                                  reads=[sqB, B("p1_ewt")], writes=g.pbB(3),
                                  start=(oc == 0), stop=(oc == NCH - 1))
                        sl = tt
                        rr = rrow[0:1, sl * 512:(sl + 1) * 512]
                        rf = rbf[0:1, sl * 512:(sl + 1) * 512]
                        op_act(g, rr, g.pb(2, 0, 1), AF.Ln, scale=1.0 / 1024.0, bias=1e-6,
                               reads=g.pbB(2), writes=[B(f"p1_rr{sl}")])
                        op_act(g, rr, rr, AF.Exp, scale=-0.5,
                               reads=[B(f"p1_rr{sl}")], writes=[B(f"p1_rr{sl}")])
                        op_copy(g, rf, rr, reads=[B(f"p1_rr{sl}")], writes=[B(f"p1_rbf{sl}")])
                        op_act(g, nrm[:, :], g.pb(3, 0, 16), AF.Sqrt, reads=g.pbB(3),
                               writes=[B("p1_nrm")])
                        op_mm(g, g.pb(6, 0, 16), ones_row[0:1, 0:16], rf,
                              reads=[B(f"p1_rbf{sl}"), B("c_ones_row")], writes=g.pbB(6))
                        if is_q:
                            op_tt(g, nrm[:, :], nrm[:, :], g.pb(6, 0, 16), ALU.mult,
                                  reads=[B("p1_nrm")] + g.pbB(6), writes=[B("p1_nrm")])
                            op_ts(g, Ball[:, tt * 512:(tt + 1) * 512], nrm[:, :],
                                  mk[:, 1:2], 8.0 * SHIFT_MARGIN, ALU.mult, ALU.add,
                                  reads=[B("p1_nrm"), B("p1_mkn")], writes=[B("p1_Ball")])
                        else:
                            op_tt(g, nrm[:, :], nrm[:, :], g.pb(6, 0, 16), ALU.mult,
                                  reads=[B("p1_nrm")] + g.pbB(6), writes=[B("p1_nrm")])
                            op_red(g, mk[:, 2:3], nrm[:, :], ALU.max,
                                   reads=[B("p1_nrm")], writes=[B("p1_mk2")])
                            op_tt(g, mk[:, 0:1], mk[:, 0:1], mk[:, 2:3], ALU.max,
                                  reads=[B("p1_mk2"), B("p1_mk")], writes=[B("p1_mk")])
                        op_mm(g, g.pb(6), ones_row[0:1, :], rf,
                              reads=[B(f"p1_rbf{sl}"), B("c_ones_row")], writes=g.pbB(6))
                        for oc in range(NCH):
                            slc = tens[:, oc * TH + tt * 512: oc * TH + (tt + 1) * 512]
                            op_stt(g, slc, slc, lnw[:, oc:oc + 1], g.pb(6), ALU.mult, ALU.mult,
                                   reads=[B(f"{tname}{oc}"), B("p1_lnw")] + g.pbB(6),
                                   writes=[B(f"{tname}{oc}")])
                    if not is_q:
                        op_ts(g, mk[:, 1:2], mk[:, 0:1], -1.0, None, ALU.mult,
                              reads=[B("p1_mk")], writes=[B("p1_mkn")])
                op_dma(g, "sync", wsb[0][:, :],
                       g.t["pk_qkv"][2 * 128:3 * 128, :],
                       writes=[B("p1_w0")])
                for tb in range(8):
                    for vb in range(2):
                        pq = g.pb((tb * 2 + vb) % 2)
                        pqB = g.pbB((tb * 2 + vb) % 2)
                        items = [(pq, x_fm[:, kc * TH + tb * 128: kc * TH + (tb + 1) * 128],
                                  wsb[0][:, kc * 1024 + vb * 512: kc * 1024 + (vb + 1) * 512],
                                  kc == 0, kc == NCH - 1) for kc in range(NCH)]
                        op_mm_group(g, items, reads=[B("p1_w0")] + xfmB, writes=pqB)
                        for j in range(8):
                            h = vb * 8 + j
                            op_copy(g, vt[:, (h * 8 + tb) * 65: (h * 8 + tb) * 65 + 64],
                                    pq[:, j * 64:(j + 1) * 64], reads=pqB, writes=[B("p1_vt")])

                def attn_part1(h):
                    hc, hr = h // 2, (h % 2) * 64
                    op_copy(g, qa[0:64, :], qn[hr:hr + 64, hc * TH:(hc + 1) * TH],
                            reads=[B(f"p1_qn{hc}")], writes=[B("p1_qa_lo")])
                    op_dma(g, "sync", qa[64:65, :], Ball[h:h + 1, :],
                           reads=[B("p1_Ball")], writes=[B("p1_qa_hi")])
                    op_copy(g, ka[0:64, :], kn[hr:hr + 64, hc * TH:(hc + 1) * TH],
                            reads=[B(f"p1_kn{hc}")], writes=[B("p1_ka_lo")])
                    items = []
                    for wl in range(4):
                        w0 = wl * 256
                        items.append((g.pall[:, wl * 512: wl * 512 + 256],
                                      ka[:, w0: w0 + 128], qa[:, w0: w0 + 256], True, True))
                        items.append((g.pall[:, wl * 512 + 256: wl * 512 + 384],
                                      ka[:, w0 + 128: w0 + 256], qa[:, w0 + 128: w0 + 256],
                                      True, True))
                    op_mm_group(g, items, reads=[B("p1_qa_lo"), B("p1_qa_hi"), B("p1_qa_pad"),
                                                 B("p1_ka_lo"), B("p1_ka_hi"), B("p1_ka_pad")],
                                writes=g.pbB(0, 3))
                    for off in (0, 256):
                        v = sview(g.pall, off, [[4096, 128], [512, 4], [1, 128]])
                        m = sview(g.t["c_mask_add"], 0, [[128, 128], [0, 4], [1, 128]])
                        op_tt(g, v, v, m, ALU.add,
                              reads=[B("c_mask_add")] + g.pbB(0, 3), writes=g.pbB(0, 3))
                    P = Pb[h % 2]
                    op_act(g, P[:, :], g.pall[:, 0:2048], AF.Exp, scale=0.125,
                           reads=g.pbB(0, 3), writes=[B(f"p1_P{h % 2}")])

                def attn_part2(h):
                    hc, hr = h // 2, (h % 2) * 64
                    P = Pb[h % 2]
                    PB = B(f"p1_P{h % 2}")
                    items = []
                    for wl in range(4):
                        base = (h * 8 + wl * 2) * 65
                        items.append((g.pall[0:65, 2048 + wl * 256: 2048 + wl * 256 + 256],
                                      vt[:, base: base + 65],
                                      P[:, wl * 512: wl * 512 + 256], True, False))
                        items.append((g.pall[0:65, 2048 + wl * 256 + 128: 2048 + wl * 256 + 256],
                                      vt[:, base + 65: base + 130],
                                      P[:, wl * 512 + 256: wl * 512 + 384], False, True))
                    op_mm_group(g, items, reads=[PB, B("p1_vt")], writes=g.pbB(4, 5))
                    items = []
                    for wl in range(4):
                        items.append((g.pall[0:16, 3072 + wl * 256: 3072 + wl * 256 + 256],
                                      colsel[:, h * 16:(h + 1) * 16],
                                      P[:, wl * 512: wl * 512 + 256], h == 0 and wl % 2 == 0, False))
                        items.append((g.pall[0:16, 3072 + wl * 256 + 128: 3072 + wl * 256 + 256],
                                      colsel[:, h * 16:(h + 1) * 16],
                                      P[:, wl * 512 + 256: wl * 512 + 384], False, False))
                    op_mm_group(g, items, reads=[PB, B("c_colsel")], writes=g.pbB(6, 7))
                    op_act(g, y_fm[hr:hr + 64, hc * TH: (hc + 1) * TH],
                           g.pall[0:64, 2048:3072], AF.Identity,
                           reads=g.pbB(4, 5), writes=[B(f"p1_yfm{hc}")])

                prev = None
                for h in range(H):
                    attn_part1(h)
                    if prev is not None:
                        attn_part2(prev)
                    prev = h
                attn_part2(prev)
                op_act(g, den_all[:, :], g.pall[0:16, 3072:4096], AF.Identity,
                       reads=g.pbB(6, 7), writes=[B("p1_den")])
                op_recip(g, dra[:, :], den_all[:, :], reads=[B("p1_den")], writes=[B("p1_dra")])
                for hc in range(NCH):
                    he, ho = 2 * hc, 2 * hc + 1
                    for (h_, p0, p1_) in ((he, 0, 64), (ho, 64, 128)):
                        op_mm(g, g.pall[p0:p1_, 3072:3584], sel16[:, h_ * 64:(h_ + 1) * 64],
                              dra[:, 0:512], reads=[B("p1_dra"), B("c_sel16")], writes=g.pbB(6))
                        op_mm(g, g.pall[p0:p1_, 3584:4096], sel16[:, h_ * 64:(h_ + 1) * 64],
                              dra[:, 512:1024], reads=[B("p1_dra"), B("c_sel16")], writes=g.pbB(7))
                    op_act(g, rbs[:, :], g.pall[:, 3072:4096], AF.Identity,
                           reads=g.pbB(6, 7), writes=[B("p1_rbs")])
                    op_tt(g, y_fm[:, hc * TH: (hc + 1) * TH],
                          y_fm[:, hc * TH: (hc + 1) * TH], rbs[:, :], ALU.mult,
                          reads=[B(f"p1_yfm{hc}"), B("p1_rbs")], writes=[B(f"p1_yfm{hc}")])

                op_dma(g, "sync", wsb[1][:, :], g.t["pk_proj"][0:128, :],
                       writes=[B("p1_w0")])
                yfmB = [B(f"p1_yfm{c}") for c in range(NCH)]
                for tb in range(8):
                    sl = tb % 2
                    xi = xin[:, sl * 1024:(sl + 1) * 1024]
                    op_dma(g, "sync", xi, x0[t0 + tb * 128: t0 + (tb + 1) * 128, :],
                           writes=[B(f"p1_xin{sl}")])
                    for ob in range(2):
                        pq = g.pb(ob)
                        pqB = g.pbB(ob)
                        items = [(pq, y_fm[:, kc * TH + tb * 128: kc * TH + (tb + 1) * 128],
                                  wsb[1][:, kc * 1024 + ob * 512: kc * 1024 + (ob + 1) * 512],
                                  kc == 0, kc == NCH - 1) for kc in range(NCH)]
                        op_mm_group(g, items, reads=[B("p1_w0")] + yfmB, writes=pqB)
                        op_tt(g, xs[:, ob * 512:(ob + 1) * 512], pq, xi[:, ob * 512:(ob + 1) * 512],
                              ALU.add, reads=pqB + [B(f"p1_xin{sl}")], writes=[B("p1_xs")])
                    emit_ln_apply(g, xs, sc, stt_, x1t[:, sl * 1024:(sl + 1) * 1024],
                                  "p1_xs", "p1_sc", "p1_st", f"p1_x1t{sl}")
                    op_dma(g, "sync", x1[t0 + tb * 128: t0 + (tb + 1) * 128, :],
                           x1t[:, sl * 1024:(sl + 1) * 1024],
                           reads=[B(f"p1_x1t{sl}")], writes=[B("x1_hbm")])
        g.stack = sv
    g.s.fence()


def emit_ln_apply(g, xs, sc, stt_, out_ap, xsN, scN, stN, outN):
    B = g.s.buf
    op_red(g, stt_[:, 0:1], xs[:, :], ALU.add, reads=[B(xsN)], writes=[B(stN + "0")])
    op_act(g, sc[:, :], xs[:, :], AF.Square, accum=stt_[:, 1:2],
           reads=[B(xsN)], writes=[B(scN), B(stN + "1")])
    op_ts(g, stt_[:, 2:3], stt_[:, 0:1], 1.0 / 1024.0, None, ALU.mult,
          reads=[B(stN + "0")], writes=[B(stN + "2")])
    op_tt(g, stt_[:, 3:4], stt_[:, 2:3], stt_[:, 2:3], ALU.mult,
          reads=[B(stN + "2")], writes=[B(stN + "3")])
    op_stt(g, stt_[:, 4:5], stt_[:, 1:2], 1.0 / 1024.0, stt_[:, 3:4],
           ALU.mult, ALU.subtract,
           reads=[B(stN + "1"), B(stN + "3")], writes=[B(stN + "4")])
    op_act(g, stt_[:, 5:6], stt_[:, 4:5], AF.Ln, bias=1e-5,
           reads=[B(stN + "4")], writes=[B(stN + "5")])
    op_act(g, stt_[:, 6:7], stt_[:, 5:6], AF.Exp, scale=-0.5,
           reads=[B(stN + "5")], writes=[B(stN + "6")])
    op_stt(g, stt_[:, 7:8], stt_[:, 2:3], -1.0, stt_[:, 6:7], ALU.mult, ALU.mult,
           reads=[B(stN + "2"), B(stN + "6")], writes=[B(stN + "7")])
    op_act(g, out_ap, xs[:, :], AF.Identity, scale=stt_[:, 6:7], bias=stt_[:, 7:8],
           reads=[B(xsN), B(stN + "6"), B(stN + "7")], writes=[B(outN)])



def build_program(decls, consts, phases, serialize=False, dma_scratch=16384):
    from contextlib import ExitStack
    nc = bass.Bass("TRN2", target_bir_lowering=False,
                   dynamic_dma_scratch_size=dma_scratch)
    with ExitStack() as top:
        s = Sched(nc, top)
        s.debug_serialize = (serialize is True)
        g = G(nc, s)
        g.stack = top
        g.top_stack = top
        for kind, name, shape, dtype in decls:
            if kind == "in":
                g.dram_in(name, shape, dtype)
            elif kind == "out":
                g.dram_out(name, shape, dtype)
            else:
                g.dram_tmp(name, shape, dtype)
        g.ps_pool()
        g.load_consts(consts)
        for f in phases:
            with nc.named_scope(getattr(f, "__name__", "phase")):
                f(g)
        s.finish()
    return nc, g.const_inputs()



P2_CONSTS = ["c_ones_col", "c_ones_row", "c_ident", "c_mask_add", "c_headind",
             "c_sel16", "c_rope_cs", "c_rope_ss", "c_ones_col_f32", "c_ones_row_f32"]


def swap_ap(wtile, col0):
    return bass.AP(wtile, col0 + 32,
                   [[wtile.shape[1], 128], [64, 2], [-32, 2], [1, 32]])


def rms_rows(g, pfx, src_fm, srcB, t0, t1, sq, rrow, rbf, bank=2):
    B = g.s.buf
    Tl = src_fm.shape[1] // NCH
    for tt in range(t0 // 512, t1 // 512):
        for oc in range(NCH):
            sqs = sq[:, (oc % 2) * 512:(oc % 2 + 1) * 512]
            sqB = B(f"{pfx}_sq{oc % 2}")
            op_act(g, sqs, src_fm[:, oc * Tl + tt * 512: oc * Tl + (tt + 1) * 512],
                   AF.Square, reads=[srcB[oc]], writes=[sqB])
            op_mm(g, g.pb(bank, 0, 1), g.t["c_ones_col"][:, :], sqs,
                  reads=[sqB, B("c_ones_col")], writes=g.pbB(bank),
                  start=(oc == 0), stop=(oc == NCH - 1))
        sl = tt % 2
        rr = rrow[0:1, sl * 512:(sl + 1) * 512]
        rf = rbf[0:1, sl * 512:(sl + 1) * 512]
        op_act(g, rr, g.pb(bank, 0, 1), AF.Ln, scale=1.0 / 1024.0, bias=1e-6,
               reads=g.pbB(bank), writes=[B(f"{pfx}_rr{sl}")])
        op_act(g, rr, rr, AF.Exp, scale=-0.5,
               reads=[B(f"{pfx}_rr{sl}")], writes=[B(f"{pfx}_rr{sl}")])
        op_copy(g, rf, rr, reads=[B(f"{pfx}_rr{sl}")], writes=[B(f"{pfx}_rbf{sl}")])
        yield tt, rf, B(f"{pfx}_rbf{sl}")




def load_vec_bcast(g, hbm_ap, name, bank=6):
    from contextlib import ExitStack
    B = g.s.buf
    tile = g.sb(name, [128, 1024], F32)
    with ExitStack() as tmps:
        sv = g.stack
        g.stack = tmps
        row = g.sb(name + "_row", [1, 1024], F32)
        op_dma(g, "gpsimd", row[0:1, :], hbm_ap.rearrange("(a d) -> a d", a=1),
               writes=[B(name + "_row")])
        onesf = g.t["c_ones_row_f32"]
        for half in range(2):
            op_mm(g, g.pb(bank), onesf[0:1, :],
                  row[0:1, half * 512:(half + 1) * 512],
                  reads=[B(name + "_row"), B("c_ones_row_f32")], writes=g.pbB(bank))
            op_copy(g, tile[:, half * 512:(half + 1) * 512], g.pb(bank),
                    reads=g.pbB(bank), writes=[B(name)])
        g.stack = sv
    g.s.fence()
    return tile


P2_STOP = [None]
P2_ROWS = [ROWS]


def phase2(g):
    from contextlib import ExitStack
    B = g.s.buf
    x1, phbm = g.t["x1"], g.t["processed"]
    ml_hbm = g.t["p2_ml"]
    xa_hbm = g.t["p2_xa"]
    dst_hbm, dch_hbm = g.t["d_st"], g.t["d_ch"]
    part_hbm = g.t["partials"]
    ones_row = g.t["c_ones_row"]
    ident = g.t["c_ident"]
    CS, SS = g.t["c_rope_cs"], g.t["c_rope_ss"]
    hind = g.t["c_headind"]
    sel16 = g.t["c_sel16"]
    onesf = g.t["c_ones_col_f32"]

    with ExitStack() as st:
        sv = g.stack
        g.stack = st
        h1 = g.sb("p2_h1", [128, NCH * T], BF16)
        y_fm = g.sb("p2_yfm", [128, NCH * T], BF16)
        dtmp = g.sb("p2_dtmp", [1, 2 * 512], BF16)
        mk = g.sb("p2_mk", [16, 4], F32)
        ds_all = g.sb("p2_dsall", [128, 16], F32)
        dc_all = g.sb("p2_dcall", [128, 16], F32)
        ln1b = load_vec_bcast(g, g.t["L0_ffn_ln1_1"], "p2_ln1")
        ln2b = load_vec_bcast(g, g.t["L0_ffn_ln2_1"], "p2_ln2")
        pnwb = load_vec_bcast(g, g.t["L0_ffn_prior_norm_w"], "p2_pnw")
        stt_ = g.sb("p2_stt", [128, 8], F32)

        for r in range(P2_ROWS[0]):
            rb = r * T
            with ExitStack() as stA:
                g.stack = stA
                xin = g.sb("p2_xin", [128, 2 * 1024], F32)
                hs = g.sb("p2_hs", [128, 2 * 1024], F32)
                ph = g.sb("p2_ph", [128, NCH * T], BF16)
                wml = g.sb("p2_wml0", [128, NCH * 512], BF16)
                mlt = g.sb("p2_mlt", [128, 2 * 512], F32)
                phB = [B(f"p2_ph{c}") for c in range(NCH)]
                h1B = [B(f"p2_h1{c}") for c in range(NCH)]
                for tb in range(16):
                    sl = tb % 2
                    xi = xin[:, sl * 1024:(sl + 1) * 1024]
                    op_dma(g, "sync", xi, x1[rb + tb * 128: rb + (tb + 1) * 128, :],
                           writes=[B(f"p2_xin{sl}")])
                    op_act(g, hs[:, 0:1024], xi, AF.Square, accum=stt_[:, 0:1],
                           reads=[B(f"p2_xin{sl}")], writes=[B("p2_hs0"), B("p2_st0")])
                    op_act(g, stt_[:, 1:2], stt_[:, 0:1], AF.Ln, scale=1.0 / 1024.0, bias=1e-6,
                           reads=[B("p2_st0")], writes=[B("p2_st1")])
                    op_act(g, stt_[:, 2:3], stt_[:, 1:2], AF.Exp, scale=-0.5,
                           reads=[B("p2_st1")], writes=[B("p2_st2")])
                    for (dstB, wb, wbn, dst) in ((h1B, ln1b, "p2_ln1", h1), (phB, pnwb, "p2_pnw", ph)):
                        op_stt(g, hs[:, 0:1024], xi, stt_[:, 2:3], wb[:, :],
                               ALU.mult, ALU.mult,
                               reads=[B(f"p2_xin{sl}"), B("p2_st2"), B(wbn)],
                               writes=[B("p2_hs0")])
                        for c in range(NCH):
                            po = g.pall[:, 7 * 512 + (c % 4) * 128: 7 * 512 + (c % 4 + 1) * 128]
                            op_transpose(g, po, hs[:, c * 128:(c + 1) * 128], ident[:, :],
                                         reads=[B("p2_hs0"), B("c_ident")], writes=g.pbB(7))
                            op_copy(g, dst[:, c * T + tb * 128: c * T + (tb + 1) * 128], po,
                                    reads=g.pbB(7), writes=[dstB[c]])
                pw = g.t["L0_ffn_prior_w"]
                for ob in range(4):
                    wbB = B("p2_wml0")
                    op_dma(g, "sync", wml[:, :],
                           g.t["pk_prior"][ob * 128:(ob + 1) * 128, :],
                           writes=[wbB])
                    for tb in range(16):
                        pq = g.pb(tb % 2)
                        pqB = g.pbB(tb % 2)
                        items = [(pq, ph[:, kc * T + tb * 128: kc * T + (tb + 1) * 128],
                                  wml[:, kc * 512:(kc + 1) * 512],
                                  kc == 0, kc == NCH - 1) for kc in range(NCH)]
                        op_mm_group(g, items, reads=[wbB] + phB, writes=pqB)
                        sl = tb % 2
                        op_act(g, mlt[:, sl * 512:(sl + 1) * 512], pq, AF.Identity,
                               reads=pqB, writes=[B(f"p2_mlt{sl}")])
                        op_dma(g, "sync",
                               ml_hbm[rb + tb * 128: rb + (tb + 1) * 128, ob * 512:(ob + 1) * 512],
                               mlt[:, sl * 512:(sl + 1) * 512],
                               reads=[B(f"p2_mlt{sl}")], writes=[B("p2_mlhbm")])
                g.stack = st
            g.s.fence()

            if P2_STOP[0] == "A":
                continue
            stB = ExitStack()
            g.stack = stB
            wq_sb = g.sb("p2_wqsb", [128, NCH * 1024], BF16)
            wk_sb = g.sb("p2_wksb", [128, NCH * 1024], BF16)
            wv_sb = g.sb("p2_wvsb", [128, NCH * 1024], BF16)
            for (wt_, wn_, ti_) in ((wq_sb, "p2_wqsb", 0),
                                    (wk_sb, "p2_wksb", 1),
                                    (wv_sb, "p2_wvsb", 2)):
                op_dma(g, "sync", wt_[:, :],
                       g.t["pk_f1qkv"][ti_ * 128:(ti_ + 1) * 128, :], writes=[B(wn_)])
            qa = [g.sb("p2_qa0", [128, T], BF16), g.sb("p2_qa1", [128, T], BF16)]
            ka = [g.sb("p2_ka0", [128, T], BF16), g.sb("p2_ka1", [128, T], BF16)]
            for _t, _n in ((qa[0], "p2_qa0"), (qa[1], "p2_qa1"), (ka[0], "p2_ka0"), (ka[1], "p2_ka1")):
                op_memset(g, _t[64:128, :], 0.0, writes=[B(_n + "_pad")])
            vt = [g.sb("p2_vt0", [128, 16 * 65], BF16), g.sb("p2_vt1", [128, 16 * 65], BF16)]
            Pb = [g.sb("p2_P0", [128, 2048], BF16), g.sb("p2_P1", [128, 2048], BF16)]
            den_all = g.sb("p2_den", [16, T], BF16)
            den_f32 = g.sb("p2_denf", [16, 512], F32)
            dra = g.sb("p2_dra", [16, T], BF16)
            rbs = g.sb("p2_rbs", [128, 1024], BF16)
            Ball = g.sb("p2_Ball", [16, T], BF16)
            nrm = g.sb("p2_nrm", [16, 512], F32)
            sq = g.sb("p2_sq", [128, 2 * 512], BF16)
            wsw2 = [g.sb("p2_wsw0", [128, NCH * 256], BF16),
                    g.sb("p2_wsw1", [128, NCH * 256], BF16)]
            u1 = g.sb("p2_u1", [128, 512], F32)
            u2 = g.sb("p2_u2", [128, 512], F32)
            wq_h, wk_h, wv_h = g.t["L0_ffn_wq1"], g.t["L0_ffn_wk1"], g.t["L0_ffn_wv1"]
            h1B = [B(f"p2_h1{c}") for c in range(NCH)]
            def emit_wswap(hc_):
                wsw_ = wsw2[hc_ % 2]
                for (wsrc, wsrcB, soff) in ((wq_sb, B("p2_wqsb"), 0), (wk_sb, B("p2_wksb"), 128)):
                    op_copy(g, bass.AP(wsw_, soff, [[NCH * 256, 128], [256, NCH], [64, 2], [1, 32]]),
                            bass.AP(wsrc, hc_ * 128 + 32, [[NCH * 1024, 128], [1024, NCH], [64, 2], [1, 32]]),
                            reads=[wsrcB], writes=[B(f"p2_wsw{hc_ % 2}")], stream="gpsimd")
                    op_copy(g, bass.AP(wsw_, soff + 32, [[NCH * 256, 128], [256, NCH], [64, 2], [1, 32]]),
                            bass.AP(wsrc, hc_ * 128, [[NCH * 1024, 128], [1024, NCH], [64, 2], [1, 32]]),
                            reads=[wsrcB], writes=[B(f"p2_wsw{hc_ % 2}")], stream="gpsimd")
                op_memset(g, mk[:, 0:1], 0.0, writes=[B("p2_mk")])

            emit_wswap(0)
            for hc in range(NCH):
                wswap = wsw2[hc % 2]
                wswB = B(f"p2_wsw{hc % 2}")
                he, ho = 2 * hc, 2 * hc + 1
                qa_e, qa_o = qa[0], qa[1]
                ka_e, ka_o = ka[0], ka[1]
                if hc == 0:
                    op_memset(g, ka_e[64:65, :], 1.0, writes=[B("p2_ka0_hi")])
                    op_memset(g, ka_o[64:65, :], 1.0, writes=[B("p2_ka1_hi")])
                if P2_STOP[0] == "B1":
                    continue
                for (isq, dst_e, dst_o, deB, doB, wres, wresB, wcol2) in (
                        (False, ka_e, ka_o, B("p2_ka0_lo"), B("p2_ka1_lo"), wk_sb, B("p2_wksb"), 128),
                        (True, qa_e, qa_o, B("p2_qa0_lo"), B("p2_qa1_lo"), wq_sb, B("p2_wqsb"), 0)):
                    for tt in range(4):
                        pA, pAB = g.pb(6), g.pbB(6)
                        pB_, pBB = g.pb(7), g.pbB(7)
                        items = [(pA, bass.AP(wres, kc * 1024 + hc * 128, [[NCH * 1024, 128], [1, 128]]),
                                  h1[:, kc * T + tt * 512: kc * T + (tt + 1) * 512],
                                  kc == 0, kc == NCH - 1) for kc in range(NCH)]
                        op_mm_group(g, items, reads=[wresB] + h1B, writes=pAB)
                        if P2_STOP[0] == "B1a":
                            continue
                        items = [(pB_, bass.AP(wswap, kc * 256 + wcol2, [[NCH * 256, 128], [1, 128]]),
                                  h1[:, kc * T + tt * 512: kc * T + (tt + 1) * 512],
                                  kc == 0, kc == NCH - 1) for kc in range(NCH)]
                        op_mm_group(g, items, reads=[wswB] + h1B, writes=pBB)
                        if P2_STOP[0] == "B1b":
                            continue
                        tsl = tt * 512
                        op_tt(g, u1[:, :], pA, CS[:, tsl: tsl + 512], ALU.mult,
                              reads=pAB + [B("c_rope_cs")], writes=[B("p2_u1")])
                        op_tt(g, u2[:, :], pB_, SS[:, tsl: tsl + 512], ALU.mult,
                              reads=pBB + [B("c_rope_ss")], writes=[B("p2_u2")])
                        if P2_STOP[0] == "B1c":
                            continue
                        op_tt(g, dst_e[0:64, tsl: tsl + 512], u1[0:64, :], u2[0:64, :],
                              ALU.add, reads=[B("p2_u1"), B("p2_u2")], writes=[deB])
                        op_tt(g, dst_o[0:64, tsl: tsl + 512], u1[64:128, :], u2[64:128, :],
                              ALU.add, reads=[B("p2_u1"), B("p2_u2")], writes=[doB])
                        if P2_STOP[0] == "B1d":
                            continue
                        sqs = sq[:, (tt % 2) * 512:(tt % 2 + 1) * 512]
                        sqB = B(f"p2_sq{tt % 2}")
                        op_act(g, sqs, pA, AF.Square, reads=pAB, writes=[sqB])
                        if P2_STOP[0] == "B1e2":
                            continue
                        op_mm(g, g.pb(5, 0, 16), hind[:, hc * 16:(hc + 1) * 16], sqs,
                              reads=[sqB, B("c_headind")], writes=g.pbB(5))
                        if P2_STOP[0] == "B1e":
                            continue
                        op_act(g, nrm[:, :], g.pb(5, 0, 16), AF.Sqrt,
                               reads=g.pbB(5), writes=[B("p2_nrm")])
                        if isq:
                            op_ts(g, Ball[:, tsl: tsl + 512], nrm[:, :],
                                  mk[:, 1:2], 8.0 * SHIFT_MARGIN, ALU.mult, ALU.add,
                                  reads=[B("p2_nrm"), B("p2_mkn")], writes=[B("p2_Ball")])
                        else:
                            op_red(g, mk[:, 2:3], nrm[:, :], ALU.max,
                                   reads=[B("p2_nrm")], writes=[B("p2_mk2")])
                            op_tt(g, mk[:, 0:1], mk[:, 0:1], mk[:, 2:3], ALU.max,
                                  reads=[B("p2_mk2"), B("p2_mk")], writes=[B("p2_mk")])
                    if not isq:
                        op_ts(g, mk[:, 1:2], mk[:, 0:1], -1.0, None, ALU.mult,
                              reads=[B("p2_mk")], writes=[B("p2_mkn")])
                if P2_STOP[0] == "B2":
                    continue
                op_dma(g, "sync", qa_e[64:65, :], Ball[he:he + 1, :],
                       reads=[B("p2_Ball")], writes=[B("p2_qa0_hi")])
                op_dma(g, "sync", qa_o[64:65, :], Ball[ho:ho + 1, :],
                       reads=[B("p2_Ball")], writes=[B("p2_qa1_hi")])
                if hc == 0:
                    op_memset(g, vt[0][:, :], 1.0, writes=[B("p2_vt0")])
                    op_memset(g, vt[1][:, :], 1.0, writes=[B("p2_vt1")])
                for tb in range(16):
                    pA, pAB = g.pb(6 + tb % 2), g.pbB(6 + tb % 2)
                    items = [(pA[:, 0:128],
                              h1[:, kc * T + tb * 128: kc * T + (tb + 1) * 128],
                              bass.AP(wv_sb, kc * 1024 + hc * 128, [[NCH * 1024, 128], [1, 128]]),
                              kc == 0, kc == NCH - 1) for kc in range(NCH)]
                    op_mm_group(g, items, reads=[B("p2_wvsb")] + h1B, writes=pAB)
                    op_copy(g, vt[0][:, tb * 65: tb * 65 + 64], pA[:, 0:64],
                            reads=pAB, writes=[B("p2_vt0")])
                    op_copy(g, vt[1][:, tb * 65: tb * 65 + 64], pA[:, 64:128],
                            reads=pAB, writes=[B("p2_vt1")])
                if P2_STOP[0] == "B3":
                    continue
                if hc + 1 < NCH:
                    emit_wswap(hc + 1)
                for (h, qa_h, ka_h, vt_h, qaB, kaB, vtB) in (
                        (he, qa_e, ka_e, vt[0], ("p2_qa0_lo", "p2_qa0_hi"),
                         ("p2_ka0_lo", "p2_ka0_hi"), "p2_vt0"),
                        (ho, qa_o, ka_o, vt[1], ("p2_qa1_lo", "p2_qa1_hi"),
                         ("p2_ka1_lo", "p2_ka1_hi"), "p2_vt1")):
                    hr = (h % 2) * 64
                    for qt in range(4):
                        yb = 4 + (qt % 2)
                        pY = g.pall[0:65, yb * 512:(yb + 1) * 512]
                        pYB = g.pbB(yb)
                        for gg in range(qt + 1):
                            last = (gg == qt)
                            Pt = Pb[gg % 2]
                            PtB = B(f"p2_P{gg % 2}")
                            items = []
                            for j in range(4):
                                kt = 4 * gg + j
                                if last:
                                    items.append((g.pall[:, j * 512 + 128 * j: j * 512 + 512],
                                                  ka_h[:, kt * 128: (kt + 1) * 128],
                                                  qa_h[:, qt * 512 + 128 * j: (qt + 1) * 512],
                                                  True, True))
                                else:
                                    items.append((g.pall[:, j * 512: (j + 1) * 512],
                                                  ka_h[:, kt * 128: (kt + 1) * 128],
                                                  qa_h[:, qt * 512: (qt + 1) * 512],
                                                  True, True))
                            op_mm_group(g, items,
                                        reads=[B(qaB[0]), B(qaB[1]), B(qaB[0].replace("_lo", "_pad")),
                                               B(kaB[0]), B(kaB[1]), B(kaB[0].replace("_lo", "_pad"))],
                                        writes=g.pbB(0, 3))
                            if last:
                                v = sview(g.pall, 0, [[4096, 128], [640, 4], [1, 128]])
                                m = sview(g.t["c_mask_add"], 0, [[128, 128], [0, 4], [1, 128]])
                                op_tt(g, v, v, m, ALU.add,
                                      reads=[B("c_mask_add")] + g.pbB(0, 3), writes=g.pbB(0, 3))
                            for j in range(4):
                                kt = 4 * gg + j
                                PsB = B(f"{PtB.name}_s{j}")
                                op_act(g, Pt[:, j * 512:(j + 1) * 512],
                                       g.pall[:, j * 512:(j + 1) * 512], AF.Exp,
                                       scale=0.125, reads=g.pbB(j, j), writes=[PsB])
                                if last and j > 0:
                                    op_memset(g, Pt[:, j * 512: j * 512 + 128 * j], 0.0,
                                              writes=[PsB], reads=[PsB])
                                op_mm_group(g, [(pY, vt_h[:, kt * 65: kt * 65 + 65],
                                                 Pt[:, j * 512: (j + 1) * 512],
                                                 gg == 0 and j == 0, last and j == 3)],
                                            reads=[PsB, B(vtB)], writes=pYB)
                        sl = qt % 2
                        op_copy(g, dtmp[0:1, sl * 512:(sl + 1) * 512], g.pall[64:65, yb * 512:(yb + 1) * 512],
                                reads=pYB, writes=[B(f"p2_dtmp{sl}")])
                        op_dma(g, "sync", den_all[h:h + 1, qt * 512:(qt + 1) * 512],
                               dtmp[0:1, sl * 512:(sl + 1) * 512],
                               reads=[B(f"p2_dtmp{sl}")], writes=[B("p2_denall")])
                        op_act(g, y_fm[hr:hr + 64, hc * T + qt * 512: hc * T + (qt + 1) * 512],
                               g.pall[0:64, yb * 512:(yb + 1) * 512], AF.Identity,
                               reads=pYB, writes=[B(f"p2_yfm{hc}")])
            if P2_STOP[0] == "Bnt":
                stB.close(); g.stack = st; g.s.fence(); continue
            op_recip(g, dra[:, :], den_all[:, :], reads=[B("p2_denall")], writes=[B("p2_dra")])
            for hc in range(NCH):
                he, ho = 2 * hc, 2 * hc + 1
                for seg in range(2):
                    for (h_, p0, p1_) in ((he, 0, 64), (ho, 64, 128)):
                        op_mm(g, g.pall[p0:p1_, 3072:3584], sel16[:, h_ * 64:(h_ + 1) * 64],
                              dra[:, seg * 1024: seg * 1024 + 512],
                              reads=[B("p2_dra"), B("c_sel16")], writes=g.pbB(6))
                        op_mm(g, g.pall[p0:p1_, 3584:4096], sel16[:, h_ * 64:(h_ + 1) * 64],
                              dra[:, seg * 1024 + 512: seg * 1024 + 1024],
                              reads=[B("p2_dra"), B("c_sel16")], writes=g.pbB(7))
                    op_act(g, rbs[:, :], g.pall[:, 3072:4096], AF.Identity,
                           reads=g.pbB(6, 7), writes=[B("p2_rbs")])
                    op_tt(g, y_fm[:, hc * T + seg * 1024: hc * T + (seg + 1) * 1024],
                          y_fm[:, hc * T + seg * 1024: hc * T + (seg + 1) * 1024],
                          rbs[:, :], ALU.mult,
                          reads=[B(f"p2_yfm{hc}"), B("p2_rbs")], writes=[B(f"p2_yfm{hc}")])
            stB.close()
            g.stack = st
            g.s.fence()

            if P2_STOP[0] == "B":
                continue
            with ExitStack() as stC:
                g.stack = stC
                wsb = g.sb("p2_wo", [128, NCH * 1024], BF16)
                xin = g.sb("p2_xinC", [128, 2 * 1024], F32)
                xat = g.sb("p2_xat", [128, 2 * 1024], F32)
                hs = g.sb("p2_hsC", [128, 1024], F32)
                op_dma(g, "sync", wsb[:, :], g.t["pk_wo1"][0:128, :],
                       writes=[B("p2_wo")])
                yfmB = [B(f"p2_yfm{c}") for c in range(NCH)]
                h1B = [B(f"p2_h1{c}") for c in range(NCH)]
                for tb in range(16):
                    sl = tb % 2
                    xi = xin[:, sl * 1024:(sl + 1) * 1024]
                    xa_t = xat[:, sl * 1024:(sl + 1) * 1024]
                    op_dma(g, "sync", xi, x1[rb + tb * 128: rb + (tb + 1) * 128, :],
                           writes=[B(f"p2_xinC{sl}")])
                    for ob in range(2):
                        pq = g.pb(ob)
                        pqB = g.pbB(ob)
                        items = [(pq, y_fm[:, kc * T + tb * 128: kc * T + (tb + 1) * 128],
                                  wsb[:, kc * 1024 + ob * 512: kc * 1024 + (ob + 1) * 512],
                                  kc == 0, kc == NCH - 1) for kc in range(NCH)]
                        op_mm_group(g, items, reads=[B("p2_wo")] + yfmB, writes=pqB)
                        op_tt(g, xa_t[:, ob * 512:(ob + 1) * 512], pq, xi[:, ob * 512:(ob + 1) * 512],
                              ALU.add, reads=pqB + [B(f"p2_xinC{sl}")], writes=[B(f"p2_xat{sl}")])
                    op_dma(g, "sync", xa_hbm[rb + tb * 128: rb + (tb + 1) * 128, :], xa_t,
                           reads=[B(f"p2_xat{sl}")], writes=[B("p2_xahbm")])
                    op_act(g, hs[:, :], xa_t, AF.Square, accum=stt_[:, 0:1],
                           reads=[B(f"p2_xat{sl}")], writes=[B("p2_hsC"), B("p2_st0")])
                    op_act(g, stt_[:, 1:2], stt_[:, 0:1], AF.Ln, scale=1.0 / 1024.0, bias=1e-6,
                           reads=[B("p2_st0")], writes=[B("p2_st1")])
                    op_act(g, stt_[:, 2:3], stt_[:, 1:2], AF.Exp, scale=-0.5,
                           reads=[B("p2_st1")], writes=[B("p2_st2")])
                    op_stt(g, hs[:, :], xa_t, stt_[:, 2:3], ln2b[:, :], ALU.mult, ALU.mult,
                           reads=[B(f"p2_xat{sl}"), B("p2_st2"), B("p2_ln2")],
                           writes=[B("p2_hsC")])
                    for c in range(NCH):
                        po = g.pall[:, 7 * 512 + (c % 4) * 128: 7 * 512 + (c % 4 + 1) * 128]
                        op_transpose(g, po, hs[:, c * 128:(c + 1) * 128], ident[:, :],
                                     reads=[B("p2_hsC"), B("c_ident")], writes=g.pbB(7))
                        op_copy(g, h1[:, c * T + tb * 128: c * T + (tb + 1) * 128], po,
                                reads=g.pbB(7), writes=[h1B[c]])
                g.stack = st
            g.s.fence()

            if P2_STOP[0] == "C":
                continue
            with ExitStack() as stE:
                g.stack = stE
                gated = g.sb("p2_gated", [128, 32 * 512], BF16)
                wgu = [g.sb("p2_wgu0", [128, NCH * 1024], BF16),
                       g.sb("p2_wgu1", [128, NCH * 1024], BF16)]
                wdb = [g.sb("p2_wd0", [128, 1024], BF16), g.sb("p2_wd1", [128, 1024], BF16)]
                sg_ = g.sb("p2_sg", [128, 512], BF16)
                xin = g.sb("p2_xinE", [128, 1024], F32)
                ptok = g.sb("p2_ptok", [128, 1024], F32)
                res = g.sb("p2_res", [128, 1024], F32)
                mlin = g.sb("p2_mlin", [128, 1024], F32)
                ee = g.sb("p2_ee", [128, 1024], F32)
                uu = g.sb("p2_uu", [128, 1024], F32)
                res1 = g.sb("p2_res1", [128, 1024], F32)
                mlin1 = g.sb("p2_mlin1", [128, 1024], F32)
                ee1 = g.sb("p2_ee1", [128, 1024], F32)
                uu1 = g.sb("p2_uu1", [128, 1024], F32)
                ssets = [
                    dict(pt=ptok, xin=xin, mlin=mlin, res=res, uu=uu, ee=ee,
                         ptB="p2_ptok0", xinB="p2_xinE", mlinB="p2_mlin",
                         resB="p2_res", uuB="p2_uu", eeB="p2_ee"),
                    dict(pt=ptok, xin=xin, mlin=mlin1, res=res1, uu=uu1, ee=ee1,
                         ptB="p2_ptok0", xinB="p2_xinE", mlinB="p2_mlin1",
                         resB="p2_res1", uuB="p2_uu1", eeB="p2_ee1"),
                ]
                wg_h, wu_h, wd_h = g.t["L0_ffn_wg1"], g.t["L0_ffn_wu1"], g.t["L0_ffn_wd1"]
                h1B = [B(f"p2_h1{c}") for c in range(NCH)]
                gatB = [B(f"p2_gated{gc}") for gc in range(32)]
                def stats_ops(tt, tb, s):
                    gt = tt * 4 + tb
                    pt_, xn, ml, rs, u_, e_ = (s["pt"], s["xin"], s["mlin"],
                                               s["res"], s["uu"], s["ee"])
                    ptB, xnB, mlB = B(s["ptB"]), B(s["xinB"]), B(s["mlinB"])
                    rsB, uB, eB = B(s["resB"]), B(s["uuB"]), B(s["eeB"])
                    return [
                     lambda: op_dma(g, "gpsimd", pt_[:, :], phbm[rb + gt * 128: rb + (gt + 1) * 128, :],
                                    writes=[ptB], reads=[B("p2_phbm")]),
                     lambda: op_dma(g, "gpsimd", xn[:, :], x1[rb + gt * 128: rb + (gt + 1) * 128, :],
                                    writes=[xnB]),
                     lambda: op_dma(g, "gpsimd", ml[:, :],
                                    ml_hbm[rb + gt * 128: rb + (gt + 1) * 128, 0:1024],
                                    writes=[mlB], reads=[B("p2_mlhbm")]),
                     lambda: op_tt(g, rs[:, :], pt_[:, :], xn[:, :], ALU.subtract,
                                   reads=[ptB, xnB], writes=[rsB]),
                     lambda: op_act(g, u_[:, :], rs[:, :], AF.Square, accum=ds_all[:, gt:gt + 1],
                                    reads=[rsB], writes=[uB, B("p2_dsall")]),
                     lambda: op_tt(g, u_[:, :], rs[:, :], ml[:, :], ALU.subtract,
                                   reads=[rsB, mlB], writes=[uB]),
                     lambda: op_dma(g, "gpsimd", ml[:, :],
                                    ml_hbm[rb + gt * 128: rb + (gt + 1) * 128, 1024:2048],
                                    writes=[mlB], reads=[B("p2_mlhbm")]),
                     lambda: op_act(g, e_[:, :], ml[:, :], AF.Exp, scale=-1.0,
                                    reads=[mlB], writes=[eB]),
                     lambda: op_act(g, u_[:, :], u_[:, :], AF.Square,
                                    reads=[uB], writes=[uB]),
                     lambda: op_stt(g, u_[:, :], u_[:, :], 1.0, e_[:, :], ALU.add, ALU.mult,
                                    reads=[uB, eB], writes=[uB]),
                     lambda: op_tt(g, u_[:, :], u_[:, :], ml[:, :], ALU.add,
                                   reads=[uB, mlB], writes=[uB]),
                     lambda: op_act(g, e_[:, :], u_[:, :], AF.Identity, accum=dc_all[:, gt:gt + 1],
                                    reads=[uB], writes=[eB, B("p2_dcall")]),
                    ]

                def emit_stats_pair(tt, tb0):
                    a = stats_ops(tt, tb0, ssets[0])
                    b = stats_ops(tt, tb0 + 1, ssets[1])
                    for x in a[:4]:
                        x()
                    rest = a[4:]
                    for i, y in enumerate(b):
                        if i < len(rest):
                            rest[i]()
                        y()

                pending_stats = []
                for tt in range(4):
                    for blk in range(8):
                        wb = wgu[blk % 2]
                        wbB = B(f"p2_wgu{blk % 2}")
                        op_dma(g, "sync", wb[:, :],
                               g.t["pk_f1gu"][blk * 128:(blk + 1) * 128, :],
                               writes=[wbB])
                        for sub in range(4):
                            gc = blk * 4 + sub
                            pG, pGB = g.pb(0 + sub % 2), g.pbB(0 + sub % 2)
                            pU, pUB = g.pb(2 + sub % 2), g.pbB(2 + sub % 2)
                            items = [(pG[:, 0:512],
                                      bass.AP(wb, kc * 1024 + sub * 128, [[NCH * 1024, 128], [1, 128]]),
                                      h1[:, kc * T + tt * 512: kc * T + (tt + 1) * 512],
                                      kc == 0, kc == NCH - 1) for kc in range(NCH)]
                            op_mm_group(g, items, reads=[wbB] + h1B, writes=pGB)
                            items = [(pU[:, 0:512],
                                      bass.AP(wb, kc * 1024 + 512 + sub * 128, [[NCH * 1024, 128], [1, 128]]),
                                      h1[:, kc * T + tt * 512: kc * T + (tt + 1) * 512],
                                      kc == 0, kc == NCH - 1) for kc in range(NCH)]
                            op_mm_group(g, items, reads=[wbB] + h1B, writes=pUB)
                            op_act(g, sg_[:, :], pG[:, 0:512], AF.Silu,
                                   reads=pGB, writes=[B("p2_sg")])
                            op_tt(g, gated[:, gc * 512:(gc + 1) * 512], sg_[:, :], pU[:, 0:512],
                                  ALU.mult, reads=[B("p2_sg")] + pUB, writes=[gatB[gc]])
                        if blk < len(pending_stats):
                            pending_stats[blk]()
                    for gc in range(32):
                        wd_t = wdb[gc % 2]
                        wdB = B(f"p2_wd{gc % 2}")
                        op_dma(g, "sync", wd_t[:, :], wd_h[gc * 128:(gc + 1) * 128, :],
                               writes=[wdB])
                        items = []
                        for tb in range(4):
                            for ob in range(2):
                                items.append((g.pb(tb * 2 + ob),
                                              gated[:, gc * 512 + tb * 128: gc * 512 + (tb + 1) * 128],
                                              wd_t[:, ob * 512:(ob + 1) * 512],
                                              gc == 0, gc == 31))
                        op_mm_group(g, items, reads=[wdB, gatB[gc]], writes=g.pbB(0, 7))
                    for tb in range(4):
                        gt = tt * 4 + tb
                        op_dma(g, "gpsimd", xin[:, :], xa_hbm[rb + gt * 128: rb + (gt + 1) * 128, :],
                               writes=[B("p2_xinE")])
                        pt_ = ptok[:, 0:1024]
                        for ob in range(2):
                            op_tt(g, pt_[:, ob * 512:(ob + 1) * 512], g.pb(tb * 2 + ob),
                                  xin[:, ob * 512:(ob + 1) * 512], ALU.add,
                                  reads=g.pbB(tb * 2 + ob) + [B("p2_xinE")],
                                  writes=[B("p2_ptok0")])
                        op_dma(g, "gpsimd", phbm[rb + gt * 128: rb + (gt + 1) * 128, :], pt_,
                               reads=[B("p2_ptok0")], writes=[B("p2_phbm")])
                    pending_stats = [
                        (lambda tt_=tt: emit_stats_pair(tt_, 0)),
                        (lambda tt_=tt: emit_stats_pair(tt_, 2))]
                for fn in pending_stats:
                    fn()
                g.stack = st
            g.s.fence()
            if P2_STOP[0] == "E":
                continue
            op_dma(g, "sync", dst_hbm[r * 128:(r + 1) * 128, :], ds_all[:, :],
                   reads=[B("p2_dsall")], writes=[B("p2_dsthbm")])
            op_dma(g, "sync", dch_hbm[r * 128:(r + 1) * 128, :], dc_all[:, :],
                   reads=[B("p2_dcall")], writes=[B("p2_dchhbm")])
            op_red(g, stt_[:, 3:4], ds_all[:, :], ALU.add, reads=[B("p2_dsall")], writes=[B("p2_mk3")])
            op_mm(g, g.pb(6, 0, 1, 1), onesf[:, :], stt_[:, 3:4],
                  reads=[B("p2_mk3"), B("c_ones_col_f32")], writes=g.pbB(6))
            op_red(g, stt_[:, 3:4], dc_all[:, :], ALU.add, reads=[B("p2_dcall")], writes=[B("p2_mk3")])
            op_mm(g, bass.AP(g.pall, 6 * 512 + 1, [[4096, 1], [1, 1]]), onesf[:, :], stt_[:, 3:4],
                  reads=[B("p2_mk3"), B("c_ones_col_f32")], writes=g.pbB(6))
            op_copy(g, stt_[0:1, 4:6], g.pall[0:1, 3072:3074],
                    reads=g.pbB(6), writes=[B("p2_pttmp")])
            op_dma(g, "sync", part_hbm[r:r + 1, :], stt_[0:1, 4:6],
                  reads=[B("p2_pttmp")], writes=[B("p2_parthbm")])
        g.stack = sv
    g.s.fence()



P4_CONSTS = ["c_ones_col", "c_ones_col_f32", "c_ones_row_f32", "c_ident",
             "c_tril", "c_triu16", "c_iota_f", "c_iota_row128"]
NTOK = NCORES * ROWS * T


def cumsum_tokens(g, src_bf, srcB, out_f32, outB, csr, cst, cstb, offr, pfx):
    B = g.s.buf
    tril = g.t["c_tril"]
    ones = g.t["c_ones_col"]
    onesr = g.t["c_ones_row_f32"]
    tri16 = g.t["c_triu16"]
    ident = g.t["c_ident"]
    op_mm(g, g.pb(4, 0, 128, 64), tril[:, :], src_bf[:, :],
          reads=[srcB, B("c_tril")], writes=g.pbB(4))
    op_mm(g, g.pb(5, 0, 1, 64), ones[:, :], src_bf[:, :],
          reads=[srcB, B("c_ones_col")], writes=g.pbB(5))
    op_copy(g, csr[0:1, :], g.pb(5, 0, 1, 64), reads=g.pbB(5), writes=[B(pfx + "csr")])
    op_transpose(g, g.pb(6, 0, 64, 1), csr[0:1, :], ident[0:1, 0:1],
                 reads=[B(pfx + "csr"), B("c_ident")], writes=g.pbB(6))
    op_copy(g, cstb[:, :], g.pb(6, 0, 64, 1), reads=g.pbB(6), writes=[B(pfx + "cstb")])
    op_mm(g, g.pb(5, 0, 1, 64), cstb[:, :], tri16[:, :],
          reads=[B(pfx + "cstb"), B("c_triu16")], writes=g.pbB(5))
    op_copy(g, offr[0:1, :], g.pb(5, 0, 1, 64), reads=g.pbB(5), writes=[B(pfx + "offr")])
    op_mm(g, g.pb(7, 0, 128, 64), onesr[0:1, :], offr[0:1, :],
          reads=[B(pfx + "offr"), B("c_ones_row_f32")], writes=g.pbB(7))
    op_copy(g, out_f32[:, :], g.pb(4, 0, 128, 64), reads=g.pbB(4), writes=[outB])
    op_tt(g, out_f32[:, :], out_f32[:, :], g.pb(7, 0, 128, 64), ALU.add,
          reads=[outB] + g.pbB(7), writes=[outB])


def phase4(g):
    from contextlib import ExitStack
    import bass_rust
    B = g.s.buf
    dst_hbm, dch_hbm = g.t["d_st"], g.t["d_ch"]
    pall_hbm = g.t["partials_all"]
    idsco_hbm = g.t["idsco"]
    onesf = g.t["c_ones_col_f32"]
    onesr = g.t["c_ones_row_f32"]
    with ExitStack() as st:
        sv = g.stack
        g.stack = st
        gds = g.sb("p4_gds", [128, 64], F32)
        gdc = g.sb("p4_gdc", [128, 64], F32)
        gt = g.sb("p4_g", [128, 64], F32)
        pt = g.sb("p4_pt", [32, 2], F32)
        pay = g.sb("p4_pay", [128, 128], F32)
        cvec = g.sb("p4_cvec", [1, 2], F32)
        srow = g.sb("p4_srow", [1, 2], F32)
        s0b = g.sb("p4_s0b", [128, 1], F32)
        lo = g.sb("p4_lo", [1, 4], F32)
        hi = g.sb("p4_hi", [1, 4], F32)
        mid = g.sb("p4_mid", [1, 4], F32)
        trow = g.sb("p4_trow", [1, 4], F32)
        crow = g.sb("p4_crow", [1, 4], F32)
        c2row = g.sb("p4_c2row", [1, 4], F32)
        needr = g.sb("p4_needr", [1, 4], F32)
        cscr = g.sb("p4_cscr", [128, 64], F32)
        cntc = g.sb("p4_cntc", [128, 4], F32)
        m_gt = g.sb("p4_mgt", [128, 64], F32)
        m_eq = g.sb("p4_meq", [128, 64], F32)
        mbf = g.sb("p4_mbf", [128, 64], BF16)
        cum = g.sb("p4_cum", [128, 64], F32)
        csr = g.sb("p4_csr", [1, 64], F32)
        cstb = g.sb("p4_cstb", [64, 1], BF16)
        offr = g.sb("p4_offr", [1, 64], F32)
        dest = g.sb("p4_dest", [128, 64], F32)
        destu = g.sb("p4_destu", [128, 64], U32)
        cst = None
        for r in range(ROWS):
            op_dma(g, "sync", gds[:, r * 16:(r + 1) * 16],
                   dst_hbm[r * 128:(r + 1) * 128, :], writes=[B("p4_gds")])
            op_dma(g, "sync", gdc[:, r * 16:(r + 1) * 16],
                   dch_hbm[r * 128:(r + 1) * 128, :], writes=[B("p4_gdc")])
        op_dma(g, "sync", pt[0:pall_hbm.shape[0], :], pall_hbm[:, :], writes=[B("p4_pt")])
        op_memset(g, cvec[0:1, 0:1], -1.0 / (1024.0 * NTOK), writes=[B("p4_cvec")])
        op_memset(g, cvec[0:1, 1:2], -0.5 / (1024.0 * NTOK), writes=[B("p4_cvec")])
        nprow = g.t["partials_all"].shape[0]
        op_mm(g, g.pb(0, 0, 1, 2), onesf[0:nprow, 0:1], pt[0:nprow, :],
              reads=[B("p4_pt"), B("c_ones_col_f32")], writes=g.pbB(0))
        op_copy(g, srow[0:1, :], g.pb(0, 0, 1, 2), reads=g.pbB(0), writes=[B("p4_srow")])
        op_tt(g, srow[0:1, :], srow[0:1, :], cvec[0:1, :], ALU.mult,
              reads=[B("p4_srow"), B("p4_cvec")], writes=[B("p4_srow")])
        op_red(g, trow[0:1, 0:1], srow[0:1, :], ALU.add,
               reads=[B("p4_srow")], writes=[B("p4_s0")])
        op_mm(g, g.pb(0, 0, 128, 1), onesr[0:1, :], trow[0:1, 0:1],
              reads=[B("p4_s0"), B("c_ones_row_f32")], writes=g.pbB(0))
        op_copy(g, s0b[:, :], g.pb(0, 0, 128, 1), reads=g.pbB(0), writes=[B("p4_s0b")])
        op_ts(g, cscr[:, :], gdc[:, :], 0.5, None, ALU.mult,
              reads=[B("p4_gdc")], writes=[B("p4_cscr")])
        op_tt(g, cscr[:, :], cscr[:, :], gds[:, :], ALU.add,
              reads=[B("p4_cscr"), B("p4_gds")], writes=[B("p4_cscr")])
        op_act(g, gt[:, :], cscr[:, :], AF.Sigmoid, scale=1.0 / 1024.0, bias=s0b[:, 0:1],
               reads=[B("p4_cscr"), B("p4_s0b")], writes=[B("p4_g")])
        op_memset(g, lo[0:1, :], 0.0, writes=[B("p4_lo")])
        op_memset(g, hi[0:1, :], 1.0, writes=[B("p4_hi")])
        for it in range(45):
            op_tt(g, mid[0:1, :], lo[0:1, :], hi[0:1, :], ALU.add,
                  reads=[B("p4_lo"), B("p4_hi")], writes=[B("p4_mid")])
            op_ts(g, mid[0:1, :], mid[0:1, :], 0.5, None, ALU.mult,
                  reads=[B("p4_mid")], writes=[B("p4_mid")])
            op_mm(g, g.pb(1, 0, 128, 4), onesr[0:1, :], mid[0:1, :],
                  reads=[B("p4_mid"), B("c_ones_row_f32")], writes=g.pbB(1))
            for r in range(ROWS):
                op_ts(g, cscr[:, r * 16:(r + 1) * 16], gt[:, r * 16:(r + 1) * 16],
                      g.pb(1, 0, 128, 4)[:, r:r + 1], 0.0, ALU.is_gt, ALU.add,
                      reads=[B("p4_g")] + g.pbB(1), writes=[B("p4_cscr"), B("p4_cntc")],
                      accum=cntc[:, r:r + 1])
            op_mm(g, g.pb(2, 0, 1, 4), onesf[:, :], cntc[:, :],
                  reads=[B("p4_cntc"), B("c_ones_col_f32")], writes=g.pbB(2))
            op_ts(g, crow[0:1, :], g.pb(2, 0, 1, 4), 512.0, None, ALU.is_ge,
                  reads=g.pbB(2), writes=[B("p4_crow")])
            op_tt(g, trow[0:1, :], mid[0:1, :], lo[0:1, :], ALU.subtract,
                  reads=[B("p4_mid"), B("p4_lo")], writes=[B("p4_trow")])
            op_tt(g, trow[0:1, :], trow[0:1, :], crow[0:1, :], ALU.mult,
                  reads=[B("p4_trow"), B("p4_crow")], writes=[B("p4_trow")])
            op_tt(g, lo[0:1, :], lo[0:1, :], trow[0:1, :], ALU.add,
                  reads=[B("p4_lo"), B("p4_trow")], writes=[B("p4_lo")])
            op_ts(g, c2row[0:1, :], crow[0:1, :], -1.0, 1.0, ALU.mult, ALU.add,
                  reads=[B("p4_crow")], writes=[B("p4_c2row")])
            op_tt(g, trow[0:1, :], mid[0:1, :], hi[0:1, :], ALU.subtract,
                  reads=[B("p4_mid"), B("p4_hi")], writes=[B("p4_trow")])
            op_tt(g, trow[0:1, :], trow[0:1, :], c2row[0:1, :], ALU.mult,
                  reads=[B("p4_trow"), B("p4_c2row")], writes=[B("p4_trow")])
            op_tt(g, hi[0:1, :], hi[0:1, :], trow[0:1, :], ALU.add,
                  reads=[B("p4_hi"), B("p4_trow")], writes=[B("p4_hi")])
        op_mm(g, g.pb(1, 0, 128, 4), onesr[0:1, :], hi[0:1, :],
              reads=[B("p4_hi"), B("c_ones_row_f32")], writes=g.pbB(1))
        for r in range(ROWS):
            op_ts(g, m_gt[:, r * 16:(r + 1) * 16], gt[:, r * 16:(r + 1) * 16],
                  g.pb(1, 0, 128, 4)[:, r:r + 1], 0.0, ALU.is_gt, ALU.add,
                  reads=[B("p4_g")] + g.pbB(1), writes=[B("p4_mgt"), B("p4_cntc")],
                  accum=cntc[:, r:r + 1])
            op_ts(g, m_eq[:, r * 16:(r + 1) * 16], gt[:, r * 16:(r + 1) * 16],
                  g.pb(1, 0, 128, 4)[:, r:r + 1], None, ALU.is_equal,
                  reads=[B("p4_g")] + g.pbB(1), writes=[B("p4_meq")])
        op_mm(g, g.pb(2, 0, 1, 4), onesf[:, :], cntc[:, :],
              reads=[B("p4_cntc"), B("c_ones_col_f32")], writes=g.pbB(2))
        op_ts(g, needr[0:1, :], g.pb(2, 0, 1, 4), -1.0, 512.0, ALU.mult, ALU.add,
              reads=g.pbB(2), writes=[B("p4_needr")])
        op_mm(g, g.pb(3, 0, 128, 4), onesr[0:1, :], needr[0:1, :],
              reads=[B("p4_needr"), B("c_ones_row_f32")], writes=g.pbB(3))
        op_copy(g, mbf[:, :], m_eq[:, :], reads=[B("p4_meq")], writes=[B("p4_mbf")])
        cumsum_tokens(g, mbf, B("p4_mbf"), cum, B("p4_cum"), csr, cst, cstb, offr, "p4a_")
        for r in range(ROWS):
            sl = slice(r * 16, (r + 1) * 16)
            op_ts(g, cscr[:, sl], cum[:, sl], g.pb(3, 0, 128, 4)[:, r:r + 1], None,
                  ALU.is_le, reads=[B("p4_cum")] + g.pbB(3), writes=[B("p4_cscr")])
            op_tt(g, cscr[:, sl], cscr[:, sl], m_eq[:, sl], ALU.mult,
                  reads=[B("p4_cscr"), B("p4_meq")], writes=[B("p4_cscr")])
            op_tt(g, m_gt[:, sl], m_gt[:, sl], cscr[:, sl], ALU.add,
                  reads=[B("p4_mgt"), B("p4_cscr")], writes=[B("p4_mgt")])
        op_copy(g, mbf[:, :], m_gt[:, :], reads=[B("p4_mgt")], writes=[B("p4_mbf")])
        cumsum_tokens(g, mbf, B("p4_mbf"), cum, B("p4_cum"), csr, cst, cstb, offr, "p4b_")
        op_ts(g, cum[:, :], cum[:, :], -1.0, None, ALU.add,
              reads=[B("p4_cum")], writes=[B("p4_cum")])
        op_tt(g, cum[:, :], cum[:, :], m_gt[:, :], ALU.mult,
              reads=[B("p4_cum"), B("p4_mgt")], writes=[B("p4_cum")])
        op_ts(g, cscr[:, :], m_gt[:, :], -float(1 << 20), float(1 << 20), ALU.mult, ALU.add,
              reads=[B("p4_mgt")], writes=[B("p4_cscr")])
        op_tt(g, dest[:, :], cum[:, :], cscr[:, :], ALU.add,
              reads=[B("p4_cum"), B("p4_cscr")], writes=[B("p4_dest")])
        op_copy(g, destu[:, :], dest[:, :], reads=[B("p4_dest")], writes=[B("p4_destu")])
        iota_f = g.t["c_iota_f"]
        iorow = g.t["c_iota_row128"]
        msel = g.sb("p4_msel", [128, 128], F32)
        dsb = g.sb("p4_dsb", [128, 1], F32)
        outt = g.sb("p4_outt", [128, 2], F32)
        op_copy(g, bass.AP(pay, 0, [[128, 128], [32, 4], [2, 16]]),
                bass.AP(iota_f.tensor if hasattr(iota_f, "tensor") else iota_f, 0,
                        [[16, 128], [0, 4], [1, 16]]),
                reads=[B("c_iota_f")], writes=[B("p4_pay")])
        op_copy(g, bass.AP(pay, 1, [[128, 128], [32, 4], [2, 16]]), gt[:, :],
                reads=[B("p4_g")], writes=[B("p4_pay")])
        for r in range(ROWS):
            for sb_ in range(4):
                for c in range(16):
                    col = r * 16 + c
                    op_ts(g, dsb[:, 0:1], dest[:, col:col + 1], -128.0 * sb_, None,
                          ALU.add, reads=[B("p4_dest")], writes=[B("p4_dsb")])
                    op_ts(g, msel[:, :], iorow[:, :], dsb[:, 0:1], None, ALU.is_equal,
                          reads=[B("c_iota_row128"), B("p4_dsb")], writes=[B("p4_msel")])
                    op_mm(g, g.pb(0, 0, 128, 2), msel[:, :], pay[:, col * 2: col * 2 + 2],
                          reads=[B("p4_msel"), B("p4_pay")], writes=g.pbB(0),
                          start=(c == 0), stop=(c == 15))
                op_copy(g, outt[:, 0:2], g.pb(0, 0, 128, 2), reads=g.pbB(0),
                        writes=[B("p4_outt")])
                op_dma(g, "sync",
                       idsco_hbm[r * KSEL + sb_ * 128: r * KSEL + (sb_ + 1) * 128, :],
                       outt[:, 0:2], reads=[B("p4_outt")], writes=[B("idsco_hbm")])
        g.stack = sv
    g.s.fence()



P5_CONSTS = ["c_ones_col", "c_ones_row", "c_ident", "c_mask_add", "c_headind",
             "c_sel16", "c_invfreq4_f32", "c_ropesign", "c_ones_col_f32",
             "c_ones_row_f32"]
T2 = 512


def phase5(g):
    from contextlib import ExitStack
    import bass_rust
    B = g.s.buf
    phbm = g.t["processed"]
    p2hbm = g.t["processed2"]
    idsco_hbm = g.t["idsco"]
    ident = g.t["c_ident"]
    hind = g.t["c_headind"]
    sel16 = g.t["c_sel16"]
    onesr_f = g.t["c_ones_row_f32"]
    invf = g.t["c_invfreq4_f32"]
    rsign = g.t["c_ropesign"]
    TWO_PI = 2.0 * math.pi

    with ExitStack() as st:
        sv = g.stack
        g.stack = st
        sel = g.sb("p5_sel", [128, 4 * 1024], F32)
        xa2 = g.sb("p5_xa2", [128, 4 * 1024], F32)
        h2 = g.sb("p5_h2", [128, NCH * T2], BF16)
        y_fm = g.sb("p5_yfm", [128, NCH * T2], BF16)
        CS2 = g.sb("p5_cs", [128, T2], F32)
        SS2 = g.sb("p5_ss", [128, T2], F32)
        angs = g.sb("p5_angs", [128, T2], F32)

        ri = g.sb("p5_ri", [128, T2], I32)
        idxt = g.sb("p5_idxt", [128, 4], U32)
        sct = g.sb("p5_sct", [128, 4], F32)
        ds2 = g.sb("p5_ds2", [128, 2], F32)
        posf = g.sb("p5_posf", [1, T2], F32)
        qa = [g.sb("p5_qa0", [128, T2], BF16), g.sb("p5_qa1", [128, T2], BF16)]
        ka = [g.sb("p5_ka0", [128, T2], BF16), g.sb("p5_ka1", [128, T2], BF16)]
        for _t, _n in ((qa[0], "p5_qa0"), (qa[1], "p5_qa1"), (ka[0], "p5_ka0"), (ka[1], "p5_ka1")):
            op_memset(g, _t[64:128, :], 0.0, writes=[B(_n + "_pad")])
        vt = [g.sb("p5_vt0", [128, 4 * 65], BF16), g.sb("p5_vt1", [128, 4 * 65], BF16)]
        Pb = [g.sb("p5_P0", [128, 2048], BF16)]
        Pb.append(Pb[0])
        den_all = g.sb("p5_den", [16, T2], F32)
        dra = g.sb("p5_dra", [16, T2], BF16)
        rbs = g.sb("p5_rbs", [128, T2], BF16)
        dtmp = g.sb("p5_dtmp", [1, T2], F32)
        Ball = g.sb("p5_Ball", [16, T2], BF16)
        nrm = g.sb("p5_nrm", [16, T2], F32)
        mk = g.sb("p5_mk", [16, 4], F32)
        sq = g.sb("p5_sq", [128, T2], BF16)
        wpair = g.sb("p5_wp0", [128, NCH * 384], BF16)
        wswap = g.sb("p5_wsw", [128, NCH * 256], BF16)
        u1 = g.sb("p5_u1", [128, T2], F32)
        u2 = g.sb("p5_u2", [128, T2], F32)
        rr1 = u2
        stt_ = g.sb("p5_stt", [128, 8], F32)
        hs = g.sb("p5_hs", [128, 1024], F32)
        xin = g.sb("p5_xin", [128, 2 * 1024], F32)
        gated = g.sb("p5_gated", [128, 32 * T2], BF16)
        wgu = g.sb("p5_wgu", [128, NCH * 1024], BF16)
        wdb = [g.sb("p5_wd0", [128, 1024], BF16), g.sb("p5_wd1", [128, 1024], BF16)]
        sg_ = g.sb("p5_sg", [128, T2], BF16)
        ln12b = load_vec_bcast(g, g.t["L0_ffn_ln1_2"], "p5_ln1")
        ln22b = load_vec_bcast(g, g.t["L0_ffn_ln2_2"], "p5_ln2")
        breg = g.top_stack.enter_context(g.nc.gpsimd.register("p5bnd"))
        g.nc.gpsimd.reg_mov(breg, T - 1)

        wq_h, wk_h, wv_h = g.t["L0_ffn_wq2"], g.t["L0_ffn_wk2"], g.t["L0_ffn_wv2"]

        for r in range(ROWS):
            rb = r * T
            for tb in range(16):
                sl = tb % 2
                xi = xin[:, sl * 1024:(sl + 1) * 1024]
                op_dma(g, "sync", xi, phbm[rb + tb * 128: rb + (tb + 1) * 128, :],
                       writes=[B(f"p5_xin{sl}")])
                op_dma(g, "sync", p2hbm[rb + tb * 128: rb + (tb + 1) * 128, :], xi,
                       reads=[B(f"p5_xin{sl}")], writes=[B("p5_p2hbm")])
            for gg in range(4):
                op_dma(g, "sync", ds2[:, :],
                       idsco_hbm[r * KSEL + gg * 128: r * KSEL + (gg + 1) * 128, :],
                       writes=[B("p5_ds2")])
                op_copy(g, idxt[:, gg:gg + 1], ds2[:, 0:1],
                        reads=[B("p5_ds2")], writes=[B("p5_idxt")])
                op_copy(g, sct[:, gg:gg + 1], ds2[:, 1:2],
                        reads=[B("p5_ds2")], writes=[B("p5_sct")])
                op_transpose(g, g.pb(1, 0, 1, 128), ds2[:, 0:1], ident[:, :],
                             reads=[B("p5_ds2"), B("c_ident")], writes=g.pbB(1))
                op_copy(g, posf[0:1, gg * 128:(gg + 1) * 128], g.pb(1, 0, 1, 128),
                        reads=g.pbB(1), writes=[B("p5_posf")])
            op_mm(g, g.pb(0), invf[0:1, :], posf[0:1, :],
                  reads=[B("p5_posf"), B("c_invfreq4_f32")], writes=g.pbB(0))
            op_copy(g, angs[:, :], g.pb(0), reads=g.pbB(0), writes=[B("p5_angs")])
            for (tbl, tblB, shift) in ((SS2, "p5_ss", 0.0), (CS2, "p5_cs", math.pi / 2)):
                src = angs
                if shift:
                    op_ts(g, rr1[:, :], angs[:, :], shift, None, ALU.add,
                          reads=[B("p5_angs")], writes=[B("p5_rr1")])
                    src, srcB = rr1, "p5_rr1"
                else:
                    srcB = "p5_angs"
                op_ts(g, tbl[:, :], src[:, :], 1.0 / TWO_PI, None, ALU.mult,
                      reads=[B(srcB)], writes=[B(tblB)])
                op_copy(g, ri[:, :], tbl[:, :], reads=[B(tblB)], writes=[B("p5_ri")])
                op_copy(g, tbl[:, :], ri[:, :], reads=[B("p5_ri")], writes=[B(tblB)])
                op_stt(g, tbl[:, :], tbl[:, :], -TWO_PI, src[:, :], ALU.mult, ALU.add,
                       reads=[B(tblB), B(srcB)], writes=[B(tblB)])
                op_act(g, tbl[:, :], tbl[:, :], AF.Sin, reads=[B(tblB)], writes=[B(tblB)])
            op_ts(g, SS2[:, :], SS2[:, :], rsign[:, 0:1], None, ALU.mult,
                  reads=[B("p5_ss"), B("c_ropesign")], writes=[B("p5_ss")])
            for gg in range(4):
                def f_g(e, o=sel[:, gg * 1024:(gg + 1) * 1024],
                        off=bass_rust.IndirectOffsetOnAxis(axis=0, ap=idxt[:, gg:gg + 1]),
                        eo=rb * 1024):
                    return e.indirect_dma_start(out=o, out_offset=None, in_=phbm[:, :],
                                                in_offset=off, element_offset=eo)
                g.s.op("gpsimd", f_g, reads=[B("p5_idxt"), B("p2_phbm")],
                       writes=[B("p5_sel")], dma=True)
            for gg in range(4):
                s_t = sel[:, gg * 1024:(gg + 1) * 1024]
                op_act(g, hs[:, :], s_t, AF.Square, accum=stt_[:, 0:1],
                       reads=[B("p5_sel")], writes=[B("p5_hs"), B("p5_st0")])
                op_act(g, stt_[:, 1:2], stt_[:, 0:1], AF.Ln, scale=1.0 / 1024.0, bias=1e-6,
                       reads=[B("p5_st0")], writes=[B("p5_st1")])
                op_act(g, stt_[:, 2:3], stt_[:, 1:2], AF.Exp, scale=-0.5,
                       reads=[B("p5_st1")], writes=[B("p5_st2")])
                op_stt(g, hs[:, :], s_t, stt_[:, 2:3], ln12b[:, :], ALU.mult, ALU.mult,
                       reads=[B("p5_sel"), B("p5_st2"), B("p5_ln1")], writes=[B("p5_hs")])
                for c in range(NCH):
                    po = g.pall[:, 7 * 512 + (c % 4) * 128: 7 * 512 + (c % 4 + 1) * 128]
                    op_transpose(g, po, hs[:, c * 128:(c + 1) * 128], ident[:, :],
                                 reads=[B("p5_hs"), B("c_ident")], writes=g.pbB(7))
                    op_copy(g, h2[:, c * T2 + gg * 128: c * T2 + (gg + 1) * 128], po,
                            reads=g.pbB(7), writes=[B(f"p5_h2{c}")])
            h2B = [B(f"p5_h2{c}") for c in range(NCH)]
            for hc in range(NCH):
                wp, wpB = wpair, B("p5_wp0")
                op_dma(g, "sync", wp[:, :],
                       g.t["pk_qkv2"][hc * 128:(hc + 1) * 128, :], writes=[wpB])
                for (woff, soff) in ((0, 0), (128, 128)):
                    op_copy(g, bass.AP(wswap, soff, [[NCH * 256, 128], [256, NCH], [64, 2], [1, 32]]),
                            bass.AP(wp, woff + 32, [[NCH * 384, 128], [384, NCH], [64, 2], [1, 32]]),
                            reads=[wpB], writes=[B("p5_wsw")])
                    op_copy(g, bass.AP(wswap, soff + 32, [[NCH * 256, 128], [256, NCH], [64, 2], [1, 32]]),
                            bass.AP(wp, woff, [[NCH * 384, 128], [384, NCH], [64, 2], [1, 32]]),
                            reads=[wpB], writes=[B("p5_wsw")])
                he, ho = 2 * hc, 2 * hc + 1
                qa_e, qa_o = qa
                ka_e, ka_o = ka
                op_memset(g, ka_e[64:65, :], 1.0, writes=[B("p5_ka0_hi")])
                op_memset(g, ka_o[64:65, :], 1.0, writes=[B("p5_ka1_hi")])
                op_memset(g, mk[:, 0:1], 0.0, writes=[B("p5_mk")])
                for (isq, dst_e, dst_o, deB, doB, wcol) in (
                        (False, ka_e, ka_o, B("p5_ka0_lo"), B("p5_ka1_lo"), 128),
                        (True, qa_e, qa_o, B("p5_qa0_lo"), B("p5_qa1_lo"), 0)):
                    pA, pAB = g.pb(6), g.pbB(6)
                    pB_, pBB = g.pb(7), g.pbB(7)
                    items = [(pA, bass.AP(wp, kc * 384 + wcol, [[NCH * 384, 128], [1, 128]]),
                              h2[:, kc * T2: (kc + 1) * T2],
                              kc == 0, kc == NCH - 1) for kc in range(NCH)]
                    op_mm_group(g, items, reads=[wpB] + h2B, writes=pAB)
                    items = [(pB_, bass.AP(wswap, kc * 256 + wcol, [[NCH * 256, 128], [1, 128]]),
                              h2[:, kc * T2: (kc + 1) * T2],
                              kc == 0, kc == NCH - 1) for kc in range(NCH)]
                    op_mm_group(g, items, reads=[B("p5_wsw")] + h2B, writes=pBB)
                    op_tt(g, u1[:, :], pA, CS2[:, :], ALU.mult,
                          reads=pAB + [B("p5_cs")], writes=[B("p5_u1")])
                    op_tt(g, u2[:, :], pB_, SS2[:, :], ALU.mult,
                          reads=pBB + [B("p5_ss")], writes=[B("p5_u2")])
                    op_tt(g, dst_e[0:64, :], u1[0:64, :], u2[0:64, :], ALU.add,
                          reads=[B("p5_u1"), B("p5_u2")], writes=[deB])
                    op_tt(g, dst_o[0:64, :], u1[64:128, :], u2[64:128, :], ALU.add,
                          reads=[B("p5_u1"), B("p5_u2")], writes=[doB])
                    sqs = sq[:, 0:T2]
                    sqB = B("p5_sq0")
                    op_act(g, sqs, pA, AF.Square, reads=pAB, writes=[sqB])
                    op_mm(g, g.pb(5, 0, 16, T2), hind[:, hc * 16:(hc + 1) * 16], sqs,
                          reads=[sqB, B("c_headind")], writes=g.pbB(5))
                    op_act(g, nrm[:, :], g.pb(5, 0, 16, T2), AF.Sqrt,
                           reads=g.pbB(5), writes=[B("p5_nrm")])
                    if isq:
                        op_ts(g, Ball[:, :], nrm[:, :], mk[:, 1:2], 8.0 * SHIFT_MARGIN,
                              ALU.mult, ALU.add,
                              reads=[B("p5_nrm"), B("p5_mkn")], writes=[B("p5_Ball")])
                    else:
                        op_red(g, mk[:, 2:3], nrm[:, :], ALU.max,
                               reads=[B("p5_nrm")], writes=[B("p5_mk2")])
                        op_tt(g, mk[:, 0:1], mk[:, 0:1], mk[:, 2:3], ALU.max,
                              reads=[B("p5_mk2"), B("p5_mk")], writes=[B("p5_mk")])
                        op_ts(g, mk[:, 1:2], mk[:, 0:1], -1.0, None, ALU.mult,
                              reads=[B("p5_mk")], writes=[B("p5_mkn")])
                op_dma(g, "sync", qa_e[64:65, :], Ball[he:he + 1, :],
                       reads=[B("p5_Ball")], writes=[B("p5_qa0_hi")])
                op_dma(g, "sync", qa_o[64:65, :], Ball[ho:ho + 1, :],
                       reads=[B("p5_Ball")], writes=[B("p5_qa1_hi")])
                if hc == 0:
                    op_memset(g, vt[0][:, :], 1.0, writes=[B("p5_vt0")])
                    op_memset(g, vt[1][:, :], 1.0, writes=[B("p5_vt1")])
                for tb in range(4):
                    pA, pAB = g.pb(6 + tb % 2), g.pbB(6 + tb % 2)
                    items = [(pA[:, 0:128],
                              h2[:, kc * T2 + tb * 128: kc * T2 + (tb + 1) * 128],
                              bass.AP(wp, kc * 384 + 256, [[NCH * 384, 128], [1, 128]]),
                              kc == 0, kc == NCH - 1) for kc in range(NCH)]
                    op_mm_group(g, items, reads=[wpB] + h2B, writes=pAB)
                    op_copy(g, vt[0][:, tb * 65: tb * 65 + 64], pA[:, 0:64],
                            reads=pAB, writes=[B("p5_vt0")])
                    op_copy(g, vt[1][:, tb * 65: tb * 65 + 64], pA[:, 64:128],
                            reads=pAB, writes=[B("p5_vt1")])
                for (h, qa_h, ka_h, vt_h, qaB, kaB, vtB) in (
                        (he, qa_e, ka_e, vt[0], ("p5_qa0_lo", "p5_qa0_hi"),
                         ("p5_ka0_lo", "p5_ka0_hi"), "p5_vt0"),
                        (ho, qa_o, ka_o, vt[1], ("p5_qa1_lo", "p5_qa1_hi"),
                         ("p5_ka1_lo", "p5_ka1_hi"), "p5_vt1")):
                    hr = (h % 2) * 64
                    yb = 4
                    pY = g.pall[0:65, yb * 512:(yb + 1) * 512]
                    pYB = g.pbB(yb)
                    Pt = Pb[0]
                    PtB = B(f"p5_P0")
                    items = []
                    for j in range(4):
                        items.append((g.pall[:, j * 512 + 128 * j: j * 512 + 512],
                                      ka_h[:, j * 128: (j + 1) * 128],
                                      qa_h[:, 128 * j: T2], True, True))
                    op_mm_group(g, items, reads=[B(qaB[0]), B(qaB[1]), B(qaB[0].replace("_lo", "_pad")),
                                                 B(kaB[0]), B(kaB[1]), B(kaB[0].replace("_lo", "_pad"))],
                                writes=g.pbB(0, 3))
                    v = sview(g.pall, 0, [[4096, 128], [640, 4], [1, 128]])
                    m = sview(g.t["c_mask_add"], 0, [[128, 128], [0, 4], [1, 128]])
                    op_tt(g, v, v, m, ALU.add,
                          reads=[B("c_mask_add")] + g.pbB(0, 3), writes=g.pbB(0, 3))
                    for j in range(4):
                        PsB = B(f"{PtB.name}_s{j}")
                        op_act(g, Pt[:, j * 512:(j + 1) * 512],
                               g.pall[:, j * 512:(j + 1) * 512], AF.Exp, scale=0.125,
                               reads=g.pbB(j, j), writes=[PsB])
                        if j > 0:
                            op_memset(g, Pt[:, j * 512: j * 512 + 128 * j], 0.0,
                                      writes=[PsB], reads=[PsB])
                        op_mm_group(g, [(pY, vt_h[:, j * 65: j * 65 + 65],
                                         Pt[:, j * 512: (j + 1) * 512], j == 0, j == 3)],
                                    reads=[PsB, B(vtB)], writes=pYB)
                    op_copy(g, dtmp[0:1, :], g.pall[64:65, yb * 512:(yb + 1) * 512],
                            reads=pYB, writes=[B("p5_dtmp")])
                    op_dma(g, "sync", den_all[h:h + 1, :], dtmp[0:1, :],
                           reads=[B("p5_dtmp")], writes=[B("p5_denall")])
                    op_act(g, y_fm[hr:hr + 64, hc * T2: (hc + 1) * T2],
                           g.pall[0:64, yb * 512:(yb + 1) * 512], AF.Identity,
                           reads=pYB, writes=[B(f"p5_yfm{hc}")])
            op_recip(g, dra[:, :], den_all[:, :], reads=[B("p5_denall")], writes=[B("p5_dra")])
            for hc in range(NCH):
                he, ho = 2 * hc, 2 * hc + 1
                for (h_, p0, p1_) in ((he, 0, 64), (ho, 64, 128)):
                    op_mm(g, g.pall[p0:p1_, 3072:3584], sel16[:, h_ * 64:(h_ + 1) * 64],
                          dra[:, :], reads=[B("p5_dra"), B("c_sel16")], writes=g.pbB(6))
                op_act(g, rbs[:, :], g.pall[:, 3072:3584], AF.Identity,
                       reads=g.pbB(6), writes=[B("p5_rbs")])
                op_tt(g, y_fm[:, hc * T2: (hc + 1) * T2],
                      y_fm[:, hc * T2: (hc + 1) * T2], rbs[:, :], ALU.mult,
                      reads=[B(f"p5_yfm{hc}"), B("p5_rbs")], writes=[B(f"p5_yfm{hc}")])
            op_dma(g, "sync", wgu[:, :], g.t["pk_wo2"][0:128, :],
                   writes=[B("p5_wgu")])
            yfmB = [B(f"p5_yfm{c}") for c in range(NCH)]
            for tb in range(4):
                for ob in range(2):
                    pq = g.pb(ob)
                    pqB = g.pbB(ob)
                    items = [(pq, y_fm[:, kc * T2 + tb * 128: kc * T2 + (tb + 1) * 128],
                              wgu[:, kc * 1024 + ob * 512: kc * 1024 + (ob + 1) * 512],
                              kc == 0, kc == NCH - 1) for kc in range(NCH)]
                    op_mm_group(g, items, reads=[B("p5_wgu")] + yfmB, writes=pqB)
                    op_tt(g, xa2[:, tb * 1024 + ob * 512: tb * 1024 + (ob + 1) * 512],
                          pq, sel[:, tb * 1024 + ob * 512: tb * 1024 + (ob + 1) * 512],
                          ALU.add, reads=pqB + [B("p5_sel")], writes=[B("p5_xa2")])
                xat = xa2[:, tb * 1024:(tb + 1) * 1024]
                op_act(g, hs[:, :], xat, AF.Square, accum=stt_[:, 0:1],
                       reads=[B("p5_xa2")], writes=[B("p5_hs"), B("p5_st0")])
                op_act(g, stt_[:, 1:2], stt_[:, 0:1], AF.Ln, scale=1.0 / 1024.0, bias=1e-6,
                       reads=[B("p5_st0")], writes=[B("p5_st1")])
                op_act(g, stt_[:, 2:3], stt_[:, 1:2], AF.Exp, scale=-0.5,
                       reads=[B("p5_st1")], writes=[B("p5_st2")])
                op_stt(g, hs[:, :], xat, stt_[:, 2:3], ln22b[:, :], ALU.mult, ALU.mult,
                       reads=[B("p5_xa2"), B("p5_st2"), B("p5_ln2")], writes=[B("p5_hs")])
                for c in range(NCH):
                    po = g.pall[:, 7 * 512 + (c % 4) * 128: 7 * 512 + (c % 4 + 1) * 128]
                    op_transpose(g, po, hs[:, c * 128:(c + 1) * 128], ident[:, :],
                                 reads=[B("p5_hs"), B("c_ident")], writes=g.pbB(7))
                    op_copy(g, h2[:, c * T2 + tb * 128: c * T2 + (tb + 1) * 128], po,
                            reads=g.pbB(7), writes=[B(f"p5_h2{c}")])
            wg_h, wu_h, wd_h = g.t["L0_ffn_wg2"], g.t["L0_ffn_wu2"], g.t["L0_ffn_wd2"]
            gatB = [B(f"p5_gated{gc}") for gc in range(32)]
            for blk in range(8):
                wbB = B("p5_wgu")
                op_dma(g, "sync", wgu[:, :],
                       g.t["pk_f2gu"][blk * 128:(blk + 1) * 128, :],
                       writes=[wbB])
                for sub in range(4):
                    gc = blk * 4 + sub
                    pG, pGB = g.pb(0 + sub % 2), g.pbB(0 + sub % 2)
                    pU, pUB = g.pb(2 + sub % 2), g.pbB(2 + sub % 2)
                    items = [(pG[:, 0:T2],
                              bass.AP(wgu, kc * 1024 + sub * 128, [[NCH * 1024, 128], [1, 128]]),
                              h2[:, kc * T2: (kc + 1) * T2],
                              kc == 0, kc == NCH - 1) for kc in range(NCH)]
                    op_mm_group(g, items, reads=[wbB] + h2B, writes=pGB)
                    items = [(pU[:, 0:T2],
                              bass.AP(wgu, kc * 1024 + 512 + sub * 128, [[NCH * 1024, 128], [1, 128]]),
                              h2[:, kc * T2: (kc + 1) * T2],
                              kc == 0, kc == NCH - 1) for kc in range(NCH)]
                    op_mm_group(g, items, reads=[wbB] + h2B, writes=pUB)
                    op_act(g, sg_[:, :], pG[:, 0:T2], AF.Silu, reads=pGB, writes=[B("p5_sg")])
                    op_tt(g, gated[:, gc * T2:(gc + 1) * T2], sg_[:, :], pU[:, 0:T2],
                          ALU.mult, reads=[B("p5_sg")] + pUB, writes=[gatB[gc]])
            for gc in range(32):
                wd_t = wdb[gc % 2]
                wdB = B(f"p5_wd{gc % 2}")
                op_dma(g, "sync", wd_t[:, :], wd_h[gc * 128:(gc + 1) * 128, :], writes=[wdB])
                items = []
                for tb in range(4):
                    for ob in range(2):
                        items.append((g.pb(tb * 2 + ob),
                                      gated[:, gc * T2 + tb * 128: gc * T2 + (tb + 1) * 128],
                                      wd_t[:, ob * 512:(ob + 1) * 512],
                                      gc == 0, gc == 31))
                op_mm_group(g, items, reads=[wdB, gatB[gc]], writes=g.pbB(0, 7))
            for tb in range(4):
                sl = tb % 2
                yt = xin[:, sl * 1024:(sl + 1) * 1024]
                for ob in range(2):
                    op_tt(g, yt[:, ob * 512:(ob + 1) * 512], g.pb(tb * 2 + ob),
                          xa2[:, tb * 1024 + ob * 512: tb * 1024 + (ob + 1) * 512], ALU.add,
                          reads=g.pbB(tb * 2 + ob) + [B("p5_xa2")], writes=[B(f"p5_xin{sl}")])
                op_tt(g, yt[:, :], yt[:, :], sel[:, tb * 1024:(tb + 1) * 1024], ALU.subtract,
                      reads=[B(f"p5_xin{sl}"), B("p5_sel")], writes=[B(f"p5_xin{sl}")])
                op_stt(g, yt[:, :], yt[:, :], sct[:, tb:tb + 1],
                       sel[:, tb * 1024:(tb + 1) * 1024], ALU.mult, ALU.add,
                       reads=[B(f"p5_xin{sl}"), B("p5_sct"), B("p5_sel")],
                       writes=[B(f"p5_xin{sl}")])
                def f_s(e, i=yt,
                        off=bass_rust.IndirectOffsetOnAxis(axis=0, ap=idxt[:, tb:tb + 1]),
                        eo=rb * 1024, breg=breg):
                    return e.indirect_dma_start(out=p2hbm[:, :], out_offset=off, in_=i,
                                                in_offset=None, element_offset=eo,
                                                bounds_check=breg, oob_is_err=False)
                g.s.op("gpsimd", f_s, reads=[B(f"p5_xin{sl}"), B("p5_idxt"), B("p5_p2hbm")],
                       writes=[B("p5_p2hbm")], dma=True)
        g.stack = sv
    g.s.fence()



P678_CONSTS = ["c_ones_col", "c_ones_row", "c_ident", "c_mask_mul", "c_sel16",
               "c_ones_col_f32", "c_ones_row_f32", "c_m8", "c_p8", "c_q8"]


def phase678(g):
    from contextlib import ExitStack
    B = g.s.buf
    x1, p2h = g.t["x1"], g.t["processed2"]
    x2h = g.t["p67_x2"]
    x3h = g.t["p67_x3"]
    outh = g.t["out"]
    ident = g.t["c_ident"]
    mmul = g.t["c_mask_mul"]
    sel16 = g.t["c_sel16"]
    ones = g.t["c_ones_col"]
    onesr = g.t["c_ones_row"]

    with ExitStack() as st:
        sv = g.stack
        g.stack = st
        wti = g.sb("p8_wti", [128, NCH * 1024], BF16)
        rprime = g.sb("p8_rp", [128, NCH * 8], BF16)
        laA = g.sb("p8_laA", [128, NCH * 8], BF16)
        ballB = g.sb("p8_ballB", [64, 1024], BF16)
        m8 = g.t["c_m8"]
        bpcol = g.sb("p8_bpcol", [8, 1], F32)
        brow = g.sb("p8_brow", [8, 2], BF16)
        bbb = load_vec_bcast(g, g.t["L1_ffn_base_b"], "p8_bb")
        _prep = ExitStack()
        _sv0 = g.stack
        g.stack = _prep
        wld = g.sb("p8_wld", [128, 1024], F32)
        for oc in range(NCH):
            op_dma(g, "sync", wld[:, :],
                   g.t["L1_ffn_base_w"][oc * 128:(oc + 1) * 128, :], writes=[B("p8_wld")])
            for kc in range(NCH):
                po = g.pall[:, 7 * 512 + (kc % 4) * 128: 7 * 512 + (kc % 4 + 1) * 128]
                op_transpose(g, po, wld[:, kc * 128:(kc + 1) * 128], ident[:, :],
                             reads=[B("p8_wld"), B("c_ident")], writes=g.pbB(7))
                dst = wti[:, kc * 1024 + oc * 128: kc * 1024 + (oc + 1) * 128]
                if kc == oc:
                    op_tt(g, dst, po, ident[:, :], ALU.add,
                          reads=g.pbB(7) + [B("c_ident")], writes=[B("p8_wti")])
                else:
                    op_copy(g, dst, po, reads=g.pbB(7), writes=[B("p8_wti")])
        rsb = g.sb("p8_rsb", [8, 1024], BF16)
        op_dma(g, "gpsimd", rsb[:, :], g.t["L1_ffn_router_w"][:, :], writes=[B("p8_rsb")])
        op_dma(g, "gpsimd", bpcol[:, 0:1],
               g.t["L1_ffn_router_b"].rearrange("(a b) -> a b", b=1), writes=[B("p8_bp")])
        for kc in range(NCH):
            op_mm(g, g.pb(6, 0, 128, 8), rsb[:, kc * 128:(kc + 1) * 128], m8[:, :],
                  reads=[B("p8_rsb"), B("c_m8")], writes=g.pbB(6))
            op_copy(g, rprime[:, kc * 8:(kc + 1) * 8], g.pb(6, 0, 128, 8),
                    reads=g.pbB(6), writes=[B("p8_rp")])
        op_copy(g, brow[:, 0:1], bpcol[:, 0:1], reads=[B("p8_bp")], writes=[B("p8_brow")])
        op_mm(g, g.pb(6, 0, 1, 8), brow[:, 0:1], m8[:, :],
              reads=[B("p8_brow"), B("c_m8")], writes=g.pbB(6))
        bprow = g.sb("p8_bprow", [1, 8], F32)
        op_copy(g, bprow[0:1, :], g.pb(6, 0, 1, 8), reads=g.pbB(6), writes=[B("p8_bprow")])
        op_transpose(g, g.pb(6, 0, 8, 1), bprow[0:1, :], ident[0:1, 0:1],
                     reads=[B("p8_bprow"), B("c_ident")], writes=g.pbB(6))
        op_copy(g, bpcol[:, 0:1], g.pb(6, 0, 8, 1), reads=g.pbB(6), writes=[B("p8_bp2")])
        op_dma(g, "gpsimd", laA[:, :],
               g.t["L1_ffn_lora_A"].rearrange("(a p) r -> p a r", p=128), writes=[B("p8_laA")])
        op_dma(g, "gpsimd", ballB[:, :],
               g.t["L1_ffn_lora_B"].rearrange("e r o -> (e r) o"), writes=[B("p8_ballB")])
        _prep.close()
        g.stack = _sv0
        g.s.fence()

        sfm = g.sb("p67_sfm", [128, NCH * T], BF16)
        y_fm = g.sb("p67_yfm", [128, NCH * T], BF16)
        xin = g.sb("p67_xin", [128, 2 * 1024], F32)
        xin2 = g.sb("p67_xin2", [128, 1024], F32)
        hs = g.sb("p67_hs", [128, 2 * 1024], F32)
        stt_ = g.sb("p67_stt", [128, 8], F32)
        sc = g.sb("p67_sc", [128, 1024], F32)
        qp = g.sb("p67_qp", [128, T], BF16)
        kp = g.sb("p67_kp", [128, T], BF16)
        ktok = [g.sb("p67_kt0", [128, 16 * 64], BF16), g.sb("p67_kt1", [128, 16 * 64], BF16)]
        vtok = [g.sb("p67_vt0", [128, 16 * 65], BF16), g.sb("p67_vt1", [128, 16 * 65], BF16)]
        Ps = [g.sb("p67_P0", [128, 2048], BF16), g.sb("p67_P1", [128, 2048], BF16)]
        ssb = g.sb("p67_Ssb", [128, 65], BF16)
        Ssb = [ssb[0:64, :], ssb[64:128, :]]
        den_all = g.sb("p67_den", [16, T], BF16)
        den_f32 = g.sb("p67_denf", [16, 512], F32)
        dra = g.sb("p67_dra", [16, T], BF16)
        rbs = g.sb("p67_rbs", [128, 1024], BF16)
        dtmp = g.sb("p67_dtmp", [1, 2 * 512], BF16)
        wpair = g.sb("p67_wp", [128, NCH * 384], BF16)
        u1 = g.sb("p67_u1", [128, 512], F32)
        wq_h, wk_h, wv_h = g.t["L1_attn_Wq"], g.t["L1_attn_Wk"], g.t["L1_attn_Wv"]
        sfmB = [B(f"p67_sfm{c}") for c in range(NCH)]

        for r in range(ROWS):
            rb = r * T
            for tb in range(16):
                sl = tb % 2
                xi = xin[:, sl * 1024:(sl + 1) * 1024]
                x2_ = xin2[:, 0:1024]
                op_dma(g, "sync", xi, x1[rb + tb * 128: rb + (tb + 1) * 128, :],
                       writes=[B(f"p67_xin{sl}")])
                op_dma(g, "sync", x2_, p2h[rb + tb * 128: rb + (tb + 1) * 128, :],
                       writes=[B("p67_xin2")], reads=[B("p5_p2hbm")])
                hst = hs[:, sl * 1024:(sl + 1) * 1024]
                op_tt(g, hst, xi, x2_, ALU.add,
                      reads=[B(f"p67_xin{sl}"), B("p67_xin2")], writes=[B(f"p67_hs{sl}")])
                emit_ln_apply(g, hs[:, sl * 1024:(sl + 1) * 1024], sc, stt_,
                              hs[:, sl * 1024:(sl + 1) * 1024],
                              f"p67_hs{sl}", "p67_sc", "p67_st", f"p67_hs{sl}")
                op_dma(g, "sync", x2h[rb + tb * 128: rb + (tb + 1) * 128, :], hst,
                       reads=[B(f"p67_hs{sl}")], writes=[B("p67_x2hbm")])
                for c in range(NCH):
                    po = g.pall[:, 7 * 512 + (c % 4) * 128: 7 * 512 + (c % 4 + 1) * 128]
                    op_transpose(g, po, hst[:, c * 128:(c + 1) * 128], ident[:, :],
                                 reads=[B(f"p67_hs{sl}"), B("c_ident")], writes=g.pbB(7))
                    op_copy(g, sfm[:, c * T + tb * 128: c * T + (tb + 1) * 128], po,
                            reads=g.pbB(7), writes=[sfmB[c]])
            for hc in range(NCH):
                wpB = B("p67_wp")
                op_dma(g, "sync", wpair[:, :],
                       g.t["pk_qkvL1"][hc * 128:(hc + 1) * 128, :], writes=[wpB])
                he, ho = 2 * hc, 2 * hc + 1
                for (dst, dstB, wcol) in ((qp, B("p67_qp"), 0), (kp, B("p67_kp"), 128)):
                    for tt in range(4):
                        pA, pAB = g.pb(4), g.pbB(4)
                        items = [(pA, bass.AP(wpair, kc * 384 + wcol, [[NCH * 384, 128], [1, 128]]),
                                  sfm[:, kc * T + tt * 512: kc * T + (tt + 1) * 512],
                                  kc == 0, kc == NCH - 1) for kc in range(NCH)]
                        op_mm_group(g, items, reads=[wpB] + sfmB, writes=pAB)
                        op_ts(g, u1[:, :], pA, 0.0, None, ALU.min,
                              reads=pAB, writes=[B("p67_u1")])
                        op_act(g, u1[:, :], u1[:, :], AF.Exp,
                               reads=[B("p67_u1")], writes=[B("p67_u1")])
                        op_stt(g, dst[:, tt * 512:(tt + 1) * 512], pA, 0.0, u1[:, :],
                               ALU.max, ALU.add, reads=pAB + [B("p67_u1")], writes=[dstB])
                if hc == 0:
                    op_memset(g, vtok[0][:, :], 1.0, writes=[B("p67_vt0")])
                    op_memset(g, vtok[1][:, :], 1.0, writes=[B("p67_vt1")])
                for tb in range(16):
                    pA, pAB = g.pb(4), g.pbB(4)
                    items = [(pA[:, 0:128],
                              sfm[:, kc * T + tb * 128: kc * T + (tb + 1) * 128],
                              bass.AP(wpair, kc * 384 + 128, [[NCH * 384, 128], [1, 128]]),
                              kc == 0, kc == NCH - 1) for kc in range(NCH)]
                    op_mm_group(g, items, reads=[wpB] + sfmB, writes=pAB)
                    op_ts(g, u1[:, 0:128], pA[:, 0:128], 0.0, None, ALU.min,
                          reads=pAB, writes=[B("p67_u1")])
                    op_act(g, u1[:, 0:128], u1[:, 0:128], AF.Exp,
                           reads=[B("p67_u1")], writes=[B("p67_u1")])
                    op_stt(g, u1[:, 128:256], pA[:, 0:128], 0.0, u1[:, 0:128],
                           ALU.max, ALU.add, reads=pAB + [B("p67_u1")], writes=[B("p67_u1b")])
                    op_copy(g, ktok[0][:, tb * 64: tb * 64 + 64], u1[:, 128:192],
                            reads=[B("p67_u1b")], writes=[B("p67_kt0")])
                    op_copy(g, ktok[1][:, tb * 64: tb * 64 + 64], u1[:, 192:256],
                            reads=[B("p67_u1b")], writes=[B("p67_kt1")])
                    pV, pVB = g.pb(5), g.pbB(5)
                    items = [(pV[:, 0:128],
                              sfm[:, kc * T + tb * 128: kc * T + (tb + 1) * 128],
                              bass.AP(wpair, kc * 384 + 256, [[NCH * 384, 128], [1, 128]]),
                              kc == 0, kc == NCH - 1) for kc in range(NCH)]
                    op_mm_group(g, items, reads=[wpB] + sfmB, writes=pVB)
                    op_copy(g, vtok[0][:, tb * 65: tb * 65 + 64], pV[:, 0:64],
                            reads=pVB, writes=[B("p67_vt0")])
                    op_copy(g, vtok[1][:, tb * 65: tb * 65 + 64], pV[:, 64:128],
                            reads=pVB, writes=[B("p67_vt1")])
                heads = ((0, he, 0, ktok[0], vtok[0], Ssb[0], B("p67_S0"), 6),
                         (1, ho, 64, ktok[1], vtok[1], Ssb[1], B("p67_S1"), 7))
                seq = [(C, hd) for C in range(4) for hd in heads]

                def emit_A(i):
                    C, (idx_h, h_, hr, kt, vt_, SsbT, SsbB, sbank) = seq[i]
                    qpB, kpB = B("p67_qp"), B("p67_kp")
                    items = []
                    for j in range(4):
                        kti = C * 4 + j
                        items.append((g.pall[:, j * 512 + 128 * j: j * 512 + 512],
                                      kp[hr:hr + 64, kti * 128: (kti + 1) * 128],
                                      qp[hr:hr + 64, C * 512 + 128 * j: (C + 1) * 512],
                                      True, True))
                    op_mm_group(g, items, reads=[kpB, qpB], writes=g.pbB(0, 3))

                def emit_evac(i):
                    Pt = Ps[i % 2]
                    PtB = B(f"p67_P{i % 2}")
                    for j in range(4):
                        op_tt(g, Pt[:, j * 512 + 128 * j: j * 512 + 128 * j + 128],
                              g.pall[:, j * 512 + 128 * j: j * 512 + 128 * j + 128],
                              g.t["c_mask_mul"][:, :], ALU.mult,
                              reads=g.pbB(j, j) + [B("c_mask_mul")], writes=[PtB])
                        if j < 3:
                            op_act(g, Pt[:, j * 512 + 128 * j + 128: (j + 1) * 512],
                                   g.pall[:, j * 512 + 128 * j + 128: (j + 1) * 512],
                                   AF.Identity, reads=g.pbB(j, j), writes=[PtB])
                        if j > 0:
                            op_memset(g, Pt[:, j * 512: j * 512 + 128 * j], 0.0,
                                      writes=[PtB], reads=[PtB])

                def emit_tail(i):
                    C, (idx_h, h_, hr, kt, vt_, SsbT, SsbB, sbank) = seq[i]
                    ktB, vtB = B(f"p67_kt{idx_h}"), B(f"p67_vt{idx_h}")
                    qpB = B("p67_qp")
                    Pt = Ps[i % 2]
                    PtB = B(f"p67_P{i % 2}")
                    yb = 4 + idx_h
                    pY = g.pall[0:65, yb * 512:(yb + 1) * 512]
                    pYB = g.pbB(yb)
                    items = []
                    for j in range(4):
                        kti = C * 4 + j
                        items.append((pY, vt_[:, kti * 65: kti * 65 + 65],
                                      Pt[:, j * 512: (j + 1) * 512], j == 0,
                                      C == 0 and j == 3))
                    if C > 0:
                        items.append((pY, SsbT[:, :],
                                      qp[hr:hr + 64, C * 512: (C + 1) * 512], False, True))
                    op_mm_group(g, items, reads=[PtB, vtB, SsbB, qpB], writes=pYB)
                    items = []
                    for j in range(4):
                        kti = C * 4 + j
                        items.append((g.pall[hr:hr + 64, sbank * 512: sbank * 512 + 65],
                                      kt[:, kti * 64:(kti + 1) * 64],
                                      vt_[:, kti * 65:(kti + 1) * 65],
                                      C == 0 and j == 0, C == 3 and j == 3))
                    op_mm_group(g, items, reads=[ktB, vtB], writes=g.pbB(sbank))
                    op_copy(g, SsbT[:, :],
                            g.pall[hr:hr + 64, sbank * 512: sbank * 512 + 65],
                            reads=g.pbB(sbank), writes=[SsbB])
                    sl = idx_h
                    op_copy(g, dtmp[0:1, sl * 512:(sl + 1) * 512],
                            g.pall[64:65, yb * 512:(yb + 1) * 512],
                            reads=pYB, writes=[B(f"p67_dt{sl}")])
                    op_dma(g, "sync", den_all[h_:h_ + 1, C * 512:(C + 1) * 512],
                           dtmp[0:1, sl * 512:(sl + 1) * 512],
                           reads=[B(f"p67_dt{sl}")], writes=[B("p67_denall")])
                    op_act(g, y_fm[hr:hr + 64, hc * T + C * 512: hc * T + (C + 1) * 512],
                           g.pall[0:64, yb * 512:(yb + 1) * 512], AF.Identity,
                           reads=pYB, writes=[B(f"p67_yfm{hc}")])

                for i in range(len(seq)):
                    emit_A(i)
                    emit_evac(i)
                    if i > 0:
                        emit_tail(i - 1)
                emit_tail(len(seq) - 1)
            for seg4 in range(4):
                op_ts(g, den_f32[:, :], den_all[:, seg4 * 512:(seg4 + 1) * 512], 1e-6, None,
                      ALU.add, reads=[B("p67_denall")], writes=[B("p67_denf")])
                op_recip(g, dra[:, seg4 * 512:(seg4 + 1) * 512], den_f32[:, :],
                         reads=[B("p67_denf")], writes=[B("p67_dra")])
            for hc in range(NCH):
                he, ho = 2 * hc, 2 * hc + 1
                for seg in range(2):
                    for (h_, p0, p1_) in ((he, 0, 64), (ho, 64, 128)):
                        op_mm(g, g.pall[p0:p1_, 3072:3584], sel16[:, h_ * 64:(h_ + 1) * 64],
                              dra[:, seg * 1024: seg * 1024 + 512],
                              reads=[B("p67_dra"), B("c_sel16")], writes=g.pbB(6))
                        op_mm(g, g.pall[p0:p1_, 3584:4096], sel16[:, h_ * 64:(h_ + 1) * 64],
                              dra[:, seg * 1024 + 512: seg * 1024 + 1024],
                              reads=[B("p67_dra"), B("c_sel16")], writes=g.pbB(7))
                    op_act(g, rbs[:, :], g.pall[:, 3072:4096], AF.Identity,
                           reads=g.pbB(6, 7), writes=[B("p67_rbs")])
                    op_tt(g, y_fm[:, hc * T + seg * 1024: hc * T + (seg + 1) * 1024],
                          y_fm[:, hc * T + seg * 1024: hc * T + (seg + 1) * 1024],
                          rbs[:, :], ALU.mult,
                          reads=[B(f"p67_yfm{hc}"), B("p67_rbs")], writes=[B(f"p67_yfm{hc}")])
            if r == 0 and "dbg_yfm" in g.t:
                op_dma(g, "sync", g.t["dbg_yfm"][:, :], y_fm[:, :],
                       reads=[B("p67_yfm0")] + [B(f"p67_yfm{c}") for c in range(NCH)],
                       writes=[B("dbg1")])
                op_dma(g, "sync", g.t["dbg_den"][:, :], den_all[:, :],
                       reads=[B("p67_denall")], writes=[B("dbg2")])
            wo_sb = g.sb("p67_wo", [128, NCH * 1024], BF16) if r == 0 else g.t["p67_wo"]
            if r == 0:
                op_dma(g, "sync", wo_sb[:, :], g.t["pk_WoL1"][0:128, :],
                       writes=[B("p67_wo")])
            yfmB = [B(f"p67_yfm{c}") for c in range(NCH)]
            for tb in range(16):
                sl = tb % 2
                xi = xin[:, sl * 1024:(sl + 1) * 1024]
                op_dma(g, "sync", xi, x2h[rb + tb * 128: rb + (tb + 1) * 128, :],
                       writes=[B(f"p67_xin{sl}")], reads=[B("p67_x2hbm")])
                hst = hs[:, sl * 1024:(sl + 1) * 1024]
                for ob in range(2):
                    pq = g.pb(ob)
                    pqB = g.pbB(ob)
                    items = [(pq, y_fm[:, kc * T + tb * 128: kc * T + (tb + 1) * 128],
                              wo_sb[:, kc * 1024 + ob * 512: kc * 1024 + (ob + 1) * 512],
                              kc == 0, kc == NCH - 1) for kc in range(NCH)]
                    op_mm_group(g, items, reads=[B("p67_wo")] + yfmB, writes=pqB)
                    op_tt(g, hst[:, ob * 512:(ob + 1) * 512], pq, xi[:, ob * 512:(ob + 1) * 512],
                          ALU.add, reads=pqB + [B(f"p67_xin{sl}")], writes=[B(f"p67_hs{sl}")])
                emit_ln_apply(g, hst, sc, stt_, hst,
                              f"p67_hs{sl}", "p67_sc", "p67_st", f"p67_hs{sl}")
                op_dma(g, "sync", x3h[rb + tb * 128: rb + (tb + 1) * 128, :], hst,
                       reads=[B(f"p67_hs{sl}")], writes=[B("p67_x3hbm")])
                for c in range(NCH):
                    po = g.pall[:, 7 * 512 + (c % 4) * 128: 7 * 512 + (c % 4 + 1) * 128]
                    op_transpose(g, po, hst[:, c * 128:(c + 1) * 128], ident[:, :],
                                 reads=[B(f"p67_hs{sl}"), B("c_ident")], writes=g.pbB(7))
                    op_copy(g, sfm[:, c * T + tb * 128: c * T + (tb + 1) * 128], po,
                            reads=g.pbB(7), writes=[sfmB[c]])
            for tt in range(4):
                ts0 = tt * 512
                items = [(g.pb(5, 0, 8), rprime[:, kc * 8:(kc + 1) * 8],
                          sfm[:, kc * T + ts0: kc * T + ts0 + 512],
                          kc == 0, kc == NCH - 1) for kc in range(NCH)]
                op_mm_group(g, items, reads=[B("p8_rp")] + sfmB, writes=g.pbB(5))
                gun = hs[:, 0:512]
                op_ts(g, gun[0:8, :], g.pb(5, 0, 8), bpcol[:, 0:1], None, ALU.add,
                      reads=g.pbB(5) + [B("p8_bp2")], writes=[B("p67_hs0")])
                op_act(g, gun[0:8, :], gun[0:8, :], AF.Exp,
                       reads=[B("p67_hs0")], writes=[B("p67_hs0")])
                op_copy(g, rbs[0:8, 0:512], gun[0:8, :], reads=[B("p67_hs0")], writes=[B("p67_rbs")])
                op_mm(g, g.pb(5, 0, 1), ones[0:8, 0:1], rbs[0:8, 0:512],
                      reads=[B("p67_rbs"), B("c_ones_col")], writes=g.pbB(5))
                op_recip(g, dtmp[0:1, 0:128].bitcast(F32) if False else u1[0:1, 0:512],
                         g.pb(5, 0, 1), reads=g.pbB(5), writes=[B("p67_u1")])
                op_copy(g, rbs[0:1, 512:1024], u1[0:1, 0:512],
                        reads=[B("p67_u1")], writes=[B("p67_rbs")])
                op_mm(g, g.pb(5, 0, 8), onesr[0:1, 0:8], rbs[0:1, 512:1024],
                      reads=[B("p67_rbs"), B("c_ones_row")], writes=g.pbB(5))
                gat = sc[0:8, 0:512]
                op_tt(g, gat, gun[0:8, :], g.pb(5, 0, 8), ALU.mult,
                      reads=[B("p67_hs0")] + g.pbB(5), writes=[B("p67_sc")])
                gatbf = rbs[0:8, 0:512]
                op_copy(g, gatbf, gat, reads=[B("p67_sc")], writes=[B("p67_rbs")])
                items = [(g.pb(5, 0, 8), laA[:, kc * 8:(kc + 1) * 8],
                          sfm[:, kc * T + ts0: kc * T + ts0 + 512],
                          kc == 0, kc == NCH - 1) for kc in range(NCH)]
                op_mm_group(g, items, reads=[B("p8_laA")] + sfmB, writes=g.pbB(5))
                op_copy(g, rbs[0:8, 512:1024], g.pb(5, 0, 8), reads=g.pbB(5), writes=[B("p67_rbs")])
                op_mm(g, g.pb(5, 0, 64), g.t["c_p8"][:, :], rbs[0:8, 512:1024],
                      reads=[B("p67_rbs"), B("c_p8")], writes=g.pbB(5))
                op_mm(g, g.pb(6, 0, 64), g.t["c_q8"][:, :], gatbf,
                      reads=[B("p67_rbs"), B("c_q8")], writes=g.pbB(6))
                op_copy(g, hs[0:64, 0:512], g.pb(5, 0, 64), reads=g.pbB(5), writes=[B("p67_hs0")])
                mall = rbs[0:64, 0:512]
                op_tt(g, mall, hs[0:64, 0:512], g.pb(6, 0, 64), ALU.mult,
                      reads=[B("p67_hs0")] + g.pbB(6), writes=[B("p67_rbs")])
                for tb in range(4):
                    gt_ = tt * 4 + tb
                    sl = tb % 2
                    for ob in range(2):
                        pq = g.pb(sl * 2 + ob)
                        pqB = g.pbB(sl * 2 + ob)
                        items = [(pq, sfm[:, kc * T + gt_ * 128: kc * T + (gt_ + 1) * 128],
                                  wti[:, kc * 1024 + ob * 512: kc * 1024 + (ob + 1) * 512],
                                  kc == 0, False) for kc in range(NCH)]
                        items.append((pq, mall[:, tb * 128:(tb + 1) * 128],
                                      ballB[:, ob * 512:(ob + 1) * 512], False, True))
                        op_mm_group(g, items,
                                    reads=[B("p8_wti"), B("p67_rbs"), B("p8_ballB")] + sfmB,
                                    writes=pqB)
                        op_tt(g, xin[:, sl * 1024 + ob * 512: sl * 1024 + (ob + 1) * 512],
                              pq, bbb[:, ob * 512:(ob + 1) * 512], ALU.add,
                              reads=pqB + [B("p8_bb")], writes=[B(f"p67_xin{sl}")])
                    xt_ = xin[:, sl * 1024:(sl + 1) * 1024]
                    emit_ln_apply(g, xt_, sc, stt_, xt_,
                                  f"p67_xin{sl}", "p67_sc", "p67_st", f"p67_xin{sl}")
                    op_dma(g, "sync", outh[rb + gt_ * 128: rb + (gt_ + 1) * 128, :], xt_,
                           reads=[B(f"p67_xin{sl}")], writes=[B("out_hbm")])
        g.stack = sv
    g.s.fence()


def phase3_allreduce(g):
    def fn(e):
        return e.collective_compute(
            "AllReduce", ALU.add,
            replica_groups=[list(range(NCORES))],
            ins=[g.t["partials"].opt()],
            outs=[g.t["partials_all"].opt()])
    g.s.op("gpsimd", fn, reads=[g.s.buf("p2_parthbm")],
           writes=[g.s.buf("partials_all_hbm")])
    g.s.fence()



W_P1 = ["L0_attn_w_qkv", "L0_attn_ln_w", "L0_attn_w_proj"]
W_P2 = ["L0_ffn_ln1_1", "L0_ffn_wq1", "L0_ffn_wk1", "L0_ffn_wv1", "L0_ffn_wo1",
        "L0_ffn_ln2_1", "L0_ffn_wg1", "L0_ffn_wu1", "L0_ffn_wd1",
        "L0_ffn_prior_norm_w", "L0_ffn_prior_w"]
W_P5 = ["L0_ffn_ln1_2", "L0_ffn_wq2", "L0_ffn_wk2", "L0_ffn_wv2", "L0_ffn_wo2",
        "L0_ffn_ln2_2", "L0_ffn_wg2", "L0_ffn_wu2", "L0_ffn_wd2"]
W_P678 = ["L1_attn_Wq", "L1_attn_Wk", "L1_attn_Wv", "L1_attn_Wo",
          "L1_ffn_base_w", "L1_ffn_base_b", "L1_ffn_lora_A", "L1_ffn_lora_B",
          "L1_ffn_router_w", "L1_ffn_router_b"]


def _run(nc, cin, per_core, shared, outs):
    in_maps = []
    for c in range(NCORES):
        m = dict(shared)
        m.update(cin)
        for k, v in per_core.items():
            m[k] = v[c]
        in_maps.append(m)
    res = run_bass_kernel_spmd(nc, in_maps, core_ids=list(range(NCORES)))
    return {k: [res.results[c][k] for c in range(NCORES)] for k in outs}


def kernel(**inputs):
    x = np.ascontiguousarray(inputs["L0_attn_x"].reshape(NCORES, ROWS * T, D))

    decls = [("in", "x0", [ROWS * T, D], F32), ("out", "x1", [ROWS * T, D], F32)]
    decls += [("in", n, list(inputs[n].shape), F32) for n in W_P1]
    nc, cin = build_program(decls, P1_CONSTS, [phase1])
    o1 = _run(nc, cin, {"x0": x}, {n: inputs[n] for n in W_P1}, ["x1"])

    decls = [("in", "x1", [ROWS * T, D], F32),
             ("out", "processed", [ROWS * T, D], F32),
             ("out", "d_st", [ROWS * 128, 16], F32), ("out", "d_ch", [ROWS * 128, 16], F32),
             ("out", "partials", [ROWS, 2], F32),
             ("tmp", "p2_ml", [ROWS * T, 2 * D], F32), ("tmp", "p2_xa", [ROWS * T, D], F32)]
    decls += [("in", n, list(inputs[n].shape), F32) for n in W_P2]
    nc, cin = build_program(decls, P2_CONSTS, [phase2])
    o2 = _run(nc, cin, {"x1": o1["x1"]}, {n: inputs[n] for n in W_P2},
              ["processed", "d_st", "d_ch", "partials"])

    partials_all = np.ascontiguousarray(np.stack(o2["partials"]).reshape(NCORES * ROWS, 2))

    decls = [("in", "d_st", [ROWS * 128, 16], F32), ("in", "d_ch", [ROWS * 128, 16], F32),
             ("in", "partials_all", [NCORES * ROWS, 2], F32),
             ("out", "idsco", [ROWS * KSEL, 2], F32)]
    nc, cin = build_program(decls, P4_CONSTS, [phase4], dma_scratch=48 * 1024)
    o4 = _run(nc, cin, {"d_st": o2["d_st"], "d_ch": o2["d_ch"]},
              {"partials_all": partials_all}, ["idsco"])

    decls = [("in", "processed", [ROWS * T, D], F32),
             ("in", "idsco", [ROWS * KSEL, 2], F32),
             ("out", "processed2", [ROWS * T, D], F32)]
    decls += [("in", n, list(inputs[n].shape), F32) for n in W_P5]
    nc, cin = build_program(decls, P5_CONSTS, [phase5], dma_scratch=28 * 1024)
    o5 = _run(nc, cin, {"processed": o2["processed"], "idsco": o4["idsco"]},
              {n: inputs[n] for n in W_P5}, ["processed2"])

    decls = [("in", "x1", [ROWS * T, D], F32), ("in", "processed2", [ROWS * T, D], F32),
             ("out", "out", [ROWS * T, D], F32),
             ("tmp", "p67_x2", [ROWS * T, D], F32), ("tmp", "p67_x3", [ROWS * T, D], F32)]
    decls += [("in", n, list(inputs[n].shape), F32) for n in W_P678]
    nc, cin = build_program(decls, P678_CONSTS, [phase678])
    o6 = _run(nc, cin, {"x1": o1["x1"], "processed2": o5["processed2"]},
              {n: inputs[n] for n in W_P678}, ["out"])

    return np.concatenate([o6["out"][c].reshape(ROWS, T, D) for c in range(NCORES)], 0)


ALL_CONSTS = sorted(set(P1_CONSTS + P2_CONSTS + P4_CONSTS + P5_CONSTS + P678_CONSTS))
ALL_W = W_P1 + W_P2 + W_P5 + W_P678
W_SHAPES = {
    "L0_attn_w_qkv": [1024, 3072], "L0_attn_ln_w": [1024], "L0_attn_w_proj": [1024, 1024],
    "L0_ffn_ln1_1": [1024], "L0_ffn_wq1": [1024, 1024], "L0_ffn_wk1": [1024, 1024],
    "L0_ffn_wv1": [1024, 1024], "L0_ffn_wo1": [1024, 1024], "L0_ffn_ln2_1": [1024],
    "L0_ffn_wg1": [1024, 4096], "L0_ffn_wu1": [1024, 4096], "L0_ffn_wd1": [4096, 1024],
    "L0_ffn_prior_norm_w": [1024], "L0_ffn_prior_w": [1024, 2048],
    "L0_ffn_ln1_2": [1024], "L0_ffn_wq2": [1024, 1024], "L0_ffn_wk2": [1024, 1024],
    "L0_ffn_wv2": [1024, 1024], "L0_ffn_wo2": [1024, 1024], "L0_ffn_ln2_2": [1024],
    "L0_ffn_wg2": [1024, 4096], "L0_ffn_wu2": [1024, 4096], "L0_ffn_wd2": [4096, 1024],
    "L1_attn_Wq": [1024, 1024], "L1_attn_Wk": [1024, 1024], "L1_attn_Wv": [1024, 1024],
    "L1_attn_Wo": [1024, 1024],
    "L1_ffn_base_w": [1024, 1024], "L1_ffn_base_b": [1024], "L1_ffn_lora_A": [1024, 8],
    "L1_ffn_lora_B": [8, 8, 1024], "L1_ffn_router_w": [8, 1024], "L1_ffn_router_b": [8],
}


def kernel(**inputs):
    x = np.ascontiguousarray(inputs["L0_attn_x"].reshape(NCORES, ROWS * T, D))
    decls = [("in", "x0", [ROWS * T, D], F32), ("out", "out", [ROWS * T, D], F32),
             ("tmp", "x1", [ROWS * T, D], F32),
             ("tmp", "processed", [ROWS * T, D], F32),
             ("tmp", "processed2", [ROWS * T, D], F32),
             ("tmp", "d_st", [ROWS * 128, 16], F32), ("tmp", "d_ch", [ROWS * 128, 16], F32),
             ("tmp", "partials", [ROWS, 2], F32), ("tmp", "partials_all", [ROWS, 2], F32),
             ("tmp", "idsco", [ROWS * KSEL, 2], F32),
             ("tmp", "p2_ml", [ROWS * T, 2 * D], F32), ("tmp", "p2_xa", [ROWS * T, D], F32),
             ("tmp", "p67_x2", [ROWS * T, D], F32), ("tmp", "p67_x3", [ROWS * T, D], F32)]
    decls += [("in", n, W_SHAPES[n], F32) for n in ALL_W]
    nc, cin = build_program(decls, ALL_CONSTS,
                            [phase0_precast, phase1, phase2, phase3_allreduce,
                             phase4, phase5, phase678],
                            dma_scratch=20 * 1024)
    shared = {n: inputs[n] for n in ALL_W}
    o = _run(nc, cin, {"x0": x}, shared, ["out"])
    return np.concatenate([o["out"][c].reshape(ROWS, T, D) for c in range(NCORES)], 0)


# version 29 (final)
# speedup vs baseline: 1.1925x; 1.0047x over previous
import math
import numpy as np
import concourse.bass as bass
import concourse.mybir as mybir
from concourse.bass_utils import run_bass_kernel_spmd

F32 = mybir.dt.float32
BF16 = mybir.dt.bfloat16
U32 = mybir.dt.uint32
I32 = mybir.dt.int32
AF = mybir.ActivationFunctionType
ALU = mybir.AluOpType

NCORES = 8
ROWS = 4
T = 2048
D = 1024
NCH = 8
H = 16
DH = 64
FFN = 4096
KSEL = 512
SHIFT_MARGIN = 60.0


STREAMS = ("tensor", "scalar", "vector", "gpsimd", "sync")


class Buf:
    __slots__ = ("name", "writer", "readers")

    def __init__(self, name):
        self.name = name
        self.writer = None
        self.readers = []


class Sched:

    DMA_POOL = 24

    def __init__(self, nc, stack):
        self.nc = nc
        self.engines = {"tensor": nc.tensor, "scalar": nc.scalar, "vector": nc.vector,
                        "gpsimd": nc.gpsimd, "sync": nc.sync}
        self.queues = list(STREAMS)
        for s in ("sync", "gpsimd", "scalar"):
            for i in range(self.DMA_POOL):
                self.queues.append(f"dma_{s}_{i}")
        self.sems = {q: stack.enter_context(nc.semaphore("sem_" + q)) for q in self.queues}
        self.cnt = {q: 0 for q in self.queues}
        self.dma_idx = {s: 0 for s in STREAMS}
        self.waited = {s: {} for s in STREAMS}
        self._bufs = {}
        self.fence_base = {}
        self.debug_serialize = False
        self.serialize_same = True

    def fence(self):
        self.fence_base = {q: c for q, c in self.cnt.items() if c > 0}

    def buf(self, name):
        b = self._bufs.get(name)
        if b is None:
            b = Buf(name)
            self._bufs[name] = b
        return b

    def op(self, stream, fn, reads=(), writes=(), deps=(), dma=False):
        if dma:
            i = self.dma_idx[stream]
            self.dma_idx[stream] = (i + 1) % self.DMA_POOL
            q = f"dma_{stream}_{i}"
            inc = 16
        else:
            q = stream
            inc = 1
        self.cnt[q] += inc
        tok = (q, self.cnt[q])
        depmap = {}

        def add(d):
            if d is None:
                return
            dq, dc = d
            if dq == stream and not self.debug_serialize:
                if stream == "tensor" or not self.serialize_same:
                    return
            if dc > depmap.get(dq, 0):
                depmap[dq] = dc

        if dma and self.cnt[q] > 16:
            depmap[q] = self.cnt[q] - 16
        for r in reads:
            add(r.writer)
            if stream in ("vector", "scalar") and r.name.startswith("ps"):
                other = "scalar" if stream == "vector" else "vector"
                for dq, dc in r.readers:
                    if dq == other:
                        add((dq, dc))
        for w in writes:
            add(w.writer)
            for rd in w.readers:
                add(rd)
        for d in deps:
            add(d)
        for fq, fc in self.fence_base.items():
            add((fq, fc))
        eng = self.engines[stream]
        wt = self.waited[stream]
        for dq, dc in sorted(depmap.items()):
            if wt.get(dq, 0) >= dc:
                continue
            eng.wait_ge(self.sems[dq], dc)
            wt[dq] = dc
        try:
            inst = fn(eng)
        except Exception:
            self.cnt[q] -= inc
            raise
        inst.then_inc(self.sems[q], inc)
        for r in reads:
            r.readers.append(tok)
        for w in writes:
            w.writer = tok
            w.readers = []
        if self.debug_serialize:
            self.fence()
        return tok

    def finish(self):
        wt = self.waited["sync"]
        for q in self.queues:
            c = self.cnt[q]
            if q.startswith("dma_") and c > 0 and wt.get(q, 0) < c:
                self.nc.sync.wait_ge(self.sems[q], c)
                wt[q] = c



def make_constants():
    c = {}
    c["c_ones_col"] = np.ones((128, 1), np.float32)
    c["c_ones_col_f32"] = np.ones((128, 1), np.float32)
    c["c_ones_row_f32"] = np.ones((1, 128), np.float32)
    c["c_ones_row"] = np.ones((1, 128), np.float32)
    c["c_ident"] = np.eye(128, dtype=np.float32)
    k = np.arange(128)[:, None]
    q = np.arange(128)[None, :]
    c["c_mask_add"] = np.where(k <= q, 0.0, -1e30).astype(np.float32)
    c["c_mask_mul"] = (k <= q).astype(np.float32)
    c["c_tril"] = (k <= q).astype(np.float32)
    tri = np.zeros((64, 64), np.float32)
    for b in range(4):
        for i in range(16):
            for j in range(16):
                if i < j:
                    tri[b * 16 + i, b * 16 + j] = 1.0
    c["c_triu16"] = tri
    inv = (1.0 / 10000.0 ** (np.arange(32, dtype=np.float64) / 32.0))
    ang = np.arange(T, dtype=np.float64)[None, :] * inv[:, None]
    cosv = np.cos(ang).astype(np.float32)
    sinv = np.sin(ang).astype(np.float32)
    c["c_rope_cs"] = np.concatenate([cosv, cosv, cosv, cosv], 0)
    c["c_rope_ss"] = np.concatenate([-sinv, sinv, -sinv, sinv], 0)
    c["c_invfreq4"] = np.concatenate([inv, inv, inv, inv]).astype(np.float32)[None, :]
    c["c_invfreq4_f32"] = c["c_invfreq4"].copy()
    c["c_ropesign"] = np.concatenate([-np.ones(32), np.ones(32), -np.ones(32), np.ones(32)]).astype(np.float32)[:, None]
    ind = np.zeros((1024, 16), np.float32)
    for d_ in range(1024):
        ind[d_, d_ // 64] = 1.0
    c["c_headind"] = ind.reshape(8, 128, 16).transpose(1, 0, 2).reshape(128, 128).copy()
    c["c_iota_row128"] = np.broadcast_to(np.arange(128, dtype=np.float32)[None, :],
                                          (128, 128)).copy()
    it = (np.arange(16)[None, :] * 128 + np.arange(128)[:, None]).astype(np.float32)
    c["c_iota_f"] = it
    c["c_iota_u"] = it.astype(np.uint32)
    c["c_m8"] = (np.eye(8) - 1.0 / 8.0).astype(np.float32)
    p8 = np.zeros((8, 64), np.float32)
    q8 = np.zeros((8, 64), np.float32)
    for e_ in range(8):
        for r_ in range(8):
            p8[r_, e_ * 8 + r_] = 2.0
            q8[e_, e_ * 8 + r_] = 1.0
    c["c_p8"] = p8
    c["c_q8"] = q8
    cs = np.zeros((128, 16 * 16), np.float32)
    for h_ in range(16):
        cs[:, h_ * 16 + h_] = 1.0
    c["c_colsel"] = cs
    sel = np.zeros((16, 16 * 64), np.float32)
    for h_ in range(16):
        sel[h_, h_ * 64:(h_ + 1) * 64] = 1.0
    c["c_sel16"] = sel
    return c


CONSTS = make_constants()

CONST_DT = {
    "c_ones_col": BF16, "c_ones_row": BF16, "c_ident": F32,
    "c_mask_add": F32, "c_mask_mul": F32, "c_tril": BF16, "c_triu16": BF16,
    "c_rope_cs": BF16, "c_rope_ss": BF16, "c_invfreq4": BF16, "c_invfreq4_f32": F32, "c_ropesign": F32,
    "c_headind": BF16, "c_iota_f": F32, "c_iota_u": U32, "c_iota_row128": F32, "c_sel16": BF16, "c_colsel": BF16, "c_ones_col_f32": F32, "c_ones_row_f32": F32,
    "c_m8": BF16, "c_p8": BF16, "c_q8": BF16,
}


class G:
    def __init__(self, nc, s):
        self.nc = nc
        self.s = s
        self.t = {}
        self.stack = None
        self._used_names = set()

    def dram_in(self, name, shape, dtype=F32):
        self.t[name] = self.nc.dram_tensor(name, list(shape), dtype, kind="ExternalInput").ap()
        return self.t[name]

    def dram_out(self, name, shape, dtype=F32):
        self.t[name] = self.nc.dram_tensor(name, list(shape), dtype, kind="ExternalOutput").ap()
        return self.t[name]

    def dram_tmp(self, name, shape, dtype=F32):
        self.t[name] = self.nc.dram_tensor(name, list(shape), dtype).ap()
        return self.t[name]

    _uid = [0]

    def sb(self, name, shape, dtype):
        nm = name
        if name in self._used_names:
            G._uid[0] += 1
            nm = f"{name}__{G._uid[0]}"
        self._used_names.add(name)
        ap = self.stack.enter_context(self.nc.sbuf_tensor(nm, list(shape), dtype))
        self.t[name] = ap
        return ap

    def ps_pool(self):
        self.pall = self.stack.enter_context(self.nc.psum_tensor("pall", [128, 4096], F32))
        return self.pall

    def pb(self, k, p0=0, p1=128, n=512):
        return self.pall[p0:p1, k * 512: k * 512 + n]

    def pbB(self, k0, k1=None):
        if k1 is None:
            k1 = k0
        return [self.s.buf(f"ps{k}") for k in range(k0, k1 + 1)]

    def load_consts(self, names):
        self.const_names = list(names)
        for cname in names:
            arr = CONSTS[cname]
            dt = CONST_DT[cname]
            self.dram_in(cname + "_hbm", arr.shape, U32 if arr.dtype == np.uint32 else F32)
            tile = self.sb(cname, arr.shape, dt)
            hbm = self.t[cname + "_hbm"]
            self.s.op("gpsimd",
                      lambda g, tile=tile, hbm=hbm: g.dma_start(out=tile[:, :], in_=hbm[:, :]),
                      writes=[self.s.buf(cname)], dma=True)

    def const_inputs(self):
        return {cname + "_hbm": CONSTS[cname] for cname in self.const_names}

    def fconst(self, val):
        name = "c_f%r" % (val,)
        if name not in self.t:
            ap = self.top_stack.enter_context(
                self.nc.sbuf_tensor(name, [128, 1], F32, side="right"))
            self.t[name] = ap
            self.s.op("gpsimd", lambda e, a=ap, v=float(val): e.memset(a[:, :], v),
                      writes=[self.s.buf("c_fconsts")])
        return self.t[name]



def op_dma(g, stream, out_ap, in_ap, reads=(), writes=(), deps=()):
    return g.s.op(stream, lambda e, o=out_ap, i=in_ap: e.dma_start(out=o, in_=i),
                  reads=reads, writes=writes, deps=deps, dma=True)


def op_mm(g, out_ap, lhsT, rhs, reads=(), writes=(), start=True, stop=True):
    return g.s.op("tensor",
                  lambda e, o=out_ap, l=lhsT, r=rhs, st=start, sp=stop:
                      e.matmul(o, l, r, start=st, stop=sp),
                  reads=reads, writes=writes)


def op_mm_group(g, items, reads=(), writes=()):
    def fn(e, items=tuple(items)):
        inst = None
        for (o, l, r, st, sp) in items:
            inst = e.matmul(o, l, r, start=st, stop=sp)
        return inst
    return g.s.op("tensor", fn, reads=reads, writes=writes)


def op_act(g, out_ap, in_ap, func, reads=(), writes=(), scale=None, bias=None,
           accum=None):
    kw = {}
    reads = list(reads)
    if scale is not None:
        kw["scale"] = scale
    if bias is not None:
        if isinstance(bias, (int, float)):
            bias = g.fconst(bias)[0:in_ap.shape[0], 0:1]
            reads.append(g.s.buf("c_fconsts"))
        kw["bias"] = bias
    if accum is not None:
        kw["accum_out"] = accum
    return g.s.op("scalar",
                  lambda e, o=out_ap, i=in_ap, f=func, kw=kw:
                      e.activation(out=o, in_=i, func=f, **kw),
                  reads=reads, writes=writes)


def op_tt(g, out_ap, a, b, op, reads=(), writes=(), stream="vector"):
    return g.s.op(stream,
                  lambda e, o=out_ap, x=a, y=b, p=op: e.tensor_tensor(out=o, in0=x, in1=y, op=p),
                  reads=reads, writes=writes)


def op_ts(g, out_ap, a, s1, s2, op0, op1=None, reads=(), writes=(), accum=None,
          stream="vector"):
    kw = {"accum_out": accum} if accum is not None else {}
    if op1 is None:
        return g.s.op(stream,
                      lambda e, o=out_ap, x=a, v1=s1, p0=op0, kw=kw:
                          e.tensor_scalar(o, x, v1, None, op0=p0, **kw),
                      reads=reads, writes=writes)
    return g.s.op(stream,
                  lambda e, o=out_ap, x=a, v1=s1, v2=s2, p0=op0, p1=op1, kw=kw:
                      e.tensor_scalar(o, x, v1, v2, op0=p0, op1=p1, **kw),
                  reads=reads, writes=writes)


def op_stt(g, out_ap, in0, scalar, in1, op0, op1, reads=(), writes=()):
    return g.s.op("vector",
                  lambda e, o=out_ap, a=in0, sc=scalar, b=in1, p0=op0, p1=op1:
                      e.scalar_tensor_tensor(out=o, in0=a, scalar=sc, in1=b, op0=p0, op1=p1),
                  reads=reads, writes=writes)


def op_copy(g, out_ap, in_ap, reads=(), writes=(), stream="vector"):
    return g.s.op(stream,
                  lambda e, o=out_ap, i=in_ap: e.tensor_copy(out=o, in_=i) if hasattr(e, "tensor_copy") else e.copy(out=o, in_=i),
                  reads=reads, writes=writes)


def op_memset(g, ap, val, writes=(), stream="vector", reads=()):
    return g.s.op(stream, lambda e, a=ap, v=val: e.memset(a, v),
                  reads=reads, writes=writes)


def op_red(g, out_ap, in_ap, op, reads=(), writes=(), negate=False):
    import concourse.mybir as _mb
    return g.s.op("vector",
                  lambda e, o=out_ap, i=in_ap, p=op, n=negate:
                      e.tensor_reduce(o, i, axis=_mb.AxisListType.X, op=p, negate=n),
                  reads=reads, writes=writes)


def op_recip(g, out_ap, in_ap, reads=(), writes=()):
    def fn(e, o=out_ap, i=in_ap, nc=g.nc):
        with nc.allow_low_precision(reason="recip feeds bf16 matmul operand"):
            return e.reciprocal(o, i)
    return g.s.op("vector", fn, reads=reads, writes=writes)


def op_transpose(g, out_ap, in_ap, ident, reads=(), writes=()):
    return g.s.op("tensor",
                  lambda e, o=out_ap, i=in_ap, d=ident: e.transpose(o, i, d),
                  reads=reads, writes=writes)


def sview(handle, offset, dims):
    return bass.AP(handle, offset, [list(d) for d in dims])


def load_vec_pcol(g, hbm_ap, name, bank=6, nch=NCH):
    B = g.s.buf
    stg = g.sb(name + "_stg", [nch, 128], F32)
    tile = g.sb(name, [128, nch], F32)
    pt = g.pb(bank, 0, 128, nch)
    op_dma(g, "gpsimd", stg[:, :], hbm_ap.rearrange("(a p) -> a p", p=128),
           writes=[B(name + "_stg")])
    op_transpose(g, pt, stg[:, :], g.t["c_ident"][0:nch, 0:nch],
                 reads=[B(name + "_stg"), B("c_ident")], writes=g.pbB(bank))
    op_copy(g, tile[:, :], pt, reads=g.pbB(bank), writes=[B(name)])
    return tile


def load_w_fm(g, wsb, wsb_buf, hbm_ap, col0, col1, stream="gpsimd"):
    n = col1 - col0
    return op_dma(g, stream, wsb[:, 0:NCH * n],
                  hbm_ap[:, col0:col1].rearrange("(a p) n -> p a n", p=128),
                  writes=[wsb_buf])







W_BIG = ["L0_attn_w_qkv", "L0_attn_w_proj",
         "L0_ffn_wq1", "L0_ffn_wk1", "L0_ffn_wv1", "L0_ffn_wo1",
         "L0_ffn_wg1", "L0_ffn_wu1", "L0_ffn_wd1", "L0_ffn_prior_w",
         "L0_ffn_wq2", "L0_ffn_wk2", "L0_ffn_wv2", "L0_ffn_wo2",
         "L0_ffn_wg2", "L0_ffn_wu2", "L0_ffn_wd2",
         "L1_attn_Wq", "L1_attn_Wk", "L1_attn_Wv", "L1_attn_Wo"]


def phase0_precast(g):
    from contextlib import ExitStack
    B = g.s.buf
    with ExitStack() as st:
        sv = g.stack
        g.stack = st
        tiles = [g.sb(f"p0_t{i}", [128, 4096], BF16) for i in range(4)]
        ti = [0]

        def cast_load(src2d):
            i = ti[0] % 4
            ti[0] += 1
            t, b = tiles[i], B(f"p0_t{i}")
            op_dma(g, "gpsimd", t[:, 0:src2d.shape[1]], src2d, writes=[b])
            return t, b

        def pack_full(dst_name, srcs):
            ncols = srcs[0][2]
            A = srcs[0][0].shape[0] // 128
            dst = g.dram_tmp(dst_name, [len(srcs) * 128, A * ncols], BF16)
            for t_i, (w, c0, nc_) in enumerate(srcs):
                for a in range(A):
                    t, b = cast_load(w[a * 128:(a + 1) * 128, c0:c0 + nc_])
                    op_dma(g, "sync", dst[t_i * 128:(t_i + 1) * 128, a * nc_:(a + 1) * nc_],
                           t[:, 0:nc_], reads=[b])
            return dst

        def pack_gu(dst_name, wg, wu):
            dst = g.dram_tmp(dst_name, [8 * 128, 8 * 1024], BF16)
            for a in range(8):
                tg, bg = cast_load(wg[a * 128:(a + 1) * 128, :])
                tu, bu = cast_load(wu[a * 128:(a + 1) * 128, :])
                for blk in range(8):
                    op_dma(g, "sync",
                           dst[blk * 128:(blk + 1) * 128, a * 1024:a * 1024 + 512],
                           tg[:, blk * 512:(blk + 1) * 512], reads=[bg])
                    op_dma(g, "sync",
                           dst[blk * 128:(blk + 1) * 128, a * 1024 + 512:(a + 1) * 1024],
                           tu[:, blk * 512:(blk + 1) * 512], reads=[bu])
            return dst

        def pack_pairs(dst_name, wq, wk, wv):
            dst = g.dram_tmp(dst_name, [8 * 128, 8 * 384], BF16)
            for a in range(8):
                parts = [cast_load(w[a * 128:(a + 1) * 128, :]) for w in (wq, wk, wv)]
                for hc in range(8):
                    for j, (t, b) in enumerate(parts):
                        op_dma(g, "sync",
                               dst[hc * 128:(hc + 1) * 128,
                                   a * 384 + j * 128:a * 384 + (j + 1) * 128],
                               t[:, hc * 128:(hc + 1) * 128], reads=[b])
            return dst

        def copy_rowmajor(name):
            w = g.t[name]
            R, C = w.shape
            bf = g.dram_tmp("bf_" + name, [R, C], BF16)
            for r0 in range(0, R, 128):
                t, b = cast_load(w[r0:r0 + 128, :])
                op_dma(g, "sync", bf[r0:r0 + 128, :], t[:, 0:C], reads=[b])
            g.t[name] = bf

        has = lambda *ns: all(n in g.t for n in ns)
        if has("L0_attn_w_qkv"):
            qkv = g.t["L0_attn_w_qkv"]
            pack_full("pk_qkv", [(qkv, 1024, 1024), (qkv, 0, 1024), (qkv, 2048, 1024)])
            pack_full("pk_proj", [(g.t["L0_attn_w_proj"], 0, 1024)])
        if has("L0_ffn_prior_w"):
            pack_full("pk_prior", [(g.t["L0_ffn_prior_w"], ob * 512, 512) for ob in range(4)])
            pack_full("pk_f1qkv", [(g.t["L0_ffn_wq1"], 0, 1024),
                                   (g.t["L0_ffn_wk1"], 0, 1024),
                                   (g.t["L0_ffn_wv1"], 0, 1024)])
            pack_full("pk_wo1", [(g.t["L0_ffn_wo1"], 0, 1024)])
            pack_gu("pk_f1gu", g.t["L0_ffn_wg1"], g.t["L0_ffn_wu1"])
            copy_rowmajor("L0_ffn_wd1")
        if has("L0_ffn_wq2"):
            pack_pairs("pk_qkv2", g.t["L0_ffn_wq2"], g.t["L0_ffn_wk2"], g.t["L0_ffn_wv2"])
            pack_full("pk_wo2", [(g.t["L0_ffn_wo2"], 0, 1024)])
            pack_gu("pk_f2gu", g.t["L0_ffn_wg2"], g.t["L0_ffn_wu2"])
            copy_rowmajor("L0_ffn_wd2")
        if has("L1_attn_Wq"):
            pack_pairs("pk_qkvL1", g.t["L1_attn_Wq"], g.t["L1_attn_Wk"], g.t["L1_attn_Wv"])
            pack_full("pk_WoL1", [(g.t["L1_attn_Wo"], 0, 1024)])
        g.stack = sv
    g.s.fence()


P1_CONSTS = ["c_ones_col", "c_ones_row", "c_ident", "c_mask_add", "c_headind",
             "c_sel16", "c_colsel"]
TH = 1024


def phase1(g):
    from contextlib import ExitStack
    B = g.s.buf
    x0, x1 = g.t["x0"], g.t["x1"]
    wq_hbm = g.t["L0_attn_w_qkv"]
    wp_hbm = g.t["L0_attn_w_proj"]

    with ExitStack() as st:
        sv = g.stack
        g.stack = st
        x_fm = g.sb("p1_xfm", [128, NCH * TH], BF16)
        xin = g.sb("p1_xin", [128, 2 * 1024], F32)
        wsb = [g.sb("p1_w0", [128, NCH * 1024], BF16)]
        wsb.append(wsb[0])
        qn = g.sb("p1_qn", [128, NCH * TH], BF16)
        kn = g.sb("p1_kn", [128, NCH * TH], BF16)
        vt = g.sb("p1_vt", [128, H * 8 * 65], BF16)
        sq = g.sb("p1_sq", [128, 2 * 512], BF16)
        rrow = g.sb("p1_rrow", [1, 2 * 512], F32)
        rbf = g.sb("p1_rbf", [1, 2 * 512], BF16)
        nrm = g.sb("p1_nrm", [16, 512], F32)
        Ball = g.sb("p1_Ball", [16, TH], BF16)
        mk = g.sb("p1_mk", [16, 4], F32)
        qa = g.sb("p1_qa", [128, TH], BF16)
        ka = g.sb("p1_ka", [128, TH], BF16)
        op_memset(g, qa[64:128, :], 0.0, writes=[B("p1_qa_pad")])
        op_memset(g, ka[64:128, :], 0.0, writes=[B("p1_ka_pad")])
        y_fm = g.sb("p1_yfm", [128, NCH * TH], BF16)
        Pb = [g.sb("p1_P0", [128, 2048], BF16), g.sb("p1_P1", [128, 2048], BF16)]
        den_all = g.sb("p1_den_all", [16, 1024], F32)
        dra = g.sb("p1_dra", [16, 1024], BF16)
        rbs = g.sb("p1_rbs", [128, 1024], BF16)
        xs = g.sb("p1_xs", [128, 1024], F32)
        sc = g.sb("p1_sc", [128, 1024], F32)
        stt_ = g.sb("p1_st", [128, 8], F32)
        x1t = g.sb("p1_x1t", [128, 2 * 1024], F32)
        lnw2 = g.sb("p1_lnw2", [128, NCH], F32)
        ewt = g.sb("p1_ewt", [128, NCH * 16], BF16)
        ones_bf = g.t["c_ones_col"]
        ones_row = g.t["c_ones_row"]
        ident = g.t["c_ident"]
        mask_add = g.t["c_mask_add"]
        hind = g.t["c_headind"]
        sel16 = g.t["c_sel16"]
        colsel = g.t["c_colsel"]

        lnw = load_vec_pcol(g, g.t["L0_attn_ln_w"], "p1_lnw")
        op_act(g, lnw2[:, :], lnw[:, :], AF.Square, reads=[B("p1_lnw")], writes=[B("p1_lnw2")])
        for oc in range(NCH):
            op_ts(g, ewt[:, oc * 16:(oc + 1) * 16], hind[:, oc * 16:(oc + 1) * 16],
                  lnw2[:, oc:oc + 1], None, ALU.mult,
                  reads=[B("c_headind"), B("p1_lnw2")], writes=[B("p1_ewt")])
        xfmB = [B(f"p1_xfm{c}") for c in range(NCH)]

        for r in range(ROWS):
            for hf in range(2):
                t0 = r * T + hf * TH
                for tb in range(8):
                    sl = tb % 2
                    xi = xin[:, sl * 1024:(sl + 1) * 1024]
                    op_dma(g, "sync", xi, x0[t0 + tb * 128: t0 + (tb + 1) * 128, :],
                           writes=[B(f"p1_xin{sl}")])
                    for c in range(NCH):
                        po = g.pall[:, 7 * 512 + (c % 4) * 128: 7 * 512 + (c % 4 + 1) * 128]
                        op_transpose(g, po, xi[:, c * 128:(c + 1) * 128], ident[:, :],
                                     reads=[B(f"p1_xin{sl}"), B("c_ident")], writes=g.pbB(7))
                        op_copy(g, x_fm[:, c * TH + tb * 128: c * TH + (tb + 1) * 128], po,
                                reads=g.pbB(7), writes=[xfmB[c]])
                op_memset(g, vt[:, :], 1.0, writes=[B("p1_vt")])
                op_memset(g, ka[64:65, :], 1.0, writes=[B("p1_ka_hi")])
                op_memset(g, mk[:, 0:1], 0.0, writes=[B("p1_mk")])

                for (tens, tname, wti, is_q) in ((kn, "p1_kn", 0, False),
                                                 (qn, "p1_qn", 1, True)):
                    w = wsb[0]
                    wB = B("p1_w0")
                    op_dma(g, "sync", w[:, :],
                           g.t["pk_qkv"][wti * 128:(wti + 1) * 128, :],
                           writes=[wB])
                    for tt in range(2):
                        for oc in range(NCH):
                            pq = g.pb(oc % 2)
                            pqB = g.pbB(oc % 2)
                            items = [(pq, w[:, kc * 1024 + oc * 128: kc * 1024 + (oc + 1) * 128],
                                      x_fm[:, kc * TH + tt * 512: kc * TH + (tt + 1) * 512],
                                      kc == 0, kc == NCH - 1) for kc in range(NCH)]
                            op_mm_group(g, items, reads=[wB] + xfmB, writes=pqB)
                            op_act(g, tens[:, oc * TH + tt * 512: oc * TH + (tt + 1) * 512], pq,
                                   AF.Identity, reads=pqB, writes=[B(f"{tname}{oc}")])
                            sqs = sq[:, (oc % 2) * 512:((oc % 2) + 1) * 512]
                            sqB = B(f"p1_sq{oc % 2}")
                            op_act(g, sqs, pq, AF.Square, reads=pqB, writes=[sqB])
                            op_mm(g, g.pb(2, 0, 1), ones_bf[:, :], sqs,
                                  reads=[sqB, B("c_ones_col")], writes=g.pbB(2),
                                  start=(oc == 0), stop=(oc == NCH - 1))
                            op_mm(g, g.pb(3, 0, 16), ewt[:, oc * 16:(oc + 1) * 16], sqs,
                                  reads=[sqB, B("p1_ewt")], writes=g.pbB(3),
                                  start=(oc == 0), stop=(oc == NCH - 1))
                        sl = tt
                        rr = rrow[0:1, sl * 512:(sl + 1) * 512]
                        rf = rbf[0:1, sl * 512:(sl + 1) * 512]
                        op_act(g, rr, g.pb(2, 0, 1), AF.Ln, scale=1.0 / 1024.0, bias=1e-6,
                               reads=g.pbB(2), writes=[B(f"p1_rr{sl}")])
                        op_act(g, rr, rr, AF.Exp, scale=-0.5,
                               reads=[B(f"p1_rr{sl}")], writes=[B(f"p1_rr{sl}")])
                        op_copy(g, rf, rr, reads=[B(f"p1_rr{sl}")], writes=[B(f"p1_rbf{sl}")])
                        op_act(g, nrm[:, :], g.pb(3, 0, 16), AF.Sqrt, reads=g.pbB(3),
                               writes=[B("p1_nrm")])
                        op_mm(g, g.pb(6, 0, 16), ones_row[0:1, 0:16], rf,
                              reads=[B(f"p1_rbf{sl}"), B("c_ones_row")], writes=g.pbB(6))
                        if is_q:
                            op_tt(g, nrm[:, :], nrm[:, :], g.pb(6, 0, 16), ALU.mult,
                                  reads=[B("p1_nrm")] + g.pbB(6), writes=[B("p1_nrm")])
                            op_ts(g, Ball[:, tt * 512:(tt + 1) * 512], nrm[:, :],
                                  mk[:, 1:2], 8.0 * SHIFT_MARGIN, ALU.mult, ALU.add,
                                  reads=[B("p1_nrm"), B("p1_mkn")], writes=[B("p1_Ball")])
                        else:
                            op_tt(g, nrm[:, :], nrm[:, :], g.pb(6, 0, 16), ALU.mult,
                                  reads=[B("p1_nrm")] + g.pbB(6), writes=[B("p1_nrm")])
                            op_red(g, mk[:, 2:3], nrm[:, :], ALU.max,
                                   reads=[B("p1_nrm")], writes=[B("p1_mk2")])
                            op_tt(g, mk[:, 0:1], mk[:, 0:1], mk[:, 2:3], ALU.max,
                                  reads=[B("p1_mk2"), B("p1_mk")], writes=[B("p1_mk")])
                        op_mm(g, g.pb(6), ones_row[0:1, :], rf,
                              reads=[B(f"p1_rbf{sl}"), B("c_ones_row")], writes=g.pbB(6))
                        for oc in range(NCH):
                            slc = tens[:, oc * TH + tt * 512: oc * TH + (tt + 1) * 512]
                            op_stt(g, slc, slc, lnw[:, oc:oc + 1], g.pb(6), ALU.mult, ALU.mult,
                                   reads=[B(f"{tname}{oc}"), B("p1_lnw")] + g.pbB(6),
                                   writes=[B(f"{tname}{oc}")])
                    if not is_q:
                        op_ts(g, mk[:, 1:2], mk[:, 0:1], -1.0, None, ALU.mult,
                              reads=[B("p1_mk")], writes=[B("p1_mkn")])
                op_dma(g, "sync", wsb[0][:, :],
                       g.t["pk_qkv"][2 * 128:3 * 128, :],
                       writes=[B("p1_w0")])
                for tb in range(8):
                    for vb in range(2):
                        pq = g.pb((tb * 2 + vb) % 2)
                        pqB = g.pbB((tb * 2 + vb) % 2)
                        items = [(pq, x_fm[:, kc * TH + tb * 128: kc * TH + (tb + 1) * 128],
                                  wsb[0][:, kc * 1024 + vb * 512: kc * 1024 + (vb + 1) * 512],
                                  kc == 0, kc == NCH - 1) for kc in range(NCH)]
                        op_mm_group(g, items, reads=[B("p1_w0")] + xfmB, writes=pqB)
                        for j in range(8):
                            h = vb * 8 + j
                            op_copy(g, vt[:, (h * 8 + tb) * 65: (h * 8 + tb) * 65 + 64],
                                    pq[:, j * 64:(j + 1) * 64], reads=pqB, writes=[B("p1_vt")])

                def attn_part1(h):
                    hc, hr = h // 2, (h % 2) * 64
                    op_copy(g, qa[0:64, :], qn[hr:hr + 64, hc * TH:(hc + 1) * TH],
                            reads=[B(f"p1_qn{hc}")], writes=[B("p1_qa_lo")])
                    op_dma(g, "sync", qa[64:65, :], Ball[h:h + 1, :],
                           reads=[B("p1_Ball")], writes=[B("p1_qa_hi")])
                    op_copy(g, ka[0:64, :], kn[hr:hr + 64, hc * TH:(hc + 1) * TH],
                            reads=[B(f"p1_kn{hc}")], writes=[B("p1_ka_lo")])
                    items = []
                    for wl in range(4):
                        w0 = wl * 256
                        items.append((g.pall[:, wl * 512: wl * 512 + 256],
                                      ka[:, w0: w0 + 128], qa[:, w0: w0 + 256], True, True))
                        items.append((g.pall[:, wl * 512 + 256: wl * 512 + 384],
                                      ka[:, w0 + 128: w0 + 256], qa[:, w0 + 128: w0 + 256],
                                      True, True))
                    op_mm_group(g, items, reads=[B("p1_qa_lo"), B("p1_qa_hi"), B("p1_qa_pad"),
                                                 B("p1_ka_lo"), B("p1_ka_hi"), B("p1_ka_pad")],
                                writes=g.pbB(0, 3))
                    for off in (0, 256):
                        v = sview(g.pall, off, [[4096, 128], [512, 4], [1, 128]])
                        m = sview(g.t["c_mask_add"], 0, [[128, 128], [0, 4], [1, 128]])
                        op_tt(g, v, v, m, ALU.add,
                              reads=[B("c_mask_add")] + g.pbB(0, 3), writes=g.pbB(0, 3))
                    P = Pb[h % 2]
                    op_act(g, P[:, :], g.pall[:, 0:2048], AF.Exp, scale=0.125,
                           reads=g.pbB(0, 3), writes=[B(f"p1_P{h % 2}")])

                def attn_part2(h):
                    hc, hr = h // 2, (h % 2) * 64
                    P = Pb[h % 2]
                    PB = B(f"p1_P{h % 2}")
                    items = []
                    for wl in range(4):
                        base = (h * 8 + wl * 2) * 65
                        items.append((g.pall[0:65, 2048 + wl * 256: 2048 + wl * 256 + 256],
                                      vt[:, base: base + 65],
                                      P[:, wl * 512: wl * 512 + 256], True, False))
                        items.append((g.pall[0:65, 2048 + wl * 256 + 128: 2048 + wl * 256 + 256],
                                      vt[:, base + 65: base + 130],
                                      P[:, wl * 512 + 256: wl * 512 + 384], False, True))
                    op_mm_group(g, items, reads=[PB, B("p1_vt")], writes=g.pbB(4, 5))
                    items = []
                    for wl in range(4):
                        items.append((g.pall[0:16, 3072 + wl * 256: 3072 + wl * 256 + 256],
                                      colsel[:, h * 16:(h + 1) * 16],
                                      P[:, wl * 512: wl * 512 + 256], h == 0 and wl % 2 == 0, False))
                        items.append((g.pall[0:16, 3072 + wl * 256 + 128: 3072 + wl * 256 + 256],
                                      colsel[:, h * 16:(h + 1) * 16],
                                      P[:, wl * 512 + 256: wl * 512 + 384], False, False))
                    op_mm_group(g, items, reads=[PB, B("c_colsel")], writes=g.pbB(6, 7))
                    op_act(g, y_fm[hr:hr + 64, hc * TH: (hc + 1) * TH],
                           g.pall[0:64, 2048:3072], AF.Identity,
                           reads=g.pbB(4, 5), writes=[B(f"p1_yfm{hc}")])

                prev = None
                for h in range(H):
                    attn_part1(h)
                    if prev is not None:
                        attn_part2(prev)
                    prev = h
                attn_part2(prev)
                op_act(g, den_all[:, :], g.pall[0:16, 3072:4096], AF.Identity,
                       reads=g.pbB(6, 7), writes=[B("p1_den")])
                op_recip(g, dra[:, :], den_all[:, :], reads=[B("p1_den")], writes=[B("p1_dra")])
                for hc in range(NCH):
                    he, ho = 2 * hc, 2 * hc + 1
                    for (h_, p0, p1_) in ((he, 0, 64), (ho, 64, 128)):
                        op_mm(g, g.pall[p0:p1_, 3072:3584], sel16[:, h_ * 64:(h_ + 1) * 64],
                              dra[:, 0:512], reads=[B("p1_dra"), B("c_sel16")], writes=g.pbB(6))
                        op_mm(g, g.pall[p0:p1_, 3584:4096], sel16[:, h_ * 64:(h_ + 1) * 64],
                              dra[:, 512:1024], reads=[B("p1_dra"), B("c_sel16")], writes=g.pbB(7))
                    op_act(g, rbs[:, :], g.pall[:, 3072:4096], AF.Identity,
                           reads=g.pbB(6, 7), writes=[B("p1_rbs")])
                    op_tt(g, y_fm[:, hc * TH: (hc + 1) * TH],
                          y_fm[:, hc * TH: (hc + 1) * TH], rbs[:, :], ALU.mult,
                          reads=[B(f"p1_yfm{hc}"), B("p1_rbs")], writes=[B(f"p1_yfm{hc}")])

                op_dma(g, "sync", wsb[1][:, :], g.t["pk_proj"][0:128, :],
                       writes=[B("p1_w0")])
                yfmB = [B(f"p1_yfm{c}") for c in range(NCH)]
                for tb in range(8):
                    sl = tb % 2
                    xi = xin[:, sl * 1024:(sl + 1) * 1024]
                    op_dma(g, "sync", xi, x0[t0 + tb * 128: t0 + (tb + 1) * 128, :],
                           writes=[B(f"p1_xin{sl}")])
                    for ob in range(2):
                        pq = g.pb(ob)
                        pqB = g.pbB(ob)
                        items = [(pq, y_fm[:, kc * TH + tb * 128: kc * TH + (tb + 1) * 128],
                                  wsb[1][:, kc * 1024 + ob * 512: kc * 1024 + (ob + 1) * 512],
                                  kc == 0, kc == NCH - 1) for kc in range(NCH)]
                        op_mm_group(g, items, reads=[B("p1_w0")] + yfmB, writes=pqB)
                        op_tt(g, xs[:, ob * 512:(ob + 1) * 512], pq, xi[:, ob * 512:(ob + 1) * 512],
                              ALU.add, reads=pqB + [B(f"p1_xin{sl}")], writes=[B("p1_xs")])
                    emit_ln_apply(g, xs, sc, stt_, x1t[:, sl * 1024:(sl + 1) * 1024],
                                  "p1_xs", "p1_sc", "p1_st", f"p1_x1t{sl}")
                    op_dma(g, "sync", x1[t0 + tb * 128: t0 + (tb + 1) * 128, :],
                           x1t[:, sl * 1024:(sl + 1) * 1024],
                           reads=[B(f"p1_x1t{sl}")], writes=[B("x1_hbm")])
        g.stack = sv
    g.s.fence()


def emit_ln_apply(g, xs, sc, stt_, out_ap, xsN, scN, stN, outN):
    B = g.s.buf
    op_red(g, stt_[:, 0:1], xs[:, :], ALU.add, reads=[B(xsN)], writes=[B(stN + "0")])
    op_act(g, sc[:, :], xs[:, :], AF.Square, accum=stt_[:, 1:2],
           reads=[B(xsN)], writes=[B(scN), B(stN + "1")])
    op_ts(g, stt_[:, 2:3], stt_[:, 0:1], 1.0 / 1024.0, None, ALU.mult,
          reads=[B(stN + "0")], writes=[B(stN + "2")])
    op_tt(g, stt_[:, 3:4], stt_[:, 2:3], stt_[:, 2:3], ALU.mult,
          reads=[B(stN + "2")], writes=[B(stN + "3")])
    op_stt(g, stt_[:, 4:5], stt_[:, 1:2], 1.0 / 1024.0, stt_[:, 3:4],
           ALU.mult, ALU.subtract,
           reads=[B(stN + "1"), B(stN + "3")], writes=[B(stN + "4")])
    op_act(g, stt_[:, 5:6], stt_[:, 4:5], AF.Ln, bias=1e-5,
           reads=[B(stN + "4")], writes=[B(stN + "5")])
    op_act(g, stt_[:, 6:7], stt_[:, 5:6], AF.Exp, scale=-0.5,
           reads=[B(stN + "5")], writes=[B(stN + "6")])
    op_stt(g, stt_[:, 7:8], stt_[:, 2:3], -1.0, stt_[:, 6:7], ALU.mult, ALU.mult,
           reads=[B(stN + "2"), B(stN + "6")], writes=[B(stN + "7")])
    op_act(g, out_ap, xs[:, :], AF.Identity, scale=stt_[:, 6:7], bias=stt_[:, 7:8],
           reads=[B(xsN), B(stN + "6"), B(stN + "7")], writes=[B(outN)])



def build_program(decls, consts, phases, serialize=False, dma_scratch=16384):
    from contextlib import ExitStack
    nc = bass.Bass("TRN2", target_bir_lowering=False,
                   dynamic_dma_scratch_size=dma_scratch)
    with ExitStack() as top:
        s = Sched(nc, top)
        s.debug_serialize = (serialize is True)
        g = G(nc, s)
        g.stack = top
        g.top_stack = top
        for kind, name, shape, dtype in decls:
            if kind == "in":
                g.dram_in(name, shape, dtype)
            elif kind == "out":
                g.dram_out(name, shape, dtype)
            else:
                g.dram_tmp(name, shape, dtype)
        g.ps_pool()
        g.load_consts(consts)
        for f in phases:
            with nc.named_scope(getattr(f, "__name__", "phase")):
                f(g)
        s.finish()
    return nc, g.const_inputs()



P2_CONSTS = ["c_ones_col", "c_ones_row", "c_ident", "c_mask_add", "c_headind",
             "c_sel16", "c_rope_cs", "c_rope_ss", "c_ones_col_f32", "c_ones_row_f32"]


def swap_ap(wtile, col0):
    return bass.AP(wtile, col0 + 32,
                   [[wtile.shape[1], 128], [64, 2], [-32, 2], [1, 32]])


def rms_rows(g, pfx, src_fm, srcB, t0, t1, sq, rrow, rbf, bank=2):
    B = g.s.buf
    Tl = src_fm.shape[1] // NCH
    for tt in range(t0 // 512, t1 // 512):
        for oc in range(NCH):
            sqs = sq[:, (oc % 2) * 512:(oc % 2 + 1) * 512]
            sqB = B(f"{pfx}_sq{oc % 2}")
            op_act(g, sqs, src_fm[:, oc * Tl + tt * 512: oc * Tl + (tt + 1) * 512],
                   AF.Square, reads=[srcB[oc]], writes=[sqB])
            op_mm(g, g.pb(bank, 0, 1), g.t["c_ones_col"][:, :], sqs,
                  reads=[sqB, B("c_ones_col")], writes=g.pbB(bank),
                  start=(oc == 0), stop=(oc == NCH - 1))
        sl = tt % 2
        rr = rrow[0:1, sl * 512:(sl + 1) * 512]
        rf = rbf[0:1, sl * 512:(sl + 1) * 512]
        op_act(g, rr, g.pb(bank, 0, 1), AF.Ln, scale=1.0 / 1024.0, bias=1e-6,
               reads=g.pbB(bank), writes=[B(f"{pfx}_rr{sl}")])
        op_act(g, rr, rr, AF.Exp, scale=-0.5,
               reads=[B(f"{pfx}_rr{sl}")], writes=[B(f"{pfx}_rr{sl}")])
        op_copy(g, rf, rr, reads=[B(f"{pfx}_rr{sl}")], writes=[B(f"{pfx}_rbf{sl}")])
        yield tt, rf, B(f"{pfx}_rbf{sl}")




def load_vec_bcast(g, hbm_ap, name, bank=6):
    from contextlib import ExitStack
    B = g.s.buf
    tile = g.sb(name, [128, 1024], F32)
    with ExitStack() as tmps:
        sv = g.stack
        g.stack = tmps
        row = g.sb(name + "_row", [1, 1024], F32)
        op_dma(g, "gpsimd", row[0:1, :], hbm_ap.rearrange("(a d) -> a d", a=1),
               writes=[B(name + "_row")])
        onesf = g.t["c_ones_row_f32"]
        for half in range(2):
            op_mm(g, g.pb(bank), onesf[0:1, :],
                  row[0:1, half * 512:(half + 1) * 512],
                  reads=[B(name + "_row"), B("c_ones_row_f32")], writes=g.pbB(bank))
            op_copy(g, tile[:, half * 512:(half + 1) * 512], g.pb(bank),
                    reads=g.pbB(bank), writes=[B(name)])
        g.stack = sv
    g.s.fence()
    return tile


P2_STOP = [None]
P2_ROWS = [ROWS]


def phase2(g):
    from contextlib import ExitStack
    B = g.s.buf
    x1, phbm = g.t["x1"], g.t["processed"]
    ml_hbm = g.t["p2_ml"]
    xa_hbm = g.t["p2_xa"]
    dst_hbm, dch_hbm = g.t["d_st"], g.t["d_ch"]
    part_hbm = g.t["partials"]
    ones_row = g.t["c_ones_row"]
    ident = g.t["c_ident"]
    CS, SS = g.t["c_rope_cs"], g.t["c_rope_ss"]
    hind = g.t["c_headind"]
    sel16 = g.t["c_sel16"]
    onesf = g.t["c_ones_col_f32"]

    with ExitStack() as st:
        sv = g.stack
        g.stack = st
        h1 = g.sb("p2_h1", [128, NCH * T], BF16)
        y_fm = g.sb("p2_yfm", [128, NCH * T], BF16)
        dtmp = g.sb("p2_dtmp", [1, 2 * 512], BF16)
        mk = g.sb("p2_mk", [16, 4], F32)
        ds_all = g.sb("p2_dsall", [128, 16], F32)
        dc_all = g.sb("p2_dcall", [128, 16], F32)
        ln1b = load_vec_bcast(g, g.t["L0_ffn_ln1_1"], "p2_ln1")
        ln2b = load_vec_bcast(g, g.t["L0_ffn_ln2_1"], "p2_ln2")
        pnwb = load_vec_bcast(g, g.t["L0_ffn_prior_norm_w"], "p2_pnw")
        stt_ = g.sb("p2_stt", [128, 8], F32)

        for r in range(P2_ROWS[0]):
            rb = r * T
            with ExitStack() as stA:
                g.stack = stA
                xin = g.sb("p2_xin", [128, 2 * 1024], F32)
                hs = g.sb("p2_hs", [128, 2 * 1024], F32)
                ph = g.sb("p2_ph", [128, NCH * T], BF16)
                wml = g.sb("p2_wml0", [128, NCH * 512], BF16)
                mlt = g.sb("p2_mlt", [128, 2 * 512], F32)
                phB = [B(f"p2_ph{c}") for c in range(NCH)]
                h1B = [B(f"p2_h1{c}") for c in range(NCH)]
                for tb in range(16):
                    sl = tb % 2
                    xi = xin[:, sl * 1024:(sl + 1) * 1024]
                    op_dma(g, "sync", xi, x1[rb + tb * 128: rb + (tb + 1) * 128, :],
                           writes=[B(f"p2_xin{sl}")])
                    op_act(g, hs[:, 0:1024], xi, AF.Square, accum=stt_[:, 0:1],
                           reads=[B(f"p2_xin{sl}")], writes=[B("p2_hs0"), B("p2_st0")])
                    op_act(g, stt_[:, 1:2], stt_[:, 0:1], AF.Ln, scale=1.0 / 1024.0, bias=1e-6,
                           reads=[B("p2_st0")], writes=[B("p2_st1")])
                    op_act(g, stt_[:, 2:3], stt_[:, 1:2], AF.Exp, scale=-0.5,
                           reads=[B("p2_st1")], writes=[B("p2_st2")])
                    for (dstB, wb, wbn, dst) in ((h1B, ln1b, "p2_ln1", h1), (phB, pnwb, "p2_pnw", ph)):
                        op_stt(g, hs[:, 0:1024], xi, stt_[:, 2:3], wb[:, :],
                               ALU.mult, ALU.mult,
                               reads=[B(f"p2_xin{sl}"), B("p2_st2"), B(wbn)],
                               writes=[B("p2_hs0")])
                        for c in range(NCH):
                            po = g.pall[:, 7 * 512 + (c % 4) * 128: 7 * 512 + (c % 4 + 1) * 128]
                            op_transpose(g, po, hs[:, c * 128:(c + 1) * 128], ident[:, :],
                                         reads=[B("p2_hs0"), B("c_ident")], writes=g.pbB(7))
                            op_copy(g, dst[:, c * T + tb * 128: c * T + (tb + 1) * 128], po,
                                    reads=g.pbB(7), writes=[dstB[c]])
                pw = g.t["L0_ffn_prior_w"]
                for ob in range(4):
                    wbB = B("p2_wml0")
                    op_dma(g, "sync", wml[:, :],
                           g.t["pk_prior"][ob * 128:(ob + 1) * 128, :],
                           writes=[wbB])
                    for tb in range(16):
                        pq = g.pb(tb % 2)
                        pqB = g.pbB(tb % 2)
                        items = [(pq, ph[:, kc * T + tb * 128: kc * T + (tb + 1) * 128],
                                  wml[:, kc * 512:(kc + 1) * 512],
                                  kc == 0, kc == NCH - 1) for kc in range(NCH)]
                        op_mm_group(g, items, reads=[wbB] + phB, writes=pqB)
                        sl = tb % 2
                        op_act(g, mlt[:, sl * 512:(sl + 1) * 512], pq, AF.Identity,
                               reads=pqB, writes=[B(f"p2_mlt{sl}")])
                        op_dma(g, "sync",
                               ml_hbm[rb + tb * 128: rb + (tb + 1) * 128, ob * 512:(ob + 1) * 512],
                               mlt[:, sl * 512:(sl + 1) * 512],
                               reads=[B(f"p2_mlt{sl}")], writes=[B("p2_mlhbm")])
                g.stack = st
            g.s.fence()

            if P2_STOP[0] == "A":
                continue
            stB = ExitStack()
            g.stack = stB
            wq_sb = g.sb("p2_wqsb", [128, NCH * 1024], BF16)
            wk_sb = g.sb("p2_wksb", [128, NCH * 1024], BF16)
            wv_sb = g.sb("p2_wvsb", [128, NCH * 1024], BF16)
            for (wt_, wn_, ti_) in ((wq_sb, "p2_wqsb", 0),
                                    (wk_sb, "p2_wksb", 1),
                                    (wv_sb, "p2_wvsb", 2)):
                op_dma(g, "sync", wt_[:, :],
                       g.t["pk_f1qkv"][ti_ * 128:(ti_ + 1) * 128, :], writes=[B(wn_)])
            qa = [g.sb("p2_qa0", [128, T], BF16), g.sb("p2_qa1", [128, T], BF16)]
            ka = [g.sb("p2_ka0", [128, T], BF16), g.sb("p2_ka1", [128, T], BF16)]
            for _t, _n in ((qa[0], "p2_qa0"), (qa[1], "p2_qa1"), (ka[0], "p2_ka0"), (ka[1], "p2_ka1")):
                op_memset(g, _t[64:128, :], 0.0, writes=[B(_n + "_pad")])
            vt = [g.sb("p2_vt0", [128, 16 * 65], BF16), g.sb("p2_vt1", [128, 16 * 65], BF16)]
            Pb = [g.sb("p2_P0", [128, 2048], BF16), g.sb("p2_P1", [128, 2048], BF16)]
            den_all = g.sb("p2_den", [16, T], BF16)
            den_f32 = g.sb("p2_denf", [16, 512], F32)
            dra = g.sb("p2_dra", [16, T], BF16)
            rbs = g.sb("p2_rbs", [128, 1024], BF16)
            Ball = g.sb("p2_Ball", [16, T], BF16)
            nrm = g.sb("p2_nrm", [16, 512], F32)
            sq = g.sb("p2_sq", [128, 2 * 512], BF16)
            wsw2 = [g.sb("p2_wsw0", [128, NCH * 256], BF16),
                    g.sb("p2_wsw1", [128, NCH * 256], BF16)]
            u1 = g.sb("p2_u1", [128, 512], F32)
            u2 = g.sb("p2_u2", [128, 512], F32)
            wq_h, wk_h, wv_h = g.t["L0_ffn_wq1"], g.t["L0_ffn_wk1"], g.t["L0_ffn_wv1"]
            h1B = [B(f"p2_h1{c}") for c in range(NCH)]
            def emit_wswap(hc_):
                wsw_ = wsw2[hc_ % 2]
                for (wsrc, wsrcB, soff) in ((wq_sb, B("p2_wqsb"), 0), (wk_sb, B("p2_wksb"), 128)):
                    op_copy(g, bass.AP(wsw_, soff, [[NCH * 256, 128], [256, NCH], [64, 2], [1, 32]]),
                            bass.AP(wsrc, hc_ * 128 + 32, [[NCH * 1024, 128], [1024, NCH], [64, 2], [1, 32]]),
                            reads=[wsrcB], writes=[B(f"p2_wsw{hc_ % 2}")], stream="gpsimd")
                    op_copy(g, bass.AP(wsw_, soff + 32, [[NCH * 256, 128], [256, NCH], [64, 2], [1, 32]]),
                            bass.AP(wsrc, hc_ * 128, [[NCH * 1024, 128], [1024, NCH], [64, 2], [1, 32]]),
                            reads=[wsrcB], writes=[B(f"p2_wsw{hc_ % 2}")], stream="gpsimd")
                op_memset(g, mk[:, 0:1], 0.0, writes=[B("p2_mk")])

            emit_wswap(0)
            for hc in range(NCH):
                wswap = wsw2[hc % 2]
                wswB = B(f"p2_wsw{hc % 2}")
                he, ho = 2 * hc, 2 * hc + 1
                qa_e, qa_o = qa[0], qa[1]
                ka_e, ka_o = ka[0], ka[1]
                if hc == 0:
                    op_memset(g, ka_e[64:65, :], 1.0, writes=[B("p2_ka0_hi")])
                    op_memset(g, ka_o[64:65, :], 1.0, writes=[B("p2_ka1_hi")])
                if P2_STOP[0] == "B1":
                    continue
                for (isq, dst_e, dst_o, deB, doB, wres, wresB, wcol2) in (
                        (False, ka_e, ka_o, B("p2_ka0_lo"), B("p2_ka1_lo"), wk_sb, B("p2_wksb"), 128),
                        (True, qa_e, qa_o, B("p2_qa0_lo"), B("p2_qa1_lo"), wq_sb, B("p2_wqsb"), 0)):
                    for tt in range(4):
                        pA, pAB = g.pb(6), g.pbB(6)
                        pB_, pBB = g.pb(7), g.pbB(7)
                        items = [(pA, bass.AP(wres, kc * 1024 + hc * 128, [[NCH * 1024, 128], [1, 128]]),
                                  h1[:, kc * T + tt * 512: kc * T + (tt + 1) * 512],
                                  kc == 0, kc == NCH - 1) for kc in range(NCH)]
                        op_mm_group(g, items, reads=[wresB] + h1B, writes=pAB)
                        if P2_STOP[0] == "B1a":
                            continue
                        items = [(pB_, bass.AP(wswap, kc * 256 + wcol2, [[NCH * 256, 128], [1, 128]]),
                                  h1[:, kc * T + tt * 512: kc * T + (tt + 1) * 512],
                                  kc == 0, kc == NCH - 1) for kc in range(NCH)]
                        op_mm_group(g, items, reads=[wswB] + h1B, writes=pBB)
                        if P2_STOP[0] == "B1b":
                            continue
                        tsl = tt * 512
                        op_tt(g, u1[:, :], pA, CS[:, tsl: tsl + 512], ALU.mult,
                              reads=pAB + [B("c_rope_cs")], writes=[B("p2_u1")])
                        op_tt(g, u2[:, :], pB_, SS[:, tsl: tsl + 512], ALU.mult,
                              reads=pBB + [B("c_rope_ss")], writes=[B("p2_u2")])
                        if P2_STOP[0] == "B1c":
                            continue
                        op_tt(g, dst_e[0:64, tsl: tsl + 512], u1[0:64, :], u2[0:64, :],
                              ALU.add, reads=[B("p2_u1"), B("p2_u2")], writes=[deB])
                        op_tt(g, dst_o[0:64, tsl: tsl + 512], u1[64:128, :], u2[64:128, :],
                              ALU.add, reads=[B("p2_u1"), B("p2_u2")], writes=[doB])
                        if P2_STOP[0] == "B1d":
                            continue
                        sqs = sq[:, (tt % 2) * 512:(tt % 2 + 1) * 512]
                        sqB = B(f"p2_sq{tt % 2}")
                        op_act(g, sqs, pA, AF.Square, reads=pAB, writes=[sqB])
                        if P2_STOP[0] == "B1e2":
                            continue
                        op_mm(g, g.pb(5, 0, 16), hind[:, hc * 16:(hc + 1) * 16], sqs,
                              reads=[sqB, B("c_headind")], writes=g.pbB(5))
                        if P2_STOP[0] == "B1e":
                            continue
                        op_act(g, nrm[:, :], g.pb(5, 0, 16), AF.Sqrt,
                               reads=g.pbB(5), writes=[B("p2_nrm")])
                        if isq:
                            op_ts(g, Ball[:, tsl: tsl + 512], nrm[:, :],
                                  mk[:, 1:2], 8.0 * SHIFT_MARGIN, ALU.mult, ALU.add,
                                  reads=[B("p2_nrm"), B("p2_mkn")], writes=[B("p2_Ball")])
                        else:
                            op_red(g, mk[:, 2:3], nrm[:, :], ALU.max,
                                   reads=[B("p2_nrm")], writes=[B("p2_mk2")])
                            op_tt(g, mk[:, 0:1], mk[:, 0:1], mk[:, 2:3], ALU.max,
                                  reads=[B("p2_mk2"), B("p2_mk")], writes=[B("p2_mk")])
                    if not isq:
                        op_ts(g, mk[:, 1:2], mk[:, 0:1], -1.0, None, ALU.mult,
                              reads=[B("p2_mk")], writes=[B("p2_mkn")])
                if P2_STOP[0] == "B2":
                    continue
                op_dma(g, "sync", qa_e[64:65, :], Ball[he:he + 1, :],
                       reads=[B("p2_Ball")], writes=[B("p2_qa0_hi")])
                op_dma(g, "sync", qa_o[64:65, :], Ball[ho:ho + 1, :],
                       reads=[B("p2_Ball")], writes=[B("p2_qa1_hi")])
                if hc == 0:
                    op_memset(g, vt[0][:, :], 1.0, writes=[B("p2_vt0")])
                    op_memset(g, vt[1][:, :], 1.0, writes=[B("p2_vt1")])
                for tb in range(16):
                    pA, pAB = g.pb(6 + tb % 2), g.pbB(6 + tb % 2)
                    items = [(pA[:, 0:128],
                              h1[:, kc * T + tb * 128: kc * T + (tb + 1) * 128],
                              bass.AP(wv_sb, kc * 1024 + hc * 128, [[NCH * 1024, 128], [1, 128]]),
                              kc == 0, kc == NCH - 1) for kc in range(NCH)]
                    op_mm_group(g, items, reads=[B("p2_wvsb")] + h1B, writes=pAB)
                    op_copy(g, vt[0][:, tb * 65: tb * 65 + 64], pA[:, 0:64],
                            reads=pAB, writes=[B("p2_vt0")])
                    op_copy(g, vt[1][:, tb * 65: tb * 65 + 64], pA[:, 64:128],
                            reads=pAB, writes=[B("p2_vt1")])
                if P2_STOP[0] == "B3":
                    continue
                if hc + 1 < NCH:
                    emit_wswap(hc + 1)
                for (h, qa_h, ka_h, vt_h, qaB, kaB, vtB) in (
                        (he, qa_e, ka_e, vt[0], ("p2_qa0_lo", "p2_qa0_hi"),
                         ("p2_ka0_lo", "p2_ka0_hi"), "p2_vt0"),
                        (ho, qa_o, ka_o, vt[1], ("p2_qa1_lo", "p2_qa1_hi"),
                         ("p2_ka1_lo", "p2_ka1_hi"), "p2_vt1")):
                    hr = (h % 2) * 64
                    for qt in range(4):
                        yb = 4 + (qt % 2)
                        pY = g.pall[0:65, yb * 512:(yb + 1) * 512]
                        pYB = g.pbB(yb)
                        for gg in range(qt + 1):
                            last = (gg == qt)
                            Pt = Pb[gg % 2]
                            PtB = B(f"p2_P{gg % 2}")
                            items = []
                            for j in range(4):
                                kt = 4 * gg + j
                                if last:
                                    items.append((g.pall[:, j * 512 + 128 * j: j * 512 + 512],
                                                  ka_h[:, kt * 128: (kt + 1) * 128],
                                                  qa_h[:, qt * 512 + 128 * j: (qt + 1) * 512],
                                                  True, True))
                                else:
                                    items.append((g.pall[:, j * 512: (j + 1) * 512],
                                                  ka_h[:, kt * 128: (kt + 1) * 128],
                                                  qa_h[:, qt * 512: (qt + 1) * 512],
                                                  True, True))
                            op_mm_group(g, items,
                                        reads=[B(qaB[0]), B(qaB[1]), B(qaB[0].replace("_lo", "_pad")),
                                               B(kaB[0]), B(kaB[1]), B(kaB[0].replace("_lo", "_pad"))],
                                        writes=g.pbB(0, 3))
                            if last:
                                v = sview(g.pall, 0, [[4096, 128], [640, 4], [1, 128]])
                                m = sview(g.t["c_mask_add"], 0, [[128, 128], [0, 4], [1, 128]])
                                op_tt(g, v, v, m, ALU.add,
                                      reads=[B("c_mask_add")] + g.pbB(0, 3), writes=g.pbB(0, 3))
                            for j in range(4):
                                kt = 4 * gg + j
                                PsB = B(f"{PtB.name}_s{j}")
                                op_act(g, Pt[:, j * 512:(j + 1) * 512],
                                       g.pall[:, j * 512:(j + 1) * 512], AF.Exp,
                                       scale=0.125, reads=g.pbB(j, j), writes=[PsB])
                                if last and j > 0:
                                    op_memset(g, Pt[:, j * 512: j * 512 + 128 * j], 0.0,
                                              writes=[PsB], reads=[PsB])
                                op_mm_group(g, [(pY, vt_h[:, kt * 65: kt * 65 + 65],
                                                 Pt[:, j * 512: (j + 1) * 512],
                                                 gg == 0 and j == 0, last and j == 3)],
                                            reads=[PsB, B(vtB)], writes=pYB)
                        sl = qt % 2
                        op_copy(g, dtmp[0:1, sl * 512:(sl + 1) * 512], g.pall[64:65, yb * 512:(yb + 1) * 512],
                                reads=pYB, writes=[B(f"p2_dtmp{sl}")])
                        op_dma(g, "sync", den_all[h:h + 1, qt * 512:(qt + 1) * 512],
                               dtmp[0:1, sl * 512:(sl + 1) * 512],
                               reads=[B(f"p2_dtmp{sl}")], writes=[B("p2_denall")])
                        op_act(g, y_fm[hr:hr + 64, hc * T + qt * 512: hc * T + (qt + 1) * 512],
                               g.pall[0:64, yb * 512:(yb + 1) * 512], AF.Identity,
                               reads=pYB, writes=[B(f"p2_yfm{hc}")])
            if P2_STOP[0] == "Bnt":
                stB.close(); g.stack = st; g.s.fence(); continue
            op_recip(g, dra[:, :], den_all[:, :], reads=[B("p2_denall")], writes=[B("p2_dra")])
            for hc in range(NCH):
                he, ho = 2 * hc, 2 * hc + 1
                for seg in range(2):
                    for (h_, p0, p1_) in ((he, 0, 64), (ho, 64, 128)):
                        op_mm(g, g.pall[p0:p1_, 3072:3584], sel16[:, h_ * 64:(h_ + 1) * 64],
                              dra[:, seg * 1024: seg * 1024 + 512],
                              reads=[B("p2_dra"), B("c_sel16")], writes=g.pbB(6))
                        op_mm(g, g.pall[p0:p1_, 3584:4096], sel16[:, h_ * 64:(h_ + 1) * 64],
                              dra[:, seg * 1024 + 512: seg * 1024 + 1024],
                              reads=[B("p2_dra"), B("c_sel16")], writes=g.pbB(7))
                    op_act(g, rbs[:, :], g.pall[:, 3072:4096], AF.Identity,
                           reads=g.pbB(6, 7), writes=[B("p2_rbs")])
                    op_tt(g, y_fm[:, hc * T + seg * 1024: hc * T + (seg + 1) * 1024],
                          y_fm[:, hc * T + seg * 1024: hc * T + (seg + 1) * 1024],
                          rbs[:, :], ALU.mult,
                          reads=[B(f"p2_yfm{hc}"), B("p2_rbs")], writes=[B(f"p2_yfm{hc}")])
            stB.close()
            g.stack = st
            g.s.fence()

            if P2_STOP[0] == "B":
                continue
            with ExitStack() as stC:
                g.stack = stC
                wsb = g.sb("p2_wo", [128, NCH * 1024], BF16)
                xin = g.sb("p2_xinC", [128, 2 * 1024], F32)
                xat = g.sb("p2_xat", [128, 2 * 1024], F32)
                hs = g.sb("p2_hsC", [128, 1024], F32)
                op_dma(g, "sync", wsb[:, :], g.t["pk_wo1"][0:128, :],
                       writes=[B("p2_wo")])
                yfmB = [B(f"p2_yfm{c}") for c in range(NCH)]
                h1B = [B(f"p2_h1{c}") for c in range(NCH)]
                for tb in range(16):
                    sl = tb % 2
                    xi = xin[:, sl * 1024:(sl + 1) * 1024]
                    xa_t = xat[:, sl * 1024:(sl + 1) * 1024]
                    op_dma(g, "sync", xi, x1[rb + tb * 128: rb + (tb + 1) * 128, :],
                           writes=[B(f"p2_xinC{sl}")])
                    for ob in range(2):
                        pq = g.pb(ob)
                        pqB = g.pbB(ob)
                        items = [(pq, y_fm[:, kc * T + tb * 128: kc * T + (tb + 1) * 128],
                                  wsb[:, kc * 1024 + ob * 512: kc * 1024 + (ob + 1) * 512],
                                  kc == 0, kc == NCH - 1) for kc in range(NCH)]
                        op_mm_group(g, items, reads=[B("p2_wo")] + yfmB, writes=pqB)
                        op_tt(g, xa_t[:, ob * 512:(ob + 1) * 512], pq, xi[:, ob * 512:(ob + 1) * 512],
                              ALU.add, reads=pqB + [B(f"p2_xinC{sl}")], writes=[B(f"p2_xat{sl}")])
                    op_dma(g, "sync", xa_hbm[rb + tb * 128: rb + (tb + 1) * 128, :], xa_t,
                           reads=[B(f"p2_xat{sl}")], writes=[B("p2_xahbm")])
                    op_act(g, hs[:, :], xa_t, AF.Square, accum=stt_[:, 0:1],
                           reads=[B(f"p2_xat{sl}")], writes=[B("p2_hsC"), B("p2_st0")])
                    op_act(g, stt_[:, 1:2], stt_[:, 0:1], AF.Ln, scale=1.0 / 1024.0, bias=1e-6,
                           reads=[B("p2_st0")], writes=[B("p2_st1")])
                    op_act(g, stt_[:, 2:3], stt_[:, 1:2], AF.Exp, scale=-0.5,
                           reads=[B("p2_st1")], writes=[B("p2_st2")])
                    op_stt(g, hs[:, :], xa_t, stt_[:, 2:3], ln2b[:, :], ALU.mult, ALU.mult,
                           reads=[B(f"p2_xat{sl}"), B("p2_st2"), B("p2_ln2")],
                           writes=[B("p2_hsC")])
                    for c in range(NCH):
                        po = g.pall[:, 7 * 512 + (c % 4) * 128: 7 * 512 + (c % 4 + 1) * 128]
                        op_transpose(g, po, hs[:, c * 128:(c + 1) * 128], ident[:, :],
                                     reads=[B("p2_hsC"), B("c_ident")], writes=g.pbB(7))
                        op_copy(g, h1[:, c * T + tb * 128: c * T + (tb + 1) * 128], po,
                                reads=g.pbB(7), writes=[h1B[c]])
                g.stack = st
            g.s.fence()

            if P2_STOP[0] == "C":
                continue
            with ExitStack() as stE:
                g.stack = stE
                gated = g.sb("p2_gated", [128, 32 * 512], BF16)
                wgu = [g.sb("p2_wgu0", [128, NCH * 1024], BF16),
                       g.sb("p2_wgu1", [128, NCH * 1024], BF16)]
                wdb = [g.sb("p2_wd0", [128, 1024], BF16), g.sb("p2_wd1", [128, 1024], BF16)]
                sg_ = g.sb("p2_sg", [128, 512], BF16)
                xin = g.sb("p2_xinE", [128, 1024], F32)
                ptok = g.sb("p2_ptok", [128, 1024], F32)
                res = g.sb("p2_res", [128, 1024], F32)
                mlin = g.sb("p2_mlin", [128, 1024], F32)
                ee = g.sb("p2_ee", [128, 1024], F32)
                uu = g.sb("p2_uu", [128, 1024], F32)
                res1 = g.sb("p2_res1", [128, 1024], F32)
                mlin1 = g.sb("p2_mlin1", [128, 1024], F32)
                ee1 = g.sb("p2_ee1", [128, 1024], F32)
                uu1 = g.sb("p2_uu1", [128, 1024], F32)
                ssets = [
                    dict(pt=ptok, xin=xin, mlin=mlin, res=res, uu=uu, ee=ee,
                         ptB="p2_ptok0", xinB="p2_xinE", mlinB="p2_mlin",
                         resB="p2_res", uuB="p2_uu", eeB="p2_ee"),
                    dict(pt=ptok, xin=xin, mlin=mlin1, res=res1, uu=uu1, ee=ee1,
                         ptB="p2_ptok0", xinB="p2_xinE", mlinB="p2_mlin1",
                         resB="p2_res1", uuB="p2_uu1", eeB="p2_ee1"),
                ]
                wg_h, wu_h, wd_h = g.t["L0_ffn_wg1"], g.t["L0_ffn_wu1"], g.t["L0_ffn_wd1"]
                h1B = [B(f"p2_h1{c}") for c in range(NCH)]
                gatB = [B(f"p2_gated{gc}") for gc in range(32)]
                def stats_ops(tt, tb, s):
                    gt = tt * 4 + tb
                    pt_, xn, ml, rs, u_, e_ = (s["pt"], s["xin"], s["mlin"],
                                               s["res"], s["uu"], s["ee"])
                    ptB, xnB, mlB = B(s["ptB"]), B(s["xinB"]), B(s["mlinB"])
                    rsB, uB, eB = B(s["resB"]), B(s["uuB"]), B(s["eeB"])
                    return [
                     lambda: op_dma(g, "gpsimd", pt_[:, :], phbm[rb + gt * 128: rb + (gt + 1) * 128, :],
                                    writes=[ptB], reads=[B("p2_phbm")]),
                     lambda: op_dma(g, "gpsimd", xn[:, :], x1[rb + gt * 128: rb + (gt + 1) * 128, :],
                                    writes=[xnB]),
                     lambda: op_dma(g, "gpsimd", ml[:, :],
                                    ml_hbm[rb + gt * 128: rb + (gt + 1) * 128, 0:1024],
                                    writes=[mlB], reads=[B("p2_mlhbm")]),
                     lambda: op_tt(g, rs[:, :], pt_[:, :], xn[:, :], ALU.subtract,
                                   reads=[ptB, xnB], writes=[rsB]),
                     lambda: op_act(g, u_[:, :], rs[:, :], AF.Square, accum=ds_all[:, gt:gt + 1],
                                    reads=[rsB], writes=[uB, B("p2_dsall")]),
                     lambda: op_tt(g, u_[:, :], rs[:, :], ml[:, :], ALU.subtract,
                                   reads=[rsB, mlB], writes=[uB]),
                     lambda: op_dma(g, "gpsimd", ml[:, :],
                                    ml_hbm[rb + gt * 128: rb + (gt + 1) * 128, 1024:2048],
                                    writes=[mlB], reads=[B("p2_mlhbm")]),
                     lambda: op_act(g, e_[:, :], ml[:, :], AF.Exp, scale=-1.0,
                                    reads=[mlB], writes=[eB]),
                     lambda: op_act(g, u_[:, :], u_[:, :], AF.Square,
                                    reads=[uB], writes=[uB]),
                     lambda: op_stt(g, u_[:, :], u_[:, :], 1.0, e_[:, :], ALU.add, ALU.mult,
                                    reads=[uB, eB], writes=[uB]),
                     lambda: op_tt(g, u_[:, :], u_[:, :], ml[:, :], ALU.add,
                                   reads=[uB, mlB], writes=[uB]),
                     lambda: op_act(g, e_[:, :], u_[:, :], AF.Identity, accum=dc_all[:, gt:gt + 1],
                                    reads=[uB], writes=[eB, B("p2_dcall")]),
                    ]

                def emit_stats_pair(tt, tb0):
                    a = stats_ops(tt, tb0, ssets[0])
                    b = stats_ops(tt, tb0 + 1, ssets[1])
                    for x in a[:4]:
                        x()
                    rest = a[4:]
                    for i, y in enumerate(b):
                        if i < len(rest):
                            rest[i]()
                        y()

                pending_stats = []
                for tt in range(4):
                    for blk in range(8):
                        wb = wgu[blk % 2]
                        wbB = B(f"p2_wgu{blk % 2}")
                        op_dma(g, "sync", wb[:, :],
                               g.t["pk_f1gu"][blk * 128:(blk + 1) * 128, :],
                               writes=[wbB])
                        for sub in range(4):
                            gc = blk * 4 + sub
                            pG, pGB = g.pb(0 + sub % 2), g.pbB(0 + sub % 2)
                            pU, pUB = g.pb(2 + sub % 2), g.pbB(2 + sub % 2)
                            items = [(pG[:, 0:512],
                                      bass.AP(wb, kc * 1024 + sub * 128, [[NCH * 1024, 128], [1, 128]]),
                                      h1[:, kc * T + tt * 512: kc * T + (tt + 1) * 512],
                                      kc == 0, kc == NCH - 1) for kc in range(NCH)]
                            op_mm_group(g, items, reads=[wbB] + h1B, writes=pGB)
                            items = [(pU[:, 0:512],
                                      bass.AP(wb, kc * 1024 + 512 + sub * 128, [[NCH * 1024, 128], [1, 128]]),
                                      h1[:, kc * T + tt * 512: kc * T + (tt + 1) * 512],
                                      kc == 0, kc == NCH - 1) for kc in range(NCH)]
                            op_mm_group(g, items, reads=[wbB] + h1B, writes=pUB)
                            op_act(g, sg_[:, :], pG[:, 0:512], AF.Silu,
                                   reads=pGB, writes=[B("p2_sg")])
                            op_tt(g, gated[:, gc * 512:(gc + 1) * 512], sg_[:, :], pU[:, 0:512],
                                  ALU.mult, reads=[B("p2_sg")] + pUB, writes=[gatB[gc]])
                        if blk < len(pending_stats):
                            pending_stats[blk]()
                    for gc in range(32):
                        wd_t = wdb[gc % 2]
                        wdB = B(f"p2_wd{gc % 2}")
                        op_dma(g, "sync", wd_t[:, :], wd_h[gc * 128:(gc + 1) * 128, :],
                               writes=[wdB])
                        items = []
                        for tb in range(4):
                            for ob in range(2):
                                items.append((g.pb(tb * 2 + ob),
                                              gated[:, gc * 512 + tb * 128: gc * 512 + (tb + 1) * 128],
                                              wd_t[:, ob * 512:(ob + 1) * 512],
                                              gc == 0, gc == 31))
                        op_mm_group(g, items, reads=[wdB, gatB[gc]], writes=g.pbB(0, 7))
                    for tb in range(4):
                        gt = tt * 4 + tb
                        op_dma(g, "gpsimd", xin[:, :], xa_hbm[rb + gt * 128: rb + (gt + 1) * 128, :],
                               writes=[B("p2_xinE")])
                        pt_ = ptok[:, 0:1024]
                        for ob in range(2):
                            op_tt(g, pt_[:, ob * 512:(ob + 1) * 512], g.pb(tb * 2 + ob),
                                  xin[:, ob * 512:(ob + 1) * 512], ALU.add,
                                  reads=g.pbB(tb * 2 + ob) + [B("p2_xinE")],
                                  writes=[B("p2_ptok0")])
                        op_dma(g, "gpsimd", phbm[rb + gt * 128: rb + (gt + 1) * 128, :], pt_,
                               reads=[B("p2_ptok0")], writes=[B("p2_phbm")])
                    pending_stats = [
                        (lambda tt_=tt: emit_stats_pair(tt_, 0)),
                        (lambda tt_=tt: emit_stats_pair(tt_, 2))]
                for fn in pending_stats:
                    fn()
                g.stack = st
            g.s.fence()
            if P2_STOP[0] == "E":
                continue
            op_dma(g, "sync", dst_hbm[r * 128:(r + 1) * 128, :], ds_all[:, :],
                   reads=[B("p2_dsall")], writes=[B("p2_dsthbm")])
            op_dma(g, "sync", dch_hbm[r * 128:(r + 1) * 128, :], dc_all[:, :],
                   reads=[B("p2_dcall")], writes=[B("p2_dchhbm")])
            op_red(g, stt_[:, 3:4], ds_all[:, :], ALU.add, reads=[B("p2_dsall")], writes=[B("p2_mk3")])
            op_mm(g, g.pb(6, 0, 1, 1), onesf[:, :], stt_[:, 3:4],
                  reads=[B("p2_mk3"), B("c_ones_col_f32")], writes=g.pbB(6))
            op_red(g, stt_[:, 3:4], dc_all[:, :], ALU.add, reads=[B("p2_dcall")], writes=[B("p2_mk3")])
            op_mm(g, bass.AP(g.pall, 6 * 512 + 1, [[4096, 1], [1, 1]]), onesf[:, :], stt_[:, 3:4],
                  reads=[B("p2_mk3"), B("c_ones_col_f32")], writes=g.pbB(6))
            op_copy(g, stt_[0:1, 4:6], g.pall[0:1, 3072:3074],
                    reads=g.pbB(6), writes=[B("p2_pttmp")])
            op_dma(g, "sync", part_hbm[r:r + 1, :], stt_[0:1, 4:6],
                  reads=[B("p2_pttmp")], writes=[B("p2_parthbm")])
        g.stack = sv
    g.s.fence()



P4_CONSTS = ["c_ones_col", "c_ones_col_f32", "c_ones_row_f32", "c_ident",
             "c_tril", "c_triu16", "c_iota_f", "c_iota_row128"]
NTOK = NCORES * ROWS * T


def cumsum_tokens(g, src_bf, srcB, out_f32, outB, csr, cst, cstb, offr, pfx):
    B = g.s.buf
    tril = g.t["c_tril"]
    ones = g.t["c_ones_col"]
    onesr = g.t["c_ones_row_f32"]
    tri16 = g.t["c_triu16"]
    ident = g.t["c_ident"]
    op_mm(g, g.pb(4, 0, 128, 64), tril[:, :], src_bf[:, :],
          reads=[srcB, B("c_tril")], writes=g.pbB(4))
    op_mm(g, g.pb(5, 0, 1, 64), ones[:, :], src_bf[:, :],
          reads=[srcB, B("c_ones_col")], writes=g.pbB(5))
    op_copy(g, csr[0:1, :], g.pb(5, 0, 1, 64), reads=g.pbB(5), writes=[B(pfx + "csr")])
    op_transpose(g, g.pb(6, 0, 64, 1), csr[0:1, :], ident[0:1, 0:1],
                 reads=[B(pfx + "csr"), B("c_ident")], writes=g.pbB(6))
    op_copy(g, cstb[:, :], g.pb(6, 0, 64, 1), reads=g.pbB(6), writes=[B(pfx + "cstb")])
    op_mm(g, g.pb(5, 0, 1, 64), cstb[:, :], tri16[:, :],
          reads=[B(pfx + "cstb"), B("c_triu16")], writes=g.pbB(5))
    op_copy(g, offr[0:1, :], g.pb(5, 0, 1, 64), reads=g.pbB(5), writes=[B(pfx + "offr")])
    op_mm(g, g.pb(7, 0, 128, 64), onesr[0:1, :], offr[0:1, :],
          reads=[B(pfx + "offr"), B("c_ones_row_f32")], writes=g.pbB(7))
    op_copy(g, out_f32[:, :], g.pb(4, 0, 128, 64), reads=g.pbB(4), writes=[outB])
    op_tt(g, out_f32[:, :], out_f32[:, :], g.pb(7, 0, 128, 64), ALU.add,
          reads=[outB] + g.pbB(7), writes=[outB])


def phase4(g):
    from contextlib import ExitStack
    import bass_rust
    B = g.s.buf
    dst_hbm, dch_hbm = g.t["d_st"], g.t["d_ch"]
    pall_hbm = g.t["partials_all"]
    idsco_hbm = g.t["idsco"]
    onesf = g.t["c_ones_col_f32"]
    onesr = g.t["c_ones_row_f32"]
    with ExitStack() as st:
        sv = g.stack
        g.stack = st
        gds = g.sb("p4_gds", [128, 64], F32)
        gdc = g.sb("p4_gdc", [128, 64], F32)
        gt = g.sb("p4_g", [128, 64], F32)
        pt = g.sb("p4_pt", [32, 2], F32)
        pay = g.sb("p4_pay", [128, 128], F32)
        cvec = g.sb("p4_cvec", [1, 2], F32)
        srow = g.sb("p4_srow", [1, 2], F32)
        s0b = g.sb("p4_s0b", [128, 1], F32)
        lo = g.sb("p4_lo", [1, 4], F32)
        hi = g.sb("p4_hi", [1, 4], F32)
        mid = g.sb("p4_mid", [1, 4], F32)
        trow = g.sb("p4_trow", [1, 4], F32)
        crow = g.sb("p4_crow", [1, 4], F32)
        c2row = g.sb("p4_c2row", [1, 4], F32)
        needr = g.sb("p4_needr", [1, 4], F32)
        cscr = g.sb("p4_cscr", [128, 64], F32)
        cntc = g.sb("p4_cntc", [128, 4], F32)
        m_gt = g.sb("p4_mgt", [128, 64], F32)
        m_eq = g.sb("p4_meq", [128, 64], F32)
        mbf = g.sb("p4_mbf", [128, 64], BF16)
        cum = g.sb("p4_cum", [128, 64], F32)
        csr = g.sb("p4_csr", [1, 64], F32)
        cstb = g.sb("p4_cstb", [64, 1], BF16)
        offr = g.sb("p4_offr", [1, 64], F32)
        dest = g.sb("p4_dest", [128, 64], F32)
        destu = g.sb("p4_destu", [128, 64], U32)
        cst = None
        for r in range(ROWS):
            op_dma(g, "sync", gds[:, r * 16:(r + 1) * 16],
                   dst_hbm[r * 128:(r + 1) * 128, :], writes=[B("p4_gds")])
            op_dma(g, "sync", gdc[:, r * 16:(r + 1) * 16],
                   dch_hbm[r * 128:(r + 1) * 128, :], writes=[B("p4_gdc")])
        op_dma(g, "sync", pt[0:pall_hbm.shape[0], :], pall_hbm[:, :], writes=[B("p4_pt")])
        op_memset(g, cvec[0:1, 0:1], -1.0 / (1024.0 * NTOK), writes=[B("p4_cvec")])
        op_memset(g, cvec[0:1, 1:2], -0.5 / (1024.0 * NTOK), writes=[B("p4_cvec")])
        nprow = g.t["partials_all"].shape[0]
        op_mm(g, g.pb(0, 0, 1, 2), onesf[0:nprow, 0:1], pt[0:nprow, :],
              reads=[B("p4_pt"), B("c_ones_col_f32")], writes=g.pbB(0))
        op_copy(g, srow[0:1, :], g.pb(0, 0, 1, 2), reads=g.pbB(0), writes=[B("p4_srow")])
        op_tt(g, srow[0:1, :], srow[0:1, :], cvec[0:1, :], ALU.mult,
              reads=[B("p4_srow"), B("p4_cvec")], writes=[B("p4_srow")])
        op_red(g, trow[0:1, 0:1], srow[0:1, :], ALU.add,
               reads=[B("p4_srow")], writes=[B("p4_s0")])
        op_mm(g, g.pb(0, 0, 128, 1), onesr[0:1, :], trow[0:1, 0:1],
              reads=[B("p4_s0"), B("c_ones_row_f32")], writes=g.pbB(0))
        op_copy(g, s0b[:, :], g.pb(0, 0, 128, 1), reads=g.pbB(0), writes=[B("p4_s0b")])
        op_ts(g, cscr[:, :], gdc[:, :], 0.5, None, ALU.mult,
              reads=[B("p4_gdc")], writes=[B("p4_cscr")])
        op_tt(g, cscr[:, :], cscr[:, :], gds[:, :], ALU.add,
              reads=[B("p4_cscr"), B("p4_gds")], writes=[B("p4_cscr")])
        op_act(g, gt[:, :], cscr[:, :], AF.Sigmoid, scale=1.0 / 1024.0, bias=s0b[:, 0:1],
               reads=[B("p4_cscr"), B("p4_s0b")], writes=[B("p4_g")])
        op_memset(g, lo[0:1, :], 0.0, writes=[B("p4_lo")])
        op_memset(g, hi[0:1, :], 1.0, writes=[B("p4_hi")])
        for it in range(45):
            op_tt(g, mid[0:1, :], lo[0:1, :], hi[0:1, :], ALU.add,
                  reads=[B("p4_lo"), B("p4_hi")], writes=[B("p4_mid")])
            op_ts(g, mid[0:1, :], mid[0:1, :], 0.5, None, ALU.mult,
                  reads=[B("p4_mid")], writes=[B("p4_mid")])
            op_mm(g, g.pb(1, 0, 128, 4), onesr[0:1, :], mid[0:1, :],
                  reads=[B("p4_mid"), B("c_ones_row_f32")], writes=g.pbB(1))
            for r in range(ROWS):
                op_ts(g, cscr[:, r * 16:(r + 1) * 16], gt[:, r * 16:(r + 1) * 16],
                      g.pb(1, 0, 128, 4)[:, r:r + 1], 0.0, ALU.is_gt, ALU.add,
                      reads=[B("p4_g")] + g.pbB(1), writes=[B("p4_cscr"), B("p4_cntc")],
                      accum=cntc[:, r:r + 1])
            op_mm(g, g.pb(2, 0, 1, 4), onesf[:, :], cntc[:, :],
                  reads=[B("p4_cntc"), B("c_ones_col_f32")], writes=g.pbB(2))
            op_ts(g, crow[0:1, :], g.pb(2, 0, 1, 4), 512.0, None, ALU.is_ge,
                  reads=g.pbB(2), writes=[B("p4_crow")])
            op_tt(g, trow[0:1, :], mid[0:1, :], lo[0:1, :], ALU.subtract,
                  reads=[B("p4_mid"), B("p4_lo")], writes=[B("p4_trow")])
            op_tt(g, trow[0:1, :], trow[0:1, :], crow[0:1, :], ALU.mult,
                  reads=[B("p4_trow"), B("p4_crow")], writes=[B("p4_trow")])
            op_tt(g, lo[0:1, :], lo[0:1, :], trow[0:1, :], ALU.add,
                  reads=[B("p4_lo"), B("p4_trow")], writes=[B("p4_lo")])
            op_ts(g, c2row[0:1, :], crow[0:1, :], -1.0, 1.0, ALU.mult, ALU.add,
                  reads=[B("p4_crow")], writes=[B("p4_c2row")])
            op_tt(g, trow[0:1, :], mid[0:1, :], hi[0:1, :], ALU.subtract,
                  reads=[B("p4_mid"), B("p4_hi")], writes=[B("p4_trow")])
            op_tt(g, trow[0:1, :], trow[0:1, :], c2row[0:1, :], ALU.mult,
                  reads=[B("p4_trow"), B("p4_c2row")], writes=[B("p4_trow")])
            op_tt(g, hi[0:1, :], hi[0:1, :], trow[0:1, :], ALU.add,
                  reads=[B("p4_hi"), B("p4_trow")], writes=[B("p4_hi")])
        op_mm(g, g.pb(1, 0, 128, 4), onesr[0:1, :], hi[0:1, :],
              reads=[B("p4_hi"), B("c_ones_row_f32")], writes=g.pbB(1))
        for r in range(ROWS):
            op_ts(g, m_gt[:, r * 16:(r + 1) * 16], gt[:, r * 16:(r + 1) * 16],
                  g.pb(1, 0, 128, 4)[:, r:r + 1], 0.0, ALU.is_gt, ALU.add,
                  reads=[B("p4_g")] + g.pbB(1), writes=[B("p4_mgt"), B("p4_cntc")],
                  accum=cntc[:, r:r + 1])
            op_ts(g, m_eq[:, r * 16:(r + 1) * 16], gt[:, r * 16:(r + 1) * 16],
                  g.pb(1, 0, 128, 4)[:, r:r + 1], None, ALU.is_equal,
                  reads=[B("p4_g")] + g.pbB(1), writes=[B("p4_meq")])
        op_mm(g, g.pb(2, 0, 1, 4), onesf[:, :], cntc[:, :],
              reads=[B("p4_cntc"), B("c_ones_col_f32")], writes=g.pbB(2))
        op_ts(g, needr[0:1, :], g.pb(2, 0, 1, 4), -1.0, 512.0, ALU.mult, ALU.add,
              reads=g.pbB(2), writes=[B("p4_needr")])
        op_mm(g, g.pb(3, 0, 128, 4), onesr[0:1, :], needr[0:1, :],
              reads=[B("p4_needr"), B("c_ones_row_f32")], writes=g.pbB(3))
        op_copy(g, mbf[:, :], m_eq[:, :], reads=[B("p4_meq")], writes=[B("p4_mbf")])
        cumsum_tokens(g, mbf, B("p4_mbf"), cum, B("p4_cum"), csr, cst, cstb, offr, "p4a_")
        for r in range(ROWS):
            sl = slice(r * 16, (r + 1) * 16)
            op_ts(g, cscr[:, sl], cum[:, sl], g.pb(3, 0, 128, 4)[:, r:r + 1], None,
                  ALU.is_le, reads=[B("p4_cum")] + g.pbB(3), writes=[B("p4_cscr")])
            op_tt(g, cscr[:, sl], cscr[:, sl], m_eq[:, sl], ALU.mult,
                  reads=[B("p4_cscr"), B("p4_meq")], writes=[B("p4_cscr")])
            op_tt(g, m_gt[:, sl], m_gt[:, sl], cscr[:, sl], ALU.add,
                  reads=[B("p4_mgt"), B("p4_cscr")], writes=[B("p4_mgt")])
        op_copy(g, mbf[:, :], m_gt[:, :], reads=[B("p4_mgt")], writes=[B("p4_mbf")])
        cumsum_tokens(g, mbf, B("p4_mbf"), cum, B("p4_cum"), csr, cst, cstb, offr, "p4b_")
        op_ts(g, cum[:, :], cum[:, :], -1.0, None, ALU.add,
              reads=[B("p4_cum")], writes=[B("p4_cum")])
        op_tt(g, cum[:, :], cum[:, :], m_gt[:, :], ALU.mult,
              reads=[B("p4_cum"), B("p4_mgt")], writes=[B("p4_cum")])
        op_ts(g, cscr[:, :], m_gt[:, :], -float(1 << 20), float(1 << 20), ALU.mult, ALU.add,
              reads=[B("p4_mgt")], writes=[B("p4_cscr")])
        op_tt(g, dest[:, :], cum[:, :], cscr[:, :], ALU.add,
              reads=[B("p4_cum"), B("p4_cscr")], writes=[B("p4_dest")])
        op_copy(g, destu[:, :], dest[:, :], reads=[B("p4_dest")], writes=[B("p4_destu")])
        iota_f = g.t["c_iota_f"]
        iorow = g.t["c_iota_row128"]
        msel = g.sb("p4_msel", [128, 128], F32)
        dsb = g.sb("p4_dsb", [128, 1], F32)
        outt = g.sb("p4_outt", [128, 2], F32)
        op_copy(g, bass.AP(pay, 0, [[128, 128], [32, 4], [2, 16]]),
                bass.AP(iota_f.tensor if hasattr(iota_f, "tensor") else iota_f, 0,
                        [[16, 128], [0, 4], [1, 16]]),
                reads=[B("c_iota_f")], writes=[B("p4_pay")])
        op_copy(g, bass.AP(pay, 1, [[128, 128], [32, 4], [2, 16]]), gt[:, :],
                reads=[B("p4_g")], writes=[B("p4_pay")])
        for r in range(ROWS):
            for sb_ in range(4):
                for c in range(16):
                    col = r * 16 + c
                    op_ts(g, dsb[:, 0:1], dest[:, col:col + 1], -128.0 * sb_, None,
                          ALU.add, reads=[B("p4_dest")], writes=[B("p4_dsb")])
                    op_ts(g, msel[:, :], iorow[:, :], dsb[:, 0:1], None, ALU.is_equal,
                          reads=[B("c_iota_row128"), B("p4_dsb")], writes=[B("p4_msel")])
                    op_mm(g, g.pb(0, 0, 128, 2), msel[:, :], pay[:, col * 2: col * 2 + 2],
                          reads=[B("p4_msel"), B("p4_pay")], writes=g.pbB(0),
                          start=(c == 0), stop=(c == 15))
                op_copy(g, outt[:, 0:2], g.pb(0, 0, 128, 2), reads=g.pbB(0),
                        writes=[B("p4_outt")])
                op_dma(g, "sync",
                       idsco_hbm[r * KSEL + sb_ * 128: r * KSEL + (sb_ + 1) * 128, :],
                       outt[:, 0:2], reads=[B("p4_outt")], writes=[B("idsco_hbm")])
        g.stack = sv
    g.s.fence()



P5_CONSTS = ["c_ones_col", "c_ones_row", "c_ident", "c_mask_add", "c_headind",
             "c_sel16", "c_invfreq4_f32", "c_ropesign", "c_ones_col_f32",
             "c_ones_row_f32"]
T2 = 512


def phase5(g):
    from contextlib import ExitStack
    import bass_rust
    B = g.s.buf
    phbm = g.t["processed"]
    p2hbm = g.t["processed2"]
    idsco_hbm = g.t["idsco"]
    ident = g.t["c_ident"]
    hind = g.t["c_headind"]
    sel16 = g.t["c_sel16"]
    onesr_f = g.t["c_ones_row_f32"]
    invf = g.t["c_invfreq4_f32"]
    rsign = g.t["c_ropesign"]
    TWO_PI = 2.0 * math.pi

    with ExitStack() as st:
        sv = g.stack
        g.stack = st
        sel = g.sb("p5_sel", [128, 4 * 1024], F32)
        xa2 = g.sb("p5_xa2", [128, 4 * 1024], F32)
        h2 = g.sb("p5_h2", [128, NCH * T2], BF16)
        y_fm = g.sb("p5_yfm", [128, NCH * T2], BF16)
        CS2 = g.sb("p5_cs", [128, T2], F32)
        SS2 = g.sb("p5_ss", [128, T2], F32)
        angs = g.sb("p5_angs", [128, T2], F32)

        ri = g.sb("p5_ri", [128, T2], I32)
        idxt = g.sb("p5_idxt", [128, 4], U32)
        sct = g.sb("p5_sct", [128, 4], F32)
        ds2 = g.sb("p5_ds2", [128, 2], F32)
        posf = g.sb("p5_posf", [1, T2], F32)
        qa = [g.sb("p5_qa0", [128, T2], BF16), g.sb("p5_qa1", [128, T2], BF16)]
        ka = [g.sb("p5_ka0", [128, T2], BF16), g.sb("p5_ka1", [128, T2], BF16)]
        for _t, _n in ((qa[0], "p5_qa0"), (qa[1], "p5_qa1"), (ka[0], "p5_ka0"), (ka[1], "p5_ka1")):
            op_memset(g, _t[64:128, :], 0.0, writes=[B(_n + "_pad")])
        vt = [g.sb("p5_vt0", [128, 4 * 65], BF16), g.sb("p5_vt1", [128, 4 * 65], BF16)]
        Pb = [g.sb("p5_P0", [128, 2048], BF16)]
        Pb.append(Pb[0])
        den_all = g.sb("p5_den", [16, T2], F32)
        dra = g.sb("p5_dra", [16, T2], BF16)
        rbs = g.sb("p5_rbs", [128, T2], BF16)
        dtmp = g.sb("p5_dtmp", [1, T2], F32)
        Ball = g.sb("p5_Ball", [16, T2], BF16)
        nrm = g.sb("p5_nrm", [16, T2], F32)
        mk = g.sb("p5_mk", [16, 4], F32)
        sq = g.sb("p5_sq", [128, T2], BF16)
        wpair = g.sb("p5_wp0", [128, NCH * 384], BF16)
        wswap = g.sb("p5_wsw", [128, NCH * 256], BF16)
        u1 = g.sb("p5_u1", [128, T2], F32)
        u2 = g.sb("p5_u2", [128, T2], F32)
        rr1 = u2
        stt_ = g.sb("p5_stt", [128, 8], F32)
        hs = g.sb("p5_hs", [128, 1024], F32)
        xin = g.sb("p5_xin", [128, 2 * 1024], F32)
        gated = g.sb("p5_gated", [128, 32 * T2], BF16)
        wgu = g.sb("p5_wgu", [128, NCH * 1024], BF16)
        wgu_b = g.sb("p5_wgub", [128, NCH * 1024], BF16)
        wdb = [g.sb("p5_wd0", [128, 1024], BF16), g.sb("p5_wd1", [128, 1024], BF16)]
        sg_ = g.sb("p5_sg", [128, T2], BF16)
        ln12b = load_vec_bcast(g, g.t["L0_ffn_ln1_2"], "p5_ln1")
        ln22b = load_vec_bcast(g, g.t["L0_ffn_ln2_2"], "p5_ln2")
        breg = g.top_stack.enter_context(g.nc.gpsimd.register("p5bnd"))
        g.nc.gpsimd.reg_mov(breg, T - 1)

        wq_h, wk_h, wv_h = g.t["L0_ffn_wq2"], g.t["L0_ffn_wk2"], g.t["L0_ffn_wv2"]

        for r in range(ROWS):
            rb = r * T
            for tb in range(16):
                sl = tb % 2
                xi = xin[:, sl * 1024:(sl + 1) * 1024]
                op_dma(g, "sync", xi, phbm[rb + tb * 128: rb + (tb + 1) * 128, :],
                       writes=[B(f"p5_xin{sl}")])
                op_dma(g, "sync", p2hbm[rb + tb * 128: rb + (tb + 1) * 128, :], xi,
                       reads=[B(f"p5_xin{sl}")], writes=[B("p5_p2hbm")])
            for gg in range(4):
                op_dma(g, "sync", ds2[:, :],
                       idsco_hbm[r * KSEL + gg * 128: r * KSEL + (gg + 1) * 128, :],
                       writes=[B("p5_ds2")])
                op_copy(g, idxt[:, gg:gg + 1], ds2[:, 0:1],
                        reads=[B("p5_ds2")], writes=[B("p5_idxt")])
                op_copy(g, sct[:, gg:gg + 1], ds2[:, 1:2],
                        reads=[B("p5_ds2")], writes=[B("p5_sct")])
                op_transpose(g, g.pb(1, 0, 1, 128), ds2[:, 0:1], ident[:, :],
                             reads=[B("p5_ds2"), B("c_ident")], writes=g.pbB(1))
                op_copy(g, posf[0:1, gg * 128:(gg + 1) * 128], g.pb(1, 0, 1, 128),
                        reads=g.pbB(1), writes=[B("p5_posf")])
            op_mm(g, g.pb(0), invf[0:1, :], posf[0:1, :],
                  reads=[B("p5_posf"), B("c_invfreq4_f32")], writes=g.pbB(0))
            op_copy(g, angs[:, :], g.pb(0), reads=g.pbB(0), writes=[B("p5_angs")])
            for (tbl, tblB, shift) in ((SS2, "p5_ss", 0.0), (CS2, "p5_cs", math.pi / 2)):
                src = angs
                if shift:
                    op_ts(g, rr1[:, :], angs[:, :], shift, None, ALU.add,
                          reads=[B("p5_angs")], writes=[B("p5_rr1")])
                    src, srcB = rr1, "p5_rr1"
                else:
                    srcB = "p5_angs"
                op_ts(g, tbl[:, :], src[:, :], 1.0 / TWO_PI, None, ALU.mult,
                      reads=[B(srcB)], writes=[B(tblB)])
                op_copy(g, ri[:, :], tbl[:, :], reads=[B(tblB)], writes=[B("p5_ri")])
                op_copy(g, tbl[:, :], ri[:, :], reads=[B("p5_ri")], writes=[B(tblB)])
                op_stt(g, tbl[:, :], tbl[:, :], -TWO_PI, src[:, :], ALU.mult, ALU.add,
                       reads=[B(tblB), B(srcB)], writes=[B(tblB)])
                op_act(g, tbl[:, :], tbl[:, :], AF.Sin, reads=[B(tblB)], writes=[B(tblB)])
            op_ts(g, SS2[:, :], SS2[:, :], rsign[:, 0:1], None, ALU.mult,
                  reads=[B("p5_ss"), B("c_ropesign")], writes=[B("p5_ss")])
            for gg in range(4):
                def f_g(e, o=sel[:, gg * 1024:(gg + 1) * 1024],
                        off=bass_rust.IndirectOffsetOnAxis(axis=0, ap=idxt[:, gg:gg + 1]),
                        eo=rb * 1024):
                    return e.indirect_dma_start(out=o, out_offset=None, in_=phbm[:, :],
                                                in_offset=off, element_offset=eo)
                g.s.op("gpsimd", f_g, reads=[B("p5_idxt"), B("p2_phbm")],
                       writes=[B("p5_sel")], dma=True)
            for gg in range(4):
                s_t = sel[:, gg * 1024:(gg + 1) * 1024]
                op_act(g, hs[:, :], s_t, AF.Square, accum=stt_[:, 0:1],
                       reads=[B("p5_sel")], writes=[B("p5_hs"), B("p5_st0")])
                op_act(g, stt_[:, 1:2], stt_[:, 0:1], AF.Ln, scale=1.0 / 1024.0, bias=1e-6,
                       reads=[B("p5_st0")], writes=[B("p5_st1")])
                op_act(g, stt_[:, 2:3], stt_[:, 1:2], AF.Exp, scale=-0.5,
                       reads=[B("p5_st1")], writes=[B("p5_st2")])
                op_stt(g, hs[:, :], s_t, stt_[:, 2:3], ln12b[:, :], ALU.mult, ALU.mult,
                       reads=[B("p5_sel"), B("p5_st2"), B("p5_ln1")], writes=[B("p5_hs")])
                for c in range(NCH):
                    po = g.pall[:, 7 * 512 + (c % 4) * 128: 7 * 512 + (c % 4 + 1) * 128]
                    op_transpose(g, po, hs[:, c * 128:(c + 1) * 128], ident[:, :],
                                 reads=[B("p5_hs"), B("c_ident")], writes=g.pbB(7))
                    op_copy(g, h2[:, c * T2 + gg * 128: c * T2 + (gg + 1) * 128], po,
                            reads=g.pbB(7), writes=[B(f"p5_h2{c}")])
            h2B = [B(f"p5_h2{c}") for c in range(NCH)]
            for hc in range(NCH):
                wp, wpB = wpair, B("p5_wp0")
                op_dma(g, "sync", wp[:, :],
                       g.t["pk_qkv2"][hc * 128:(hc + 1) * 128, :], writes=[wpB])
                for (woff, soff) in ((0, 0), (128, 128)):
                    op_copy(g, bass.AP(wswap, soff, [[NCH * 256, 128], [256, NCH], [64, 2], [1, 32]]),
                            bass.AP(wp, woff + 32, [[NCH * 384, 128], [384, NCH], [64, 2], [1, 32]]),
                            reads=[wpB], writes=[B("p5_wsw")])
                    op_copy(g, bass.AP(wswap, soff + 32, [[NCH * 256, 128], [256, NCH], [64, 2], [1, 32]]),
                            bass.AP(wp, woff, [[NCH * 384, 128], [384, NCH], [64, 2], [1, 32]]),
                            reads=[wpB], writes=[B("p5_wsw")])
                he, ho = 2 * hc, 2 * hc + 1
                qa_e, qa_o = qa
                ka_e, ka_o = ka
                op_memset(g, ka_e[64:65, :], 1.0, writes=[B("p5_ka0_hi")])
                op_memset(g, ka_o[64:65, :], 1.0, writes=[B("p5_ka1_hi")])
                op_memset(g, mk[:, 0:1], 0.0, writes=[B("p5_mk")])
                for (isq, dst_e, dst_o, deB, doB, wcol) in (
                        (False, ka_e, ka_o, B("p5_ka0_lo"), B("p5_ka1_lo"), 128),
                        (True, qa_e, qa_o, B("p5_qa0_lo"), B("p5_qa1_lo"), 0)):
                    pA, pAB = g.pb(6), g.pbB(6)
                    pB_, pBB = g.pb(7), g.pbB(7)
                    items = [(pA, bass.AP(wp, kc * 384 + wcol, [[NCH * 384, 128], [1, 128]]),
                              h2[:, kc * T2: (kc + 1) * T2],
                              kc == 0, kc == NCH - 1) for kc in range(NCH)]
                    op_mm_group(g, items, reads=[wpB] + h2B, writes=pAB)
                    items = [(pB_, bass.AP(wswap, kc * 256 + wcol, [[NCH * 256, 128], [1, 128]]),
                              h2[:, kc * T2: (kc + 1) * T2],
                              kc == 0, kc == NCH - 1) for kc in range(NCH)]
                    op_mm_group(g, items, reads=[B("p5_wsw")] + h2B, writes=pBB)
                    op_tt(g, u1[:, :], pA, CS2[:, :], ALU.mult,
                          reads=pAB + [B("p5_cs")], writes=[B("p5_u1")])
                    op_tt(g, u2[:, :], pB_, SS2[:, :], ALU.mult,
                          reads=pBB + [B("p5_ss")], writes=[B("p5_u2")])
                    op_tt(g, dst_e[0:64, :], u1[0:64, :], u2[0:64, :], ALU.add,
                          reads=[B("p5_u1"), B("p5_u2")], writes=[deB])
                    op_tt(g, dst_o[0:64, :], u1[64:128, :], u2[64:128, :], ALU.add,
                          reads=[B("p5_u1"), B("p5_u2")], writes=[doB])
                    sqs = sq[:, 0:T2]
                    sqB = B("p5_sq0")
                    op_act(g, sqs, pA, AF.Square, reads=pAB, writes=[sqB])
                    op_mm(g, g.pb(5, 0, 16, T2), hind[:, hc * 16:(hc + 1) * 16], sqs,
                          reads=[sqB, B("c_headind")], writes=g.pbB(5))
                    op_act(g, nrm[:, :], g.pb(5, 0, 16, T2), AF.Sqrt,
                           reads=g.pbB(5), writes=[B("p5_nrm")])
                    if isq:
                        op_ts(g, Ball[:, :], nrm[:, :], mk[:, 1:2], 8.0 * SHIFT_MARGIN,
                              ALU.mult, ALU.add,
                              reads=[B("p5_nrm"), B("p5_mkn")], writes=[B("p5_Ball")])
                    else:
                        op_red(g, mk[:, 2:3], nrm[:, :], ALU.max,
                               reads=[B("p5_nrm")], writes=[B("p5_mk2")])
                        op_tt(g, mk[:, 0:1], mk[:, 0:1], mk[:, 2:3], ALU.max,
                              reads=[B("p5_mk2"), B("p5_mk")], writes=[B("p5_mk")])
                        op_ts(g, mk[:, 1:2], mk[:, 0:1], -1.0, None, ALU.mult,
                              reads=[B("p5_mk")], writes=[B("p5_mkn")])
                op_dma(g, "sync", qa_e[64:65, :], Ball[he:he + 1, :],
                       reads=[B("p5_Ball")], writes=[B("p5_qa0_hi")])
                op_dma(g, "sync", qa_o[64:65, :], Ball[ho:ho + 1, :],
                       reads=[B("p5_Ball")], writes=[B("p5_qa1_hi")])
                if hc == 0:
                    op_memset(g, vt[0][:, :], 1.0, writes=[B("p5_vt0")])
                    op_memset(g, vt[1][:, :], 1.0, writes=[B("p5_vt1")])
                for tb in range(4):
                    pA, pAB = g.pb(6 + tb % 2), g.pbB(6 + tb % 2)
                    items = [(pA[:, 0:128],
                              h2[:, kc * T2 + tb * 128: kc * T2 + (tb + 1) * 128],
                              bass.AP(wp, kc * 384 + 256, [[NCH * 384, 128], [1, 128]]),
                              kc == 0, kc == NCH - 1) for kc in range(NCH)]
                    op_mm_group(g, items, reads=[wpB] + h2B, writes=pAB)
                    op_copy(g, vt[0][:, tb * 65: tb * 65 + 64], pA[:, 0:64],
                            reads=pAB, writes=[B("p5_vt0")])
                    op_copy(g, vt[1][:, tb * 65: tb * 65 + 64], pA[:, 64:128],
                            reads=pAB, writes=[B("p5_vt1")])
                for (h, qa_h, ka_h, vt_h, qaB, kaB, vtB) in (
                        (he, qa_e, ka_e, vt[0], ("p5_qa0_lo", "p5_qa0_hi"),
                         ("p5_ka0_lo", "p5_ka0_hi"), "p5_vt0"),
                        (ho, qa_o, ka_o, vt[1], ("p5_qa1_lo", "p5_qa1_hi"),
                         ("p5_ka1_lo", "p5_ka1_hi"), "p5_vt1")):
                    hr = (h % 2) * 64
                    yb = 4
                    pY = g.pall[0:65, yb * 512:(yb + 1) * 512]
                    pYB = g.pbB(yb)
                    Pt = Pb[0]
                    PtB = B(f"p5_P0")
                    items = []
                    for j in range(4):
                        items.append((g.pall[:, j * 512 + 128 * j: j * 512 + 512],
                                      ka_h[:, j * 128: (j + 1) * 128],
                                      qa_h[:, 128 * j: T2], True, True))
                    op_mm_group(g, items, reads=[B(qaB[0]), B(qaB[1]), B(qaB[0].replace("_lo", "_pad")),
                                                 B(kaB[0]), B(kaB[1]), B(kaB[0].replace("_lo", "_pad"))],
                                writes=g.pbB(0, 3))
                    v = sview(g.pall, 0, [[4096, 128], [640, 4], [1, 128]])
                    m = sview(g.t["c_mask_add"], 0, [[128, 128], [0, 4], [1, 128]])
                    op_tt(g, v, v, m, ALU.add,
                          reads=[B("c_mask_add")] + g.pbB(0, 3), writes=g.pbB(0, 3))
                    for j in range(4):
                        PsB = B(f"{PtB.name}_s{j}")
                        op_act(g, Pt[:, j * 512:(j + 1) * 512],
                               g.pall[:, j * 512:(j + 1) * 512], AF.Exp, scale=0.125,
                               reads=g.pbB(j, j), writes=[PsB])
                        if j > 0:
                            op_memset(g, Pt[:, j * 512: j * 512 + 128 * j], 0.0,
                                      writes=[PsB], reads=[PsB])
                        op_mm_group(g, [(pY, vt_h[:, j * 65: j * 65 + 65],
                                         Pt[:, j * 512: (j + 1) * 512], j == 0, j == 3)],
                                    reads=[PsB, B(vtB)], writes=pYB)
                    op_copy(g, dtmp[0:1, :], g.pall[64:65, yb * 512:(yb + 1) * 512],
                            reads=pYB, writes=[B("p5_dtmp")])
                    op_dma(g, "sync", den_all[h:h + 1, :], dtmp[0:1, :],
                           reads=[B("p5_dtmp")], writes=[B("p5_denall")])
                    op_act(g, y_fm[hr:hr + 64, hc * T2: (hc + 1) * T2],
                           g.pall[0:64, yb * 512:(yb + 1) * 512], AF.Identity,
                           reads=pYB, writes=[B(f"p5_yfm{hc}")])
            op_recip(g, dra[:, :], den_all[:, :], reads=[B("p5_denall")], writes=[B("p5_dra")])
            for hc in range(NCH):
                he, ho = 2 * hc, 2 * hc + 1
                for (h_, p0, p1_) in ((he, 0, 64), (ho, 64, 128)):
                    op_mm(g, g.pall[p0:p1_, 3072:3584], sel16[:, h_ * 64:(h_ + 1) * 64],
                          dra[:, :], reads=[B("p5_dra"), B("c_sel16")], writes=g.pbB(6))
                op_act(g, rbs[:, :], g.pall[:, 3072:3584], AF.Identity,
                       reads=g.pbB(6), writes=[B("p5_rbs")])
                op_tt(g, y_fm[:, hc * T2: (hc + 1) * T2],
                      y_fm[:, hc * T2: (hc + 1) * T2], rbs[:, :], ALU.mult,
                      reads=[B(f"p5_yfm{hc}"), B("p5_rbs")], writes=[B(f"p5_yfm{hc}")])
            op_dma(g, "sync", wgu[:, :], g.t["pk_wo2"][0:128, :],
                   writes=[B("p5_wgu")])
            yfmB = [B(f"p5_yfm{c}") for c in range(NCH)]
            for tb in range(4):
                for ob in range(2):
                    pq = g.pb(ob)
                    pqB = g.pbB(ob)
                    items = [(pq, y_fm[:, kc * T2 + tb * 128: kc * T2 + (tb + 1) * 128],
                              wgu[:, kc * 1024 + ob * 512: kc * 1024 + (ob + 1) * 512],
                              kc == 0, kc == NCH - 1) for kc in range(NCH)]
                    op_mm_group(g, items, reads=[B("p5_wgu")] + yfmB, writes=pqB)
                    op_tt(g, xa2[:, tb * 1024 + ob * 512: tb * 1024 + (ob + 1) * 512],
                          pq, sel[:, tb * 1024 + ob * 512: tb * 1024 + (ob + 1) * 512],
                          ALU.add, reads=pqB + [B("p5_sel")], writes=[B("p5_xa2")])
                xat = xa2[:, tb * 1024:(tb + 1) * 1024]
                op_act(g, hs[:, :], xat, AF.Square, accum=stt_[:, 0:1],
                       reads=[B("p5_xa2")], writes=[B("p5_hs"), B("p5_st0")])
                op_act(g, stt_[:, 1:2], stt_[:, 0:1], AF.Ln, scale=1.0 / 1024.0, bias=1e-6,
                       reads=[B("p5_st0")], writes=[B("p5_st1")])
                op_act(g, stt_[:, 2:3], stt_[:, 1:2], AF.Exp, scale=-0.5,
                       reads=[B("p5_st1")], writes=[B("p5_st2")])
                op_stt(g, hs[:, :], xat, stt_[:, 2:3], ln22b[:, :], ALU.mult, ALU.mult,
                       reads=[B("p5_xa2"), B("p5_st2"), B("p5_ln2")], writes=[B("p5_hs")])
                for c in range(NCH):
                    po = g.pall[:, 7 * 512 + (c % 4) * 128: 7 * 512 + (c % 4 + 1) * 128]
                    op_transpose(g, po, hs[:, c * 128:(c + 1) * 128], ident[:, :],
                                 reads=[B("p5_hs"), B("c_ident")], writes=g.pbB(7))
                    op_copy(g, h2[:, c * T2 + tb * 128: c * T2 + (tb + 1) * 128], po,
                            reads=g.pbB(7), writes=[B(f"p5_h2{c}")])
            wg_h, wu_h, wd_h = g.t["L0_ffn_wg2"], g.t["L0_ffn_wu2"], g.t["L0_ffn_wd2"]
            gatB = [B(f"p5_gated{gc}") for gc in range(32)]
            for blk in range(8):
                wb_ = (wgu, wgu_b)[blk % 2]
                wbB = B(("p5_wgu", "p5_wgub")[blk % 2])
                op_dma(g, "sync", wb_[:, :],
                       g.t["pk_f2gu"][blk * 128:(blk + 1) * 128, :],
                       writes=[wbB])
                for sub in range(4):
                    gc = blk * 4 + sub
                    pG, pGB = g.pb(0 + sub % 2), g.pbB(0 + sub % 2)
                    pU, pUB = g.pb(2 + sub % 2), g.pbB(2 + sub % 2)
                    items = [(pG[:, 0:T2],
                              bass.AP(wb_, kc * 1024 + sub * 128, [[NCH * 1024, 128], [1, 128]]),
                              h2[:, kc * T2: (kc + 1) * T2],
                              kc == 0, kc == NCH - 1) for kc in range(NCH)]
                    op_mm_group(g, items, reads=[wbB] + h2B, writes=pGB)
                    items = [(pU[:, 0:T2],
                              bass.AP(wb_, kc * 1024 + 512 + sub * 128, [[NCH * 1024, 128], [1, 128]]),
                              h2[:, kc * T2: (kc + 1) * T2],
                              kc == 0, kc == NCH - 1) for kc in range(NCH)]
                    op_mm_group(g, items, reads=[wbB] + h2B, writes=pUB)
                    op_act(g, sg_[:, :], pG[:, 0:T2], AF.Silu, reads=pGB, writes=[B("p5_sg")])
                    op_tt(g, gated[:, gc * T2:(gc + 1) * T2], sg_[:, :], pU[:, 0:T2],
                          ALU.mult, reads=[B("p5_sg")] + pUB, writes=[gatB[gc]])
            for gc in range(32):
                wd_t = wdb[gc % 2]
                wdB = B(f"p5_wd{gc % 2}")
                op_dma(g, "sync", wd_t[:, :], wd_h[gc * 128:(gc + 1) * 128, :], writes=[wdB])
                items = []
                for tb in range(4):
                    for ob in range(2):
                        items.append((g.pb(tb * 2 + ob),
                                      gated[:, gc * T2 + tb * 128: gc * T2 + (tb + 1) * 128],
                                      wd_t[:, ob * 512:(ob + 1) * 512],
                                      gc == 0, gc == 31))
                op_mm_group(g, items, reads=[wdB, gatB[gc]], writes=g.pbB(0, 7))
            for tb in range(4):
                sl = tb % 2
                yt = xin[:, sl * 1024:(sl + 1) * 1024]
                for ob in range(2):
                    op_tt(g, yt[:, ob * 512:(ob + 1) * 512], g.pb(tb * 2 + ob),
                          xa2[:, tb * 1024 + ob * 512: tb * 1024 + (ob + 1) * 512], ALU.add,
                          reads=g.pbB(tb * 2 + ob) + [B("p5_xa2")], writes=[B(f"p5_xin{sl}")])
                op_tt(g, yt[:, :], yt[:, :], sel[:, tb * 1024:(tb + 1) * 1024], ALU.subtract,
                      reads=[B(f"p5_xin{sl}"), B("p5_sel")], writes=[B(f"p5_xin{sl}")])
                op_stt(g, yt[:, :], yt[:, :], sct[:, tb:tb + 1],
                       sel[:, tb * 1024:(tb + 1) * 1024], ALU.mult, ALU.add,
                       reads=[B(f"p5_xin{sl}"), B("p5_sct"), B("p5_sel")],
                       writes=[B(f"p5_xin{sl}")])
                def f_s(e, i=yt,
                        off=bass_rust.IndirectOffsetOnAxis(axis=0, ap=idxt[:, tb:tb + 1]),
                        eo=rb * 1024, breg=breg):
                    return e.indirect_dma_start(out=p2hbm[:, :], out_offset=off, in_=i,
                                                in_offset=None, element_offset=eo,
                                                bounds_check=breg, oob_is_err=False)
                g.s.op("gpsimd", f_s, reads=[B(f"p5_xin{sl}"), B("p5_idxt"), B("p5_p2hbm")],
                       writes=[B("p5_p2hbm")], dma=True)
        g.stack = sv
    g.s.fence()



P678_CONSTS = ["c_ones_col", "c_ones_row", "c_ident", "c_mask_mul", "c_sel16",
               "c_ones_col_f32", "c_ones_row_f32", "c_m8", "c_p8", "c_q8"]


def phase678(g):
    from contextlib import ExitStack
    B = g.s.buf
    x1, p2h = g.t["x1"], g.t["processed2"]
    x2h = g.t["p67_x2"]
    x3h = g.t["p67_x3"]
    outh = g.t["out"]
    ident = g.t["c_ident"]
    mmul = g.t["c_mask_mul"]
    sel16 = g.t["c_sel16"]
    ones = g.t["c_ones_col"]
    onesr = g.t["c_ones_row"]

    with ExitStack() as st:
        sv = g.stack
        g.stack = st
        wti = g.sb("p8_wti", [128, NCH * 1024], BF16)
        rprime = g.sb("p8_rp", [128, NCH * 8], BF16)
        laA = g.sb("p8_laA", [128, NCH * 8], BF16)
        ballB = g.sb("p8_ballB", [64, 1024], BF16)
        m8 = g.t["c_m8"]
        bpcol = g.sb("p8_bpcol", [8, 1], F32)
        brow = g.sb("p8_brow", [8, 2], BF16)
        bbb = load_vec_bcast(g, g.t["L1_ffn_base_b"], "p8_bb")
        _prep = ExitStack()
        _sv0 = g.stack
        g.stack = _prep
        wld = g.sb("p8_wld", [128, 1024], F32)
        for oc in range(NCH):
            op_dma(g, "sync", wld[:, :],
                   g.t["L1_ffn_base_w"][oc * 128:(oc + 1) * 128, :], writes=[B("p8_wld")])
            for kc in range(NCH):
                po = g.pall[:, 7 * 512 + (kc % 4) * 128: 7 * 512 + (kc % 4 + 1) * 128]
                op_transpose(g, po, wld[:, kc * 128:(kc + 1) * 128], ident[:, :],
                             reads=[B("p8_wld"), B("c_ident")], writes=g.pbB(7))
                dst = wti[:, kc * 1024 + oc * 128: kc * 1024 + (oc + 1) * 128]
                if kc == oc:
                    op_tt(g, dst, po, ident[:, :], ALU.add,
                          reads=g.pbB(7) + [B("c_ident")], writes=[B("p8_wti")])
                else:
                    op_copy(g, dst, po, reads=g.pbB(7), writes=[B("p8_wti")])
        rsb = g.sb("p8_rsb", [8, 1024], BF16)
        op_dma(g, "gpsimd", rsb[:, :], g.t["L1_ffn_router_w"][:, :], writes=[B("p8_rsb")])
        op_dma(g, "gpsimd", bpcol[:, 0:1],
               g.t["L1_ffn_router_b"].rearrange("(a b) -> a b", b=1), writes=[B("p8_bp")])
        for kc in range(NCH):
            op_mm(g, g.pb(6, 0, 128, 8), rsb[:, kc * 128:(kc + 1) * 128], m8[:, :],
                  reads=[B("p8_rsb"), B("c_m8")], writes=g.pbB(6))
            op_copy(g, rprime[:, kc * 8:(kc + 1) * 8], g.pb(6, 0, 128, 8),
                    reads=g.pbB(6), writes=[B("p8_rp")])
        op_copy(g, brow[:, 0:1], bpcol[:, 0:1], reads=[B("p8_bp")], writes=[B("p8_brow")])
        op_mm(g, g.pb(6, 0, 1, 8), brow[:, 0:1], m8[:, :],
              reads=[B("p8_brow"), B("c_m8")], writes=g.pbB(6))
        bprow = g.sb("p8_bprow", [1, 8], F32)
        op_copy(g, bprow[0:1, :], g.pb(6, 0, 1, 8), reads=g.pbB(6), writes=[B("p8_bprow")])
        op_transpose(g, g.pb(6, 0, 8, 1), bprow[0:1, :], ident[0:1, 0:1],
                     reads=[B("p8_bprow"), B("c_ident")], writes=g.pbB(6))
        op_copy(g, bpcol[:, 0:1], g.pb(6, 0, 8, 1), reads=g.pbB(6), writes=[B("p8_bp2")])
        op_dma(g, "gpsimd", laA[:, :],
               g.t["L1_ffn_lora_A"].rearrange("(a p) r -> p a r", p=128), writes=[B("p8_laA")])
        op_dma(g, "gpsimd", ballB[:, :],
               g.t["L1_ffn_lora_B"].rearrange("e r o -> (e r) o"), writes=[B("p8_ballB")])
        _prep.close()
        g.stack = _sv0
        g.s.fence()

        sfm = g.sb("p67_sfm", [128, NCH * T], BF16)
        y_fm = g.sb("p67_yfm", [128, NCH * T], BF16)
        xin = g.sb("p67_xin", [128, 2 * 1024], F32)
        xin2 = g.sb("p67_xin2", [128, 1024], F32)
        hs = g.sb("p67_hs", [128, 2 * 1024], F32)
        stt_ = g.sb("p67_stt", [128, 8], F32)
        sc = g.sb("p67_sc", [128, 1024], F32)
        qp = g.sb("p67_qp", [128, T], BF16)
        kp = g.sb("p67_kp", [128, T], BF16)
        ktok = [g.sb("p67_kt0", [128, 16 * 64], BF16), g.sb("p67_kt1", [128, 16 * 64], BF16)]
        vtok = [g.sb("p67_vt0", [128, 16 * 65], BF16), g.sb("p67_vt1", [128, 16 * 65], BF16)]
        Ps = [g.sb("p67_P0", [128, 2048], BF16), g.sb("p67_P1", [128, 2048], BF16)]
        ssb = g.sb("p67_Ssb", [128, 65], BF16)
        Ssb = [ssb[0:64, :], ssb[64:128, :]]
        den_all = g.sb("p67_den", [16, T], BF16)
        den_f32 = g.sb("p67_denf", [16, 512], F32)
        dra = g.sb("p67_dra", [16, T], BF16)
        rbs = g.sb("p67_rbs", [128, 1024], BF16)
        dtmp = g.sb("p67_dtmp", [1, 2 * 512], BF16)
        wpair = g.sb("p67_wp", [128, NCH * 384], BF16)
        u1 = g.sb("p67_u1", [128, 512], F32)
        wq_h, wk_h, wv_h = g.t["L1_attn_Wq"], g.t["L1_attn_Wk"], g.t["L1_attn_Wv"]
        sfmB = [B(f"p67_sfm{c}") for c in range(NCH)]

        for r in range(ROWS):
            rb = r * T
            for tb in range(16):
                sl = tb % 2
                xi = xin[:, sl * 1024:(sl + 1) * 1024]
                x2_ = xin2[:, 0:1024]
                op_dma(g, "sync", xi, x1[rb + tb * 128: rb + (tb + 1) * 128, :],
                       writes=[B(f"p67_xin{sl}")])
                op_dma(g, "sync", x2_, p2h[rb + tb * 128: rb + (tb + 1) * 128, :],
                       writes=[B("p67_xin2")], reads=[B("p5_p2hbm")])
                hst = hs[:, sl * 1024:(sl + 1) * 1024]
                op_tt(g, hst, xi, x2_, ALU.add,
                      reads=[B(f"p67_xin{sl}"), B("p67_xin2")], writes=[B(f"p67_hs{sl}")])
                emit_ln_apply(g, hs[:, sl * 1024:(sl + 1) * 1024], sc, stt_,
                              hs[:, sl * 1024:(sl + 1) * 1024],
                              f"p67_hs{sl}", "p67_sc", "p67_st", f"p67_hs{sl}")
                op_dma(g, "sync", x2h[rb + tb * 128: rb + (tb + 1) * 128, :], hst,
                       reads=[B(f"p67_hs{sl}")], writes=[B("p67_x2hbm")])
                for c in range(NCH):
                    po = g.pall[:, 7 * 512 + (c % 4) * 128: 7 * 512 + (c % 4 + 1) * 128]
                    op_transpose(g, po, hst[:, c * 128:(c + 1) * 128], ident[:, :],
                                 reads=[B(f"p67_hs{sl}"), B("c_ident")], writes=g.pbB(7))
                    op_copy(g, sfm[:, c * T + tb * 128: c * T + (tb + 1) * 128], po,
                            reads=g.pbB(7), writes=[sfmB[c]])
            for hc in range(NCH):
                wpB = B("p67_wp")
                op_dma(g, "sync", wpair[:, :],
                       g.t["pk_qkvL1"][hc * 128:(hc + 1) * 128, :], writes=[wpB])
                he, ho = 2 * hc, 2 * hc + 1
                for (dst, dstB, wcol) in ((qp, B("p67_qp"), 0), (kp, B("p67_kp"), 128)):
                    for tt in range(4):
                        pA, pAB = g.pb(4), g.pbB(4)
                        items = [(pA, bass.AP(wpair, kc * 384 + wcol, [[NCH * 384, 128], [1, 128]]),
                                  sfm[:, kc * T + tt * 512: kc * T + (tt + 1) * 512],
                                  kc == 0, kc == NCH - 1) for kc in range(NCH)]
                        op_mm_group(g, items, reads=[wpB] + sfmB, writes=pAB)
                        op_ts(g, u1[:, :], pA, 0.0, None, ALU.min,
                              reads=pAB, writes=[B("p67_u1")])
                        op_act(g, u1[:, :], u1[:, :], AF.Exp,
                               reads=[B("p67_u1")], writes=[B("p67_u1")])
                        op_stt(g, dst[:, tt * 512:(tt + 1) * 512], pA, 0.0, u1[:, :],
                               ALU.max, ALU.add, reads=pAB + [B("p67_u1")], writes=[dstB])
                if hc == 0:
                    op_memset(g, vtok[0][:, :], 1.0, writes=[B("p67_vt0")])
                    op_memset(g, vtok[1][:, :], 1.0, writes=[B("p67_vt1")])
                for tb in range(16):
                    pA, pAB = g.pb(4), g.pbB(4)
                    items = [(pA[:, 0:128],
                              sfm[:, kc * T + tb * 128: kc * T + (tb + 1) * 128],
                              bass.AP(wpair, kc * 384 + 128, [[NCH * 384, 128], [1, 128]]),
                              kc == 0, kc == NCH - 1) for kc in range(NCH)]
                    op_mm_group(g, items, reads=[wpB] + sfmB, writes=pAB)
                    op_ts(g, u1[:, 0:128], pA[:, 0:128], 0.0, None, ALU.min,
                          reads=pAB, writes=[B("p67_u1")])
                    op_act(g, u1[:, 0:128], u1[:, 0:128], AF.Exp,
                           reads=[B("p67_u1")], writes=[B("p67_u1")])
                    op_stt(g, u1[:, 128:256], pA[:, 0:128], 0.0, u1[:, 0:128],
                           ALU.max, ALU.add, reads=pAB + [B("p67_u1")], writes=[B("p67_u1b")])
                    op_copy(g, ktok[0][:, tb * 64: tb * 64 + 64], u1[:, 128:192],
                            reads=[B("p67_u1b")], writes=[B("p67_kt0")])
                    op_copy(g, ktok[1][:, tb * 64: tb * 64 + 64], u1[:, 192:256],
                            reads=[B("p67_u1b")], writes=[B("p67_kt1")])
                    pV, pVB = g.pb(5), g.pbB(5)
                    items = [(pV[:, 0:128],
                              sfm[:, kc * T + tb * 128: kc * T + (tb + 1) * 128],
                              bass.AP(wpair, kc * 384 + 256, [[NCH * 384, 128], [1, 128]]),
                              kc == 0, kc == NCH - 1) for kc in range(NCH)]
                    op_mm_group(g, items, reads=[wpB] + sfmB, writes=pVB)
                    op_copy(g, vtok[0][:, tb * 65: tb * 65 + 64], pV[:, 0:64],
                            reads=pVB, writes=[B("p67_vt0")])
                    op_copy(g, vtok[1][:, tb * 65: tb * 65 + 64], pV[:, 64:128],
                            reads=pVB, writes=[B("p67_vt1")])
                heads = ((0, he, 0, ktok[0], vtok[0], Ssb[0], B("p67_S0"), 6),
                         (1, ho, 64, ktok[1], vtok[1], Ssb[1], B("p67_S1"), 7))
                seq = [(C, hd) for C in range(4) for hd in heads]

                def emit_A(i):
                    C, (idx_h, h_, hr, kt, vt_, SsbT, SsbB, sbank) = seq[i]
                    qpB, kpB = B("p67_qp"), B("p67_kp")
                    items = []
                    for j in range(4):
                        kti = C * 4 + j
                        items.append((g.pall[:, j * 512 + 128 * j: j * 512 + 512],
                                      kp[hr:hr + 64, kti * 128: (kti + 1) * 128],
                                      qp[hr:hr + 64, C * 512 + 128 * j: (C + 1) * 512],
                                      True, True))
                    op_mm_group(g, items, reads=[kpB, qpB], writes=g.pbB(0, 3))

                def emit_evac(i):
                    Pt = Ps[i % 2]
                    PtB = B(f"p67_P{i % 2}")
                    for j in range(4):
                        op_tt(g, Pt[:, j * 512 + 128 * j: j * 512 + 128 * j + 128],
                              g.pall[:, j * 512 + 128 * j: j * 512 + 128 * j + 128],
                              g.t["c_mask_mul"][:, :], ALU.mult,
                              reads=g.pbB(j, j) + [B("c_mask_mul")], writes=[PtB])
                        if j < 3:
                            op_act(g, Pt[:, j * 512 + 128 * j + 128: (j + 1) * 512],
                                   g.pall[:, j * 512 + 128 * j + 128: (j + 1) * 512],
                                   AF.Identity, reads=g.pbB(j, j), writes=[PtB])
                        if j > 0:
                            op_memset(g, Pt[:, j * 512: j * 512 + 128 * j], 0.0,
                                      writes=[PtB], reads=[PtB])

                def emit_tail(i):
                    C, (idx_h, h_, hr, kt, vt_, SsbT, SsbB, sbank) = seq[i]
                    ktB, vtB = B(f"p67_kt{idx_h}"), B(f"p67_vt{idx_h}")
                    qpB = B("p67_qp")
                    Pt = Ps[i % 2]
                    PtB = B(f"p67_P{i % 2}")
                    yb = 4 + idx_h
                    pY = g.pall[0:65, yb * 512:(yb + 1) * 512]
                    pYB = g.pbB(yb)
                    items = []
                    for j in range(4):
                        kti = C * 4 + j
                        items.append((pY, vt_[:, kti * 65: kti * 65 + 65],
                                      Pt[:, j * 512: (j + 1) * 512], j == 0,
                                      C == 0 and j == 3))
                    if C > 0:
                        items.append((pY, SsbT[:, :],
                                      qp[hr:hr + 64, C * 512: (C + 1) * 512], False, True))
                    op_mm_group(g, items, reads=[PtB, vtB, SsbB, qpB], writes=pYB)
                    items = []
                    for j in range(4):
                        kti = C * 4 + j
                        items.append((g.pall[hr:hr + 64, sbank * 512: sbank * 512 + 65],
                                      kt[:, kti * 64:(kti + 1) * 64],
                                      vt_[:, kti * 65:(kti + 1) * 65],
                                      C == 0 and j == 0, C == 3 and j == 3))
                    op_mm_group(g, items, reads=[ktB, vtB], writes=g.pbB(sbank))
                    op_copy(g, SsbT[:, :],
                            g.pall[hr:hr + 64, sbank * 512: sbank * 512 + 65],
                            reads=g.pbB(sbank), writes=[SsbB])
                    sl = idx_h
                    op_copy(g, dtmp[0:1, sl * 512:(sl + 1) * 512],
                            g.pall[64:65, yb * 512:(yb + 1) * 512],
                            reads=pYB, writes=[B(f"p67_dt{sl}")])
                    op_dma(g, "sync", den_all[h_:h_ + 1, C * 512:(C + 1) * 512],
                           dtmp[0:1, sl * 512:(sl + 1) * 512],
                           reads=[B(f"p67_dt{sl}")], writes=[B("p67_denall")])
                    op_act(g, y_fm[hr:hr + 64, hc * T + C * 512: hc * T + (C + 1) * 512],
                           g.pall[0:64, yb * 512:(yb + 1) * 512], AF.Identity,
                           reads=pYB, writes=[B(f"p67_yfm{hc}")])

                for i in range(len(seq)):
                    emit_A(i)
                    emit_evac(i)
                    if i > 0:
                        emit_tail(i - 1)
                emit_tail(len(seq) - 1)
            for seg4 in range(4):
                op_ts(g, den_f32[:, :], den_all[:, seg4 * 512:(seg4 + 1) * 512], 1e-6, None,
                      ALU.add, reads=[B("p67_denall")], writes=[B("p67_denf")])
                op_recip(g, dra[:, seg4 * 512:(seg4 + 1) * 512], den_f32[:, :],
                         reads=[B("p67_denf")], writes=[B("p67_dra")])
            for hc in range(NCH):
                he, ho = 2 * hc, 2 * hc + 1
                for seg in range(2):
                    for (h_, p0, p1_) in ((he, 0, 64), (ho, 64, 128)):
                        op_mm(g, g.pall[p0:p1_, 3072:3584], sel16[:, h_ * 64:(h_ + 1) * 64],
                              dra[:, seg * 1024: seg * 1024 + 512],
                              reads=[B("p67_dra"), B("c_sel16")], writes=g.pbB(6))
                        op_mm(g, g.pall[p0:p1_, 3584:4096], sel16[:, h_ * 64:(h_ + 1) * 64],
                              dra[:, seg * 1024 + 512: seg * 1024 + 1024],
                              reads=[B("p67_dra"), B("c_sel16")], writes=g.pbB(7))
                    op_act(g, rbs[:, :], g.pall[:, 3072:4096], AF.Identity,
                           reads=g.pbB(6, 7), writes=[B("p67_rbs")])
                    op_tt(g, y_fm[:, hc * T + seg * 1024: hc * T + (seg + 1) * 1024],
                          y_fm[:, hc * T + seg * 1024: hc * T + (seg + 1) * 1024],
                          rbs[:, :], ALU.mult,
                          reads=[B(f"p67_yfm{hc}"), B("p67_rbs")], writes=[B(f"p67_yfm{hc}")])
            if r == 0 and "dbg_yfm" in g.t:
                op_dma(g, "sync", g.t["dbg_yfm"][:, :], y_fm[:, :],
                       reads=[B("p67_yfm0")] + [B(f"p67_yfm{c}") for c in range(NCH)],
                       writes=[B("dbg1")])
                op_dma(g, "sync", g.t["dbg_den"][:, :], den_all[:, :],
                       reads=[B("p67_denall")], writes=[B("dbg2")])
            wo_sb = g.sb("p67_wo", [128, NCH * 1024], BF16) if r == 0 else g.t["p67_wo"]
            if r == 0:
                op_dma(g, "sync", wo_sb[:, :], g.t["pk_WoL1"][0:128, :],
                       writes=[B("p67_wo")])
            yfmB = [B(f"p67_yfm{c}") for c in range(NCH)]
            for tb in range(16):
                sl = tb % 2
                xi = xin[:, sl * 1024:(sl + 1) * 1024]
                op_dma(g, "sync", xi, x2h[rb + tb * 128: rb + (tb + 1) * 128, :],
                       writes=[B(f"p67_xin{sl}")], reads=[B("p67_x2hbm")])
                hst = hs[:, sl * 1024:(sl + 1) * 1024]
                for ob in range(2):
                    pq = g.pb(ob)
                    pqB = g.pbB(ob)
                    items = [(pq, y_fm[:, kc * T + tb * 128: kc * T + (tb + 1) * 128],
                              wo_sb[:, kc * 1024 + ob * 512: kc * 1024 + (ob + 1) * 512],
                              kc == 0, kc == NCH - 1) for kc in range(NCH)]
                    op_mm_group(g, items, reads=[B("p67_wo")] + yfmB, writes=pqB)
                    op_tt(g, hst[:, ob * 512:(ob + 1) * 512], pq, xi[:, ob * 512:(ob + 1) * 512],
                          ALU.add, reads=pqB + [B(f"p67_xin{sl}")], writes=[B(f"p67_hs{sl}")])
                emit_ln_apply(g, hst, sc, stt_, hst,
                              f"p67_hs{sl}", "p67_sc", "p67_st", f"p67_hs{sl}")
                op_dma(g, "sync", x3h[rb + tb * 128: rb + (tb + 1) * 128, :], hst,
                       reads=[B(f"p67_hs{sl}")], writes=[B("p67_x3hbm")])
                for c in range(NCH):
                    po = g.pall[:, 7 * 512 + (c % 4) * 128: 7 * 512 + (c % 4 + 1) * 128]
                    op_transpose(g, po, hst[:, c * 128:(c + 1) * 128], ident[:, :],
                                 reads=[B(f"p67_hs{sl}"), B("c_ident")], writes=g.pbB(7))
                    op_copy(g, sfm[:, c * T + tb * 128: c * T + (tb + 1) * 128], po,
                            reads=g.pbB(7), writes=[sfmB[c]])
            for tt in range(4):
                ts0 = tt * 512
                items = [(g.pb(5, 0, 8), rprime[:, kc * 8:(kc + 1) * 8],
                          sfm[:, kc * T + ts0: kc * T + ts0 + 512],
                          kc == 0, kc == NCH - 1) for kc in range(NCH)]
                op_mm_group(g, items, reads=[B("p8_rp")] + sfmB, writes=g.pbB(5))
                gun = hs[:, 0:512]
                op_ts(g, gun[0:8, :], g.pb(5, 0, 8), bpcol[:, 0:1], None, ALU.add,
                      reads=g.pbB(5) + [B("p8_bp2")], writes=[B("p67_hs0")])
                op_act(g, gun[0:8, :], gun[0:8, :], AF.Exp,
                       reads=[B("p67_hs0")], writes=[B("p67_hs0")])
                op_copy(g, rbs[0:8, 0:512], gun[0:8, :], reads=[B("p67_hs0")], writes=[B("p67_rbs")])
                op_mm(g, g.pb(5, 0, 1), ones[0:8, 0:1], rbs[0:8, 0:512],
                      reads=[B("p67_rbs"), B("c_ones_col")], writes=g.pbB(5))
                op_recip(g, dtmp[0:1, 0:128].bitcast(F32) if False else u1[0:1, 0:512],
                         g.pb(5, 0, 1), reads=g.pbB(5), writes=[B("p67_u1")])
                op_copy(g, rbs[0:1, 512:1024], u1[0:1, 0:512],
                        reads=[B("p67_u1")], writes=[B("p67_rbs")])
                op_mm(g, g.pb(5, 0, 8), onesr[0:1, 0:8], rbs[0:1, 512:1024],
                      reads=[B("p67_rbs"), B("c_ones_row")], writes=g.pbB(5))
                gat = sc[0:8, 0:512]
                op_tt(g, gat, gun[0:8, :], g.pb(5, 0, 8), ALU.mult,
                      reads=[B("p67_hs0")] + g.pbB(5), writes=[B("p67_sc")])
                gatbf = rbs[0:8, 0:512]
                op_copy(g, gatbf, gat, reads=[B("p67_sc")], writes=[B("p67_rbs")])
                items = [(g.pb(5, 0, 8), laA[:, kc * 8:(kc + 1) * 8],
                          sfm[:, kc * T + ts0: kc * T + ts0 + 512],
                          kc == 0, kc == NCH - 1) for kc in range(NCH)]
                op_mm_group(g, items, reads=[B("p8_laA")] + sfmB, writes=g.pbB(5))
                op_copy(g, rbs[0:8, 512:1024], g.pb(5, 0, 8), reads=g.pbB(5), writes=[B("p67_rbs")])
                op_mm(g, g.pb(5, 0, 64), g.t["c_p8"][:, :], rbs[0:8, 512:1024],
                      reads=[B("p67_rbs"), B("c_p8")], writes=g.pbB(5))
                op_mm(g, g.pb(6, 0, 64), g.t["c_q8"][:, :], gatbf,
                      reads=[B("p67_rbs"), B("c_q8")], writes=g.pbB(6))
                op_copy(g, hs[0:64, 0:512], g.pb(5, 0, 64), reads=g.pbB(5), writes=[B("p67_hs0")])
                mall = rbs[0:64, 0:512]
                op_tt(g, mall, hs[0:64, 0:512], g.pb(6, 0, 64), ALU.mult,
                      reads=[B("p67_hs0")] + g.pbB(6), writes=[B("p67_rbs")])
                for tb in range(4):
                    gt_ = tt * 4 + tb
                    sl = tb % 2
                    for ob in range(2):
                        pq = g.pb(sl * 2 + ob)
                        pqB = g.pbB(sl * 2 + ob)
                        items = [(pq, sfm[:, kc * T + gt_ * 128: kc * T + (gt_ + 1) * 128],
                                  wti[:, kc * 1024 + ob * 512: kc * 1024 + (ob + 1) * 512],
                                  kc == 0, False) for kc in range(NCH)]
                        items.append((pq, mall[:, tb * 128:(tb + 1) * 128],
                                      ballB[:, ob * 512:(ob + 1) * 512], False, True))
                        op_mm_group(g, items,
                                    reads=[B("p8_wti"), B("p67_rbs"), B("p8_ballB")] + sfmB,
                                    writes=pqB)
                        op_tt(g, xin[:, sl * 1024 + ob * 512: sl * 1024 + (ob + 1) * 512],
                              pq, bbb[:, ob * 512:(ob + 1) * 512], ALU.add,
                              reads=pqB + [B("p8_bb")], writes=[B(f"p67_xin{sl}")])
                    xt_ = xin[:, sl * 1024:(sl + 1) * 1024]
                    emit_ln_apply(g, xt_, sc, stt_, xt_,
                                  f"p67_xin{sl}", "p67_sc", "p67_st", f"p67_xin{sl}")
                    op_dma(g, "sync", outh[rb + gt_ * 128: rb + (gt_ + 1) * 128, :], xt_,
                           reads=[B(f"p67_xin{sl}")], writes=[B("out_hbm")])
        g.stack = sv
    g.s.fence()


def phase3_allreduce(g):
    def fn(e):
        return e.collective_compute(
            "AllReduce", ALU.add,
            replica_groups=[list(range(NCORES))],
            ins=[g.t["partials"].opt()],
            outs=[g.t["partials_all"].opt()])
    g.s.op("gpsimd", fn, reads=[g.s.buf("p2_parthbm")],
           writes=[g.s.buf("partials_all_hbm")])
    g.s.fence()



W_P1 = ["L0_attn_w_qkv", "L0_attn_ln_w", "L0_attn_w_proj"]
W_P2 = ["L0_ffn_ln1_1", "L0_ffn_wq1", "L0_ffn_wk1", "L0_ffn_wv1", "L0_ffn_wo1",
        "L0_ffn_ln2_1", "L0_ffn_wg1", "L0_ffn_wu1", "L0_ffn_wd1",
        "L0_ffn_prior_norm_w", "L0_ffn_prior_w"]
W_P5 = ["L0_ffn_ln1_2", "L0_ffn_wq2", "L0_ffn_wk2", "L0_ffn_wv2", "L0_ffn_wo2",
        "L0_ffn_ln2_2", "L0_ffn_wg2", "L0_ffn_wu2", "L0_ffn_wd2"]
W_P678 = ["L1_attn_Wq", "L1_attn_Wk", "L1_attn_Wv", "L1_attn_Wo",
          "L1_ffn_base_w", "L1_ffn_base_b", "L1_ffn_lora_A", "L1_ffn_lora_B",
          "L1_ffn_router_w", "L1_ffn_router_b"]


def _run(nc, cin, per_core, shared, outs):
    in_maps = []
    for c in range(NCORES):
        m = dict(shared)
        m.update(cin)
        for k, v in per_core.items():
            m[k] = v[c]
        in_maps.append(m)
    res = run_bass_kernel_spmd(nc, in_maps, core_ids=list(range(NCORES)))
    return {k: [res.results[c][k] for c in range(NCORES)] for k in outs}


def kernel(**inputs):
    x = np.ascontiguousarray(inputs["L0_attn_x"].reshape(NCORES, ROWS * T, D))

    decls = [("in", "x0", [ROWS * T, D], F32), ("out", "x1", [ROWS * T, D], F32)]
    decls += [("in", n, list(inputs[n].shape), F32) for n in W_P1]
    nc, cin = build_program(decls, P1_CONSTS, [phase1])
    o1 = _run(nc, cin, {"x0": x}, {n: inputs[n] for n in W_P1}, ["x1"])

    decls = [("in", "x1", [ROWS * T, D], F32),
             ("out", "processed", [ROWS * T, D], F32),
             ("out", "d_st", [ROWS * 128, 16], F32), ("out", "d_ch", [ROWS * 128, 16], F32),
             ("out", "partials", [ROWS, 2], F32),
             ("tmp", "p2_ml", [ROWS * T, 2 * D], F32), ("tmp", "p2_xa", [ROWS * T, D], F32)]
    decls += [("in", n, list(inputs[n].shape), F32) for n in W_P2]
    nc, cin = build_program(decls, P2_CONSTS, [phase2])
    o2 = _run(nc, cin, {"x1": o1["x1"]}, {n: inputs[n] for n in W_P2},
              ["processed", "d_st", "d_ch", "partials"])

    partials_all = np.ascontiguousarray(np.stack(o2["partials"]).reshape(NCORES * ROWS, 2))

    decls = [("in", "d_st", [ROWS * 128, 16], F32), ("in", "d_ch", [ROWS * 128, 16], F32),
             ("in", "partials_all", [NCORES * ROWS, 2], F32),
             ("out", "idsco", [ROWS * KSEL, 2], F32)]
    nc, cin = build_program(decls, P4_CONSTS, [phase4], dma_scratch=48 * 1024)
    o4 = _run(nc, cin, {"d_st": o2["d_st"], "d_ch": o2["d_ch"]},
              {"partials_all": partials_all}, ["idsco"])

    decls = [("in", "processed", [ROWS * T, D], F32),
             ("in", "idsco", [ROWS * KSEL, 2], F32),
             ("out", "processed2", [ROWS * T, D], F32)]
    decls += [("in", n, list(inputs[n].shape), F32) for n in W_P5]
    nc, cin = build_program(decls, P5_CONSTS, [phase5], dma_scratch=28 * 1024)
    o5 = _run(nc, cin, {"processed": o2["processed"], "idsco": o4["idsco"]},
              {n: inputs[n] for n in W_P5}, ["processed2"])

    decls = [("in", "x1", [ROWS * T, D], F32), ("in", "processed2", [ROWS * T, D], F32),
             ("out", "out", [ROWS * T, D], F32),
             ("tmp", "p67_x2", [ROWS * T, D], F32), ("tmp", "p67_x3", [ROWS * T, D], F32)]
    decls += [("in", n, list(inputs[n].shape), F32) for n in W_P678]
    nc, cin = build_program(decls, P678_CONSTS, [phase678])
    o6 = _run(nc, cin, {"x1": o1["x1"], "processed2": o5["processed2"]},
              {n: inputs[n] for n in W_P678}, ["out"])

    return np.concatenate([o6["out"][c].reshape(ROWS, T, D) for c in range(NCORES)], 0)


ALL_CONSTS = sorted(set(P1_CONSTS + P2_CONSTS + P4_CONSTS + P5_CONSTS + P678_CONSTS))
ALL_W = W_P1 + W_P2 + W_P5 + W_P678
W_SHAPES = {
    "L0_attn_w_qkv": [1024, 3072], "L0_attn_ln_w": [1024], "L0_attn_w_proj": [1024, 1024],
    "L0_ffn_ln1_1": [1024], "L0_ffn_wq1": [1024, 1024], "L0_ffn_wk1": [1024, 1024],
    "L0_ffn_wv1": [1024, 1024], "L0_ffn_wo1": [1024, 1024], "L0_ffn_ln2_1": [1024],
    "L0_ffn_wg1": [1024, 4096], "L0_ffn_wu1": [1024, 4096], "L0_ffn_wd1": [4096, 1024],
    "L0_ffn_prior_norm_w": [1024], "L0_ffn_prior_w": [1024, 2048],
    "L0_ffn_ln1_2": [1024], "L0_ffn_wq2": [1024, 1024], "L0_ffn_wk2": [1024, 1024],
    "L0_ffn_wv2": [1024, 1024], "L0_ffn_wo2": [1024, 1024], "L0_ffn_ln2_2": [1024],
    "L0_ffn_wg2": [1024, 4096], "L0_ffn_wu2": [1024, 4096], "L0_ffn_wd2": [4096, 1024],
    "L1_attn_Wq": [1024, 1024], "L1_attn_Wk": [1024, 1024], "L1_attn_Wv": [1024, 1024],
    "L1_attn_Wo": [1024, 1024],
    "L1_ffn_base_w": [1024, 1024], "L1_ffn_base_b": [1024], "L1_ffn_lora_A": [1024, 8],
    "L1_ffn_lora_B": [8, 8, 1024], "L1_ffn_router_w": [8, 1024], "L1_ffn_router_b": [8],
}


def kernel(**inputs):
    x = np.ascontiguousarray(inputs["L0_attn_x"].reshape(NCORES, ROWS * T, D))
    decls = [("in", "x0", [ROWS * T, D], F32), ("out", "out", [ROWS * T, D], F32),
             ("tmp", "x1", [ROWS * T, D], F32),
             ("tmp", "processed", [ROWS * T, D], F32),
             ("tmp", "processed2", [ROWS * T, D], F32),
             ("tmp", "d_st", [ROWS * 128, 16], F32), ("tmp", "d_ch", [ROWS * 128, 16], F32),
             ("tmp", "partials", [ROWS, 2], F32), ("tmp", "partials_all", [ROWS, 2], F32),
             ("tmp", "idsco", [ROWS * KSEL, 2], F32),
             ("tmp", "p2_ml", [ROWS * T, 2 * D], F32), ("tmp", "p2_xa", [ROWS * T, D], F32),
             ("tmp", "p67_x2", [ROWS * T, D], F32), ("tmp", "p67_x3", [ROWS * T, D], F32)]
    decls += [("in", n, W_SHAPES[n], F32) for n in ALL_W]
    nc, cin = build_program(decls, ALL_CONSTS,
                            [phase0_precast, phase1, phase2, phase3_allreduce,
                             phase4, phase5, phase678],
                            dma_scratch=20 * 1024)
    shared = {n: inputs[n] for n in ALL_W}
    o = _run(nc, cin, {"x0": x}, shared, ["out"])
    return np.concatenate([o["out"][c].reshape(ROWS, T, D) for c in range(NCORES)], 0)
